# version 03
# baseline (speedup 1.0000x reference)
import contextlib
import numpy as np
import ml_dtypes
import concourse.bass as bass
import concourse.mybir as mybir
from concourse.bass_utils import run_bass_kernel_spmd

F32 = mybir.dt.float32
BF16 = mybir.dt.bfloat16
I32 = mybir.dt.int32
U8 = mybir.dt.uint8
AF = mybir.ActivationFunctionType
ALU = mybir.AluOpType
AX = mybir.AxisListType

D = 2048
NMLP = 16384
NCH = D // 128

GELU_C1 = 0.7978845608028654
GELU_C2 = GELU_C1 * 0.044715
BF = ml_dtypes.bfloat16



def _bytes_lo(packed_i32):
    a = np.ascontiguousarray(packed_i32)
    return np.ascontiguousarray(a.view(np.uint8).reshape(a.shape[0], a.shape[1], 4)[:, :, 0])


def _nib_planes(bT):
    m = np.empty((bT.shape[0] * 2, bT.shape[1]), np.uint8)
    s = np.empty_like(m)
    m[0::2] = bT & np.uint8(7)
    m[1::2] = (bT >> 4) & np.uint8(7)
    s[0::2] = (bT >> 3) & np.uint8(1)
    s[1::2] = (bT >> 7) & np.uint8(1)
    return np.ascontiguousarray(m), np.ascontiguousarray(s)


def prep_l0_ffn_host(gp, gs, up, us, dp, ds):
    out = {}
    for mn, sn, p in (("gmp", "gsp", gp), ("ump", "usp", up)):
        b = _bytes_lo(p)
        out[mn], out[sn] = _nib_planes(np.ascontiguousarray(b.T))
    db = _bytes_lo(dp)
    out["dmp"], out["dsp"] = _nib_planes(np.ascontiguousarray(db.T))
    out["g_scT"] = np.ascontiguousarray(gs.T)
    out["u_scT"] = np.ascontiguousarray(us.T)
    out["d_scT"] = np.ascontiguousarray(ds.T)
    return out


def host_consts():
    c = {}
    sel = np.zeros((16, 64, 128), dtype=np.float32)
    for j in range(16):
        for p in range(128):
            sel[j, 4 * j + p // 32, p] = 1.0
    c["sel16"] = np.ascontiguousarray(sel.astype(BF).reshape(16 * 64, 128))
    c["nibcols_f"] = np.concatenate([
        np.full((128, 1), -4.0),
        np.full((128, 1), -16.0),
        np.full((128, 1), 0.5),
    ], axis=1).astype(np.float32)
    c["onescol_bf"] = np.ones((128, 1), dtype=BF)
    c["onesrow_f32"] = np.ones((1, 128), dtype=np.float32)
    return c


class Sem:
    def __init__(self, es, nc, name):
        self.h = es.enter_context(nc.semaphore(name))
        self.n = 0


class ChainState:
    def __init__(self, es, nc, name):
        self.sem = Sem(es, nc, name)
        self.last = None


_CHAIN_PASSTHROUGH = {"wait_ge", "dma_start", "nop", "sem_inc", "memzero"}


class Chain:
    def __init__(self, eng, st):
        object.__setattr__(self, "eng", eng)
        object.__setattr__(self, "st", st)

    def __getattr__(self, name):
        target = getattr(self.eng, name)
        if name in _CHAIN_PASSTHROUGH or not callable(target):
            return target

        st = self.st
        eng = self.eng

        def wrapped(*a, _inc=None, **kw):
            if st.last is not None:
                eng.wait_ge(*st.last)
            ins = target(*a, **kw)
            if _inc is not None:
                _inc.n += 1
                ins.then_inc(_inc.h, 1)
                st.last = (_inc.h, _inc.n)
            else:
                st.sem.n += 1
                ins.then_inc(st.sem.h, 1)
                st.last = (st.sem.h, st.sem.n)
            return ins

        return wrapped



def emit_layernorm(nc, es, T, prefix, x_in_r, x_ln, cst, ch):
    TT = 512
    NTT = T // TT
    with contextlib.ExitStack() as ln_es:
        xs = [ln_es.enter_context(nc.sbuf_tensor(prefix + f"lnx{i}", [128, NCH * TT], F32)) for i in range(2)]
        xbf = [ln_es.enter_context(nc.sbuf_tensor(prefix + f"lnb{i}", [128, NCH * TT], BF16)) for i in range(2)]
        sq = ln_es.enter_context(nc.sbuf_tensor(prefix + "lnsq", [128, NCH * TT], BF16))
        rows = ln_es.enter_context(nc.sbuf_tensor(prefix + "lnrw", [1, 6 * TT], F32))
        tmp = ln_es.enter_context(nc.sbuf_tensor(prefix + "lntmp", [128, TT], F32))
        ps_sum = ln_es.enter_context(nc.psum_tensor(prefix + "ps_sum", [1, TT], F32))
        ps_sq = ln_es.enter_context(nc.psum_tensor(prefix + "ps_sq", [1, TT], F32))
        ps_br = ln_es.enter_context(nc.psum_tensor(prefix + "ps_br", [128, TT], F32))
        ps_bm = ln_es.enter_context(nc.psum_tensor(prefix + "ps_bm", [128, TT], F32))

        ldx = Sem(es, nc, prefix + "ln_ldx")
        ldb = Sem(es, nc, prefix + "ln_ldb")
        sqd = Sem(es, nc, prefix + "ln_sqd")
        mmd = Sem(es, nc, prefix + "ln_mmd")
        rwd = Sem(es, nc, prefix + "ln_rwd")
        act = Sem(es, nc, prefix + "ln_act")
        apd = Sem(es, nc, prefix + "ln_apd")

        with nc.Block() as blk:
            @blk.sync
            def _(sync):
                for i in range(NTT):
                    b = i % 2
                    sync.wait_ge(ldx.h, 16 * i)
                    if i >= 2:
                        sync.wait_ge(apd.h, i - 1)
                    sync.dma_start(out=xs[b][:].rearrange("p (c t) -> p c t", t=TT),
                                   in_=x_in_r[:, :, i * TT:(i + 1) * TT]).then_inc(ldx.h, 16)

            @blk.gpsimd
            def _(g):
                for i in range(NTT):
                    b = i % 2
                    g.wait_ge(ldb.h, 16 * i)
                    if i >= 2:
                        g.wait_ge(apd.h, i - 1)
                    g.dma_start(out=xbf[b][:].rearrange("p (c t) -> p c t", t=TT),
                                in_=x_in_r[:, :, i * TT:(i + 1) * TT]).then_inc(ldb.h, 16)

            @blk.scalar
            def _(sc):
                sc = Chain(sc, ch["s"])
                for i in range(NTT):
                    b = i % 2
                    sc.wait_ge(ldb.h, 16 * (i + 1))
                    if i >= 1:
                        sc.wait_ge(mmd.h, 3 * i - 2)
                    sc.activation(sq[:], xbf[b][:], AF.Square, _inc=sqd)
                    sc.wait_ge(rwd.h, 2 * i + 1)
                    sc.activation(rows[0:1, 4 * TT:5 * TT], rows[0:1, 3 * TT:4 * TT], AF.Ln)
                    sc.activation(rows[0:1, 4 * TT:5 * TT], rows[0:1, 4 * TT:5 * TT], AF.Exp,
                                  scale=-0.5, _inc=act)

            @blk.tensor
            def _(te):
                for i in range(NTT):
                    b = i % 2
                    te.wait_ge(ldx.h, 16 * (i + 1))
                    te.wait_ge(sqd.h, i + 1)
                    if i >= 1:
                        te.wait_ge(rwd.h, 2 * i - 1)
                    for c in range(NCH):
                        te.matmul(ps_sum[:], cst["onescol_bf"][:], xbf[b][:, c * TT:(c + 1) * TT],
                                  start=(c == 0), stop=(c == NCH - 1))
                    for c in range(NCH):
                        mm = te.matmul(ps_sq[:], cst["onescol_bf"][:], sq[:, c * TT:(c + 1) * TT],
                                       start=(c == 0), stop=(c == NCH - 1))
                    mm.then_inc(mmd.h, 1)
                    te.wait_ge(rwd.h, 2 * i + 2)
                    if i >= 1:
                        te.wait_ge(apd.h, i)
                    te.matmul(ps_br[:], cst["onesrow_f32"][:], rows[0:1, 4 * TT:5 * TT],
                              start=True, stop=True).then_inc(mmd.h, 1)
                    te.matmul(ps_bm[:], cst["onesrow_f32"][:], rows[0:1, 5 * TT:6 * TT],
                              start=True, stop=True).then_inc(mmd.h, 1)

            @blk.vector
            def _(v):
                v = Chain(v, ch["v"])
                for i in range(NTT):
                    b = i % 2
                    v.wait_ge(mmd.h, 3 * i + 1)
                    v.tensor_scalar(rows[0:1, 0:TT], ps_sum[:], 1.0 / D, None, op0=ALU.mult)
                    v.tensor_scalar(rows[0:1, TT:2 * TT], ps_sq[:], 1.0 / D, 1e-5,
                                    op0=ALU.mult, op1=ALU.add)
                    v.tensor_mul(rows[0:1, 2 * TT:3 * TT], rows[0:1, 0:TT], rows[0:1, 0:TT])
                    v.scalar_tensor_tensor(rows[0:1, 3 * TT:4 * TT], rows[0:1, 2 * TT:3 * TT], -1.0,
                                           rows[0:1, TT:2 * TT], op0=ALU.mult,
                                           op1=ALU.add, _inc=rwd)
                    v.wait_ge(act.h, i + 1)
                    v.scalar_tensor_tensor(rows[0:1, 5 * TT:6 * TT], rows[0:1, 0:TT], -1.0,
                                           rows[0:1, 4 * TT:5 * TT], op0=ALU.mult,
                                           op1=ALU.mult, _inc=rwd)
                    v.wait_ge(mmd.h, 3 * i + 3)
                    for c in range(NCH):
                        v.tensor_mul(tmp[:], xs[b][:, c * TT:(c + 1) * TT], ps_br[:])
                        v.tensor_add(x_ln[:, c * T + i * TT:c * T + (i + 1) * TT], tmp[:], ps_bm[:],
                                     _inc=apd if c == NCH - 1 else None)



def emit_l0_ffn_body(nc, es, T, px, x_in, x_out, acc_d, dram, ch=None, bufes=None):
    TT = 512
    NTT = T // TT
    SUP = 1024
    SCH = SUP // 128
    NSUP = NMLP // SUP
    NC = NMLP // 128

    if bufes is None:
        bufes = es
    cst = {}
    cst["sel"] = bufes.enter_context(nc.sbuf_tensor(px + "sel", [64, 16 * 128], BF16))
    cst["nibf"] = bufes.enter_context(nc.sbuf_tensor(px + "nibf", [128, 3], F32))
    cst["onescol_bf"] = bufes.enter_context(nc.sbuf_tensor(px + "onesc", [128, 1], BF16))
    cst["onesrow_f32"] = bufes.enter_context(nc.sbuf_tensor(px + "onesr", [1, 128], F32))
    x_ln = bufes.enter_context(nc.sbuf_tensor(px + "x_ln", [128, NCH * T], BF16))

    ld0 = Sem(es, nc, px + "ld0")
    if ch is None:
        ch = {"v": ChainState(es, nc, px + "chv"), "s": ChainState(es, nc, px + "chs")}
    with nc.Block() as blk0:
        @blk0.sync
        def _(sync):
            sync.dma_start(out=cst["sel"][:].rearrange("r (j p) -> r j p", p=128),
                           in_=dram["sel16"].rearrange("(j r) p -> r j p", r=64)).then_inc(ld0.h, 16)
            sync.dma_start(out=cst["nibf"][:], in_=dram["nibcols_f"]).then_inc(ld0.h, 16)
            sync.dma_start(out=cst["onescol_bf"][:], in_=dram["onescol_bf"]).then_inc(ld0.h, 16)
            sync.dma_start(out=cst["onesrow_f32"][:], in_=dram["onesrow_f32"]).then_inc(ld0.h, 16)

        @blk0.tensor
        def _(te):
            te.wait_ge(ld0.h, 64)

        @blk0.vector
        def _(v):
            v.wait_ge(ld0.h, 64)

        @blk0.gpsimd
        def _(g):
            g.wait_ge(ld0.h, 64)

        @blk0.scalar
        def _(s):
            s.wait_ge(ld0.h, 64)

    x_in_r = x_in.rearrange("(c p) t -> p c t", p=128)
    x_out_r = x_out.rearrange("(c p) t -> p c t", p=128)
    acc_r = acc_d.rearrange("(c p) t -> p c t", p=128)

    emit_layernorm(nc, es, T, px, x_in_r, x_ln, cst, ch)

    with contextlib.ExitStack() as fes:
        wg = [fes.enter_context(nc.sbuf_tensor(px + f"wg{i}", [128, NCH * 128], BF16)) for i in range(2)]
        wu = [fes.enter_context(nc.sbuf_tensor(px + f"wu{i}", [128, NCH * 128], BF16)) for i in range(2)]
        gmb = [fes.enter_context(nc.sbuf_tensor(px + f"gmb{i}", [128, NCH * 128], U8)) for i in range(2)]
        gsb = [fes.enter_context(nc.sbuf_tensor(px + f"gsb{i}", [128, NCH * 128], U8)) for i in range(2)]
        umb = [fes.enter_context(nc.sbuf_tensor(px + f"umb{i}", [128, NCH * 128], U8)) for i in range(2)]
        usb = [fes.enter_context(nc.sbuf_tensor(px + f"usb{i}", [128, NCH * 128], U8)) for i in range(2)]
        dmb = fes.enter_context(nc.sbuf_tensor(px + "dmb", [128, D], U8))
        dsb = fes.enter_context(nc.sbuf_tensor(px + "dsb", [128, D], U8))
        gsc = [fes.enter_context(nc.sbuf_tensor(px + f"gsc{i}", [64, 128], BF16)) for i in range(2)]
        usc = [fes.enter_context(nc.sbuf_tensor(px + f"usc{i}", [64, 128], BF16)) for i in range(2)]
        wd = fes.enter_context(nc.sbuf_tensor(px + "wd", [128, SCH * D], BF16))
        dsc = fes.enter_context(nc.sbuf_tensor(px + "dsc", [64, D], BF16))
        inter = fes.enter_context(nc.sbuf_tensor(px + "inter", [128, SCH * NTT * TT], BF16))
        scC = [fes.enter_context(nc.sbuf_tensor(px + f"scC{i}", [128, 1024], F32)) for i in range(2)]
        scD = [fes.enter_context(nc.sbuf_tensor(px + f"scD{i}", [128, 1024], F32)) for i in range(2)]
        scF = [fes.enter_context(nc.sbuf_tensor(px + f"scF{i}", [128, 1024], F32)) for i in range(2)]
        gel = [fes.enter_context(nc.sbuf_tensor(px + f"gel{i}", [128, TT], F32)) for i in range(2)]
        accs = [fes.enter_context(nc.sbuf_tensor(px + f"accs{i}", [128, TT], F32)) for i in range(2)]
        xres = fes.enter_context(nc.sbuf_tensor(px + "xres", [128, TT], F32))
        pg = [fes.enter_context(nc.psum_tensor(px + f"pg{i}", [128, TT], F32)) for i in range(2)]
        pu = [fes.enter_context(nc.psum_tensor(px + f"pu{i}", [128, TT], F32)) for i in range(2)]
        pbg = [fes.enter_context(nc.psum_tensor(px + f"pbg{i}", [128, 512], F32)) for i in range(2)]
        pd = [fes.enter_context(nc.psum_tensor(px + f"pd{i}", [128, 512], F32)) for i in range(2)]

        s_ad = Sem(es, nc, px + "s_ad")
        s_sb = Sem(es, nc, px + "s_sb")
        s_bmg = Sem(es, nc, px + "s_bmg")
        s_bmu = Sem(es, nc, px + "s_bmu")
        s_bmd = Sem(es, nc, px + "s_bmd")
        s_bsg = Sem(es, nc, px + "s_bsg")
        s_bsu = Sem(es, nc, px + "s_bsu")
        s_bsd = Sem(es, nc, px + "s_bsd")
        s_scg = Sem(es, nc, px + "s_scg")
        s_scu = Sem(es, nc, px + "s_scu")
        s_scd = Sem(es, nc, px + "s_scd")
        s_bg = Sem(es, nc, px + "s_bg")
        s_bgc = Sem(es, nc, px + "s_bgc")
        s_bd = Sem(es, nc, px + "s_bd")
        s_bdc = Sem(es, nc, px + "s_bdc")
        s_mm = Sem(es, nc, px + "s_mm")
        s_gact = Sem(es, nc, px + "s_gact")
        s_gtan = Sem(es, nc, px + "s_gtan")
        s_gdve = Sem(es, nc, px + "s_gdve")
        s_gz = Sem(es, nc, px + "s_gz")
        s_int = Sem(es, nc, px + "s_int")
        s_dmm = Sem(es, nc, px + "s_dmm")
        s_accl = Sem(es, nc, px + "s_accl")
        s_xr = Sem(es, nc, px + "s_xr")
        s_ev = Sem(es, nc, px + "s_ev")
        s_st = Sem(es, nc, px + "s_st")

        gm_r = dram["gmp"].rearrange("(c p) n -> p c n", p=128)
        gs_r = dram["gsp"].rearrange("(c p) n -> p c n", p=128)
        um_r = dram["ump"].rearrange("(c p) n -> p c n", p=128)
        us_r = dram["usp"].rearrange("(c p) n -> p c n", p=128)
        dm_r = dram["dmp"]
        ds_r = dram["dsp"]

        def kof(ncg, tt):
            return ncg * NTT + tt

        def mslab(ncg, slot):
            mat, h = slot // 2, slot % 2
            buf = dmb if mat == 2 else (gmb, umb)[mat][ncg % 2]
            return buf[:, h * 1024:(h + 1) * 1024]

        def sslab(ncg, slot):
            mat, h = slot // 2, slot % 2
            buf = dsb if mat == 2 else (gsb, usb)[mat][ncg % 2]
            return buf[:, h * 1024:(h + 1) * 1024]

        def ad_base(ncg, slot):
            return 18 * ncg + 3 * slot

        def decode_act(sc, ncg, slot):
            nf = cst["nibf"]
            m_ap = mslab(ncg, slot)
            h = slot % 2
            C, Dd, F = scC[h], scD[h], scF[h]
            if slot < 2:
                if ncg >= 1:
                    sc.wait_ge(s_bdc.h, 4 * (ncg - 1) + 2 * (h + 1))
            elif slot < 4:
                sc.wait_ge(s_bgc.h, 8 * ncg + 2 * (h + 1))
            else:
                sc.wait_ge(s_bgc.h, 8 * ncg + 2 * (3 + h))
            sc.activation(C[:], m_ap, AF.Identity, scale=2.0, bias=nf[:, 0:1], _inc=s_ad)
            sc.activation(Dd[:], m_ap, AF.Identity, scale=4.0, bias=nf[:, 1:2], _inc=s_ad)
            sc.activation(F[:], sslab(ncg, slot), AF.Identity, scale=-1.0,
                          bias=nf[:, 2:3], _inc=s_ad)

        def decode_dve(v, ncg, slot):
            mat, h = slot // 2, slot % 2
            b = ncg % 2
            C, Dd, F = scC[slot % 2], scD[slot % 2], scF[slot % 2]
            v.wait_ge(s_ad.h, ad_base(ncg, slot) + 2)
            v.tensor_tensor(C[:], C[:], Dd[:], ALU.max)
            v.scalar_tensor_tensor(C[:], mslab(ncg, slot), 1.0, C[:],
                                   op0=ALU.mult, op1=ALU.max, _inc=s_sb)
            v.wait_ge(s_ad.h, ad_base(ncg, slot) + 3)
            v.tensor_mul(F[:], F[:], C[:])
            for i in range(2):
                if mat < 2:
                    qq = 8 * ncg + 2 * slot + i
                    v.wait_ge(s_bg.h, qq + 1)
                    wdec = wg[b] if mat == 0 else wu[b]
                    dst = wdec[:, h * 1024 + i * 512:h * 1024 + (i + 1) * 512]
                    v.tensor_mul(dst, F[:, i * 512:(i + 1) * 512], pbg[qq % 2][:],
                                 _inc=s_bgc)
                else:
                    dq = 4 * ncg + 2 * h + i
                    v.wait_ge(s_bd.h, dq + 1)
                    cl = ncg % SCH
                    dst = wd[:, cl * D + (2 * h + i) * 512:cl * D + (2 * h + i + 1) * 512]
                    v.tensor_mul(dst, F[:, i * 512:(i + 1) * 512], pd[dq % 2][:],
                                 _inc=s_bdc)

        with nc.Block() as blk:
            @blk.sync
            def _(sync):
                nacc = 0

                def accflow(sup):
                    nonlocal nacc
                    for tt in range(NTT):
                        for dm in range(NCH):
                            k = (sup * NTT + tt) * NCH + dm
                            bb = k % 2
                            sync.wait_ge(s_st.h, 16 * k)
                            if k >= 2:
                                sync.wait_ge(s_ev.h, k - 1)
                            if sup > 0:
                                sync.wait_ge(s_accl.h, 16 * nacc)
                                sync.dma_start(out=accs[bb][:],
                                               in_=acc_r[:, dm, tt * TT:(tt + 1) * TT]).then_inc(s_accl.h, 16)
                                nacc += 1
                            if sup == NSUP - 1:
                                sync.wait_ge(s_xr.h, 16 * (tt * NCH + dm))
                                if tt * NCH + dm >= 1:
                                    sync.wait_ge(s_ev.h, k)
                                sync.dma_start(out=xres[:],
                                               in_=x_in_r[:, dm, tt * TT:(tt + 1) * TT]).then_inc(s_xr.h, 16)
                            sync.wait_ge(s_ev.h, k + 1)
                            dst = x_out_r if sup == NSUP - 1 else acc_r
                            sync.dma_start(out=dst[:, dm, tt * TT:(tt + 1) * TT],
                                           in_=accs[bb][:]).then_inc(s_st.h, 16)

                def gu_bytes(ncg):
                    b = ncg % 2
                    sync.wait_ge(s_bmg.h, 16 * ncg)
                    if ncg >= 2:
                        sync.wait_ge(s_sb.h, 6 * (ncg - 2) + 2)
                    sync.dma_start(out=gmb[b][:].rearrange("p (c w) -> p c w", w=128),
                                   in_=gm_r[:, :, ncg * 128:(ncg + 1) * 128]).then_inc(s_bmg.h, 16)
                    sync.wait_ge(s_bmu.h, 16 * ncg)
                    if ncg >= 2:
                        sync.wait_ge(s_sb.h, 6 * (ncg - 2) + 4)
                    sync.dma_start(out=umb[b][:].rearrange("p (c w) -> p c w", w=128),
                                   in_=um_r[:, :, ncg * 128:(ncg + 1) * 128]).then_inc(s_bmu.h, 16)

                def d_bytes(ncg):
                    sync.wait_ge(s_bmd.h, 16 * ncg)
                    if ncg >= 1:
                        sync.wait_ge(s_sb.h, 6 * ncg)
                    sync.dma_start(out=dmb[:], in_=dm_r[ncg * 128:(ncg + 1) * 128, :]).then_inc(s_bmd.h, 16)

                for sup in range(NSUP):
                    gu_bytes(sup * SCH)
                    gu_bytes(sup * SCH + 1)
                    d_bytes(sup * SCH)
                    if sup >= 1:
                        accflow(sup - 1)
                    d_bytes(sup * SCH + 1)
                    for cl in range(2, SCH):
                        gu_bytes(sup * SCH + cl)
                        d_bytes(sup * SCH + cl)
                accflow(NSUP - 1)
                sync.wait_ge(s_st.h, 16 * NSUP * NTT * NCH)

            @blk.gpsimd
            def _(g):
                for ncg in range(NC):
                    b = ncg % 2
                    g.wait_ge(s_scg.h, 16 * ncg)
                    g.wait_ge(s_scu.h, 16 * ncg)
                    if ncg >= 2:
                        g.wait_ge(s_bg.h, 8 * (ncg - 2) + 8)
                    g.dma_start(out=gsc[b][:], in_=dram["g_scT"][:, ncg * 128:(ncg + 1) * 128]).then_inc(s_scg.h, 16)
                    g.dma_start(out=usc[b][:], in_=dram["u_scT"][:, ncg * 128:(ncg + 1) * 128]).then_inc(s_scu.h, 16)
                    if ncg % 16 == 0:
                        g.wait_ge(s_scd.h, 16 * (ncg // 16))
                        if ncg > 0:
                            g.wait_ge(s_bd.h, 4 * ncg)
                        g.dma_start(out=dsc[:],
                                    in_=dram["d_scT"][(ncg // 16) * 64:(ncg // 16) * 64 + 64, :]).then_inc(s_scd.h, 16)
                    g.wait_ge(s_bsg.h, 16 * ncg)
                    if ncg >= 2:
                        g.wait_ge(s_ad.h, 18 * (ncg - 2) + 6)
                    g.dma_start(out=gsb[b][:].rearrange("p (c w) -> p c w", w=128),
                                in_=gs_r[:, :, ncg * 128:(ncg + 1) * 128]).then_inc(s_bsg.h, 16)
                    g.wait_ge(s_bsu.h, 16 * ncg)
                    if ncg >= 2:
                        g.wait_ge(s_ad.h, 18 * (ncg - 2) + 12)
                    g.dma_start(out=usb[b][:].rearrange("p (c w) -> p c w", w=128),
                                in_=us_r[:, :, ncg * 128:(ncg + 1) * 128]).then_inc(s_bsu.h, 16)
                    g.wait_ge(s_bsd.h, 16 * ncg)
                    if ncg >= 1:
                        g.wait_ge(s_ad.h, 18 * (ncg - 1) + 18)
                    g.dma_start(out=dsb[:], in_=ds_r[ncg * 128:(ncg + 1) * 128, :]).then_inc(s_bsd.h, 16)

            @blk.tensor
            def _(te):
                def gu_bcasts(ncg):
                    b = ncg % 2
                    te.wait_ge(s_scg.h, 16 * (ncg + 1))
                    te.wait_ge(s_scu.h, 16 * (ncg + 1))
                    for slot in range(4):
                        sc_slab = gsc[b] if slot < 2 else usc[b]
                        h = slot % 2
                        for i in range(2):
                            qq = 8 * ncg + 2 * slot + i
                            if qq >= 2:
                                te.wait_ge(s_bgc.h, qq - 1)
                            for j in range(4):
                                dc = 8 * h + 4 * i + j
                                mm = te.matmul(pbg[qq % 2][:, j * 128:(j + 1) * 128],
                                               cst["sel"][:, dc * 128:(dc + 1) * 128],
                                               sc_slab[:], start=True, stop=True)
                            mm.then_inc(s_bg.h, 1)

                def d_bcasts(ncg, ev_wait=None):
                    te.wait_ge(s_scd.h, 16 * (ncg // 16 + 1))
                    if ev_wait is not None:
                        te.wait_ge(s_ev.h, ev_wait)
                    sel_j = cst["sel"][:, (ncg % 16) * 128:(ncg % 16 + 1) * 128]
                    for qd in range(4):
                        dq = 4 * ncg + qd
                        if dq >= 2:
                            te.wait_ge(s_bdc.h, dq - 1)
                        te.matmul(pd[dq % 2][:], sel_j, dsc[:, qd * 512:(qd + 1) * 512],
                                  start=True, stop=True).then_inc(s_bd.h, 1)

                gu_bcasts(0)
                d_bcasts(0)
                for sup in range(NSUP):
                    for cl in range(SCH):
                        ncg = sup * SCH + cl
                        b = ncg % 2
                        if ncg + 1 < NC:
                            gu_bcasts(ncg + 1)
                        te.wait_ge(s_bgc.h, 8 * ncg + 8)
                        for tt in range(NTT):
                            k = kof(ncg, tt)
                            if k >= 2:
                                te.wait_ge(s_gz.h, k - 1)
                                te.wait_ge(s_int.h, k - 1)
                            for dc in range(NCH):
                                mm = te.matmul(pg[k % 2][:], wg[b][:, dc * 128:(dc + 1) * 128],
                                               x_ln[:, dc * T + tt * TT:dc * T + (tt + 1) * TT],
                                               start=(dc == 0), stop=(dc == NCH - 1))
                            mm.then_inc(s_mm.h, 1)
                            for dc in range(NCH):
                                mm = te.matmul(pu[k % 2][:], wu[b][:, dc * 128:(dc + 1) * 128],
                                               x_ln[:, dc * T + tt * TT:dc * T + (tt + 1) * TT],
                                               start=(dc == 0), stop=(dc == NCH - 1))
                            mm.then_inc(s_mm.h, 1)
                        if cl < SCH - 1 and ncg + 1 < NC:
                            d_bcasts(ncg + 1)
                    te.wait_ge(s_bdc.h, 4 * SCH * (sup + 1))
                    for tt in range(NTT):
                        te.wait_ge(s_int.h, kof(sup * SCH + SCH - 1, tt) + 1)
                        for dm in range(NCH):
                            k = (sup * NTT + tt) * NCH + dm
                            if k >= 2:
                                te.wait_ge(s_ev.h, k - 1)
                            for i in range(SCH):
                                mm = te.matmul(pd[k % 2][:], wd[:, i * D + dm * 128:i * D + (dm + 1) * 128],
                                               inter[:, (i * NTT + tt) * TT:(i * NTT + tt + 1) * TT],
                                               start=(i == 0), stop=(i == SCH - 1))
                            mm.then_inc(s_dmm.h, 1)
                    if sup + 1 < NSUP:
                        d_bcasts((sup + 1) * SCH, ev_wait=(sup + 1) * NTT * NCH)

            @blk.vector
            def _(v):
                v = Chain(v, ch["v"])
                prev_k = []

                def tail(pk):
                    v.wait_ge(s_gtan.h, pk + 1)
                    v.scalar_tensor_tensor(gel[pk % 2][:], gel[pk % 2][:], 1.0, pg[pk % 2][:],
                                           op0=ALU.add, op1=ALU.mult, _inc=s_gz)
                    v.wait_ge(s_mm.h, 2 * pk + 2)
                    pcl = (pk // NTT) % SCH
                    v.scalar_tensor_tensor(
                        inter[:, (pcl * NTT + (pk % NTT)) * TT:(pcl * NTT + (pk % NTT) + 1) * TT],
                        gel[pk % 2][:], 0.5, pu[pk % 2][:],
                        op0=ALU.mult, op1=ALU.mult, _inc=s_int)

                def gelu_dve(ncg):
                    for tt in range(NTT):
                        k = kof(ncg, tt)
                        v.wait_ge(s_gact.h, k + 1)
                        v.tensor_scalar(gel[k % 2][:], gel[k % 2][:], GELU_C2, GELU_C1,
                                        op0=ALU.mult, op1=ALU.add)
                        v.tensor_tensor(gel[k % 2][:], gel[k % 2][:], pg[k % 2][:],
                                        ALU.mult, _inc=s_gdve)
                        if prev_k:
                            tail(prev_k.pop(0))
                        prev_k.append(k)

                def evacs(sup):
                    for tt in range(NTT):
                        for dm in range(NCH):
                            k = (sup * NTT + tt) * NCH + dm
                            bb = k % 2
                            v.wait_ge(s_dmm.h, k + 1)
                            if k >= 2:
                                v.wait_ge(s_st.h, 16 * (k - 1))
                            last = sup != NSUP - 1
                            if sup == 0:
                                v.tensor_copy(accs[bb][:], pd[bb][:], _inc=s_ev if last else None)
                            else:
                                v.wait_ge(s_accl.h, 16 * (sup * NTT * NCH + tt * NCH + dm + 1 - NTT * NCH))
                                v.tensor_add(accs[bb][:], accs[bb][:], pd[bb][:],
                                             _inc=s_ev if last else None)
                            if sup == NSUP - 1:
                                v.wait_ge(s_xr.h, 16 * (tt * NCH + dm + 1))
                                v.tensor_add(accs[bb][:], accs[bb][:], xres[:], _inc=s_ev)

                for ncg in range(NC):
                    for slot in range(4):
                        decode_dve(v, ncg, slot)
                    if ncg >= 1:
                        gelu_dve(ncg - 1)
                        if ncg % SCH == 0:
                            while prev_k:
                                tail(prev_k.pop(0))
                            evacs(ncg // SCH - 1)
                    if ncg // SCH > 0:
                        v.wait_ge(s_dmm.h, NTT * NCH * (ncg // SCH))
                    decode_dve(v, ncg, 4)
                    decode_dve(v, ncg, 5)
                gelu_dve(NC - 1)
                while prev_k:
                    tail(prev_k.pop(0))
                evacs(NSUP - 1)

            @blk.scalar
            def _(sc):
                sc = Chain(sc, ch["s"])

                def tanh_op(pk):
                    sc.wait_ge(s_gdve.h, pk + 1)
                    sc.activation(gel[pk % 2][:], gel[pk % 2][:], AF.Tanh, _inc=s_gtan)

                def gelu_act(ncg, skip_first):
                    for tt in range(NTT):
                        k = kof(ncg, tt)
                        if k >= 1 and not (skip_first and tt == 0):
                            tanh_op(k - 1)
                        sc.wait_ge(s_mm.h, 2 * k + 1)
                        if k >= 2:
                            sc.wait_ge(s_gz.h, k - 1)
                        sc.activation(gel[k % 2][:], pg[k % 2][:], AF.Square, _inc=s_gact)

                for ncg in range(NC):
                    if ncg >= 2:
                        tanh_op(kof(ncg - 2, NTT - 1))
                    sc.wait_ge(s_bmg.h, 16 * (ncg + 1))
                    sc.wait_ge(s_bsg.h, 16 * (ncg + 1))
                    decode_act(sc, ncg, 0)
                    decode_act(sc, ncg, 1)
                    sc.wait_ge(s_bmu.h, 16 * (ncg + 1))
                    sc.wait_ge(s_bsu.h, 16 * (ncg + 1))
                    decode_act(sc, ncg, 2)
                    decode_act(sc, ncg, 3)
                    if ncg >= 1:
                        gelu_act(ncg - 1, skip_first=(ncg >= 2))
                    sc.wait_ge(s_bmd.h, 16 * (ncg + 1))
                    sc.wait_ge(s_bsd.h, 16 * (ncg + 1))
                    decode_act(sc, ncg, 4)
                    decode_act(sc, ncg, 5)
                tanh_op(kof(NC - 2, NTT - 1))
                gelu_act(NC - 1, skip_first=True)
                tanh_op(kof(NC - 1, NTT - 1))

    return



def build_l0_ffn(T=2048):
    nc = bass.Bass("TRN2", target_bir_lowering=False)
    ins = {}
    ins["x_in"] = nc.dram_tensor("x_in", [D, T], F32, kind="ExternalInput").ap()
    for nm, shp in (("gmp", [D, NMLP]), ("gsp", [D, NMLP]), ("ump", [D, NMLP]),
                    ("usp", [D, NMLP]), ("dmp", [NMLP, D]), ("dsp", [NMLP, D])):
        ins[nm] = nc.dram_tensor(nm, shp, U8, kind="ExternalInput").ap()
    ins["g_scT"] = nc.dram_tensor("g_scT", [64, NMLP], F32, kind="ExternalInput").ap()
    ins["u_scT"] = nc.dram_tensor("u_scT", [64, NMLP], F32, kind="ExternalInput").ap()
    ins["d_scT"] = nc.dram_tensor("d_scT", [512, D], F32, kind="ExternalInput").ap()
    ins["sel16"] = nc.dram_tensor("sel16", [16 * 64, 128], BF16, kind="ExternalInput").ap()
    ins["nibcols_f"] = nc.dram_tensor("nibcols_f", [128, 3], F32, kind="ExternalInput").ap()
    ins["onescol_bf"] = nc.dram_tensor("onescol_bf", [128, 1], BF16, kind="ExternalInput").ap()
    ins["onesrow_f32"] = nc.dram_tensor("onesrow_f32", [1, 128], F32, kind="ExternalInput").ap()
    x_out = nc.dram_tensor("x_out", [D, T], F32, kind="ExternalOutput").ap()
    acc_d = nc.dram_tensor("acc", [D, T], F32).ap()
    es = contextlib.ExitStack()
    with es:
        emit_l0_ffn_body(nc, es, T, "f_", ins["x_in"], x_out, acc_d,
                         {k: v for k, v in ins.items() if k != "x_in"})
    return nc




L1_SCALE = 128.0 ** -0.5
EXP_SHIFT_MARGIN = 40.0


def prep_l1_attn_host(Wq, Wk, Wv, Wo, TF=4096):
    out = {
        "wqT": np.ascontiguousarray(Wq.T),
        "wkT": np.ascontiguousarray(Wk.T),
        "wvT": np.ascontiguousarray(Wv.T),
        "woT": np.ascontiguousarray(Wo.T),
    }
    inv = 1.0 / 10000.0 ** (np.arange(0, 128, 2) / 128.0)
    t = np.arange(TF, dtype=np.float64)
    emb = np.concatenate([np.outer(inv, t), np.outer(inv, t)], axis=0)
    out["ropeC"] = np.cos(emb).astype(np.float32)
    out["ropeS"] = np.sin(emb).astype(np.float32)
    Rt = np.zeros((128, 128), dtype=np.float32)
    for i in range(64):
        Rt[i + 64, i] = -1.0
        Rt[i, i + 64] = 1.0
    out["rot128"] = Rt.astype(BF)
    return out


def emit_l1_attn_body(nc, es, TF, TO, half, px, x_in, x_out, dram, ch, bufes=None):
    TT = 512
    NTF = TF // TT
    NQT = TO // TT
    OFF = half * TO
    NH, NKV = 16, 4
    GPK = NH // NKV
    NC32 = TF // 128

    if bufes is None:
        bufes = es
    onesc = bufes.enter_context(nc.sbuf_tensor(px + "onesc", [128, 1], BF16))
    onesr = bufes.enter_context(nc.sbuf_tensor(px + "onesr", [1, 128], F32))
    rot_s = bufes.enter_context(nc.sbuf_tensor(px + "rot", [128, 128], BF16))

    q_d = nc.dram_tensor(px + "q_d", [2048, TO], BF16).ap()
    k_d = nc.dram_tensor(px + "k_d", [512, TF], BF16).ap()
    v_d = nc.dram_tensor(px + "v_d", [TF, 512], BF16).ap()
    ao_d = nc.dram_tensor(px + "ao_d", [2048, TO], BF16).ap()

    x_in_r = x_in.rearrange("(c p) t -> p c t", p=128)
    x_out_r = x_out.rearrange("(c p) t -> p c t", p=128)

    ld0 = Sem(es, nc, px + "ld0")
    with nc.Block() as blk0:
        @blk0.sync
        def _(sync):
            sync.dma_start(out=onesc[:], in_=dram["onescol_bf"]).then_inc(ld0.h, 16)
            sync.dma_start(out=onesr[:], in_=dram["onesrow_f32"]).then_inc(ld0.h, 16)
            sync.dma_start(out=rot_s[:], in_=dram["rot128"]).then_inc(ld0.h, 16)

        @blk0.tensor
        def _(e):
            e.wait_ge(ld0.h, 48)

        @blk0.vector
        def _(e):
            e.wait_ge(ld0.h, 48)

        @blk0.gpsimd
        def _(e):
            e.wait_ge(ld0.h, 48)

        @blk0.scalar
        def _(e):
            e.wait_ge(ld0.h, 48)

    own_tiles = sorted(set(range(OFF // TT, (OFF + TO) // TT)))
    sched = []
    g = kq = vv = qn = 0
    for i in range(NTF):
        for oc in range(NKV):
            sched.append(dict(i=i, kind="K", oc=oc, g=g, kq=kq)); g += 1; kq += 1
        for ts in range(4):
            sched.append(dict(i=i, kind="V", oc=ts, g=g, v=vv)); g += 1; vv += 1
        if i in own_tiles:
            for oc in range(NH):
                sched.append(dict(i=i, kind="Q", oc=oc, g=g, kq=kq, qn=qn))
                g += 1; kq += 1; qn += 1
    qgroups = [s for s in sched if s["kind"] == "Q"]

    with contextlib.ExitStack() as aes:
        xsl = aes.enter_context(nc.sbuf_tensor(px + "xsl", [128, NCH * TT], F32))
        xbf = aes.enter_context(nc.sbuf_tensor(px + "xbf", [128, NCH * TT], BF16))
        sq_b = aes.enter_context(nc.sbuf_tensor(px + "sq_b", [128, NCH * TT], BF16))
        xln = aes.enter_context(nc.sbuf_tensor(px + "xln", [128, NCH * TT], BF16))
        rows = aes.enter_context(nc.sbuf_tensor(px + "rows", [1, 6 * TT], F32))
        lntmp = aes.enter_context(nc.sbuf_tensor(px + "lntmp", [128, TT], F32))
        wks = aes.enter_context(nc.sbuf_tensor(px + "wks", [128, 16 * 512], BF16))
        wvs = aes.enter_context(nc.sbuf_tensor(px + "wvs", [128, 16 * 512], BF16))
        wqt = [aes.enter_context(nc.sbuf_tensor(px + f"wqt{j}", [128, 16 * 128], BF16)) for j in range(2)]
        ropc = aes.enter_context(nc.sbuf_tensor(px + "ropc", [128, TT], F32))
        rops = aes.enter_context(nc.sbuf_tensor(px + "rops", [128, TT], F32))
        kbf = [aes.enter_context(nc.sbuf_tensor(px + f"kbf{j}", [128, TT], BF16)) for j in range(2)]
        tmpA = aes.enter_context(nc.sbuf_tensor(px + "tmpA", [128, TT], F32))
        tmpB = aes.enter_context(nc.sbuf_tensor(px + "tmpB", [128, TT], F32))
        kout = [aes.enter_context(nc.sbuf_tensor(px + f"kout{j}", [128, TT], BF16)) for j in range(2)]
        vout = [aes.enter_context(nc.sbuf_tensor(px + f"vout{j}", [128, TT], BF16)) for j in range(2)]
        ps_ls = aes.enter_context(nc.psum_tensor(px + "ps_ls", [1, TT], F32))
        ps_lq = aes.enter_context(nc.psum_tensor(px + "ps_lq", [1, TT], F32))
        ps_br = aes.enter_context(nc.psum_tensor(px + "ps_br", [128, TT], F32))
        ps_bm = aes.enter_context(nc.psum_tensor(px + "ps_bm", [128, TT], F32))
        ppj = [aes.enter_context(nc.psum_tensor(px + f"ppj{j}", [128, TT], F32)) for j in range(2)]
        prot = aes.enter_context(nc.psum_tensor(px + "prot", [128, TT], F32))

        s_lx = Sem(es, nc, px + "s_lx")
        s_lb = Sem(es, nc, px + "s_lb")
        s_rp = Sem(es, nc, px + "s_rp")
        s_wkv = Sem(es, nc, px + "s_wkv")
        s_wq = Sem(es, nc, px + "s_wq")
        sqd = Sem(es, nc, px + "sqd")
        lmm = Sem(es, nc, px + "lmm")
        rwd = Sem(es, nc, px + "rwd")
        lact = Sem(es, nc, px + "lact")
        apd = Sem(es, nc, px + "apd")
        pjd = Sem(es, nc, px + "pjd")
        evd = Sem(es, nc, px + "evd")
        rotd = Sem(es, nc, px + "rotd")
        cmb = Sem(es, nc, px + "cmb")
        s_stkq = Sem(es, nc, px + "s_stkq")
        s_stv = Sem(es, nc, px + "s_stv")

        with nc.Block() as blk:
            @blk.sync
            def _(sync):
                def stores(i):
                    for s in [s for s in sched if s["i"] == i]:
                        if s["kind"] in ("K", "Q"):
                            j = s["kq"]
                            sync.wait_ge(s_stkq.h, 16 * j)
                            sync.wait_ge(cmb.h, j + 1)
                            if s["kind"] == "K":
                                dst = k_d[s["oc"] * 128:(s["oc"] + 1) * 128, s["i"] * TT:(s["i"] + 1) * TT]
                            else:
                                dst = q_d[s["oc"] * 128:(s["oc"] + 1) * 128,
                                          s["i"] * TT - OFF:(s["i"] + 1) * TT - OFF]
                            sync.dma_start(out=dst, in_=kout[j % 2][:]).then_inc(s_stkq.h, 16)
                        else:
                            j = s["v"]
                            sync.wait_ge(s_stv.h, 16 * j)
                            sync.wait_ge(evd.h, s["g"] + 1)
                            dst = v_d[s["i"] * TT + s["oc"] * 128:s["i"] * TT + (s["oc"] + 1) * 128, :]
                            sync.dma_start(out=dst, in_=vout[j % 2][:]).then_inc(s_stv.h, 16)

                for i in range(NTF):
                    sync.wait_ge(s_lx.h, 16 * i)
                    if i >= 1:
                        sync.wait_ge(apd.h, i)
                    sync.dma_start(out=xsl[:].rearrange("p (c t) -> p c t", t=TT),
                                   in_=x_in_r[:, :, i * TT:(i + 1) * TT]).then_inc(s_lx.h, 16)
                    if i >= 1:
                        stores(i - 1)
                    sync.wait_ge(s_rp.h, 32 * i)
                    if i >= 1:
                        last_kq = max(s["kq"] for s in sched if s["i"] == i - 1 and s["kind"] != "V")
                        sync.wait_ge(cmb.h, last_kq + 1)
                    sync.dma_start(out=ropc[:], in_=dram["ropeC"][:, i * TT:(i + 1) * TT]).then_inc(s_rp.h, 16)
                    sync.dma_start(out=rops[:], in_=dram["ropeS"][:, i * TT:(i + 1) * TT]).then_inc(s_rp.h, 16)
                stores(NTF - 1)
                nkq_all = sum(1 for s in sched if s["kind"] != "V")
                nv_all = sum(1 for s in sched if s["kind"] == "V")
                sync.wait_ge(s_stkq.h, 16 * nkq_all)
                sync.wait_ge(s_stv.h, 16 * nv_all)

            @blk.gpsimd
            def _(g_):
                g_.dma_start(out=wks[:].rearrange("p (c f) -> p c f", f=512),
                             in_=dram["wkT"].rearrange("(c p) f -> p c f", p=128)).then_inc(s_wkv.h, 16)
                g_.dma_start(out=wvs[:].rearrange("p (c f) -> p c f", f=512),
                             in_=dram["wvT"].rearrange("(c p) f -> p c f", p=128)).then_inc(s_wkv.h, 16)
                for i in range(NTF):
                    g_.wait_ge(s_lb.h, 16 * i)
                    if i >= 1:
                        g_.wait_ge(lmm.h, 3 * i - 2)
                    g_.dma_start(out=xbf[:].rearrange("p (c t) -> p c t", t=TT),
                                 in_=x_in_r[:, :, i * TT:(i + 1) * TT]).then_inc(s_lb.h, 16)
                    for s in [s for s in sched if s["i"] == i and s["kind"] == "Q"]:
                        nq = s["qn"]
                        g_.wait_ge(s_wq.h, 16 * nq)
                        if nq >= 2:
                            g_.wait_ge(pjd.h, qgroups[nq - 2]["g"] + 1)
                        g_.dma_start(out=wqt[nq % 2][:].rearrange("p (c f) -> p c f", f=128),
                                     in_=dram["wqT"].rearrange("(c p) f -> p c f", p=128)
                                     [:, :, s["oc"] * 128:(s["oc"] + 1) * 128]).then_inc(s_wq.h, 16)

            @blk.tensor
            def _(te):
                for i in range(NTF):
                    te.wait_ge(sqd.h, i + 1)
                    if i >= 1:
                        te.wait_ge(rwd.h, 2 * i - 1)
                    for c in range(NCH):
                        te.matmul(ps_ls[:], onesc[:], xbf[:, c * TT:(c + 1) * TT],
                                  start=(c == 0), stop=(c == NCH - 1))
                    for c in range(NCH):
                        mm = te.matmul(ps_lq[:], onesc[:], sq_b[:, c * TT:(c + 1) * TT],
                                       start=(c == 0), stop=(c == NCH - 1))
                    mm.then_inc(lmm.h, 1)
                    te.wait_ge(rwd.h, 2 * i + 2)
                    if i >= 1:
                        te.wait_ge(apd.h, i)
                    te.matmul(ps_br[:], onesr[:], rows[0:1, 4 * TT:5 * TT],
                              start=True, stop=True).then_inc(lmm.h, 1)
                    te.matmul(ps_bm[:], onesr[:], rows[0:1, 5 * TT:6 * TT],
                              start=True, stop=True).then_inc(lmm.h, 1)
                    te.wait_ge(apd.h, i + 1)
                    for s in [s for s in sched if s["i"] == i]:
                        gg = s["g"]
                        if gg >= 2:
                            te.wait_ge(evd.h, gg - 1)
                        if s["kind"] == "K":
                            te.wait_ge(s_wkv.h, 32)
                            for ic in range(NCH):
                                mm = te.matmul(ppj[gg % 2][:],
                                               wks[:, ic * 512 + s["oc"] * 128:ic * 512 + (s["oc"] + 1) * 128],
                                               xln[:, ic * TT:(ic + 1) * TT],
                                               start=(ic == 0), stop=(ic == NCH - 1))
                            mm.then_inc(pjd.h, 1)
                        elif s["kind"] == "Q":
                            nq = s["qn"]
                            te.wait_ge(s_wq.h, 16 * (nq + 1))
                            for ic in range(NCH):
                                mm = te.matmul(ppj[gg % 2][:],
                                               wqt[nq % 2][:, ic * 128:(ic + 1) * 128],
                                               xln[:, ic * TT:(ic + 1) * TT],
                                               start=(ic == 0), stop=(ic == NCH - 1))
                            mm.then_inc(pjd.h, 1)
                        else:
                            te.wait_ge(s_wkv.h, 32)
                            ts = s["oc"]
                            for ic in range(NCH):
                                mm = te.matmul(ppj[gg % 2][:],
                                               xln[:, ic * TT + ts * 128:ic * TT + (ts + 1) * 128],
                                               wvs[:, ic * 512:(ic + 1) * 512],
                                               start=(ic == 0), stop=(ic == NCH - 1))
                            mm.then_inc(pjd.h, 1)
                        if s["kind"] in ("K", "Q"):
                            j = s["kq"]
                            te.wait_ge(evd.h, gg + 1)
                            if j >= 1:
                                te.wait_ge(cmb.h, j)
                            te.matmul(prot[:], rot_s[:], kbf[j % 2][:],
                                      start=True, stop=True).then_inc(rotd.h, 1)

            @blk.scalar
            def _(sc):
                sc = Chain(sc, ch["s"])
                for i in range(NTF):
                    sc.wait_ge(s_lb.h, 16 * (i + 1))
                    if i >= 1:
                        sc.wait_ge(lmm.h, 3 * i - 2)
                    sc.activation(sq_b[:], xbf[:], AF.Square, _inc=sqd)
                    sc.wait_ge(rwd.h, 2 * i + 1)
                    sc.activation(rows[0:1, 4 * TT:5 * TT], rows[0:1, 3 * TT:4 * TT], AF.Ln)
                    sc.activation(rows[0:1, 4 * TT:5 * TT], rows[0:1, 4 * TT:5 * TT], AF.Exp,
                                  scale=-0.5, _inc=lact)
                    for s in [s for s in sched if s["i"] == i]:
                        gg = s["g"]
                        sc.wait_ge(pjd.h, gg + 1)
                        if s["kind"] in ("K", "Q"):
                            j = s["kq"]
                            if j >= 2:
                                sc.wait_ge(rotd.h, j - 1)
                                sc.wait_ge(cmb.h, j - 1)
                            sc.activation(kbf[j % 2][:], ppj[gg % 2][:], AF.Copy, _inc=evd)
                        else:
                            jv = s["v"]
                            if jv >= 2:
                                sc.wait_ge(s_stv.h, 16 * (jv - 1))
                            sc.activation(vout[jv % 2][:], ppj[gg % 2][:], AF.Copy, _inc=evd)

            @blk.vector
            def _(v):
                v = Chain(v, ch["v"])
                for i in range(NTF):
                    v.wait_ge(lmm.h, 3 * i + 1)
                    v.tensor_scalar(rows[0:1, 0:TT], ps_ls[:], 1.0 / D, None, op0=ALU.mult)
                    v.tensor_scalar(rows[0:1, TT:2 * TT], ps_lq[:], 1.0 / D, 1e-5,
                                    op0=ALU.mult, op1=ALU.add)
                    v.tensor_mul(rows[0:1, 2 * TT:3 * TT], rows[0:1, 0:TT], rows[0:1, 0:TT])
                    v.scalar_tensor_tensor(rows[0:1, 3 * TT:4 * TT], rows[0:1, 2 * TT:3 * TT], -1.0,
                                           rows[0:1, TT:2 * TT], op0=ALU.mult, op1=ALU.add, _inc=rwd)
                    v.wait_ge(lact.h, i + 1)
                    v.scalar_tensor_tensor(rows[0:1, 5 * TT:6 * TT], rows[0:1, 0:TT], -1.0,
                                           rows[0:1, 4 * TT:5 * TT], op0=ALU.mult, op1=ALU.mult, _inc=rwd)
                    v.wait_ge(lmm.h, 3 * i + 3)
                    v.wait_ge(s_lx.h, 16 * (i + 1))
                    for c in range(NCH):
                        v.tensor_mul(lntmp[:], xsl[:, c * TT:(c + 1) * TT], ps_br[:])
                        v.tensor_add(xln[:, c * TT:(c + 1) * TT], lntmp[:], ps_bm[:],
                                     _inc=apd if c == NCH - 1 else None)
                    v.wait_ge(s_rp.h, 32 * (i + 1))
                    for s in [s for s in sched if s["i"] == i and s["kind"] != "V"]:
                        j = s["kq"]
                        v.wait_ge(evd.h, s["g"] + 1)
                        v.tensor_mul(tmpA[:], kbf[j % 2][:], ropc[:])
                        v.wait_ge(rotd.h, j + 1)
                        v.tensor_mul(tmpB[:], prot[:], rops[:])
                        if j >= 2:
                            v.wait_ge(s_stkq.h, 16 * (j - 1))
                        v.tensor_add(kout[j % 2][:], tmpA[:], tmpB[:], _inc=cmb)

    NRED_K = TF // TT
    NRED_Q = TO // TT

    def red_base(kvh, qj=None):
        b = kvh * (NRED_K + GPK * NRED_Q)
        if qj is None:
            return b
        return b + NRED_K + qj * NRED_Q

    with contextlib.ExitStack() as bes:
        khs = bes.enter_context(nc.sbuf_tensor(px + "khs", [128, TF], BF16))
        vhs = bes.enter_context(nc.sbuf_tensor(px + "vhs", [128, NC32 * 128], BF16))
        qhs = bes.enter_context(nc.sbuf_tensor(px + "qhs", [128, TO], BF16))
        nsq = bes.enter_context(nc.sbuf_tensor(px + "nsq", [128, TF], BF16))
        nrow = bes.enter_context(nc.sbuf_tensor(px + "nrow", [1, 16], F32))
        srow = bes.enter_context(nc.sbuf_tensor(px + "srow", [1, TT], F32))
        ccol = bes.enter_context(nc.sbuf_tensor(px + "ccol", [128, 1], F32))
        att = [bes.enter_context(nc.sbuf_tensor(px + f"att{j}", [128, TT], BF16)) for j in range(2)]
        aob = [bes.enter_context(nc.sbuf_tensor(px + f"aob{j}", [128, TT], BF16)) for j in range(2)]
        rbc = bes.enter_context(nc.sbuf_tensor(px + "rbc", [128, TT], F32))
        ps_s = [bes.enter_context(nc.psum_tensor(px + f"ps_s{j}", [128, TT], F32)) for j in range(2)]
        po = bes.enter_context(nc.psum_tensor(px + "po", [128, TT], F32))
        prow = bes.enter_context(nc.psum_tensor(px + "prow", [1, TT], F32))
        pnrm = [bes.enter_context(nc.psum_tensor(px + f"pnrm{j}", [1, TT], F32)) for j in range(2)]
        pbc = bes.enter_context(nc.psum_tensor(px + "pbc", [128, TT], F32))

        b_kh = Sem(es, nc, px + "b_kh")
        b_vh = Sem(es, nc, px + "b_vh")
        b_qh = Sem(es, nc, px + "b_qh")
        b_sq = Sem(es, nc, px + "b_sq")
        b_nmm = Sem(es, nc, px + "b_nmm")
        b_nrd = Sem(es, nc, px + "b_nrd")
        b_nrm = Sem(es, nc, px + "b_nrm")
        b_ncc = Sem(es, nc, px + "b_ncc")
        b_cb = Sem(es, nc, px + "b_cb")
        b_cc = Sem(es, nc, px + "b_cc")
        b_sm = Sem(es, nc, px + "b_sm")
        b_ex = Sem(es, nc, px + "b_ex")
        b_av = Sem(es, nc, px + "b_av")
        b_rc = Sem(es, nc, px + "b_rc")
        b_bc = Sem(es, nc, px + "b_bc")
        b_bcc = Sem(es, nc, px + "b_bcc")
        b_ao = Sem(es, nc, px + "b_ao")
        b_sao = Sem(es, nc, px + "b_sao")

        with nc.Block() as blk:
            @blk.sync
            def _(sync):
                grp = 0
                for kvh in range(NKV):
                    sync.wait_ge(b_kh.h, 16 * kvh)
                    sync.wait_ge(b_vh.h, 16 * kvh)
                    if kvh >= 1:
                        sync.wait_ge(b_av.h, NC32 * NQT * GPK * kvh)
                    sync.dma_start(out=khs[:], in_=k_d[kvh * 128:(kvh + 1) * 128, :]).then_inc(b_kh.h, 16)
                    sync.dma_start(out=vhs[:].rearrange("p (c f) -> p c f", f=128),
                                   in_=v_d.rearrange("(c p) f -> p c f", p=128)
                                   [:, :, kvh * 128:(kvh + 1) * 128]).then_inc(b_vh.h, 16)
                    for qj in range(GPK):
                        qh = kvh * GPK + qj
                        sync.wait_ge(b_qh.h, 16 * qh)
                        if qh >= 1:
                            sync.wait_ge(b_av.h, NC32 * NQT * qh)
                        sync.dma_start(out=qhs[:], in_=q_d[qh * 128:(qh + 1) * 128, :]).then_inc(b_qh.h, 16)
                        for qt in range(NQT):
                            sync.wait_ge(b_sao.h, 16 * grp)
                            sync.wait_ge(b_ao.h, grp + 1)
                            sync.dma_start(out=ao_d[qh * 128:(qh + 1) * 128, qt * TT:(qt + 1) * TT],
                                           in_=aob[grp % 2][:]).then_inc(b_sao.h, 16)
                            grp += 1

            @blk.tensor
            def _(te):
                grp = 0
                icnt = 0
                red = 0
                for kvh in range(NKV):
                    te.wait_ge(b_sq.h, (GPK + 1) * kvh + 1)
                    te.wait_ge(b_vh.h, 16 * (kvh + 1))
                    for j in range(NRED_K):
                        if red >= 2:
                            te.wait_ge(b_nrd.h, red - 1)
                        te.matmul(pnrm[red % 2][:], onesc[:], nsq[:, j * TT:(j + 1) * TT],
                                  start=True, stop=True).then_inc(b_nmm.h, 1)
                        red += 1
                    for qj in range(GPK):
                        qh = kvh * GPK + qj
                        te.wait_ge(b_sq.h, (GPK + 1) * kvh + qj + 2)
                        for j in range(NRED_Q):
                            if red >= 2:
                                te.wait_ge(b_nrd.h, red - 1)
                            te.matmul(pnrm[red % 2][:], onesc[:], nsq[:, j * TT:(j + 1) * TT],
                                      start=True, stop=True).then_inc(b_nmm.h, 1)
                            red += 1
                        te.wait_ge(b_ncc.h, qh + 1)
                        if grp >= 1:
                            te.wait_ge(b_ao.h, grp)
                        te.matmul(pbc[:, 0:1], onesr[:], nrow[0:1, 15:16],
                                  start=True, stop=True).then_inc(b_cb.h, 1)
                        for qt in range(NQT):
                            te.wait_ge(b_cc.h, qh + 1)
                            if grp >= 1:
                                te.wait_ge(b_ao.h, grp)
                            for c in range(NC32):
                                if icnt >= 2:
                                    te.wait_ge(b_ex.h, icnt - 1)
                                te.matmul(ps_s[icnt % 2][:], khs[:, c * 128:(c + 1) * 128],
                                          qhs[:, qt * TT:(qt + 1) * TT],
                                          start=True, stop=True).then_inc(b_sm.h, 1)
                                te.wait_ge(b_ex.h, icnt + 1)
                                te.matmul(po[:], vhs[:, c * 128:(c + 1) * 128], att[icnt % 2][:],
                                          start=(c == 0), stop=(c == NC32 - 1))
                                te.matmul(prow[:], onesc[:], att[icnt % 2][:],
                                          start=(c == 0), stop=(c == NC32 - 1)).then_inc(b_av.h, 1)
                                icnt += 1
                            te.wait_ge(b_rc.h, grp + 1)
                            te.matmul(pbc[:], onesr[:], srow[0:1, :],
                                      start=True, stop=True).then_inc(b_bc.h, 1)
                            grp += 1

            @blk.scalar
            def _(sc):
                sc = Chain(sc, ch["s"])
                icnt = 0
                for kvh in range(NKV):
                    sc.wait_ge(b_kh.h, 16 * (kvh + 1))
                    if kvh >= 1:
                        sc.wait_ge(b_nmm.h, red_base(kvh))
                    sc.activation(nsq[:], khs[:], AF.Square, _inc=b_sq)
                    for qj in range(GPK):
                        qh = kvh * GPK + qj
                        sc.wait_ge(b_qh.h, 16 * (qh + 1))
                        sc.wait_ge(b_nmm.h, red_base(kvh, qj))
                        sc.activation(nsq[:, 0:TO], qhs[:], AF.Square, _inc=b_sq)
                        sc.wait_ge(b_nrm.h, qh + 1)
                        sc.activation(nrow[0:1, 12:13], nrow[0:1, 11:12], AF.Copy,
                                      scale=nrow[0:1, 10:11])
                        sc.activation(nrow[0:1, 13:14], nrow[0:1, 12:13], AF.Sqrt)
                        sc.activation(nrow[0:1, 14:15], nrow[0:1, 13:14], AF.Copy,
                                      scale=L1_SCALE, bias=-EXP_SHIFT_MARGIN)
                        sc.activation(nrow[0:1, 12:13], nrow[0:1, 14:15], AF.Relu)
                        sc.activation(nrow[0:1, 15:16], nrow[0:1, 12:13], AF.Copy,
                                      scale=-1.0, _inc=b_ncc)
                        sc.wait_ge(b_cb.h, qh + 1)
                        sc.activation(ccol[:], pbc[:, 0:1], AF.Copy, _inc=b_cc)
                        for qt in range(NQT):
                            for c in range(NC32):
                                sc.wait_ge(b_sm.h, icnt + 1)
                                if icnt >= 2:
                                    sc.wait_ge(b_av.h, icnt - 1)
                                sc.activation(att[icnt % 2][:], ps_s[icnt % 2][:], AF.Exp,
                                              bias=ccol[:, 0:1], scale=L1_SCALE, _inc=b_ex)
                                icnt += 1
                            grp_l = qh * NQT + qt
                            sc.wait_ge(b_bc.h, grp_l + 1)
                            if grp_l >= 1:
                                sc.wait_ge(b_ao.h, grp_l)
                            sc.activation(rbc[:], pbc[:], AF.Copy, _inc=b_bcc)

            @blk.vector
            def _(v):
                v = Chain(v, ch["v"])
                grp = 0
                red = 0
                for kvh in range(NKV):
                    if kvh >= 1:
                        v.wait_ge(b_ncc.h, GPK * kvh)
                    for j in range(NRED_K):
                        v.wait_ge(b_nmm.h, red + 1)
                        dstc = 10 if j == NRED_K - 1 else (0 if j == 0 else 1)
                        if j == 0:
                            v.reduce_max(nrow[0:1, 0:1], pnrm[red % 2][:], axis=AX.X, _inc=b_nrd)
                        else:
                            v.reduce_max(nrow[0:1, 1:2], pnrm[red % 2][:], axis=AX.X, _inc=b_nrd)
                            v.tensor_max(nrow[0:1, 0:1], nrow[0:1, 0:1], nrow[0:1, 1:2])
                        red += 1
                    v.tensor_max(nrow[0:1, 10:11], nrow[0:1, 0:1], nrow[0:1, 0:1])
                    for qj in range(GPK):
                        qh = kvh * GPK + qj
                        for j in range(NRED_Q):
                            v.wait_ge(b_nmm.h, red + 1)
                            if j == 0:
                                v.reduce_max(nrow[0:1, 0:1], pnrm[red % 2][:], axis=AX.X, _inc=b_nrd)
                            else:
                                v.reduce_max(nrow[0:1, 1:2], pnrm[red % 2][:], axis=AX.X, _inc=b_nrd)
                                v.tensor_max(nrow[0:1, 0:1], nrow[0:1, 0:1], nrow[0:1, 1:2])
                            red += 1
                        v.tensor_max(nrow[0:1, 11:12], nrow[0:1, 0:1], nrow[0:1, 0:1],
                                     _inc=b_nrm)
                        for qt in range(NQT):
                            v.wait_ge(b_av.h, NC32 * (grp + 1))
                            v.tensor_copy(srow[0:1, :], prow[:])
                            v.reciprocal(srow[0:1, :], srow[0:1, :], _inc=b_rc)
                            v.wait_ge(b_bcc.h, grp + 1)
                            if grp >= 2:
                                v.wait_ge(b_sao.h, 16 * (grp - 1))
                            v.tensor_mul(aob[grp % 2][:], po[:], rbc[:], _inc=b_ao)
                            grp += 1

    with contextlib.ExitStack() as ces:
        wos = ces.enter_context(nc.sbuf_tensor(px + "wos", [128, 16 * 2048], BF16))
        aos = ces.enter_context(nc.sbuf_tensor(px + "aos", [128, 16 * TT], BF16))
        xt = [ces.enter_context(nc.sbuf_tensor(px + f"xt{j}", [128, TT], F32)) for j in range(2)]
        ot = [ces.enter_context(nc.sbuf_tensor(px + f"ot{j}", [128, TT], F32)) for j in range(2)]
        pc = [ces.enter_context(nc.psum_tensor(px + f"pc{j}", [128, TT], F32)) for j in range(2)]

        c_lw = Sem(es, nc, px + "c_lw")
        c_lao = Sem(es, nc, px + "c_lao")
        c_lx = Sem(es, nc, px + "c_lx")
        c_mm = Sem(es, nc, px + "c_mm")
        c_ev = Sem(es, nc, px + "c_ev")
        c_st = Sem(es, nc, px + "c_st")

        ao_r = ao_d.rearrange("(c p) t -> p c t", p=128)

        with nc.Block() as blk:
            @blk.gpsimd
            def _(g_):
                g_.wait_ge(b_sao.h, 16 * NH * NQT)
                g_.dma_start(out=wos[:].rearrange("p (c f) -> p c f", f=2048),
                             in_=dram["woT"].rearrange("(c p) f -> p c f", p=128)).then_inc(c_lw.h, 16)

            @blk.sync
            def _(sync):
                sync.wait_ge(b_sao.h, 16 * NH * NQT)
                idx = 0
                for qt in range(NQT):
                    sync.wait_ge(c_lao.h, 16 * qt)
                    if qt >= 1:
                        sync.wait_ge(c_mm.h, 16 * qt)
                    sync.dma_start(out=aos[:].rearrange("p (c t) -> p c t", t=TT),
                                   in_=ao_r[:, :, qt * TT:(qt + 1) * TT]).then_inc(c_lao.h, 16)
                    for dm in range(NCH):
                        sync.wait_ge(c_lx.h, 16 * idx)
                        if idx >= 2:
                            sync.wait_ge(c_ev.h, idx - 1)
                        sync.dma_start(out=xt[idx % 2][:],
                                       in_=x_in_r[:, dm, OFF + qt * TT:OFF + (qt + 1) * TT]).then_inc(c_lx.h, 16)
                        sync.wait_ge(c_ev.h, idx + 1)
                        sync.wait_ge(c_st.h, 16 * idx)
                        sync.dma_start(out=x_out_r[:, dm, qt * TT:(qt + 1) * TT],
                                       in_=ot[idx % 2][:]).then_inc(c_st.h, 16)
                        idx += 1
                sync.wait_ge(c_st.h, 16 * NQT * NCH)

            @blk.tensor
            def _(te):
                te.wait_ge(c_lw.h, 16)
                idx = 0
                for qt in range(NQT):
                    te.wait_ge(c_lao.h, 16 * (qt + 1))
                    for dm in range(NCH):
                        if idx >= 2:
                            te.wait_ge(c_ev.h, idx - 1)
                        for fc in range(NCH):
                            mm = te.matmul(pc[idx % 2][:],
                                           wos[:, fc * 2048 + dm * 128:fc * 2048 + (dm + 1) * 128],
                                           aos[:, fc * TT:(fc + 1) * TT],
                                           start=(fc == 0), stop=(fc == NCH - 1))
                        mm.then_inc(c_mm.h, 1)
                        idx += 1

            @blk.vector
            def _(v):
                v = Chain(v, ch["v"])
                idx = 0
                for qt in range(NQT):
                    for dm in range(NCH):
                        v.wait_ge(c_mm.h, idx + 1)
                        v.wait_ge(c_lx.h, 16 * (idx + 1))
                        if idx >= 2:
                            v.wait_ge(c_st.h, 16 * (idx - 1))
                        v.tensor_add(ot[idx % 2][:], xt[idx % 2][:], pc[idx % 2][:], _inc=c_ev)
                        idx += 1


def build_l1_attn(TF=4096, TO=2048, half=0):
    nc = bass.Bass("TRN2", target_bir_lowering=False)
    ins = {}
    ins["x_in"] = nc.dram_tensor("x_in", [D, TF], F32, kind="ExternalInput").ap()
    for nm, shp, dt in (("wqT", [2048, 2048], F32), ("wkT", [2048, 512], F32),
                        ("wvT", [2048, 512], F32), ("woT", [2048, 2048], F32),
                        ("ropeC", [128, TF], F32), ("ropeS", [128, TF], F32),
                        ("rot128", [128, 128], BF16),
                        ("onescol_bf", [128, 1], BF16), ("onesrow_f32", [1, 128], F32)):
        ins[nm] = nc.dram_tensor(nm, shp, dt, kind="ExternalInput").ap()
    x_out = nc.dram_tensor("x_out", [D, TO], F32, kind="ExternalOutput").ap()
    es = contextlib.ExitStack()
    with es:
        ch = {"v": ChainState(es, nc, "a_chv"), "s": ChainState(es, nc, "a_chs")}
        emit_l1_attn_body(nc, es, TF, TO, half, "a_", ins["x_in"], x_out,
                          {k: v for k, v in ins.items() if k != "x_in"}, ch)
    return nc



L0_SCALE = 192.0 ** -0.5


def prep_l0_attn_host(w, TF=4096):
    out = {
        "wdkvT": np.ascontiguousarray(w["W_down_kv"].T),
        "wdqT": np.ascontiguousarray(w["W_down_q"].T),
        "wupkT": np.ascontiguousarray(w["W_up_k"].T),
        "wupvT": np.ascontiguousarray(w["W_up_v"].T),
        "wupqT": np.ascontiguousarray(w["W_up_q"].T),
        "wqrT": np.ascontiguousarray(w["W_qr"].T),
        "wkrT": np.ascontiguousarray(w["W_kr"].T),
        "wfcT": np.ascontiguousarray(w["W_fc"].T),
    }

    def col(b):
        return np.ascontiguousarray(b.reshape(-1, 128).T.astype(np.float32))

    out["bdkv"] = col(w["b_down_kv"])
    out["bdq"] = col(w["b_down_q"])
    out["bupk"] = col(w["b_up_k"])
    out["bupv"] = col(w["b_up_v"])
    out["bupq"] = col(w["b_up_q"])
    out["bqr"] = col(w["b_qr"])
    out["bkr"] = np.ascontiguousarray(w["b_kr"].reshape(64, 1).astype(np.float32))
    out["bfc"] = col(w["b_fc"])
    theta = 1.0 / 10000.0 ** (np.arange(0, 64, 2) / 64.0)
    ang = np.outer(theta, np.arange(TF, dtype=np.float64))
    CP64 = np.repeat(np.sin(ang), 2, axis=0)
    SP64 = np.repeat(np.cos(ang), 2, axis=0)
    out["cp128"] = np.ascontiguousarray(np.tile(CP64, (2, 1)).astype(np.float32))
    out["sp128"] = np.ascontiguousarray(np.tile(SP64, (2, 1)).astype(np.float32))
    R64 = np.zeros((64, 64), dtype=np.float32)
    for i in range(32):
        R64[2 * i + 1, 2 * i] = -1.0
        R64[2 * i, 2 * i + 1] = 1.0
    R128 = np.zeros((128, 128), dtype=np.float32)
    R128[0:64, 0:64] = R64
    R128[64:128, 64:128] = R64
    out["rot128i"] = R128.astype(BF)
    return out


def emit_l0_attn_body(nc, es, TF, TO, half, px, x_in, x_out, dram, ch, bufes=None):
    TT = 512
    NTF = TF // TT
    NQT = TO // TT
    OFF = half * TO
    NH = 16
    NC32 = TF // 128

    if bufes is None:
        bufes = es
    onesc = bufes.enter_context(nc.sbuf_tensor(px + "onesc", [128, 1], BF16))
    onesr = bufes.enter_context(nc.sbuf_tensor(px + "onesr", [1, 128], F32))
    rot_s = bufes.enter_context(nc.sbuf_tensor(px + "rot", [128, 128], BF16))

    q_d = nc.dram_tensor(px + "q_d", [2048, TO], BF16).ap()
    k_d = nc.dram_tensor(px + "k_d", [2048, TF], BF16).ap()
    v_d = nc.dram_tensor(px + "v_d", [TF, 1024], BF16).ap()
    ao_d = nc.dram_tensor(px + "ao_d", [1024, TO], BF16).ap()

    x_in_r = x_in.rearrange("(c p) t -> p c t", p=128)
    x_out_r = x_out.rearrange("(c p) t -> p c t", p=128)
    q_hsd = q_d.rearrange("(h s d) t -> h s d t", s=2, d=64)
    k_hsd = k_d.rearrange("(h s d) t -> h s d t", s=2, d=64)

    ld0 = Sem(es, nc, px + "ld0")
    bias_s = {}
    bias_shapes = dict(bdkv=4, bdq=4, bupk=8, bupv=8, bupq=8, bqr=8, bfc=16)
    for nm, w_ in bias_shapes.items():
        bias_s[nm] = bufes.enter_context(nc.sbuf_tensor(px + nm, [128, w_], F32))
    bias_s["bkr"] = bufes.enter_context(nc.sbuf_tensor(px + "bkr", [64, 1], F32))

    with nc.Block() as blk0:
        @blk0.sync
        def _(sync):
            n = 0
            for nm, ap_ in (("onescol_bf", onesc), ("onesrow_f32", onesr), ("rot128i", rot_s)):
                sync.dma_start(out=ap_[:], in_=dram[nm]).then_inc(ld0.h, 16)
                n += 16
            for nm in list(bias_shapes) + ["bkr"]:
                sync.dma_start(out=bias_s[nm][:], in_=dram[nm]).then_inc(ld0.h, 16)
                n += 16
            ld0.n = n

        @blk0.tensor
        def _(e):
            e.wait_ge(ld0.h, 176)

        @blk0.vector
        def _(e):
            e.wait_ge(ld0.h, 176)

        @blk0.gpsimd
        def _(e):
            e.wait_ge(ld0.h, 176)

        @blk0.scalar
        def _(e):
            e.wait_ge(ld0.h, 176)

    own_tiles = sorted(set(range(OFF // TT, (OFF + TO) // TT)))
    sched = []
    g = jr = je = 0
    for i in range(NTF):
        own = i in own_tiles
        def add(kind, oc, **kw):
            nonlocal g, jr, je
            d = dict(i=i, kind=kind, oc=oc, g=g, **kw)
            if kind in ("QR", "KR"):
                d["jr"] = jr; jr += 1
            elif kind in ("KC", "VC", "QC"):
                d["je"] = je; je += 1
            sched.append(d); g += 1
        for oc in range(4):
            add("CKV", oc)
        for oc in range(8):
            add("KC", oc)
        for oc in range(8):
            add("VC", oc)
        add("KR", 0)
        if own:
            for oc in range(4):
                add("CQ", oc)
            for oc in range(8):
                add("QC", oc)
            for oc in range(8):
                add("QR", oc)
    NJR, NJE = jr, je
    rgroups = [s for s in sched if s["kind"] in ("QR", "KR")]

    with contextlib.ExitStack() as aes:
        xsl = aes.enter_context(nc.sbuf_tensor(px + "xsl", [128, NCH * TT], F32))
        xbf = aes.enter_context(nc.sbuf_tensor(px + "xbf", [128, NCH * TT], BF16))
        sq_b = aes.enter_context(nc.sbuf_tensor(px + "sq_b", [128, NCH * TT], BF16))
        xln = aes.enter_context(nc.sbuf_tensor(px + "xln", [128, NCH * TT], BF16))
        rows = aes.enter_context(nc.sbuf_tensor(px + "rows", [1, 6 * TT], F32))
        lntmp = aes.enter_context(nc.sbuf_tensor(px + "lntmp", [128, TT], F32))
        wdkv = aes.enter_context(nc.sbuf_tensor(px + "wdkv", [128, 16 * 512], BF16))
        wdq = aes.enter_context(nc.sbuf_tensor(px + "wdq", [128, 16 * 512], BF16))
        wupk = aes.enter_context(nc.sbuf_tensor(px + "wupk", [128, 4 * 1024], BF16))
        wupv = aes.enter_context(nc.sbuf_tensor(px + "wupv", [128, 4 * 1024], BF16))
        wupq = aes.enter_context(nc.sbuf_tensor(px + "wupq", [128, 4 * 1024], BF16))
        wqr = aes.enter_context(nc.sbuf_tensor(px + "wqr", [128, 4 * 1024], BF16))
        wkr = aes.enter_context(nc.sbuf_tensor(px + "wkr", [128, 16 * 64], BF16))
        ckv_sb = aes.enter_context(nc.sbuf_tensor(px + "ckv", [128, 4 * TT], BF16))
        cq_sb = aes.enter_context(nc.sbuf_tensor(px + "cq", [128, 4 * TT], BF16))
        cp_s = aes.enter_context(nc.sbuf_tensor(px + "cp", [128, TT], F32))
        sp_s = aes.enter_context(nc.sbuf_tensor(px + "sp", [128, TT], F32))
        qrbf = [aes.enter_context(nc.sbuf_tensor(px + f"qrbf{j}", [128, TT], BF16)) for j in range(2)]
        tmpA = aes.enter_context(nc.sbuf_tensor(px + "tmpA", [128, TT], F32))
        tmpB = aes.enter_context(nc.sbuf_tensor(px + "tmpB", [128, TT], F32))
        kout = [aes.enter_context(nc.sbuf_tensor(px + f"kout{j}", [128, TT], BF16)) for j in range(2)]
        eout = [aes.enter_context(nc.sbuf_tensor(px + f"eout{j}", [128, TT], BF16)) for j in range(2)]
        ps_ls = aes.enter_context(nc.psum_tensor(px + "ps_ls", [1, TT], F32))
        ps_lq = aes.enter_context(nc.psum_tensor(px + "ps_lq", [1, TT], F32))
        ps_br = aes.enter_context(nc.psum_tensor(px + "ps_br", [128, TT], F32))
        ps_bm = aes.enter_context(nc.psum_tensor(px + "ps_bm", [128, TT], F32))
        ppj = [aes.enter_context(nc.psum_tensor(px + f"ppj{j}", [128, TT], F32)) for j in range(2)]
        prot = aes.enter_context(nc.psum_tensor(px + "prot", [128, TT], F32))

        s_lx = Sem(es, nc, px + "s_lx")
        s_lb = Sem(es, nc, px + "s_lb")
        s_rp = Sem(es, nc, px + "s_rp")
        s_w = Sem(es, nc, px + "s_w")
        sqd = Sem(es, nc, px + "sqd")
        lmm = Sem(es, nc, px + "lmm")
        rwd = Sem(es, nc, px + "rwd")
        lact = Sem(es, nc, px + "lact")
        apd = Sem(es, nc, px + "apd")
        pjd = Sem(es, nc, px + "pjd")
        evd = Sem(es, nc, px + "evd")
        rotd = Sem(es, nc, px + "rotd")
        cmb = Sem(es, nc, px + "cmb")
        s_stkq = Sem(es, nc, px + "s_stkq")
        s_ste = Sem(es, nc, px + "s_ste")

        def dst_of(s):
            i, oc = s["i"], s["oc"]
            if s["kind"] == "KC":
                return k_hsd[2 * oc:2 * oc + 2, 0, :, i * TT:(i + 1) * TT]
            if s["kind"] == "QC":
                return q_hsd[2 * oc:2 * oc + 2, 0, :, i * TT - OFF:(i + 1) * TT - OFF]
            if s["kind"] == "VC":
                ts, fh = oc % 4, oc // 4
                r0 = i * TT + ts * 128
                return v_d[r0:r0 + 128, fh * 512:(fh + 1) * 512]
            raise AssertionError

        with nc.Block() as blk:
            @blk.sync
            def _(sync):
                def stores(i):
                    for s in [s for s in sched if s["i"] == i]:
                        if s["kind"] in ("KC", "VC", "QC"):
                            j = s["je"]
                            sync.wait_ge(s_ste.h, 16 * j)
                            sync.wait_ge(evd.h, s["g"] + 1)
                            sync.dma_start(out=dst_of(s), in_=eout[j % 2][:]).then_inc(s_ste.h, 16)
                        elif s["kind"] == "QR":
                            j = s["jr"]
                            sync.wait_ge(s_stkq.h, 16 * j)
                            sync.wait_ge(cmb.h, j + 1)
                            oc = s["oc"]
                            sync.dma_start(out=q_hsd[2 * oc:2 * oc + 2, 1, :, s["i"] * TT - OFF:(s["i"] + 1) * TT - OFF],
                                           in_=kout[j % 2][:]).then_inc(s_stkq.h, 16)
                        elif s["kind"] == "KR":
                            j = s["jr"]
                            sync.wait_ge(s_stkq.h, 16 * j)
                            sync.wait_ge(cmb.h, j + 1)
                            sync.dma_start(out=k_d.rearrange("(h s d) t -> d h s t", s=2, d=64)
                                           [:, :, 1, s["i"] * TT:(s["i"] + 1) * TT],
                                           in_=kout[j % 2][0:64, :].rearrange("d (o t) -> d o t", o=1)
                                           .broadcast_to([64, NH, TT])).then_inc(s_stkq.h, 16)

                for i in range(NTF):
                    sync.wait_ge(s_lx.h, 16 * i)
                    if i >= 1:
                        sync.wait_ge(apd.h, i)
                    sync.dma_start(out=xsl[:].rearrange("p (c t) -> p c t", t=TT),
                                   in_=x_in_r[:, :, i * TT:(i + 1) * TT]).then_inc(s_lx.h, 16)
                    if i >= 1:
                        stores(i - 1)
                    sync.wait_ge(s_rp.h, 32 * i)
                    if i >= 1:
                        last_jr = max(s["jr"] for s in sched if s["i"] == i - 1 and s["kind"] in ("QR", "KR"))
                        sync.wait_ge(cmb.h, last_jr + 1)
                    sync.dma_start(out=cp_s[:], in_=dram["cp128"][:, i * TT:(i + 1) * TT]).then_inc(s_rp.h, 16)
                    sync.dma_start(out=sp_s[:], in_=dram["sp128"][:, i * TT:(i + 1) * TT]).then_inc(s_rp.h, 16)
                stores(NTF - 1)
                sync.wait_ge(s_stkq.h, 16 * NJR)
                sync.wait_ge(s_ste.h, 16 * NJE)

            @blk.gpsimd
            def _(g_):
                for nm, ap_, f in (("wdkvT", wdkv, 512), ("wdqT", wdq, 512), ("wupkT", wupk, 1024),
                                   ("wupvT", wupv, 1024), ("wupqT", wupq, 1024), ("wqrT", wqr, 1024),
                                   ("wkrT", wkr, 64)):
                    g_.dma_start(out=ap_[:].rearrange("p (c f) -> p c f", f=f),
                                 in_=dram[nm].rearrange("(c p) f -> p c f", p=128)).then_inc(s_w.h, 16)
                for i in range(NTF):
                    g_.wait_ge(s_lb.h, 16 * i)
                    if i >= 1:
                        g_.wait_ge(lmm.h, 3 * i - 2)
                    g_.dma_start(out=xbf[:].rearrange("p (c t) -> p c t", t=TT),
                                 in_=x_in_r[:, :, i * TT:(i + 1) * TT]).then_inc(s_lb.h, 16)

            @blk.tensor
            def _(te):
                te.wait_ge(s_w.h, 16 * 7)
                for i in range(NTF):
                    te.wait_ge(sqd.h, i + 1)
                    if i >= 1:
                        te.wait_ge(rwd.h, 2 * i - 1)
                    for c in range(NCH):
                        te.matmul(ps_ls[:], onesc[:], xbf[:, c * TT:(c + 1) * TT],
                                  start=(c == 0), stop=(c == NCH - 1))
                    for c in range(NCH):
                        mm = te.matmul(ps_lq[:], onesc[:], sq_b[:, c * TT:(c + 1) * TT],
                                       start=(c == 0), stop=(c == NCH - 1))
                    mm.then_inc(lmm.h, 1)
                    te.wait_ge(rwd.h, 2 * i + 2)
                    if i >= 1:
                        te.wait_ge(apd.h, i)
                    te.matmul(ps_br[:], onesr[:], rows[0:1, 4 * TT:5 * TT],
                              start=True, stop=True).then_inc(lmm.h, 1)
                    te.matmul(ps_bm[:], onesr[:], rows[0:1, 5 * TT:6 * TT],
                              start=True, stop=True).then_inc(lmm.h, 1)
                    te.wait_ge(apd.h, i + 1)
                    tile_sched = [s for s in sched if s["i"] == i]
                    ckv_done = max(s["g"] for s in tile_sched if s["kind"] == "CKV") + 1
                    cq_done = max([s["g"] for s in tile_sched if s["kind"] == "CQ"], default=-1) + 1
                    for s in tile_sched:
                        gg, oc = s["g"], s["oc"]
                        if gg >= 2:
                            te.wait_ge(evd.h, gg - 1)
                        kind = s["kind"]
                        if kind in ("KC", "VC"):
                            te.wait_ge(evd.h, ckv_done)
                        elif kind in ("QC", "QR"):
                            te.wait_ge(evd.h, cq_done)
                        if kind in ("CKV", "CQ"):
                            w_ = wdkv if kind == "CKV" else wdq
                            for ic in range(NCH):
                                mm = te.matmul(ppj[gg % 2][:],
                                               w_[:, ic * 512 + oc * 128:ic * 512 + (oc + 1) * 128],
                                               xln[:, ic * TT:(ic + 1) * TT],
                                               start=(ic == 0), stop=(ic == NCH - 1))
                            mm.then_inc(pjd.h, 1)
                        elif kind in ("KC", "QC", "QR"):
                            w_ = {"KC": wupk, "QC": wupq, "QR": wqr}[kind]
                            src = ckv_sb if kind == "KC" else cq_sb
                            for ic in range(4):
                                mm = te.matmul(ppj[gg % 2][:],
                                               w_[:, ic * 1024 + oc * 128:ic * 1024 + (oc + 1) * 128],
                                               src[:, ic * TT:(ic + 1) * TT],
                                               start=(ic == 0), stop=(ic == 3))
                            mm.then_inc(pjd.h, 1)
                        elif kind == "VC":
                            ts, fh = oc % 4, oc // 4
                            for ic in range(4):
                                mm = te.matmul(ppj[gg % 2][:],
                                               ckv_sb[:, ic * TT + ts * 128:ic * TT + (ts + 1) * 128],
                                               wupv[:, ic * 1024 + fh * 512:ic * 1024 + (fh + 1) * 512],
                                               start=(ic == 0), stop=(ic == 3))
                            mm.then_inc(pjd.h, 1)
                        else:
                            for ic in range(NCH):
                                mm = te.matmul(ppj[gg % 2][0:64, :],
                                               wkr[:, ic * 64:(ic + 1) * 64],
                                               xln[:, ic * TT:(ic + 1) * TT],
                                               start=(ic == 0), stop=(ic == NCH - 1))
                            mm.then_inc(pjd.h, 1)
                        if kind in ("QR", "KR"):
                            j = s["jr"]
                            te.wait_ge(evd.h, gg + 1)
                            if j >= 1:
                                te.wait_ge(cmb.h, j)
                            if kind == "QR":
                                te.matmul(prot[:], rot_s[:], qrbf[j % 2][:],
                                          start=True, stop=True).then_inc(rotd.h, 1)
                            else:
                                te.matmul(prot[0:64, :], rot_s[0:64, 0:64], qrbf[j % 2][0:64, :],
                                          start=True, stop=True).then_inc(rotd.h, 1)

            @blk.scalar
            def _(sc):
                sc = Chain(sc, ch["s"])
                for i in range(NTF):
                    sc.wait_ge(s_lb.h, 16 * (i + 1))
                    if i >= 1:
                        sc.wait_ge(lmm.h, 3 * i - 2)
                    sc.activation(sq_b[:], xbf[:], AF.Square, _inc=sqd)
                    sc.wait_ge(rwd.h, 2 * i + 1)
                    sc.activation(rows[0:1, 4 * TT:5 * TT], rows[0:1, 3 * TT:4 * TT], AF.Ln)
                    sc.activation(rows[0:1, 4 * TT:5 * TT], rows[0:1, 4 * TT:5 * TT], AF.Exp,
                                  scale=-0.5, _inc=lact)
                    for s in [s for s in sched if s["i"] == i]:
                        gg, oc = s["g"], s["oc"]
                        kind = s["kind"]
                        sc.wait_ge(pjd.h, gg + 1)
                        if kind == "CKV":
                            if i >= 1:
                                prevg = max(x["g"] for x in sched if x["i"] == i - 1 and x["kind"] in ("KC", "VC"))
                                sc.wait_ge(pjd.h, prevg + 1)
                            sc.activation(ckv_sb[:, oc * TT:(oc + 1) * TT], ppj[gg % 2][:], AF.Identity,
                                          bias=bias_s["bdkv"][:, oc:oc + 1], _inc=evd)
                        elif kind == "CQ":
                            prev_own = [t for t in own_tiles if t < i]
                            if prev_own:
                                prevg = max(x["g"] for x in sched if x["i"] == prev_own[-1]
                                            and x["kind"] in ("QC", "QR"))
                                sc.wait_ge(pjd.h, prevg + 1)
                            sc.activation(cq_sb[:, oc * TT:(oc + 1) * TT], ppj[gg % 2][:], AF.Identity,
                                          bias=bias_s["bdq"][:, oc:oc + 1], _inc=evd)
                        elif kind in ("KC", "QC"):
                            j = s["je"]
                            if j >= 2:
                                sc.wait_ge(s_ste.h, 16 * (j - 1))
                            bnm = "bupk" if kind == "KC" else "bupq"
                            sc.activation(eout[j % 2][:], ppj[gg % 2][:], AF.Identity,
                                          bias=bias_s[bnm][:, oc:oc + 1], _inc=evd)
                        elif kind == "VC":
                            j = s["je"]
                            if j >= 2:
                                sc.wait_ge(s_ste.h, 16 * (j - 1))
                            sc.activation(eout[j % 2][:], ppj[gg % 2][:], AF.Copy, _inc=evd)
                        else:
                            j = s["jr"]
                            if j >= 2:
                                sc.wait_ge(rotd.h, j - 1)
                                sc.wait_ge(cmb.h, j - 1)
                            if kind == "QR":
                                sc.activation(qrbf[j % 2][:], ppj[gg % 2][:], AF.Identity,
                                              bias=bias_s["bqr"][:, oc:oc + 1], _inc=evd)
                            else:
                                sc.activation(qrbf[j % 2][0:64, :], ppj[gg % 2][0:64, :], AF.Identity,
                                              bias=bias_s["bkr"][:, 0:1], _inc=evd)

            @blk.vector
            def _(v):
                v = Chain(v, ch["v"])
                for i in range(NTF):
                    v.wait_ge(lmm.h, 3 * i + 1)
                    v.tensor_scalar(rows[0:1, 0:TT], ps_ls[:], 1.0 / D, None, op0=ALU.mult)
                    v.tensor_scalar(rows[0:1, TT:2 * TT], ps_lq[:], 1.0 / D, 1e-5,
                                    op0=ALU.mult, op1=ALU.add)
                    v.tensor_mul(rows[0:1, 2 * TT:3 * TT], rows[0:1, 0:TT], rows[0:1, 0:TT])
                    v.scalar_tensor_tensor(rows[0:1, 3 * TT:4 * TT], rows[0:1, 2 * TT:3 * TT], -1.0,
                                           rows[0:1, TT:2 * TT], op0=ALU.mult, op1=ALU.add, _inc=rwd)
                    v.wait_ge(lact.h, i + 1)
                    v.scalar_tensor_tensor(rows[0:1, 5 * TT:6 * TT], rows[0:1, 0:TT], -1.0,
                                           rows[0:1, 4 * TT:5 * TT], op0=ALU.mult, op1=ALU.mult, _inc=rwd)
                    v.wait_ge(lmm.h, 3 * i + 3)
                    v.wait_ge(s_lx.h, 16 * (i + 1))
                    for c in range(NCH):
                        v.tensor_mul(lntmp[:], xsl[:, c * TT:(c + 1) * TT], ps_br[:])
                        v.tensor_add(xln[:, c * TT:(c + 1) * TT], lntmp[:], ps_bm[:],
                                     _inc=apd if c == NCH - 1 else None)
                    v.wait_ge(s_rp.h, 32 * (i + 1))
                    for s in [s for s in sched if s["i"] == i and s["kind"] in ("QR", "KR")]:
                        j = s["jr"]
                        v.wait_ge(evd.h, s["g"] + 1)
                        if s["kind"] == "QR":
                            v.tensor_mul(tmpA[:], qrbf[j % 2][:], cp_s[:])
                            v.wait_ge(rotd.h, j + 1)
                            v.tensor_mul(tmpB[:], prot[:], sp_s[:])
                            if j >= 2:
                                v.wait_ge(s_stkq.h, 16 * (j - 1))
                            v.tensor_add(kout[j % 2][:], tmpA[:], tmpB[:], _inc=cmb)
                        else:
                            v.tensor_mul(tmpA[0:64, :], qrbf[j % 2][0:64, :], cp_s[0:64, :])
                            v.wait_ge(rotd.h, j + 1)
                            v.tensor_mul(tmpB[0:64, :], prot[0:64, :], sp_s[0:64, :])
                            if j >= 2:
                                v.wait_ge(s_stkq.h, 16 * (j - 1))
                            v.tensor_add(kout[j % 2][0:64, :], tmpA[0:64, :], tmpB[0:64, :], _inc=cmb)

    with contextlib.ExitStack() as bes:
        khs = bes.enter_context(nc.sbuf_tensor(px + "khs", [128, TF], BF16))
        vhs = bes.enter_context(nc.sbuf_tensor(px + "vhs", [128, NC32 * 65], BF16))
        qhs = bes.enter_context(nc.sbuf_tensor(px + "qhs", [128, TO], BF16))
        nsq = bes.enter_context(nc.sbuf_tensor(px + "nsq", [128, TF], BF16))
        nrow = bes.enter_context(nc.sbuf_tensor(px + "nrow", [1, 16], F32))
        srow = bes.enter_context(nc.sbuf_tensor(px + "srow", [1, TT], F32))
        ccol = bes.enter_context(nc.sbuf_tensor(px + "ccol", [128, 1], F32))
        att = [bes.enter_context(nc.sbuf_tensor(px + f"att{j}", [128, TT], BF16)) for j in range(2)]
        aof = bes.enter_context(nc.sbuf_tensor(px + "aof", [128, TT], F32))
        aob = [bes.enter_context(nc.sbuf_tensor(px + f"aob{j}", [128, TT], BF16)) for j in range(2)]
        rbc = bes.enter_context(nc.sbuf_tensor(px + "rbc", [128, TT], F32))
        ps_s = [bes.enter_context(nc.psum_tensor(px + f"ps_s{j}", [128, TT], F32)) for j in range(2)]
        po = bes.enter_context(nc.psum_tensor(px + "po", [128, TT], F32))
        pnrm = [bes.enter_context(nc.psum_tensor(px + f"pnrm{j}", [1, TT], F32)) for j in range(2)]
        pbc = bes.enter_context(nc.psum_tensor(px + "pbc", [128, TT], F32))

        b_kh = Sem(es, nc, px + "b_kh")
        b_vh = Sem(es, nc, px + "b_vh")
        b_qh = Sem(es, nc, px + "b_qh")
        b_ones = Sem(es, nc, px + "b_ones")
        b_sq = Sem(es, nc, px + "b_sq")
        b_nmm = Sem(es, nc, px + "b_nmm")
        b_nrd = Sem(es, nc, px + "b_nrd")
        b_nrm = Sem(es, nc, px + "b_nrm")
        b_ncc = Sem(es, nc, px + "b_ncc")
        b_cb = Sem(es, nc, px + "b_cb")
        b_cc = Sem(es, nc, px + "b_cc")
        b_sm = Sem(es, nc, px + "b_sm")
        b_ex = Sem(es, nc, px + "b_ex")
        b_av = Sem(es, nc, px + "b_av")
        b_rc = Sem(es, nc, px + "b_rc")
        b_bc = Sem(es, nc, px + "b_bc")
        b_bcc = Sem(es, nc, px + "b_bcc")
        b_aot = Sem(es, nc, px + "b_aot")
        b_ao = Sem(es, nc, px + "b_ao")
        b_sao = Sem(es, nc, px + "b_sao")

        NRED_K = TF // TT
        NRED_Q = TO // TT

        def red_base(qh, with_q=0):
            return qh * (NRED_K + NRED_Q) + with_q

        with nc.Block() as blk:
            @blk.vector
            def _(v):
                v = Chain(v, ch["v"])
                v.memset(vhs[:].rearrange("p (c w) -> p c w", w=65)[:, :, 64:65], 1.0, _inc=b_ones)
                grp = 0
                red = 0
                for qh in range(NH):
                    if qh >= 1:
                        v.wait_ge(b_ncc.h, qh)
                    for j in range(NRED_K):
                        v.wait_ge(b_nmm.h, red + 1)
                        if j == 0:
                            v.reduce_max(nrow[0:1, 0:1], pnrm[red % 2][:], axis=AX.X, _inc=b_nrd)
                        else:
                            v.reduce_max(nrow[0:1, 1:2], pnrm[red % 2][:], axis=AX.X, _inc=b_nrd)
                            v.tensor_max(nrow[0:1, 0:1], nrow[0:1, 0:1], nrow[0:1, 1:2])
                        red += 1
                    v.tensor_max(nrow[0:1, 10:11], nrow[0:1, 0:1], nrow[0:1, 0:1])
                    for j in range(NRED_Q):
                        v.wait_ge(b_nmm.h, red + 1)
                        if j == 0:
                            v.reduce_max(nrow[0:1, 0:1], pnrm[red % 2][:], axis=AX.X, _inc=b_nrd)
                        else:
                            v.reduce_max(nrow[0:1, 1:2], pnrm[red % 2][:], axis=AX.X, _inc=b_nrd)
                            v.tensor_max(nrow[0:1, 0:1], nrow[0:1, 0:1], nrow[0:1, 1:2])
                        red += 1
                    v.tensor_max(nrow[0:1, 11:12], nrow[0:1, 0:1], nrow[0:1, 0:1], _inc=b_nrm)
                    for qt in range(NQT):
                        v.wait_ge(b_av.h, NC32 * (grp + 1))
                        v.tensor_copy(srow[0:1, :], po[64:65, :])
                        v.reciprocal(srow[0:1, :], srow[0:1, :], _inc=b_rc)
                        v.wait_ge(b_bcc.h, grp + 1)
                        v.tensor_mul(aof[0:64, :], po[0:64, :], rbc[0:64, :], _inc=b_aot)
                        grp += 1

            @blk.sync
            def _(sync):
                grp = 0
                for qh in range(NH):
                    sync.wait_ge(b_kh.h, 16 * qh)
                    sync.wait_ge(b_vh.h, 16 * qh)
                    if qh == 0:
                        sync.wait_ge(b_ones.h, 1)
                    if qh >= 1:
                        sync.wait_ge(b_av.h, NC32 * NQT * qh)
                    sync.dma_start(out=khs[:], in_=k_d[qh * 128:(qh + 1) * 128, :]).then_inc(b_kh.h, 16)
                    sync.dma_start(out=vhs[:].rearrange("p (c w) -> p c w", w=65)[:, :, 0:64],
                                   in_=v_d.rearrange("(c p) f -> p c f", p=128)
                                   [:, :, qh * 64:(qh + 1) * 64]).then_inc(b_vh.h, 16)
                    sync.wait_ge(b_qh.h, 16 * qh)
                    sync.dma_start(out=qhs[:], in_=q_d[qh * 128:(qh + 1) * 128, :]).then_inc(b_qh.h, 16)
                    for qt in range(NQT):
                        sync.wait_ge(b_sao.h, 16 * grp)
                        sync.wait_ge(b_ao.h, grp + 1)
                        sync.dma_start(out=ao_d[qh * 64:(qh + 1) * 64, qt * TT:(qt + 1) * TT],
                                       in_=aob[grp % 2][0:64, :]).then_inc(b_sao.h, 16)
                        grp += 1

            @blk.tensor
            def _(te):
                grp = 0
                icnt = 0
                red = 0
                for qh in range(NH):
                    te.wait_ge(b_sq.h, 2 * qh + 1)
                    te.wait_ge(b_vh.h, 16 * (qh + 1))
                    for j in range(NRED_K):
                        if red >= 2:
                            te.wait_ge(b_nrd.h, red - 1)
                        te.matmul(pnrm[red % 2][:], onesc[:], nsq[:, j * TT:(j + 1) * TT],
                                  start=True, stop=True).then_inc(b_nmm.h, 1)
                        red += 1
                    te.wait_ge(b_sq.h, 2 * qh + 2)
                    for j in range(NRED_Q):
                        if red >= 2:
                            te.wait_ge(b_nrd.h, red - 1)
                        te.matmul(pnrm[red % 2][:], onesc[:], nsq[:, j * TT:(j + 1) * TT],
                                  start=True, stop=True).then_inc(b_nmm.h, 1)
                        red += 1
                    te.wait_ge(b_ncc.h, qh + 1)
                    if grp >= 1:
                        te.wait_ge(b_aot.h, grp)
                    te.matmul(pbc[:, 0:1], onesr[:], nrow[0:1, 15:16],
                              start=True, stop=True).then_inc(b_cb.h, 1)
                    for qt in range(NQT):
                        te.wait_ge(b_cc.h, qh + 1)
                        if grp >= 1:
                            te.wait_ge(b_aot.h, grp)
                        for c in range(NC32):
                            if icnt >= 2:
                                te.wait_ge(b_ex.h, icnt - 1)
                            te.matmul(ps_s[icnt % 2][:], khs[:, c * 128:(c + 1) * 128],
                                      qhs[:, qt * TT:(qt + 1) * TT],
                                      start=True, stop=True).then_inc(b_sm.h, 1)
                            te.wait_ge(b_ex.h, icnt + 1)
                            te.matmul(po[0:65, :], vhs[:, c * 65:(c + 1) * 65], att[icnt % 2][:],
                                      start=(c == 0), stop=(c == NC32 - 1)).then_inc(b_av.h, 1)
                            icnt += 1
                        te.wait_ge(b_rc.h, grp + 1)
                        te.matmul(pbc[:], onesr[:], srow[0:1, :],
                                  start=True, stop=True).then_inc(b_bc.h, 1)
                        grp += 1

            @blk.scalar
            def _(sc):
                sc = Chain(sc, ch["s"])
                icnt = 0
                for qh in range(NH):
                    sc.wait_ge(b_kh.h, 16 * (qh + 1))
                    if qh >= 1:
                        sc.wait_ge(b_nmm.h, red_base(qh))
                    sc.activation(nsq[:], khs[:], AF.Square, _inc=b_sq)
                    sc.wait_ge(b_qh.h, 16 * (qh + 1))
                    sc.wait_ge(b_nmm.h, red_base(qh, NRED_K))
                    sc.activation(nsq[:, 0:TO], qhs[:], AF.Square, _inc=b_sq)
                    sc.wait_ge(b_nrm.h, qh + 1)
                    sc.activation(nrow[0:1, 12:13], nrow[0:1, 11:12], AF.Copy,
                                  scale=nrow[0:1, 10:11])
                    sc.activation(nrow[0:1, 13:14], nrow[0:1, 12:13], AF.Sqrt)
                    sc.activation(nrow[0:1, 14:15], nrow[0:1, 13:14], AF.Copy,
                                  scale=L0_SCALE, bias=-EXP_SHIFT_MARGIN)
                    sc.activation(nrow[0:1, 12:13], nrow[0:1, 14:15], AF.Relu)
                    sc.activation(nrow[0:1, 15:16], nrow[0:1, 12:13], AF.Copy,
                                  scale=-1.0, _inc=b_ncc)
                    sc.wait_ge(b_cb.h, qh + 1)
                    sc.activation(ccol[:], pbc[:, 0:1], AF.Copy, _inc=b_cc)
                    for qt in range(NQT):
                        grp_l = qh * NQT + qt
                        for c in range(NC32):
                            sc.wait_ge(b_sm.h, icnt + 1)
                            if icnt >= 2:
                                sc.wait_ge(b_av.h, icnt - 1)
                            sc.activation(att[icnt % 2][:], ps_s[icnt % 2][:], AF.Exp,
                                          bias=ccol[:, 0:1], scale=L0_SCALE, _inc=b_ex)
                            icnt += 1
                        sc.wait_ge(b_bc.h, grp_l + 1)
                        if grp_l >= 1:
                            sc.wait_ge(b_aot.h, grp_l)
                        sc.activation(rbc[:], pbc[:], AF.Copy, _inc=b_bcc)
                        sc.wait_ge(b_aot.h, grp_l + 1)
                        if grp_l >= 2:
                            sc.wait_ge(b_sao.h, 16 * (grp_l - 1))
                        sc.activation(aob[grp_l % 2][0:64, :], aof[0:64, :], AF.Identity,
                                      bias=bias_s["bupv"][(qh % 2) * 64:(qh % 2) * 64 + 64, qh // 2:qh // 2 + 1],
                                      _inc=b_ao)

    with contextlib.ExitStack() as ces:
        wfc = ces.enter_context(nc.sbuf_tensor(px + "wfc", [128, 8 * 2048], BF16))
        aos = ces.enter_context(nc.sbuf_tensor(px + "aos", [128, 8 * TT], BF16))
        xt = [ces.enter_context(nc.sbuf_tensor(px + f"xt{j}", [128, TT], F32)) for j in range(2)]
        ot = [ces.enter_context(nc.sbuf_tensor(px + f"ot{j}", [128, TT], F32)) for j in range(2)]
        obf = ces.enter_context(nc.sbuf_tensor(px + "obf", [128, TT], F32))
        pc = [ces.enter_context(nc.psum_tensor(px + f"pc{j}", [128, TT], F32)) for j in range(2)]

        c_lw = Sem(es, nc, px + "c_lw")
        c_lao = Sem(es, nc, px + "c_lao")
        c_lx = Sem(es, nc, px + "c_lx")
        c_mm = Sem(es, nc, px + "c_mm")
        c_bi = Sem(es, nc, px + "c_bi")
        c_ev = Sem(es, nc, px + "c_ev")
        c_st = Sem(es, nc, px + "c_st")

        ao_r = ao_d.rearrange("(c p) t -> p c t", p=128)

        with nc.Block() as blk:
            @blk.gpsimd
            def _(g_):
                g_.wait_ge(b_sao.h, 16 * NH * NQT)
                g_.dma_start(out=wfc[:].rearrange("p (c f) -> p c f", f=2048),
                             in_=dram["wfcT"].rearrange("(c p) f -> p c f", p=128)).then_inc(c_lw.h, 16)

            @blk.sync
            def _(sync):
                sync.wait_ge(b_sao.h, 16 * NH * NQT)
                idx = 0
                for qt in range(NQT):
                    sync.wait_ge(c_lao.h, 16 * qt)
                    if qt >= 1:
                        sync.wait_ge(c_mm.h, 16 * qt)
                    sync.dma_start(out=aos[:].rearrange("p (c t) -> p c t", t=TT),
                                   in_=ao_r[:, :, qt * TT:(qt + 1) * TT]).then_inc(c_lao.h, 16)
                    for dm in range(NCH):
                        sync.wait_ge(c_lx.h, 16 * idx)
                        if idx >= 2:
                            sync.wait_ge(c_ev.h, idx - 1)
                        sync.dma_start(out=xt[idx % 2][:],
                                       in_=x_in_r[:, dm, OFF + qt * TT:OFF + (qt + 1) * TT]).then_inc(c_lx.h, 16)
                        sync.wait_ge(c_ev.h, idx + 1)
                        sync.wait_ge(c_st.h, 16 * idx)
                        sync.dma_start(out=x_out_r[:, dm, qt * TT:(qt + 1) * TT],
                                       in_=ot[idx % 2][:]).then_inc(c_st.h, 16)
                        idx += 1
                sync.wait_ge(c_st.h, 16 * NQT * NCH)

            @blk.tensor
            def _(te):
                te.wait_ge(c_lw.h, 16)
                idx = 0
                for qt in range(NQT):
                    te.wait_ge(c_lao.h, 16 * (qt + 1))
                    for dm in range(NCH):
                        if idx >= 2:
                            te.wait_ge(c_ev.h, idx - 1)
                        for fc in range(8):
                            mm = te.matmul(pc[idx % 2][:],
                                           wfc[:, fc * 2048 + dm * 128:fc * 2048 + (dm + 1) * 128],
                                           aos[:, fc * TT:(fc + 1) * TT],
                                           start=(fc == 0), stop=(fc == 7))
                        mm.then_inc(c_mm.h, 1)
                        idx += 1

            @blk.scalar
            def _(sc):
                sc = Chain(sc, ch["s"])
                idx = 0
                for qt in range(NQT):
                    for dm in range(NCH):
                        sc.wait_ge(c_mm.h, idx + 1)
                        if idx >= 1:
                            sc.wait_ge(c_ev.h, idx)
                        sc.activation(obf[:], pc[idx % 2][:], AF.Identity,
                                      bias=bias_s["bfc"][:, dm:dm + 1], _inc=c_bi)
                        idx += 1

            @blk.vector
            def _(v):
                v = Chain(v, ch["v"])
                idx = 0
                for qt in range(NQT):
                    for dm in range(NCH):
                        v.wait_ge(c_bi.h, idx + 1)
                        v.wait_ge(c_lx.h, 16 * (idx + 1))
                        if idx >= 2:
                            v.wait_ge(c_st.h, 16 * (idx - 1))
                        v.tensor_add(ot[idx % 2][:], xt[idx % 2][:], obf[:], _inc=c_ev)
                        idx += 1


def build_l0_attn(TF=4096, TO=2048, half=0):
    nc = bass.Bass("TRN2", target_bir_lowering=False)
    ins = {}
    ins["x_in"] = nc.dram_tensor("x_in", [D, TF], F32, kind="ExternalInput").ap()
    for nm, shp, dt in (("wdkvT", [2048, 512], F32), ("wdqT", [2048, 512], F32),
                        ("wupkT", [512, 1024], F32), ("wupvT", [512, 1024], F32),
                        ("wupqT", [512, 1024], F32), ("wqrT", [512, 1024], F32),
                        ("wkrT", [2048, 64], F32), ("wfcT", [1024, 2048], F32),
                        ("bdkv", [128, 4], F32), ("bdq", [128, 4], F32),
                        ("bupk", [128, 8], F32), ("bupv", [128, 8], F32),
                        ("bupq", [128, 8], F32), ("bqr", [128, 8], F32),
                        ("bkr", [64, 1], F32), ("bfc", [128, 16], F32),
                        ("cp128", [128, TF], F32), ("sp128", [128, TF], F32),
                        ("rot128i", [128, 128], BF16),
                        ("onescol_bf", [128, 1], BF16), ("onesrow_f32", [1, 128], F32)):
        ins[nm] = nc.dram_tensor(nm, shp, dt, kind="ExternalInput").ap()
    x_out = nc.dram_tensor("x_out", [D, TO], F32, kind="ExternalOutput").ap()
    es = contextlib.ExitStack()
    with es:
        ch = {"v": ChainState(es, nc, "m_chv"), "s": ChainState(es, nc, "m_chs")}
        emit_l0_attn_body(nc, es, TF, TO, half, "m_", ins["x_in"], x_out,
                          {k: v for k, v in ins.items() if k != "x_in"}, ch)
    return nc



def prep_l1_moe_host(gate_w, Wg, Wu, Wd, sWg, sWu, sWd, shared_gate_w):
    out = {
        "gwT": np.ascontiguousarray(gate_w.T),
        "sgw": np.ascontiguousarray(shared_gate_w.reshape(16, 128).T),
        "ident_f32": np.eye(128, dtype=np.float32),
    }
    out["eWg"] = np.ascontiguousarray(Wg)
    out["eWu"] = np.ascontiguousarray(Wu)
    out["eWd"] = np.ascontiguousarray(Wd)
    out["sWg"] = np.ascontiguousarray(sWg)
    out["sWu"] = np.ascontiguousarray(sWu)
    out["sWd"] = np.ascontiguousarray(sWd)
    return out


def emit_l1_moe_body(nc, es, T, px, x_in, x_out, dram, ch, bufes=None):
    TT = 512
    NTT = T // TT
    NE = 16
    NSUB = T // 128

    if bufes is None:
        bufes = es
    cst = {}
    cst["onescol_bf"] = bufes.enter_context(nc.sbuf_tensor(px + "onesc", [128, 1], BF16))
    cst["onesrow_f32"] = bufes.enter_context(nc.sbuf_tensor(px + "onesr", [1, 128], F32))
    ident = bufes.enter_context(nc.sbuf_tensor(px + "ident", [128, 128], F32))
    gws = bufes.enter_context(nc.sbuf_tensor(px + "gws", [128, 16 * 16], BF16))
    sgws = bufes.enter_context(nc.sbuf_tensor(px + "sgws", [128, 16], BF16))
    x_ln = bufes.enter_context(nc.sbuf_tensor(px + "x_ln", [128, NCH * T], BF16))
    combT_d = nc.dram_tensor(px + "combT_d", [17, T], F32).ap()

    ld0 = Sem(es, nc, px + "ld0")
    ld0g = Sem(es, nc, px + "ld0g")
    with nc.Block() as blk0:
        @blk0.sync
        def _(sync):
            sync.dma_start(out=cst["onescol_bf"][:], in_=dram["onescol_bf"]).then_inc(ld0.h, 16)
            sync.dma_start(out=cst["onesrow_f32"][:], in_=dram["onesrow_f32"]).then_inc(ld0.h, 16)
            sync.dma_start(out=ident[:], in_=dram["ident_f32"]).then_inc(ld0.h, 16)

        @blk0.gpsimd
        def _(g_):
            g_.dma_start(out=gws[:].rearrange("p (c f) -> p c f", f=16),
                         in_=dram["gwT"].rearrange("(c p) f -> p c f", p=128)).then_inc(ld0g.h, 16)
            g_.dma_start(out=sgws[:], in_=dram["sgw"]).then_inc(ld0g.h, 16)
            g_.wait_ge(ld0.h, 48)
            g_.wait_ge(ld0g.h, 32)

        @blk0.tensor
        def _(e):
            e.wait_ge(ld0.h, 48)
            e.wait_ge(ld0g.h, 32)

        @blk0.vector
        def _(e):
            e.wait_ge(ld0.h, 48)
            e.wait_ge(ld0g.h, 32)

        @blk0.scalar
        def _(e):
            e.wait_ge(ld0.h, 48)
            e.wait_ge(ld0g.h, 32)

    x_in_r = x_in.rearrange("(c p) t -> p c t", p=128)
    x_out_r = x_out.rearrange("(c p) t -> p c t", p=128)

    emit_layernorm(nc, es, T, px, x_in_r, x_ln, cst, ch)

    with contextlib.ExitStack() as res:
        lgs = res.enter_context(nc.sbuf_tensor(px + "lgs", [16, TT], F32))
        rsb = res.enter_context(nc.sbuf_tensor(px + "rsb", [128, 16], F32))
        sgr = res.enter_context(nc.sbuf_tensor(px + "sgr", [1, TT], F32))
        prb = res.enter_context(nc.sbuf_tensor(px + "prb", [128, 16], F32))
        wrk = res.enter_context(nc.sbuf_tensor(px + "wrk", [128, 16], F32))
        cmp_ = res.enter_context(nc.sbuf_tensor(px + "cmp", [128, 16], F32))
        ncol = res.enter_context(nc.sbuf_tensor(px + "ncol", [128, 8], F32))
        ctile = [res.enter_context(nc.sbuf_tensor(px + f"ctile{j}", [16, 128], F32)) for j in range(2)]
        plog = res.enter_context(nc.psum_tensor(px + "plog", [16, TT], F32))
        psg = res.enter_context(nc.psum_tensor(px + "psg", [1, TT], F32))
        ptr = res.enter_context(nc.psum_tensor(px + "ptr", [128, 16], F32))
        ptc = res.enter_context(nc.psum_tensor(px + "ptc", [16, 128], F32))

        r_mm = Sem(es, nc, px + "r_mm")
        r_ev = Sem(es, nc, px + "r_ev")
        r_tr = Sem(es, nc, px + "r_tr")
        r_te = Sem(es, nc, px + "r_te")
        r_cw = Sem(es, nc, px + "r_cw")
        r_tc = Sem(es, nc, px + "r_tc")
        r_ce = Sem(es, nc, px + "r_ce")
        r_sg = Sem(es, nc, px + "r_sg")
        r_ex = Sem(es, nc, px + "r_ex")
        r_sgm = Sem(es, nc, px + "r_sgm")
        r_cd = Sem(es, nc, px + "r_cd")

        with nc.Block() as blk:
            @blk.tensor
            def _(te):
                for tt in range(NTT):
                    if tt >= 1:
                        te.wait_ge(r_ev.h, tt)
                        te.wait_ge(r_sg.h, tt)
                    for c in range(NCH):
                        mm = te.matmul(plog[:], gws[:, c * 16:(c + 1) * 16],
                                       x_ln[:, c * T + tt * TT:c * T + (tt + 1) * TT],
                                       start=(c == 0), stop=(c == NCH - 1))
                    mm.then_inc(r_mm.h, 1)
                    for c in range(NCH):
                        mm = te.matmul(psg[:], sgws[:, c:c + 1],
                                       x_ln[:, c * T + tt * TT:c * T + (tt + 1) * TT],
                                       start=(c == 0), stop=(c == NCH - 1))
                    mm.then_inc(r_sgm.h, 1)
                    for sb in range(4):
                        s = tt * 4 + sb
                        te.wait_ge(r_ev.h, tt + 1)
                        if s >= 1:
                            te.wait_ge(r_te.h, s)
                        te.transpose(ptr[:], lgs[:, sb * 128:(sb + 1) * 128],
                                     ident[0:16, 0:16]).then_inc(r_tr.h, 1)
                        te.wait_ge(r_cw.h, s + 1)
                        if s >= 1:
                            te.wait_ge(r_ce.h, s)
                        te.transpose(ptc[:], prb[:], ident[:]).then_inc(r_tc.h, 1)

            @blk.scalar
            def _(sc):
                sc = Chain(sc, ch["s"])
                for tt in range(NTT):
                    sc.wait_ge(r_mm.h, tt + 1)
                    sc.activation(lgs[:], plog[:], AF.Copy, _inc=r_ev)
                    sc.wait_ge(r_sgm.h, tt + 1)
                    if tt >= 1:
                        sc.wait_ge(r_cd.h, 16 * (tt * 5))
                    sc.activation(sgr[0:1, :], psg[:], AF.Sigmoid, _inc=r_sg)
                    for sb in range(4):
                        s = tt * 4 + sb
                        sc.wait_ge(r_te.h, s + 1)
                        if s >= 1:
                            sc.wait_ge(r_cw.h, s)
                        sc.activation(wrk[:], rsb[:], AF.Exp, bias=ncol[:, 0:1], _inc=r_ex)

            @blk.vector
            def _(v):
                v = Chain(v, ch["v"])
                for tt in range(NTT):
                    for sb in range(4):
                        s = tt * 4 + sb
                        v.wait_ge(r_tr.h, s + 1)
                        if s >= 1:
                            v.wait_ge(r_ex.h, s)
                        v.tensor_copy(rsb[:], ptr[:])
                        v.reduce_max(ncol[:, 1:2], rsb[:], axis=AX.X)
                        v.tensor_scalar(ncol[:, 0:1], ncol[:, 1:2], -1.0, None,
                                        op0=ALU.mult, _inc=r_te)
                        v.wait_ge(r_ex.h, s + 1)
                        v.reduce_sum(ncol[:, 2:3], wrk[:], axis=AX.X)
                        v.reciprocal(ncol[:, 3:4], ncol[:, 2:3])
                        v.tensor_single_scalar(prb[:], wrk[:], ncol[:, 3:4], ALU.mult)
                        v.tensor_copy(wrk[:], prb[:])
                        for it in range(4):
                            v.reduce_max(ncol[:, 4:5], wrk[:], axis=AX.X)
                            if it < 3:
                                v.tensor_single_scalar(cmp_[:], wrk[:], ncol[:, 4:5], ALU.is_equal)
                                v.scalar_tensor_tensor(wrk[:], cmp_[:], -1e30, wrk[:],
                                                       op0=ALU.mult, op1=ALU.add)
                        v.tensor_single_scalar(cmp_[:], prb[:], ncol[:, 4:5], ALU.is_ge)
                        v.tensor_mul(prb[:], prb[:], cmp_[:])
                        v.reduce_sum(ncol[:, 5:6], prb[:], axis=AX.X)
                        v.reciprocal(ncol[:, 6:7], ncol[:, 5:6])
                        v.tensor_single_scalar(prb[:], prb[:], ncol[:, 6:7],
                                               ALU.mult, _inc=r_cw)
                        v.wait_ge(r_tc.h, s + 1)
                        if s >= 2:
                            v.wait_ge(r_cd.h, 16 * (s - 1 + (s - 1) // 4 + 1))
                        v.tensor_copy(ctile[s % 2][:], ptc[:], _inc=r_ce)

            @blk.sync
            def _(sync):
                nd = 0
                for tt in range(NTT):
                    for sb in range(4):
                        s = tt * 4 + sb
                        sync.wait_ge(r_cd.h, 16 * nd)
                        sync.wait_ge(r_ce.h, s + 1)
                        sync.dma_start(out=combT_d.rearrange("e (u t) -> e u t", t=128)
                                       [0:16, s, :],
                                       in_=ctile[s % 2][:]).then_inc(r_cd.h, 16)
                        nd += 1
                    sync.wait_ge(r_cd.h, 16 * nd)
                    sync.wait_ge(r_sg.h, tt + 1)
                    sync.dma_start(out=combT_d[16:17, tt * TT:(tt + 1) * TT],
                                   in_=sgr[0:1, :]).then_inc(r_cd.h, 16)
                    nd += 1

    with contextlib.ExitStack() as mes:
        wge = [mes.enter_context(nc.sbuf_tensor(px + f"wge{j}", [128, 16 * 512], BF16)) for j in range(2)]
        wue = [mes.enter_context(nc.sbuf_tensor(px + f"wue{j}", [128, 16 * 512], BF16)) for j in range(2)]
        wde = mes.enter_context(nc.sbuf_tensor(px + "wde", [128, 4 * 2048], BF16))
        silu_sb = mes.enter_context(nc.sbuf_tensor(px + "silu", [128, TT], F32))
        int0 = mes.enter_context(nc.sbuf_tensor(px + "int0", [128, TT], F32))
        inter = mes.enter_context(nc.sbuf_tensor(px + "inter", [128, 4 * TT], BF16))
        macc = mes.enter_context(nc.sbuf_tensor(px + "macc", [128, NCH * TT], F32))
        xres = [mes.enter_context(nc.sbuf_tensor(px + f"xres{j}", [128, TT], F32)) for j in range(2)]
        oout = [mes.enter_context(nc.sbuf_tensor(px + f"oout{j}", [128, TT], F32)) for j in range(2)]
        pg = [mes.enter_context(nc.psum_tensor(px + f"pg{j}", [128, TT], F32)) for j in range(2)]
        pu = [mes.enter_context(nc.psum_tensor(px + f"pu{j}", [128, TT], F32)) for j in range(2)]
        pd = [mes.enter_context(nc.psum_tensor(px + f"pd{j}", [128, TT], F32)) for j in range(2)]
        pcb = mes.enter_context(nc.psum_tensor(px + "pcb", [128, TT], F32))

        m_wg = Sem(es, nc, px + "m_wg")
        m_wd = Sem(es, nc, px + "m_wd")
        m_cb = Sem(es, nc, px + "m_cb")
        m_cbc = Sem(es, nc, px + "m_cbc")
        m_gu = Sem(es, nc, px + "m_gu")
        m_si = Sem(es, nc, px + "m_si")
        m_i0 = Sem(es, nc, px + "m_i0")
        m_it = Sem(es, nc, px + "m_it")
        m_dm = Sem(es, nc, px + "m_dm")
        m_ac = Sem(es, nc, px + "m_ac")
        m_xr = Sem(es, nc, px + "m_xr")
        m_st = Sem(es, nc, px + "m_st")
        m_ev = Sem(es, nc, px + "m_ev")

        cb_sb = mes.enter_context(nc.sbuf_tensor(px + "cb_sb", [128, TT], F32))
        crow = [mes.enter_context(nc.sbuf_tensor(px + f"crow{j}", [1, TT], F32)) for j in range(2)]
        m_cr = Sem(es, nc, px + "m_cr")

        NEX = NE + 1

        def wsrc(e, kind):
            if e < NE:
                t = {"g": "eWg", "u": "eWu", "d": "eWd"}[kind]
                return dram[t][e]
            return dram[{"g": "sWg", "u": "sWu", "d": "sWd"}[kind]]

        with nc.Block() as blk:
            @blk.gpsimd
            def _(g_):
                g_.wait_ge(r_cd.h, 16 * NTT * 5)
                for tt in range(NTT):
                    for e in range(NEX):
                        k = tt * NEX + e
                        b = k % 2
                        g_.wait_ge(m_wg.h, 32 * k)
                        if k >= 2:
                            g_.wait_ge(m_gu.h, 8 * (k - 1))
                        g_.dma_start(out=wge[b][:].rearrange("p (c f) -> p c f", f=512),
                                     in_=wsrc(e, "g").rearrange("(c p) f -> p c f", p=128)).then_inc(m_wg.h, 16)
                        g_.dma_start(out=wue[b][:].rearrange("p (c f) -> p c f", f=512),
                                     in_=wsrc(e, "u").rearrange("(c p) f -> p c f", p=128)).then_inc(m_wg.h, 16)
                        g_.wait_ge(m_wd.h, 16 * k)
                        if k >= 1:
                            g_.wait_ge(m_dm.h, 16 * k)
                        g_.dma_start(out=wde[:].rearrange("p (c f) -> p c f", f=2048),
                                     in_=wsrc(e, "d").rearrange("(c p) f -> p c f", p=128)).then_inc(m_wd.h, 16)

            @blk.sync
            def _(sync):
                sync.wait_ge(r_cd.h, 16 * NTT * 5)
                idx = 0

                def xflow(tt):
                    nonlocal idx
                    for dm in range(NCH):
                        sync.wait_ge(m_xr.h, 16 * idx)
                        if idx >= 2:
                            sync.wait_ge(m_ev.h, idx - 1)
                        sync.dma_start(out=xres[idx % 2][:],
                                       in_=x_in_r[:, dm, tt * TT:(tt + 1) * TT]).then_inc(m_xr.h, 16)
                        sync.wait_ge(m_ev.h, idx + 1)
                        sync.wait_ge(m_st.h, 16 * idx)
                        sync.dma_start(out=x_out_r[:, dm, tt * TT:(tt + 1) * TT],
                                       in_=oout[idx % 2][:]).then_inc(m_st.h, 16)
                        idx += 1

                for tt in range(NTT):
                    if tt >= 1:
                        xflow(tt - 1)
                    for e in range(NEX):
                        k = tt * NEX + e
                        sync.wait_ge(m_cr.h, 16 * k)
                        if k >= 2:
                            sync.wait_ge(m_cb.h, k - 1)
                        sync.dma_start(out=crow[k % 2][:],
                                       in_=combT_d[e:e + 1, tt * TT:(tt + 1) * TT]).then_inc(m_cr.h, 16)
                xflow(NTT - 1)
                sync.wait_ge(m_st.h, 16 * NTT * NCH)

            @blk.tensor
            def _(te):
                for tt in range(NTT):
                    for e in range(NEX):
                        k = tt * NEX + e
                        te.wait_ge(m_cr.h, 16 * (k + 1))
                        if k >= 1:
                            te.wait_ge(m_cbc.h, k)
                        te.matmul(pcb[:], cst["onesrow_f32"][:], crow[k % 2][:],
                                  start=True, stop=True).then_inc(m_cb.h, 1)
                        te.wait_ge(m_wg.h, 32 * (k + 1))
                        for oc in range(4):
                            j = 4 * k + oc
                            if j >= 2:
                                te.wait_ge(m_si.h, j - 1)
                            for ic in range(NCH):
                                mm = te.matmul(pg[j % 2][:],
                                               wge[k % 2][:, ic * 512 + oc * 128:ic * 512 + (oc + 1) * 128],
                                               x_ln[:, ic * T + tt * TT:ic * T + (tt + 1) * TT],
                                               start=(ic == 0), stop=(ic == NCH - 1))
                            mm.then_inc(m_gu.h, 1)
                            if j >= 2:
                                te.wait_ge(m_it.h, j - 1)
                            for ic in range(NCH):
                                mm = te.matmul(pu[j % 2][:],
                                               wue[k % 2][:, ic * 512 + oc * 128:ic * 512 + (oc + 1) * 128],
                                               x_ln[:, ic * T + tt * TT:ic * T + (tt + 1) * TT],
                                               start=(ic == 0), stop=(ic == NCH - 1))
                            mm.then_inc(m_gu.h, 1)
                        te.wait_ge(m_wd.h, 16 * (k + 1))
                        te.wait_ge(m_it.h, 4 * (k + 1))
                        for dm in range(NCH):
                            kd = k * NCH + dm
                            if kd >= 2:
                                te.wait_ge(m_ac.h, kd - 1)
                            for fc in range(4):
                                mm = te.matmul(pd[kd % 2][:],
                                               wde[:, fc * 2048 + dm * 128:fc * 2048 + (dm + 1) * 128],
                                               inter[:, fc * TT:(fc + 1) * TT],
                                               start=(fc == 0), stop=(fc == 3))
                            mm.then_inc(m_dm.h, 1)

            @blk.scalar
            def _(sc):
                sc = Chain(sc, ch["s"])
                for tt in range(NTT):
                    for e in range(NEX):
                        k = tt * NEX + e
                        sc.wait_ge(m_cb.h, k + 1)
                        if k >= 1:
                            sc.wait_ge(m_i0.h, 4 * k)
                        sc.activation(cb_sb[:], pcb[:], AF.Copy, _inc=m_cbc)
                        for oc in range(4):
                            j = 4 * k + oc
                            sc.wait_ge(m_gu.h, 2 * j + 1)
                            if j >= 1:
                                sc.wait_ge(m_i0.h, j)
                            sc.activation(silu_sb[:], pg[j % 2][:], AF.Silu, _inc=m_si)

            @blk.vector
            def _(v):
                v = Chain(v, ch["v"])
                idx = 0
                for tt in range(NTT):
                    for e in range(NEX):
                        k = tt * NEX + e
                        if k >= 1:
                            v.wait_ge(m_dm.h, 16 * k)
                        for oc in range(4):
                            j = 4 * k + oc
                            v.wait_ge(m_si.h, j + 1)
                            v.wait_ge(m_gu.h, 2 * j + 2)
                            v.tensor_mul(int0[:], silu_sb[:], pu[j % 2][:], _inc=m_i0)
                            v.wait_ge(m_cbc.h, k + 1)
                            v.tensor_mul(inter[:, oc * TT:(oc + 1) * TT], int0[:],
                                         cb_sb[:], _inc=m_it)
                        for dm in range(NCH):
                            kd = k * NCH + dm
                            v.wait_ge(m_dm.h, kd + 1)
                            if e == 0:
                                v.tensor_copy(macc[:, dm * TT:(dm + 1) * TT], pd[kd % 2][:],
                                              _inc=m_ac)
                            elif e < NEX - 1:
                                v.tensor_add(macc[:, dm * TT:(dm + 1) * TT],
                                             macc[:, dm * TT:(dm + 1) * TT], pd[kd % 2][:],
                                             _inc=m_ac)
                            else:
                                v.tensor_add(macc[:, dm * TT:(dm + 1) * TT],
                                             macc[:, dm * TT:(dm + 1) * TT], pd[kd % 2][:],
                                             _inc=m_ac)
                                v.wait_ge(m_xr.h, 16 * (idx + 1))
                                if idx >= 2:
                                    v.wait_ge(m_st.h, 16 * (idx - 1))
                                v.tensor_add(oout[idx % 2][:], macc[:, dm * TT:(dm + 1) * TT],
                                             xres[idx % 2][:], _inc=m_ev)
                                idx += 1


def build_l1_moe(T=2048):
    nc = bass.Bass("TRN2", target_bir_lowering=False)
    ins = {}
    ins["x_in"] = nc.dram_tensor("x_in", [D, T], F32, kind="ExternalInput").ap()
    for nm, shp in (("gwT", [2048, 16]), ("sgw", [128, 16]), ("ident_f32", [128, 128]),
                    ("eWg", [16, 2048, 512]), ("eWu", [16, 2048, 512]), ("eWd", [16, 512, 2048]),
                    ("sWg", [2048, 512]), ("sWu", [2048, 512]), ("sWd", [512, 2048])):
        ins[nm] = nc.dram_tensor(nm, shp, F32, kind="ExternalInput").ap()
    ins["onescol_bf"] = nc.dram_tensor("onescol_bf", [128, 1], BF16, kind="ExternalInput").ap()
    ins["onesrow_f32"] = nc.dram_tensor("onesrow_f32", [1, 128], F32, kind="ExternalInput").ap()
    x_out = nc.dram_tensor("x_out", [D, T], F32, kind="ExternalOutput").ap()
    es = contextlib.ExitStack()
    with es:
        ch = {"v": ChainState(es, nc, "e_chv"), "s": ChainState(es, nc, "e_chs")}
        emit_l1_moe_body(nc, es, T, "e_", ins["x_in"], x_out,
                         {k: v for k, v in ins.items() if k != "x_in"}, ch)
    return nc



def emit_boundary(nc, es, px, ch, free_sems=None):
    bar = Sem(es, nc, px + "bar")
    with nc.Block() as blk:
        @blk.tensor
        def _(te):
            te.nop().then_inc(bar.h, 1)
            te.wait_ge(bar.h, 5 if free_sems is None else 6)

        @blk.sync
        def _(s):
            s.nop().then_inc(bar.h, 1)
            s.wait_ge(bar.h, 5 if free_sems is None else 6)

        @blk.gpsimd
        def _(g):
            g.nop().then_inc(bar.h, 1)
            g.wait_ge(bar.h, 5)
            if free_sems is not None:
                nc.clear_and_free_semaphores(free_sems)
                g.nop().then_inc(bar.h, 1)
            else:
                pass

        @blk.vector
        def _(v):
            if ch["v"].last is not None:
                v.wait_ge(*ch["v"].last)
            v.nop().then_inc(bar.h, 1)
            v.wait_ge(bar.h, 5 if free_sems is None else 6)

        @blk.scalar
        def _(s):
            if ch["s"].last is not None:
                s.wait_ge(*ch["s"].last)
            s.nop().then_inc(bar.h, 1)
            s.wait_ge(bar.h, 5 if free_sems is None else 6)

    ch["v"].last = None
    ch["s"].last = None
    if "p" in ch:
        ch["p"].last = None


def emit_exchange(nc, es, px, x2_d, x2f_d, poff_in, TO=2048, groups=None):
    cc_in = nc.dram_tensor(px + "cc_in", [D, TO], F32).ap()
    cc_out_t = nc.dram_tensor(px + "cc_out", [2 * D, TO], F32)
    cc_out = cc_out_t.ap()
    poff_s = es.enter_context(nc.sbuf_tensor(px + "poff", [1, 1], I32))
    s1 = Sem(es, nc, px + "s1")
    s2 = Sem(es, nc, px + "s2")
    s3 = Sem(es, nc, px + "s3")

    with nc.Block() as blk:
        @blk.sync
        def _(sync):
            sync.dma_start(out=cc_in, in_=x2_d).then_inc(s1.h, 16)
            sync.dma_start(out=x2f_d[:, 0:TO], in_=x2_d).then_inc(s1.h, 16)
            sync.dma_start(out=poff_s[:], in_=poff_in).then_inc(s1.h, 16)
            sync.wait_ge(s3.h, 16)

        @blk.gpsimd
        def _(g):
            g.wait_ge(s1.h, 48)
            g.collective_compute(
                "AllGather", ALU.bypass,
                replica_groups=groups or [[0, 1], [2, 3], [4, 5], [6, 7]],
                ins=[cc_in.opt()], outs=[cc_out.opt()],
            ).then_inc(s2.h, 1)
            g.wait_ge(s2.h, 1)
            with g.register(px + "poffr") as reg:
                g.reg_load(reg, poff_s[0:1, 0:1])
                g.dma_start(out=x2f_d[:, TO:2 * TO],
                            in_=bass.AP(cc_out_t, reg, [[TO, D], [1, TO]])).then_inc(s3.h, 16)
            g.wait_ge(s3.h, 16)



TFULL = 4096
TOWN = 2048


def _perm_cols(a, h):
    if h == 0:
        return np.ascontiguousarray(a)
    return np.ascontiguousarray(np.concatenate([a[:, TOWN:], a[:, :TOWN]], axis=1))


def build_fused(TF=4096, TO=2048, groups=None):
    nc = bass.Bass("TRN2", target_bir_lowering=False)
    ins = {}

    def di(nm, shp, dt=F32):
        ins[nm] = nc.dram_tensor(nm, shp, dt, kind="ExternalInput").ap()
        return ins[nm]

    di("x0", [D, TF])
    di("poff", [1, 1], I32)
    for nm, shp, dt in (("wdkvT", [2048, 512], F32), ("wdqT", [2048, 512], F32),
                        ("wupkT", [512, 1024], F32), ("wupvT", [512, 1024], F32),
                        ("wupqT", [512, 1024], F32), ("wqrT", [512, 1024], F32),
                        ("wkrT", [2048, 64], F32), ("wfcT", [1024, 2048], F32),
                        ("bdkv", [128, 4], F32), ("bdq", [128, 4], F32),
                        ("bupk", [128, 8], F32), ("bupv", [128, 8], F32),
                        ("bupq", [128, 8], F32), ("bqr", [128, 8], F32),
                        ("bkr", [64, 1], F32), ("bfc", [128, 16], F32),
                        ("cp128", [128, TF], F32), ("sp128", [128, TF], F32),
                        ("rot128i", [128, 128], BF16)):
        di(nm, shp, dt)
    for nm, shp, dt in (("gmp", [D, NMLP], U8), ("gsp", [D, NMLP], U8),
                        ("ump", [D, NMLP], U8), ("usp", [D, NMLP], U8),
                        ("dmp", [NMLP, D], U8), ("dsp", [NMLP, D], U8),
                        ("g_scT", [64, NMLP], F32), ("u_scT", [64, NMLP], F32),
                        ("d_scT", [512, D], F32), ("sel16", [16 * 64, 128], BF16),
                        ("nibcols_f", [128, 3], F32)):
        di(nm, shp, dt)
    for nm, shp, dt in (("wqT", [2048, 2048], F32), ("wkT", [2048, 512], F32),
                        ("wvT", [2048, 512], F32), ("woT", [2048, 2048], F32),
                        ("ropeC", [128, TF], F32), ("ropeS", [128, TF], F32),
                        ("rot128", [128, 128], BF16)):
        di(nm, shp, dt)
    for nm, shp in (("gwT", [2048, 16]), ("sgw", [128, 16]), ("ident_f32", [128, 128]),
                    ("eWg", [16, 2048, 512]), ("eWu", [16, 2048, 512]), ("eWd", [16, 512, 2048]),
                    ("sWg", [2048, 512]), ("sWu", [2048, 512]), ("sWd", [512, 2048])):
        di(nm, shp, F32)
    di("onescol_bf", [128, 1], BF16)
    di("onesrow_f32", [1, 128], F32)

    x4 = nc.dram_tensor("x_out", [D, TO], F32, kind="ExternalOutput").ap()
    x1_d = nc.dram_tensor("x1_d", [D, TO], F32).ap()
    x2_d = nc.dram_tensor("x2_d", [D, TO], F32).ap()
    x2f_d = nc.dram_tensor("x2f_d", [D, TF], F32).ap()
    x3_d = nc.dram_tensor("x3_d", [D, TO], F32).ap()
    acc_d = nc.dram_tensor("facc", [D, TO], F32).ap()

    es = contextlib.ExitStack()
    with es:
        ch = {"v": ChainState(es, nc, "chv"), "s": ChainState(es, nc, "chs")}
        blocks_es = contextlib.ExitStack()

        snap = nc.snapshot_sems()
        with contextlib.ExitStack() as b1:
            emit_l0_attn_body(nc, es, TF, TO, 0, "m_", ins["x0"], x1_d, ins, ch, bufes=b1)
        emit_boundary(nc, es, "bd1", ch, nc.allocated_since(snap))

        snap = nc.snapshot_sems()
        with contextlib.ExitStack() as b2:
            emit_l0_ffn_body(nc, es, TO, "f_", x1_d, x2_d, acc_d,
                             {k: ins[k] for k in ("gmp", "gsp", "ump", "usp", "dmp", "dsp",
                                                  "g_scT", "u_scT", "d_scT", "sel16",
                                                  "nibcols_f", "onescol_bf", "onesrow_f32")},
                             ch=ch, bufes=b2)
        emit_boundary(nc, es, "bd2", ch, nc.allocated_since(snap))

        snap = nc.snapshot_sems()
        emit_exchange(nc, es, "xc_", x2_d, x2f_d, ins["poff"], TO, groups)
        emit_boundary(nc, es, "bd3", ch, nc.allocated_since(snap))

        snap = nc.snapshot_sems()
        with contextlib.ExitStack() as b3:
            emit_l1_attn_body(nc, es, TF, TO, 0, "a_", x2f_d, x3_d, ins, ch, bufes=b3)
        emit_boundary(nc, es, "bd4", ch, nc.allocated_since(snap))

        with contextlib.ExitStack() as b4:
            emit_l1_moe_body(nc, es, TO, "e_", x3_d, x4, ins, ch, bufes=b4)
    return nc


def kernel(**inputs):
    hc = host_consts()
    x0 = np.asarray(inputs["L0_attn_h"])
    l0w = {k[len("L0_attn_"):]: np.asarray(v) for k, v in inputs.items()
           if k.startswith("L0_attn_") and k != "L0_attn_h"}
    p0 = prep_l0_attn_host(l0w, TFULL)
    pf = prep_l0_ffn_host(np.asarray(inputs["L0_ffn_gate_W_packed"]), np.asarray(inputs["L0_ffn_gate_scales"]),
                          np.asarray(inputs["L0_ffn_up_W_packed"]), np.asarray(inputs["L0_ffn_up_scales"]),
                          np.asarray(inputs["L0_ffn_down_W_packed"]), np.asarray(inputs["L0_ffn_down_scales"]))
    p1 = prep_l1_attn_host(np.asarray(inputs["L1_attn_Wq"]), np.asarray(inputs["L1_attn_Wk"]),
                           np.asarray(inputs["L1_attn_Wv"]), np.asarray(inputs["L1_attn_Wo"]), TFULL)
    pm = prep_l1_moe_host(np.asarray(inputs["L1_ffn_gate_w"]), np.asarray(inputs["L1_ffn_Wg"]),
                          np.asarray(inputs["L1_ffn_Wu"]), np.asarray(inputs["L1_ffn_Wd"]),
                          np.asarray(inputs["L1_ffn_sWg"]), np.asarray(inputs["L1_ffn_sWu"]),
                          np.asarray(inputs["L1_ffn_sWd"]), np.asarray(inputs["L1_ffn_shared_gate_w"]))

    nc = build_fused(TFULL, TOWN)
    maps = []
    for c in range(8):
        h = c % 2
        m = {}
        m.update({k: v for k, v in p0.items() if k not in ("cp128", "sp128")})
        m["cp128"] = _perm_cols(p0["cp128"], h)
        m["sp128"] = _perm_cols(p0["sp128"], h)
        m.update(pf)
        m.update({k: v for k, v in p1.items() if k not in ("ropeC", "ropeS")})
        m["ropeC"] = _perm_cols(p1["ropeC"], h)
        m["ropeS"] = _perm_cols(p1["ropeS"], h)
        m.update(pm)
        m.update({k: hc[k] for k in ("sel16", "nibcols_f", "onescol_bf", "onesrow_f32")})
        m["x0"] = _perm_cols(np.ascontiguousarray(x0[c // 2].T), h)
        m["poff"] = np.array([[(1 - h) * D * TOWN]], dtype=np.int32)
        maps.append(m)
    res = run_bass_kernel_spmd(nc, maps, core_ids=list(range(8)))
    out = np.empty((4, 4096, 2048), dtype=np.float32)
    for c in range(8):
        s, h = c // 2, c % 2
        out[s, h * TOWN:(h + 1) * TOWN, :] = res.results[c]["x_out"].T
    return out


# version 04
# speedup vs baseline: 1.3085x; 1.3085x over previous
import contextlib
import numpy as np
import ml_dtypes
import concourse.bass as bass
import concourse.mybir as mybir
from concourse.bass_utils import run_bass_kernel_spmd

F32 = mybir.dt.float32
BF16 = mybir.dt.bfloat16
I32 = mybir.dt.int32
U8 = mybir.dt.uint8
AF = mybir.ActivationFunctionType
ALU = mybir.AluOpType
AX = mybir.AxisListType

D = 2048
NMLP = 16384
NCH = D // 128

GELU_C1 = 0.7978845608028654
GELU_C2 = GELU_C1 * 0.044715
BF = ml_dtypes.bfloat16



def _bytes_lo(packed_i32):
    a = np.ascontiguousarray(packed_i32)
    return np.ascontiguousarray(a.view(np.uint8).reshape(a.shape[0], a.shape[1], 4)[:, :, 0])


def _nib_planes(bT):
    m = np.empty((bT.shape[0] * 2, bT.shape[1]), np.uint8)
    s = np.empty_like(m)
    m[0::2] = bT & np.uint8(7)
    m[1::2] = (bT >> 4) & np.uint8(7)
    s[0::2] = (bT >> 3) & np.uint8(1)
    s[1::2] = (bT >> 7) & np.uint8(1)
    return np.ascontiguousarray(m), np.ascontiguousarray(s)


def prep_l0_ffn_host(gp, gs, up, us, dp, ds):
    out = {}
    for mn, sn, p in (("gmp", "gsp", gp), ("ump", "usp", up)):
        b = _bytes_lo(p)
        out[mn], out[sn] = _nib_planes(np.ascontiguousarray(b.T))
    db = _bytes_lo(dp)
    out["dmp"], out["dsp"] = _nib_planes(np.ascontiguousarray(db.T))
    out["g_scT"] = np.ascontiguousarray(gs.T)
    out["u_scT"] = np.ascontiguousarray(us.T)
    out["d_scT"] = np.ascontiguousarray(ds.T)
    return out


def host_consts():
    c = {}
    sel = np.zeros((16, 64, 128), dtype=np.float32)
    for j in range(16):
        for p in range(128):
            sel[j, 4 * j + p // 32, p] = 1.0
    c["sel16"] = np.ascontiguousarray(sel.astype(BF).reshape(16 * 64, 128))
    c["nibcols_f"] = np.concatenate([
        np.full((128, 1), -4.0),
        np.full((128, 1), -16.0),
        np.full((128, 1), 0.5),
    ], axis=1).astype(np.float32)
    c["onescol_bf"] = np.ones((128, 1), dtype=BF)
    c["onesrow_f32"] = np.ones((1, 128), dtype=np.float32)
    return c


class Sem:
    def __init__(self, es, nc, name):
        self.h = es.enter_context(nc.semaphore(name))
        self.n = 0


class ChainState:
    def __init__(self, es, nc, name):
        self.sem = Sem(es, nc, name)
        self.last = None


_CHAIN_PASSTHROUGH = {"wait_ge", "dma_start", "nop", "sem_inc", "memzero"}


class Chain:
    def __init__(self, eng, st):
        object.__setattr__(self, "eng", eng)
        object.__setattr__(self, "st", st)

    def __getattr__(self, name):
        target = getattr(self.eng, name)
        if name in _CHAIN_PASSTHROUGH or not callable(target):
            return target

        st = self.st
        eng = self.eng

        def wrapped(*a, _inc=None, **kw):
            if st.last is not None:
                eng.wait_ge(*st.last)
            ins = target(*a, **kw)
            if _inc is not None:
                _inc.n += 1
                ins.then_inc(_inc.h, 1)
                st.last = (_inc.h, _inc.n)
            else:
                st.sem.n += 1
                ins.then_inc(st.sem.h, 1)
                st.last = (st.sem.h, st.sem.n)
            return ins

        return wrapped



def emit_layernorm(nc, es, T, prefix, x_in_r, x_ln, cst, ch):
    TT = 512
    NTT = T // TT
    with contextlib.ExitStack() as ln_es:
        xs = [ln_es.enter_context(nc.sbuf_tensor(prefix + f"lnx{i}", [128, NCH * TT], F32)) for i in range(2)]
        xbf = [ln_es.enter_context(nc.sbuf_tensor(prefix + f"lnb{i}", [128, NCH * TT], BF16)) for i in range(2)]
        sq = ln_es.enter_context(nc.sbuf_tensor(prefix + "lnsq", [128, NCH * TT], BF16))
        rows = ln_es.enter_context(nc.sbuf_tensor(prefix + "lnrw", [1, 6 * TT], F32))
        tmp = ln_es.enter_context(nc.sbuf_tensor(prefix + "lntmp", [128, TT], F32))
        ps_sum = ln_es.enter_context(nc.psum_tensor(prefix + "ps_sum", [1, TT], F32))
        ps_sq = ln_es.enter_context(nc.psum_tensor(prefix + "ps_sq", [1, TT], F32))
        ps_br = ln_es.enter_context(nc.psum_tensor(prefix + "ps_br", [128, TT], F32))
        ps_bm = ln_es.enter_context(nc.psum_tensor(prefix + "ps_bm", [128, TT], F32))

        ldx = Sem(es, nc, prefix + "ln_ldx")
        ldb = Sem(es, nc, prefix + "ln_ldb")
        sqd = Sem(es, nc, prefix + "ln_sqd")
        mmd = Sem(es, nc, prefix + "ln_mmd")
        rwd = Sem(es, nc, prefix + "ln_rwd")
        act = Sem(es, nc, prefix + "ln_act")
        apd = Sem(es, nc, prefix + "ln_apd")

        with nc.Block() as blk:
            @blk.sync
            def _(sync):
                for i in range(NTT):
                    b = i % 2
                    sync.wait_ge(ldx.h, 16 * i)
                    if i >= 2:
                        sync.wait_ge(apd.h, i - 1)
                    sync.dma_start(out=xs[b][:].rearrange("p (c t) -> p c t", t=TT),
                                   in_=x_in_r[:, :, i * TT:(i + 1) * TT]).then_inc(ldx.h, 16)

            @blk.gpsimd
            def _(g):
                for i in range(NTT):
                    b = i % 2
                    g.wait_ge(ldb.h, 16 * i)
                    if i >= 2:
                        g.wait_ge(apd.h, i - 1)
                    g.dma_start(out=xbf[b][:].rearrange("p (c t) -> p c t", t=TT),
                                in_=x_in_r[:, :, i * TT:(i + 1) * TT]).then_inc(ldb.h, 16)

            @blk.scalar
            def _(sc):
                sc = Chain(sc, ch["s"])
                for i in range(NTT):
                    b = i % 2
                    sc.wait_ge(ldb.h, 16 * (i + 1))
                    if i >= 1:
                        sc.wait_ge(mmd.h, 3 * i - 2)
                    sc.activation(sq[:], xbf[b][:], AF.Square, _inc=sqd)
                    sc.wait_ge(rwd.h, 2 * i + 1)
                    sc.activation(rows[0:1, 4 * TT:5 * TT], rows[0:1, 3 * TT:4 * TT], AF.Ln)
                    sc.activation(rows[0:1, 4 * TT:5 * TT], rows[0:1, 4 * TT:5 * TT], AF.Exp,
                                  scale=-0.5, _inc=act)

            @blk.tensor
            def _(te):
                for i in range(NTT):
                    b = i % 2
                    te.wait_ge(ldx.h, 16 * (i + 1))
                    te.wait_ge(sqd.h, i + 1)
                    if i >= 1:
                        te.wait_ge(rwd.h, 2 * i - 1)
                    for c in range(NCH):
                        te.matmul(ps_sum[:], cst["onescol_bf"][:], xbf[b][:, c * TT:(c + 1) * TT],
                                  start=(c == 0), stop=(c == NCH - 1))
                    for c in range(NCH):
                        mm = te.matmul(ps_sq[:], cst["onescol_bf"][:], sq[:, c * TT:(c + 1) * TT],
                                       start=(c == 0), stop=(c == NCH - 1))
                    mm.then_inc(mmd.h, 1)
                    te.wait_ge(rwd.h, 2 * i + 2)
                    if i >= 1:
                        te.wait_ge(apd.h, i)
                    te.matmul(ps_br[:], cst["onesrow_f32"][:], rows[0:1, 4 * TT:5 * TT],
                              start=True, stop=True).then_inc(mmd.h, 1)
                    te.matmul(ps_bm[:], cst["onesrow_f32"][:], rows[0:1, 5 * TT:6 * TT],
                              start=True, stop=True).then_inc(mmd.h, 1)

            @blk.vector
            def _(v):
                v = Chain(v, ch["v"])
                for i in range(NTT):
                    b = i % 2
                    v.wait_ge(mmd.h, 3 * i + 1)
                    v.tensor_scalar(rows[0:1, 0:TT], ps_sum[:], 1.0 / D, None, op0=ALU.mult)
                    v.tensor_scalar(rows[0:1, TT:2 * TT], ps_sq[:], 1.0 / D, 1e-5,
                                    op0=ALU.mult, op1=ALU.add)
                    v.tensor_mul(rows[0:1, 2 * TT:3 * TT], rows[0:1, 0:TT], rows[0:1, 0:TT])
                    v.scalar_tensor_tensor(rows[0:1, 3 * TT:4 * TT], rows[0:1, 2 * TT:3 * TT], -1.0,
                                           rows[0:1, TT:2 * TT], op0=ALU.mult,
                                           op1=ALU.add, _inc=rwd)
                    v.wait_ge(act.h, i + 1)
                    v.scalar_tensor_tensor(rows[0:1, 5 * TT:6 * TT], rows[0:1, 0:TT], -1.0,
                                           rows[0:1, 4 * TT:5 * TT], op0=ALU.mult,
                                           op1=ALU.mult, _inc=rwd)
                    v.wait_ge(mmd.h, 3 * i + 3)
                    for c in range(NCH):
                        v.tensor_mul(tmp[:], xs[b][:, c * TT:(c + 1) * TT], ps_br[:])
                        v.tensor_add(x_ln[:, c * T + i * TT:c * T + (i + 1) * TT], tmp[:], ps_bm[:],
                                     _inc=apd if c == NCH - 1 else None)



def emit_l0_ffn_body(nc, es, T, px, x_in, x_out, acc_d, dram, ch=None, bufes=None):
    TT = 512
    NTT = T // TT
    SUP = 1024
    SCH = SUP // 128
    NSUP = NMLP // SUP
    NC = NMLP // 128

    if bufes is None:
        bufes = es
    cst = {}
    cst["nibf"] = bufes.enter_context(nc.sbuf_tensor(px + "nibf", [128, 3], F32))
    cst["onescol_bf"] = bufes.enter_context(nc.sbuf_tensor(px + "onesc", [128, 1], BF16))
    cst["onesrow_f32"] = bufes.enter_context(nc.sbuf_tensor(px + "onesr", [1, 128], F32))
    x_ln = bufes.enter_context(nc.sbuf_tensor(px + "x_ln", [128, NCH * T], BF16))

    ld0 = Sem(es, nc, px + "ld0")
    if ch is None:
        ch = {"v": ChainState(es, nc, px + "chv"), "s": ChainState(es, nc, px + "chs")}
    with nc.Block() as blk0:
        @blk0.sync
        def _(sync):
            sync.dma_start(out=cst["nibf"][:], in_=dram["nibcols_f"]).then_inc(ld0.h, 16)
            sync.dma_start(out=cst["onescol_bf"][:], in_=dram["onescol_bf"]).then_inc(ld0.h, 16)
            sync.dma_start(out=cst["onesrow_f32"][:], in_=dram["onesrow_f32"]).then_inc(ld0.h, 16)

        @blk0.tensor
        def _(te):
            te.wait_ge(ld0.h, 48)

        @blk0.vector
        def _(v):
            v.wait_ge(ld0.h, 48)

        @blk0.gpsimd
        def _(g):
            g.wait_ge(ld0.h, 48)

        @blk0.scalar
        def _(s):
            s.wait_ge(ld0.h, 48)

    x_in_r = x_in.rearrange("(c p) t -> p c t", p=128)
    x_out_r = x_out.rearrange("(c p) t -> p c t", p=128)
    acc_r = acc_d.rearrange("(c p) t -> p c t", p=128)

    emit_layernorm(nc, es, T, px, x_in_r, x_ln, cst, ch)

    with contextlib.ExitStack() as fes:
        wg = [fes.enter_context(nc.sbuf_tensor(px + f"wg{i}", [128, NCH * 128], BF16)) for i in range(2)]
        wu = [fes.enter_context(nc.sbuf_tensor(px + f"wu{i}", [128, NCH * 128], BF16)) for i in range(2)]
        gmb = [fes.enter_context(nc.sbuf_tensor(px + f"gmb{i}", [128, NCH * 128], U8)) for i in range(2)]
        gsb = [fes.enter_context(nc.sbuf_tensor(px + f"gsb{i}", [128, NCH * 128], U8)) for i in range(2)]
        umb = [fes.enter_context(nc.sbuf_tensor(px + f"umb{i}", [128, NCH * 128], U8)) for i in range(2)]
        usb = [fes.enter_context(nc.sbuf_tensor(px + f"usb{i}", [128, NCH * 128], U8)) for i in range(2)]
        dmb = fes.enter_context(nc.sbuf_tensor(px + "dmb", [128, D], U8))
        dsb = fes.enter_context(nc.sbuf_tensor(px + "dsb", [128, D], U8))
        gscb = fes.enter_context(nc.sbuf_tensor(px + "gscb", [128, NCH * 128], BF16))
        uscb = fes.enter_context(nc.sbuf_tensor(px + "uscb", [128, NCH * 128], BF16))
        dscb = fes.enter_context(nc.sbuf_tensor(px + "dscb", [128, D], BF16))
        wd = fes.enter_context(nc.sbuf_tensor(px + "wd", [128, SCH * D], BF16))
        inter = fes.enter_context(nc.sbuf_tensor(px + "inter", [128, SCH * NTT * TT], BF16))
        scC = [fes.enter_context(nc.sbuf_tensor(px + f"scC{i}", [128, 1024], BF16)) for i in range(2)]
        scD = [fes.enter_context(nc.sbuf_tensor(px + f"scD{i}", [128, 1024], BF16)) for i in range(2)]
        scF = [fes.enter_context(nc.sbuf_tensor(px + f"scF{i}", [128, 1024], BF16)) for i in range(2)]
        gel = [fes.enter_context(nc.sbuf_tensor(px + f"gel{i}", [128, TT], F32)) for i in range(2)]
        accs = [fes.enter_context(nc.sbuf_tensor(px + f"accs{i}", [128, TT], F32)) for i in range(4)]
        xres = [fes.enter_context(nc.sbuf_tensor(px + f"xres{i}", [128, TT], F32)) for i in range(2)]
        pg = [fes.enter_context(nc.psum_tensor(px + f"pg{i}", [128, TT], F32)) for i in range(2)]
        pu = [fes.enter_context(nc.psum_tensor(px + f"pu{i}", [128, TT], F32)) for i in range(2)]
        pd = [fes.enter_context(nc.psum_tensor(px + f"pd{i}", [128, 512], F32)) for i in range(4)]

        s_ad = Sem(es, nc, px + "s_ad")
        s_sb = Sem(es, nc, px + "s_sb")
        s_bmg = Sem(es, nc, px + "s_bmg")
        s_bmu = Sem(es, nc, px + "s_bmu")
        s_bmd = Sem(es, nc, px + "s_bmd")
        s_bsg = Sem(es, nc, px + "s_bsg")
        s_bsu = Sem(es, nc, px + "s_bsu")
        s_bsd = Sem(es, nc, px + "s_bsd")
        s_scg = Sem(es, nc, px + "s_scg")
        s_scu = Sem(es, nc, px + "s_scu")
        s_scd = Sem(es, nc, px + "s_scd")
        s_bgc = Sem(es, nc, px + "s_bgc")
        s_bdc = Sem(es, nc, px + "s_bdc")
        s_mm = Sem(es, nc, px + "s_mm")
        s_gact = Sem(es, nc, px + "s_gact")
        s_gtan = Sem(es, nc, px + "s_gtan")
        s_gdve = Sem(es, nc, px + "s_gdve")
        s_gz = Sem(es, nc, px + "s_gz")
        s_int = Sem(es, nc, px + "s_int")
        s_dmm = Sem(es, nc, px + "s_dmm")
        s_accl = Sem(es, nc, px + "s_accl")
        s_xr = Sem(es, nc, px + "s_xr")
        s_ev = Sem(es, nc, px + "s_ev")
        s_st = Sem(es, nc, px + "s_st")

        gm_r = dram["gmp"].rearrange("(c p) n -> p c n", p=128)
        gs_r = dram["gsp"].rearrange("(c p) n -> p c n", p=128)
        um_r = dram["ump"].rearrange("(c p) n -> p c n", p=128)
        us_r = dram["usp"].rearrange("(c p) n -> p c n", p=128)
        dm_r = dram["dmp"]
        ds_r = dram["dsp"]

        def kof(ncg, tt):
            return ncg * NTT + tt

        def mslab(ncg, slot):
            mat, h = slot // 2, slot % 2
            buf = dmb if mat == 2 else (gmb, umb)[mat][ncg % 2]
            return buf[:, h * 1024:(h + 1) * 1024]

        def sslab(ncg, slot):
            mat, h = slot // 2, slot % 2
            buf = dsb if mat == 2 else (gsb, usb)[mat][ncg % 2]
            return buf[:, h * 1024:(h + 1) * 1024]

        def ad_base(ncg, slot):
            return 18 * ncg + 3 * slot

        def decode_act(sc, ncg, slot):
            nf = cst["nibf"]
            m_ap = mslab(ncg, slot)
            h = slot % 2
            C, Dd, F = scC[h], scD[h], scF[h]
            if slot < 2:
                if ncg >= 1:
                    sc.wait_ge(s_bdc.h, 2 * (ncg - 1) + h + 1)
            elif slot < 4:
                sc.wait_ge(s_bgc.h, 4 * ncg + h + 1)
            else:
                sc.wait_ge(s_bgc.h, 4 * ncg + 3 + h)
            sc.activation(C[:], m_ap, AF.Identity, scale=2.0, bias=nf[:, 0:1], _inc=s_ad)
            sc.activation(Dd[:], m_ap, AF.Identity, scale=4.0, bias=nf[:, 1:2], _inc=s_ad)
            sc.activation(F[:], sslab(ncg, slot), AF.Identity, scale=-1.0,
                          bias=nf[:, 2:3], _inc=s_ad)

        def decode_dve(v, ncg, slot):
            mat, h = slot // 2, slot % 2
            b = ncg % 2
            C, Dd, F = scC[slot % 2], scD[slot % 2], scF[slot % 2]
            v.wait_ge(s_ad.h, ad_base(ncg, slot) + 2)
            v.tensor_tensor(C[:], C[:], Dd[:], ALU.max)
            v.scalar_tensor_tensor(C[:], mslab(ncg, slot), 1.0, C[:],
                                   op0=ALU.mult, op1=ALU.max, _inc=s_sb)
            v.wait_ge(s_ad.h, ad_base(ncg, slot) + 3)
            v.tensor_mul(F[:], F[:], C[:])
            if mat < 2:
                sem = s_scg if mat == 0 else s_scu
                v.wait_ge(sem.h, 64 * (ncg + 1))
                if slot == 0 and ncg >= 2:
                    v.wait_ge(s_mm.h, 2 * NTT * (ncg - 1))
                wdec = wg[b] if mat == 0 else wu[b]
                slab = gscb if mat == 0 else uscb
                v.tensor_mul(wdec[:, h * 1024:(h + 1) * 1024], F[:],
                             slab[:, h * 1024:(h + 1) * 1024], _inc=s_bgc)
            else:
                v.wait_ge(s_scd.h, 64 * (ncg + 1))
                cl = ncg % SCH
                v.tensor_mul(wd[:, cl * D + h * 1024:cl * D + (h + 1) * 1024], F[:],
                             dscb[:, h * 1024:(h + 1) * 1024], _inc=s_bdc)

        with nc.Block() as blk:
            @blk.sync
            def _(sync):
                def accflow(sup):
                    for tt in range(NTT):
                        for dm in range(NCH):
                            k = (sup * NTT + tt) * NCH + dm
                            sync.wait_ge(s_st.h, 16 * k)
                            if sup == NSUP - 1:
                                idx = tt * NCH + dm
                                sync.wait_ge(s_xr.h, 16 * idx)
                                if idx >= 2:
                                    sync.wait_ge(s_ev.h, k - 1)
                                sync.dma_start(out=xres[idx % 2][:],
                                               in_=x_in_r[:, dm, tt * TT:(tt + 1) * TT]).then_inc(s_xr.h, 16)
                            sync.wait_ge(s_ev.h, k + 1)
                            dst = x_out_r if sup == NSUP - 1 else acc_r
                            sync.dma_start(out=dst[:, dm, tt * TT:(tt + 1) * TT],
                                           in_=accs[k % 4][:]).then_inc(s_st.h, 16)

                def gu_bytes(ncg):
                    b = ncg % 2
                    sync.wait_ge(s_bmg.h, 16 * ncg)
                    if ncg >= 2:
                        sync.wait_ge(s_sb.h, 6 * (ncg - 2) + 2)
                    sync.dma_start(out=gmb[b][:].rearrange("p (c w) -> p c w", w=128),
                                   in_=gm_r[:, :, ncg * 128:(ncg + 1) * 128]).then_inc(s_bmg.h, 16)
                    sync.wait_ge(s_bmu.h, 16 * ncg)
                    if ncg >= 2:
                        sync.wait_ge(s_sb.h, 6 * (ncg - 2) + 4)
                    sync.dma_start(out=umb[b][:].rearrange("p (c w) -> p c w", w=128),
                                   in_=um_r[:, :, ncg * 128:(ncg + 1) * 128]).then_inc(s_bmu.h, 16)
                    sync.wait_ge(s_bsg.h, 16 * ncg)
                    if ncg >= 2:
                        sync.wait_ge(s_ad.h, 18 * (ncg - 2) + 6)
                    sync.dma_start(out=gsb[b][:].rearrange("p (c w) -> p c w", w=128),
                                   in_=gs_r[:, :, ncg * 128:(ncg + 1) * 128]).then_inc(s_bsg.h, 16)
                    sync.wait_ge(s_bsu.h, 16 * ncg)
                    if ncg >= 2:
                        sync.wait_ge(s_ad.h, 18 * (ncg - 2) + 12)
                    sync.dma_start(out=usb[b][:].rearrange("p (c w) -> p c w", w=128),
                                   in_=us_r[:, :, ncg * 128:(ncg + 1) * 128]).then_inc(s_bsu.h, 16)

                def d_bytes(ncg):
                    sync.wait_ge(s_bmd.h, 16 * ncg)
                    if ncg >= 1:
                        sync.wait_ge(s_sb.h, 6 * ncg)
                    sync.dma_start(out=dmb[:], in_=dm_r[ncg * 128:(ncg + 1) * 128, :]).then_inc(s_bmd.h, 16)
                    sync.wait_ge(s_bsd.h, 16 * ncg)
                    if ncg >= 1:
                        sync.wait_ge(s_ad.h, 18 * (ncg - 1) + 18)
                    sync.dma_start(out=dsb[:], in_=ds_r[ncg * 128:(ncg + 1) * 128, :]).then_inc(s_bsd.h, 16)

                for sup in range(NSUP):
                    gu_bytes(sup * SCH)
                    gu_bytes(sup * SCH + 1)
                    d_bytes(sup * SCH)
                    if sup >= 1:
                        accflow(sup - 1)
                    d_bytes(sup * SCH + 1)
                    for cl in range(2, SCH):
                        gu_bytes(sup * SCH + cl)
                        d_bytes(sup * SCH + cl)
                accflow(NSUP - 1)
                sync.wait_ge(s_st.h, 16 * NSUP * NTT * NCH)

            @blk.gpsimd
            def _(g):
                nacc = 0

                def acc_loads(sup):
                    nonlocal nacc
                    for tt in range(NTT):
                        for dm in range(NCH):
                            k = (sup * NTT + tt) * NCH + dm
                            g.wait_ge(s_accl.h, 16 * nacc)
                            g.wait_ge(s_st.h, 16 * (k - 3))
                            g.dma_start(out=accs[k % 4][:],
                                        in_=acc_r[:, dm, tt * TT:(tt + 1) * TT]).then_inc(s_accl.h, 16)
                            nacc += 1

                gsT = dram["g_scT"]
                usT = dram["u_scT"]
                dsT = dram["d_scT"]
                for ncg in range(NC):
                    b = ncg % 2
                    gsl = gsT[:, ncg * 128:(ncg + 1) * 128].rearrange("(c pa) w -> pa c w", pa=4)
                    usl = usT[:, ncg * 128:(ncg + 1) * 128].rearrange("(c pa) w -> pa c w", pa=4)
                    for pa in range(4):
                        g.wait_ge(s_scg.h, 16 * (4 * ncg + pa))
                        if pa == 0 and ncg >= 1:
                            g.wait_ge(s_bgc.h, 4 * (ncg - 1) + 2)
                        g.dma_start(out=gscb[32 * pa:32 * pa + 32, :].rearrange("p (c w) -> p c w", w=128),
                                    in_=gsl[pa:pa + 1].broadcast_to([32, NCH, 128])).then_inc(s_scg.h, 16)
                    for pa in range(4):
                        g.wait_ge(s_scu.h, 16 * (4 * ncg + pa))
                        if pa == 0 and ncg >= 1:
                            g.wait_ge(s_bgc.h, 4 * (ncg - 1) + 4)
                        g.dma_start(out=uscb[32 * pa:32 * pa + 32, :].rearrange("p (c w) -> p c w", w=128),
                                    in_=usl[pa:pa + 1].broadcast_to([32, NCH, 128])).then_inc(s_scu.h, 16)
                    for pa in range(4):
                        g.wait_ge(s_scd.h, 16 * (4 * ncg + pa))
                        if pa == 0 and ncg >= 1:
                            g.wait_ge(s_bdc.h, 2 * ncg)
                        g.dma_start(out=dscb[32 * pa:32 * pa + 32, :],
                                    in_=dsT[4 * ncg + pa:4 * ncg + pa + 1, :]
                                    .broadcast_to([32, D])).then_inc(s_scd.h, 16)
                    if ncg % SCH == 0 and ncg >= 2 * SCH:
                        acc_loads(ncg // SCH - 1)
                acc_loads(NSUP - 1)

            @blk.tensor
            def _(te):
                for sup in range(NSUP):
                    for cl in range(SCH):
                        ncg = sup * SCH + cl
                        b = ncg % 2
                        te.wait_ge(s_bgc.h, 4 * ncg + 4)
                        for tt in range(NTT):
                            k = kof(ncg, tt)
                            if k >= 2:
                                te.wait_ge(s_gz.h, k - 1)
                                te.wait_ge(s_int.h, k - 1)
                            for dc in range(NCH):
                                mm = te.matmul(pg[k % 2][:], wg[b][:, dc * 128:(dc + 1) * 128],
                                               x_ln[:, dc * T + tt * TT:dc * T + (tt + 1) * TT],
                                               start=(dc == 0), stop=(dc == NCH - 1))
                            mm.then_inc(s_mm.h, 1)
                            for dc in range(NCH):
                                mm = te.matmul(pu[k % 2][:], wu[b][:, dc * 128:(dc + 1) * 128],
                                               x_ln[:, dc * T + tt * TT:dc * T + (tt + 1) * TT],
                                               start=(dc == 0), stop=(dc == NCH - 1))
                            mm.then_inc(s_mm.h, 1)
                    te.wait_ge(s_bdc.h, 2 * SCH * (sup + 1))
                    for tt in range(NTT):
                        te.wait_ge(s_int.h, kof(sup * SCH + SCH - 1, tt) + 1)
                        for dm in range(NCH):
                            k = (sup * NTT + tt) * NCH + dm
                            if k >= 4:
                                te.wait_ge(s_ev.h, k - 3)
                            for i in range(SCH):
                                mm = te.matmul(pd[k % 4][:], wd[:, i * D + dm * 128:i * D + (dm + 1) * 128],
                                               inter[:, (i * NTT + tt) * TT:(i * NTT + tt + 1) * TT],
                                               start=(i == 0), stop=(i == SCH - 1))
                            mm.then_inc(s_dmm.h, 1)

            @blk.vector
            def _(v):
                v = Chain(v, ch["v"])
                prev_k = []

                def tail(pk):
                    v.wait_ge(s_gtan.h, pk + 1)
                    v.scalar_tensor_tensor(gel[pk % 2][:], gel[pk % 2][:], 1.0, pg[pk % 2][:],
                                           op0=ALU.add, op1=ALU.mult, _inc=s_gz)
                    v.wait_ge(s_mm.h, 2 * pk + 2)
                    pcl = (pk // NTT) % SCH
                    v.scalar_tensor_tensor(
                        inter[:, (pcl * NTT + (pk % NTT)) * TT:(pcl * NTT + (pk % NTT) + 1) * TT],
                        gel[pk % 2][:], 0.5, pu[pk % 2][:],
                        op0=ALU.mult, op1=ALU.mult, _inc=s_int)

                def gelu_dve(ncg):
                    for tt in range(NTT):
                        k = kof(ncg, tt)
                        v.wait_ge(s_gact.h, k + 1)
                        v.scalar_tensor_tensor(gel[k % 2][:], gel[k % 2][:], GELU_C1,
                                               pg[k % 2][:], op0=ALU.add, op1=ALU.mult,
                                               _inc=s_gdve)
                        if prev_k:
                            tail(prev_k.pop(0))
                        prev_k.append(k)

                def evacs(sup):
                    for tt in range(NTT):
                        for dm in range(NCH):
                            k = (sup * NTT + tt) * NCH + dm
                            bb = k % 4
                            v.wait_ge(s_dmm.h, k + 1)
                            if sup == 0 and k >= 4:
                                v.wait_ge(s_st.h, 16 * (k - 3))
                            last = sup != NSUP - 1
                            if sup == 0:
                                v.tensor_copy(accs[bb][:], pd[bb][:], _inc=s_ev if last else None)
                            else:
                                v.wait_ge(s_accl.h, 16 * (sup * NTT * NCH + tt * NCH + dm + 1 - NTT * NCH))
                                v.tensor_add(accs[bb][:], accs[bb][:], pd[bb][:],
                                             _inc=s_ev if last else None)
                            if sup == NSUP - 1:
                                v.wait_ge(s_xr.h, 16 * (tt * NCH + dm + 1))
                                v.tensor_add(accs[bb][:], accs[bb][:], xres[k % 2][:], _inc=s_ev)

                for ncg in range(NC):
                    for slot in range(4):
                        decode_dve(v, ncg, slot)
                    if ncg >= 1:
                        gelu_dve(ncg - 1)
                        if ncg % SCH == 0:
                            while prev_k:
                                tail(prev_k.pop(0))
                            evacs(ncg // SCH - 1)
                    if ncg // SCH > 0:
                        v.wait_ge(s_dmm.h, NTT * NCH * (ncg // SCH))
                    decode_dve(v, ncg, 4)
                    decode_dve(v, ncg, 5)
                gelu_dve(NC - 1)
                while prev_k:
                    tail(prev_k.pop(0))
                evacs(NSUP - 1)

            @blk.scalar
            def _(sc):
                sc = Chain(sc, ch["s"])

                def tanh_op(pk):
                    sc.wait_ge(s_gdve.h, pk + 1)
                    sc.activation(gel[pk % 2][:], gel[pk % 2][:], AF.Tanh, _inc=s_gtan)

                def gelu_act(ncg, skip_first):
                    for tt in range(NTT):
                        k = kof(ncg, tt)
                        if k >= 1 and not (skip_first and tt == 0):
                            tanh_op(k - 1)
                        sc.wait_ge(s_mm.h, 2 * k + 1)
                        if k >= 2:
                            sc.wait_ge(s_gz.h, k - 1)
                        sc.activation(gel[k % 2][:], pg[k % 2][:], AF.Square,
                                      scale=GELU_C2 ** 0.5, _inc=s_gact)

                for ncg in range(NC):
                    if ncg >= 2:
                        tanh_op(kof(ncg - 2, NTT - 1))
                    sc.wait_ge(s_bmg.h, 16 * (ncg + 1))
                    sc.wait_ge(s_bsg.h, 16 * (ncg + 1))
                    decode_act(sc, ncg, 0)
                    decode_act(sc, ncg, 1)
                    sc.wait_ge(s_bmu.h, 16 * (ncg + 1))
                    sc.wait_ge(s_bsu.h, 16 * (ncg + 1))
                    decode_act(sc, ncg, 2)
                    decode_act(sc, ncg, 3)
                    if ncg >= 1:
                        gelu_act(ncg - 1, skip_first=(ncg >= 2))
                    sc.wait_ge(s_bmd.h, 16 * (ncg + 1))
                    sc.wait_ge(s_bsd.h, 16 * (ncg + 1))
                    decode_act(sc, ncg, 4)
                    decode_act(sc, ncg, 5)
                tanh_op(kof(NC - 2, NTT - 1))
                gelu_act(NC - 1, skip_first=True)
                tanh_op(kof(NC - 1, NTT - 1))

    return



def build_l0_ffn(T=2048):
    nc = bass.Bass("TRN2", target_bir_lowering=False)
    ins = {}
    ins["x_in"] = nc.dram_tensor("x_in", [D, T], F32, kind="ExternalInput").ap()
    for nm, shp in (("gmp", [D, NMLP]), ("gsp", [D, NMLP]), ("ump", [D, NMLP]),
                    ("usp", [D, NMLP]), ("dmp", [NMLP, D]), ("dsp", [NMLP, D])):
        ins[nm] = nc.dram_tensor(nm, shp, U8, kind="ExternalInput").ap()
    ins["g_scT"] = nc.dram_tensor("g_scT", [64, NMLP], F32, kind="ExternalInput").ap()
    ins["u_scT"] = nc.dram_tensor("u_scT", [64, NMLP], F32, kind="ExternalInput").ap()
    ins["d_scT"] = nc.dram_tensor("d_scT", [512, D], F32, kind="ExternalInput").ap()
    ins["sel16"] = nc.dram_tensor("sel16", [16 * 64, 128], BF16, kind="ExternalInput").ap()
    ins["nibcols_f"] = nc.dram_tensor("nibcols_f", [128, 3], F32, kind="ExternalInput").ap()
    ins["onescol_bf"] = nc.dram_tensor("onescol_bf", [128, 1], BF16, kind="ExternalInput").ap()
    ins["onesrow_f32"] = nc.dram_tensor("onesrow_f32", [1, 128], F32, kind="ExternalInput").ap()
    x_out = nc.dram_tensor("x_out", [D, T], F32, kind="ExternalOutput").ap()
    acc_d = nc.dram_tensor("acc", [D, T], F32).ap()
    es = contextlib.ExitStack()
    with es:
        emit_l0_ffn_body(nc, es, T, "f_", ins["x_in"], x_out, acc_d,
                         {k: v for k, v in ins.items() if k != "x_in"})
    return nc




L1_SCALE = 128.0 ** -0.5
EXP_SHIFT_MARGIN = 40.0


def prep_l1_attn_host(Wq, Wk, Wv, Wo, TF=4096):
    out = {
        "wqT": np.ascontiguousarray(Wq.T),
        "wkT": np.ascontiguousarray(Wk.T),
        "wvT": np.ascontiguousarray(Wv.T),
        "woT": np.ascontiguousarray(Wo.T),
    }
    inv = 1.0 / 10000.0 ** (np.arange(0, 128, 2) / 128.0)
    t = np.arange(TF, dtype=np.float64)
    emb = np.concatenate([np.outer(inv, t), np.outer(inv, t)], axis=0)
    out["ropeC"] = np.cos(emb).astype(np.float32)
    out["ropeS"] = np.sin(emb).astype(np.float32)
    Rt = np.zeros((128, 128), dtype=np.float32)
    for i in range(64):
        Rt[i + 64, i] = -1.0
        Rt[i, i + 64] = 1.0
    out["rot128"] = Rt.astype(BF)
    return out


def emit_l1_attn_body(nc, es, TF, TO, half, px, x_in, x_out, dram, ch, bufes=None):
    TT = 512
    NTF = TF // TT
    NQT = TO // TT
    OFF = half * TO
    NH, NKV = 16, 4
    GPK = NH // NKV
    NC32 = TF // 128

    if bufes is None:
        bufes = es
    onesc = bufes.enter_context(nc.sbuf_tensor(px + "onesc", [128, 1], BF16))
    onesr = bufes.enter_context(nc.sbuf_tensor(px + "onesr", [1, 128], F32))
    rot_s = bufes.enter_context(nc.sbuf_tensor(px + "rot", [128, 128], BF16))

    q_d = nc.dram_tensor(px + "q_d", [2048, TO], BF16).ap()
    k_d = nc.dram_tensor(px + "k_d", [512, TF], BF16).ap()
    v_d = nc.dram_tensor(px + "v_d", [TF, 512], BF16).ap()
    ao_d = nc.dram_tensor(px + "ao_d", [2048, TO], BF16).ap()

    x_in_r = x_in.rearrange("(c p) t -> p c t", p=128)
    x_out_r = x_out.rearrange("(c p) t -> p c t", p=128)

    ld0 = Sem(es, nc, px + "ld0")
    with nc.Block() as blk0:
        @blk0.sync
        def _(sync):
            sync.dma_start(out=onesc[:], in_=dram["onescol_bf"]).then_inc(ld0.h, 16)
            sync.dma_start(out=onesr[:], in_=dram["onesrow_f32"]).then_inc(ld0.h, 16)
            sync.dma_start(out=rot_s[:], in_=dram["rot128"]).then_inc(ld0.h, 16)

        @blk0.tensor
        def _(e):
            e.wait_ge(ld0.h, 48)

        @blk0.vector
        def _(e):
            e.wait_ge(ld0.h, 48)

        @blk0.gpsimd
        def _(e):
            e.wait_ge(ld0.h, 48)

        @blk0.scalar
        def _(e):
            e.wait_ge(ld0.h, 48)

    own_tiles = sorted(set(range(OFF // TT, (OFF + TO) // TT)))
    sched = []
    g = kq = vv = qn = 0
    for i in range(NTF):
        for oc in range(NKV):
            sched.append(dict(i=i, kind="K", oc=oc, g=g, kq=kq)); g += 1; kq += 1
        for ts in range(4):
            sched.append(dict(i=i, kind="V", oc=ts, g=g, v=vv)); g += 1; vv += 1
        if i in own_tiles:
            for oc in range(NH):
                sched.append(dict(i=i, kind="Q", oc=oc, g=g, kq=kq, qn=qn))
                g += 1; kq += 1; qn += 1
    qgroups = [s for s in sched if s["kind"] == "Q"]

    with contextlib.ExitStack() as aes:
        xsl = aes.enter_context(nc.sbuf_tensor(px + "xsl", [128, NCH * TT], F32))
        xbf = aes.enter_context(nc.sbuf_tensor(px + "xbf", [128, NCH * TT], BF16))
        sq_b = aes.enter_context(nc.sbuf_tensor(px + "sq_b", [128, NCH * TT], BF16))
        xln = aes.enter_context(nc.sbuf_tensor(px + "xln", [128, NCH * TT], BF16))
        rows = aes.enter_context(nc.sbuf_tensor(px + "rows", [1, 6 * TT], F32))
        lntmp = aes.enter_context(nc.sbuf_tensor(px + "lntmp", [128, TT], F32))
        wks = aes.enter_context(nc.sbuf_tensor(px + "wks", [128, 16 * 512], BF16))
        wvs = aes.enter_context(nc.sbuf_tensor(px + "wvs", [128, 16 * 512], BF16))
        wqt = [aes.enter_context(nc.sbuf_tensor(px + f"wqt{j}", [128, 16 * 128], BF16)) for j in range(2)]
        ropc = aes.enter_context(nc.sbuf_tensor(px + "ropc", [128, TT], F32))
        rops = aes.enter_context(nc.sbuf_tensor(px + "rops", [128, TT], F32))
        kbf = [aes.enter_context(nc.sbuf_tensor(px + f"kbf{j}", [128, TT], BF16)) for j in range(2)]
        tmpA = aes.enter_context(nc.sbuf_tensor(px + "tmpA", [128, TT], F32))
        tmpB = aes.enter_context(nc.sbuf_tensor(px + "tmpB", [128, TT], F32))
        kout = [aes.enter_context(nc.sbuf_tensor(px + f"kout{j}", [128, TT], BF16)) for j in range(2)]
        vout = [aes.enter_context(nc.sbuf_tensor(px + f"vout{j}", [128, TT], BF16)) for j in range(2)]
        ps_ls = aes.enter_context(nc.psum_tensor(px + "ps_ls", [1, TT], F32))
        ps_lq = aes.enter_context(nc.psum_tensor(px + "ps_lq", [1, TT], F32))
        ps_br = aes.enter_context(nc.psum_tensor(px + "ps_br", [128, TT], F32))
        ps_bm = aes.enter_context(nc.psum_tensor(px + "ps_bm", [128, TT], F32))
        ppj = [aes.enter_context(nc.psum_tensor(px + f"ppj{j}", [128, TT], F32)) for j in range(2)]
        prot = aes.enter_context(nc.psum_tensor(px + "prot", [128, TT], F32))

        s_lx = Sem(es, nc, px + "s_lx")
        s_lb = Sem(es, nc, px + "s_lb")
        s_rp = Sem(es, nc, px + "s_rp")
        s_wkv = Sem(es, nc, px + "s_wkv")
        s_wq = Sem(es, nc, px + "s_wq")
        sqd = Sem(es, nc, px + "sqd")
        lmm = Sem(es, nc, px + "lmm")
        rwd = Sem(es, nc, px + "rwd")
        lact = Sem(es, nc, px + "lact")
        apd = Sem(es, nc, px + "apd")
        pjd = Sem(es, nc, px + "pjd")
        evd = Sem(es, nc, px + "evd")
        rotd = Sem(es, nc, px + "rotd")
        cmb = Sem(es, nc, px + "cmb")
        s_stkq = Sem(es, nc, px + "s_stkq")
        s_stv = Sem(es, nc, px + "s_stv")

        with nc.Block() as blk:
            @blk.sync
            def _(sync):
                def stores(i):
                    for s in [s for s in sched if s["i"] == i]:
                        if s["kind"] in ("K", "Q"):
                            j = s["kq"]
                            sync.wait_ge(s_stkq.h, 16 * j)
                            sync.wait_ge(cmb.h, j + 1)
                            if s["kind"] == "K":
                                dst = k_d[s["oc"] * 128:(s["oc"] + 1) * 128, s["i"] * TT:(s["i"] + 1) * TT]
                            else:
                                dst = q_d[s["oc"] * 128:(s["oc"] + 1) * 128,
                                          s["i"] * TT - OFF:(s["i"] + 1) * TT - OFF]
                            sync.dma_start(out=dst, in_=kout[j % 2][:]).then_inc(s_stkq.h, 16)
                        else:
                            j = s["v"]
                            sync.wait_ge(s_stv.h, 16 * j)
                            sync.wait_ge(evd.h, s["g"] + 1)
                            dst = v_d[s["i"] * TT + s["oc"] * 128:s["i"] * TT + (s["oc"] + 1) * 128, :]
                            sync.dma_start(out=dst, in_=vout[j % 2][:]).then_inc(s_stv.h, 16)

                for i in range(NTF):
                    sync.wait_ge(s_lx.h, 16 * i)
                    if i >= 1:
                        sync.wait_ge(apd.h, i)
                    sync.dma_start(out=xsl[:].rearrange("p (c t) -> p c t", t=TT),
                                   in_=x_in_r[:, :, i * TT:(i + 1) * TT]).then_inc(s_lx.h, 16)
                    if i >= 1:
                        stores(i - 1)
                    sync.wait_ge(s_rp.h, 32 * i)
                    if i >= 1:
                        last_kq = max(s["kq"] for s in sched if s["i"] == i - 1 and s["kind"] != "V")
                        sync.wait_ge(cmb.h, last_kq + 1)
                    sync.dma_start(out=ropc[:], in_=dram["ropeC"][:, i * TT:(i + 1) * TT]).then_inc(s_rp.h, 16)
                    sync.dma_start(out=rops[:], in_=dram["ropeS"][:, i * TT:(i + 1) * TT]).then_inc(s_rp.h, 16)
                stores(NTF - 1)
                nkq_all = sum(1 for s in sched if s["kind"] != "V")
                nv_all = sum(1 for s in sched if s["kind"] == "V")
                sync.wait_ge(s_stkq.h, 16 * nkq_all)
                sync.wait_ge(s_stv.h, 16 * nv_all)

            @blk.gpsimd
            def _(g_):
                g_.dma_start(out=wks[:].rearrange("p (c f) -> p c f", f=512),
                             in_=dram["wkT"].rearrange("(c p) f -> p c f", p=128)).then_inc(s_wkv.h, 16)
                g_.dma_start(out=wvs[:].rearrange("p (c f) -> p c f", f=512),
                             in_=dram["wvT"].rearrange("(c p) f -> p c f", p=128)).then_inc(s_wkv.h, 16)
                for i in range(NTF):
                    g_.wait_ge(s_lb.h, 16 * i)
                    if i >= 1:
                        g_.wait_ge(lmm.h, 3 * i - 2)
                    g_.dma_start(out=xbf[:].rearrange("p (c t) -> p c t", t=TT),
                                 in_=x_in_r[:, :, i * TT:(i + 1) * TT]).then_inc(s_lb.h, 16)
                    for s in [s for s in sched if s["i"] == i and s["kind"] == "Q"]:
                        nq = s["qn"]
                        g_.wait_ge(s_wq.h, 16 * nq)
                        if nq >= 2:
                            g_.wait_ge(pjd.h, qgroups[nq - 2]["g"] + 1)
                        g_.dma_start(out=wqt[nq % 2][:].rearrange("p (c f) -> p c f", f=128),
                                     in_=dram["wqT"].rearrange("(c p) f -> p c f", p=128)
                                     [:, :, s["oc"] * 128:(s["oc"] + 1) * 128]).then_inc(s_wq.h, 16)

            @blk.tensor
            def _(te):
                for i in range(NTF):
                    te.wait_ge(sqd.h, i + 1)
                    if i >= 1:
                        te.wait_ge(rwd.h, 2 * i - 1)
                    for c in range(NCH):
                        te.matmul(ps_ls[:], onesc[:], xbf[:, c * TT:(c + 1) * TT],
                                  start=(c == 0), stop=(c == NCH - 1))
                    for c in range(NCH):
                        mm = te.matmul(ps_lq[:], onesc[:], sq_b[:, c * TT:(c + 1) * TT],
                                       start=(c == 0), stop=(c == NCH - 1))
                    mm.then_inc(lmm.h, 1)
                    te.wait_ge(rwd.h, 2 * i + 2)
                    if i >= 1:
                        te.wait_ge(apd.h, i)
                    te.matmul(ps_br[:], onesr[:], rows[0:1, 4 * TT:5 * TT],
                              start=True, stop=True).then_inc(lmm.h, 1)
                    te.matmul(ps_bm[:], onesr[:], rows[0:1, 5 * TT:6 * TT],
                              start=True, stop=True).then_inc(lmm.h, 1)
                    te.wait_ge(apd.h, i + 1)
                    for s in [s for s in sched if s["i"] == i]:
                        gg = s["g"]
                        if gg >= 2:
                            te.wait_ge(evd.h, gg - 1)
                        if s["kind"] == "K":
                            te.wait_ge(s_wkv.h, 32)
                            for ic in range(NCH):
                                mm = te.matmul(ppj[gg % 2][:],
                                               wks[:, ic * 512 + s["oc"] * 128:ic * 512 + (s["oc"] + 1) * 128],
                                               xln[:, ic * TT:(ic + 1) * TT],
                                               start=(ic == 0), stop=(ic == NCH - 1))
                            mm.then_inc(pjd.h, 1)
                        elif s["kind"] == "Q":
                            nq = s["qn"]
                            te.wait_ge(s_wq.h, 16 * (nq + 1))
                            for ic in range(NCH):
                                mm = te.matmul(ppj[gg % 2][:],
                                               wqt[nq % 2][:, ic * 128:(ic + 1) * 128],
                                               xln[:, ic * TT:(ic + 1) * TT],
                                               start=(ic == 0), stop=(ic == NCH - 1))
                            mm.then_inc(pjd.h, 1)
                        else:
                            te.wait_ge(s_wkv.h, 32)
                            ts = s["oc"]
                            for ic in range(NCH):
                                mm = te.matmul(ppj[gg % 2][:],
                                               xln[:, ic * TT + ts * 128:ic * TT + (ts + 1) * 128],
                                               wvs[:, ic * 512:(ic + 1) * 512],
                                               start=(ic == 0), stop=(ic == NCH - 1))
                            mm.then_inc(pjd.h, 1)
                        if s["kind"] in ("K", "Q"):
                            j = s["kq"]
                            te.wait_ge(evd.h, gg + 1)
                            if j >= 1:
                                te.wait_ge(cmb.h, j)
                            te.matmul(prot[:], rot_s[:], kbf[j % 2][:],
                                      start=True, stop=True).then_inc(rotd.h, 1)

            @blk.scalar
            def _(sc):
                sc = Chain(sc, ch["s"])
                for i in range(NTF):
                    sc.wait_ge(s_lb.h, 16 * (i + 1))
                    if i >= 1:
                        sc.wait_ge(lmm.h, 3 * i - 2)
                    sc.activation(sq_b[:], xbf[:], AF.Square, _inc=sqd)
                    sc.wait_ge(rwd.h, 2 * i + 1)
                    sc.activation(rows[0:1, 4 * TT:5 * TT], rows[0:1, 3 * TT:4 * TT], AF.Ln)
                    sc.activation(rows[0:1, 4 * TT:5 * TT], rows[0:1, 4 * TT:5 * TT], AF.Exp,
                                  scale=-0.5, _inc=lact)
                    for s in [s for s in sched if s["i"] == i]:
                        gg = s["g"]
                        sc.wait_ge(pjd.h, gg + 1)
                        if s["kind"] in ("K", "Q"):
                            j = s["kq"]
                            if j >= 2:
                                sc.wait_ge(rotd.h, j - 1)
                                sc.wait_ge(cmb.h, j - 1)
                            sc.activation(kbf[j % 2][:], ppj[gg % 2][:], AF.Copy, _inc=evd)
                        else:
                            jv = s["v"]
                            if jv >= 2:
                                sc.wait_ge(s_stv.h, 16 * (jv - 1))
                            sc.activation(vout[jv % 2][:], ppj[gg % 2][:], AF.Copy, _inc=evd)

            @blk.vector
            def _(v):
                v = Chain(v, ch["v"])
                for i in range(NTF):
                    v.wait_ge(lmm.h, 3 * i + 1)
                    v.tensor_scalar(rows[0:1, 0:TT], ps_ls[:], 1.0 / D, None, op0=ALU.mult)
                    v.tensor_scalar(rows[0:1, TT:2 * TT], ps_lq[:], 1.0 / D, 1e-5,
                                    op0=ALU.mult, op1=ALU.add)
                    v.tensor_mul(rows[0:1, 2 * TT:3 * TT], rows[0:1, 0:TT], rows[0:1, 0:TT])
                    v.scalar_tensor_tensor(rows[0:1, 3 * TT:4 * TT], rows[0:1, 2 * TT:3 * TT], -1.0,
                                           rows[0:1, TT:2 * TT], op0=ALU.mult, op1=ALU.add, _inc=rwd)
                    v.wait_ge(lact.h, i + 1)
                    v.scalar_tensor_tensor(rows[0:1, 5 * TT:6 * TT], rows[0:1, 0:TT], -1.0,
                                           rows[0:1, 4 * TT:5 * TT], op0=ALU.mult, op1=ALU.mult, _inc=rwd)
                    v.wait_ge(lmm.h, 3 * i + 3)
                    v.wait_ge(s_lx.h, 16 * (i + 1))
                    for c in range(NCH):
                        v.tensor_mul(lntmp[:], xsl[:, c * TT:(c + 1) * TT], ps_br[:])
                        v.tensor_add(xln[:, c * TT:(c + 1) * TT], lntmp[:], ps_bm[:],
                                     _inc=apd if c == NCH - 1 else None)
                    v.wait_ge(s_rp.h, 32 * (i + 1))
                    for s in [s for s in sched if s["i"] == i and s["kind"] != "V"]:
                        j = s["kq"]
                        v.wait_ge(evd.h, s["g"] + 1)
                        v.tensor_mul(tmpA[:], kbf[j % 2][:], ropc[:])
                        v.wait_ge(rotd.h, j + 1)
                        v.tensor_mul(tmpB[:], prot[:], rops[:])
                        if j >= 2:
                            v.wait_ge(s_stkq.h, 16 * (j - 1))
                        v.tensor_add(kout[j % 2][:], tmpA[:], tmpB[:], _inc=cmb)

    NRED_K = TF // TT
    NRED_Q = TO // TT

    def red_base(kvh, qj=None):
        b = kvh * (NRED_K + GPK * NRED_Q)
        if qj is None:
            return b
        return b + NRED_K + qj * NRED_Q

    with contextlib.ExitStack() as bes:
        khs = bes.enter_context(nc.sbuf_tensor(px + "khs", [128, TF], BF16))
        vhs = bes.enter_context(nc.sbuf_tensor(px + "vhs", [128, NC32 * 128], BF16))
        qhs = bes.enter_context(nc.sbuf_tensor(px + "qhs", [128, TO], BF16))
        nsq = bes.enter_context(nc.sbuf_tensor(px + "nsq", [128, TF], BF16))
        nrow = bes.enter_context(nc.sbuf_tensor(px + "nrow", [1, 16], F32))
        srow = bes.enter_context(nc.sbuf_tensor(px + "srow", [1, TT], F32))
        ccol = bes.enter_context(nc.sbuf_tensor(px + "ccol", [128, 1], F32))
        att = [bes.enter_context(nc.sbuf_tensor(px + f"att{j}", [128, TT], BF16)) for j in range(2)]
        aob = [bes.enter_context(nc.sbuf_tensor(px + f"aob{j}", [128, TT], BF16)) for j in range(2)]
        rbc = bes.enter_context(nc.sbuf_tensor(px + "rbc", [128, TT], F32))
        ps_s = [bes.enter_context(nc.psum_tensor(px + f"ps_s{j}", [128, TT], F32)) for j in range(2)]
        po = bes.enter_context(nc.psum_tensor(px + "po", [128, TT], F32))
        prow = bes.enter_context(nc.psum_tensor(px + "prow", [1, TT], F32))
        pnrm = [bes.enter_context(nc.psum_tensor(px + f"pnrm{j}", [1, TT], F32)) for j in range(2)]
        pbc = bes.enter_context(nc.psum_tensor(px + "pbc", [128, TT], F32))

        b_kh = Sem(es, nc, px + "b_kh")
        b_vh = Sem(es, nc, px + "b_vh")
        b_qh = Sem(es, nc, px + "b_qh")
        b_sq = Sem(es, nc, px + "b_sq")
        b_nmm = Sem(es, nc, px + "b_nmm")
        b_nrd = Sem(es, nc, px + "b_nrd")
        b_nrm = Sem(es, nc, px + "b_nrm")
        b_ncc = Sem(es, nc, px + "b_ncc")
        b_cb = Sem(es, nc, px + "b_cb")
        b_cc = Sem(es, nc, px + "b_cc")
        b_sm = Sem(es, nc, px + "b_sm")
        b_ex = Sem(es, nc, px + "b_ex")
        b_av = Sem(es, nc, px + "b_av")
        b_rc = Sem(es, nc, px + "b_rc")
        b_bc = Sem(es, nc, px + "b_bc")
        b_bcc = Sem(es, nc, px + "b_bcc")
        b_ao = Sem(es, nc, px + "b_ao")
        b_sao = Sem(es, nc, px + "b_sao")

        with nc.Block() as blk:
            @blk.sync
            def _(sync):
                grp = 0
                for kvh in range(NKV):
                    sync.wait_ge(b_kh.h, 16 * kvh)
                    sync.wait_ge(b_vh.h, 16 * kvh)
                    if kvh >= 1:
                        sync.wait_ge(b_av.h, NC32 * NQT * GPK * kvh)
                    sync.dma_start(out=khs[:], in_=k_d[kvh * 128:(kvh + 1) * 128, :]).then_inc(b_kh.h, 16)
                    sync.dma_start(out=vhs[:].rearrange("p (c f) -> p c f", f=128),
                                   in_=v_d.rearrange("(c p) f -> p c f", p=128)
                                   [:, :, kvh * 128:(kvh + 1) * 128]).then_inc(b_vh.h, 16)
                    for qj in range(GPK):
                        qh = kvh * GPK + qj
                        sync.wait_ge(b_qh.h, 16 * qh)
                        if qh >= 1:
                            sync.wait_ge(b_av.h, NC32 * NQT * qh)
                        sync.dma_start(out=qhs[:], in_=q_d[qh * 128:(qh + 1) * 128, :]).then_inc(b_qh.h, 16)
                        for qt in range(NQT):
                            sync.wait_ge(b_sao.h, 16 * grp)
                            sync.wait_ge(b_ao.h, grp + 1)
                            sync.dma_start(out=ao_d[qh * 128:(qh + 1) * 128, qt * TT:(qt + 1) * TT],
                                           in_=aob[grp % 2][:]).then_inc(b_sao.h, 16)
                            grp += 1

            @blk.tensor
            def _(te):
                grp = 0
                icnt = 0
                red = 0
                for kvh in range(NKV):
                    te.wait_ge(b_sq.h, (GPK + 1) * kvh + 1)
                    te.wait_ge(b_vh.h, 16 * (kvh + 1))
                    for j in range(NRED_K):
                        if red >= 2:
                            te.wait_ge(b_nrd.h, red - 1)
                        te.matmul(pnrm[red % 2][:], onesc[:], nsq[:, j * TT:(j + 1) * TT],
                                  start=True, stop=True).then_inc(b_nmm.h, 1)
                        red += 1
                    for qj in range(GPK):
                        qh = kvh * GPK + qj
                        te.wait_ge(b_sq.h, (GPK + 1) * kvh + qj + 2)
                        for j in range(NRED_Q):
                            if red >= 2:
                                te.wait_ge(b_nrd.h, red - 1)
                            te.matmul(pnrm[red % 2][:], onesc[:], nsq[:, j * TT:(j + 1) * TT],
                                      start=True, stop=True).then_inc(b_nmm.h, 1)
                            red += 1
                        te.wait_ge(b_ncc.h, qh + 1)
                        if grp >= 1:
                            te.wait_ge(b_ao.h, grp)
                        te.matmul(pbc[:, 0:1], onesr[:], nrow[0:1, 15:16],
                                  start=True, stop=True).then_inc(b_cb.h, 1)
                        for qt in range(NQT):
                            te.wait_ge(b_cc.h, qh + 1)
                            if grp >= 1:
                                te.wait_ge(b_ao.h, grp)
                            for c in range(NC32):
                                if icnt >= 2:
                                    te.wait_ge(b_ex.h, icnt - 1)
                                te.matmul(ps_s[icnt % 2][:], khs[:, c * 128:(c + 1) * 128],
                                          qhs[:, qt * TT:(qt + 1) * TT],
                                          start=True, stop=True).then_inc(b_sm.h, 1)
                                te.wait_ge(b_ex.h, icnt + 1)
                                te.matmul(po[:], vhs[:, c * 128:(c + 1) * 128], att[icnt % 2][:],
                                          start=(c == 0), stop=(c == NC32 - 1))
                                te.matmul(prow[:], onesc[:], att[icnt % 2][:],
                                          start=(c == 0), stop=(c == NC32 - 1)).then_inc(b_av.h, 1)
                                icnt += 1
                            te.wait_ge(b_rc.h, grp + 1)
                            te.matmul(pbc[:], onesr[:], srow[0:1, :],
                                      start=True, stop=True).then_inc(b_bc.h, 1)
                            grp += 1

            @blk.scalar
            def _(sc):
                sc = Chain(sc, ch["s"])
                icnt = 0
                for kvh in range(NKV):
                    sc.wait_ge(b_kh.h, 16 * (kvh + 1))
                    if kvh >= 1:
                        sc.wait_ge(b_nmm.h, red_base(kvh))
                    sc.activation(nsq[:], khs[:], AF.Square, _inc=b_sq)
                    for qj in range(GPK):
                        qh = kvh * GPK + qj
                        sc.wait_ge(b_qh.h, 16 * (qh + 1))
                        sc.wait_ge(b_nmm.h, red_base(kvh, qj))
                        sc.activation(nsq[:, 0:TO], qhs[:], AF.Square, _inc=b_sq)
                        sc.wait_ge(b_nrm.h, qh + 1)
                        sc.activation(nrow[0:1, 12:13], nrow[0:1, 11:12], AF.Copy,
                                      scale=nrow[0:1, 10:11])
                        sc.activation(nrow[0:1, 13:14], nrow[0:1, 12:13], AF.Sqrt)
                        sc.activation(nrow[0:1, 14:15], nrow[0:1, 13:14], AF.Copy,
                                      scale=L1_SCALE, bias=-EXP_SHIFT_MARGIN)
                        sc.activation(nrow[0:1, 12:13], nrow[0:1, 14:15], AF.Relu)
                        sc.activation(nrow[0:1, 15:16], nrow[0:1, 12:13], AF.Copy,
                                      scale=-1.0, _inc=b_ncc)
                        sc.wait_ge(b_cb.h, qh + 1)
                        sc.activation(ccol[:], pbc[:, 0:1], AF.Copy, _inc=b_cc)
                        for qt in range(NQT):
                            for c in range(NC32):
                                sc.wait_ge(b_sm.h, icnt + 1)
                                if icnt >= 2:
                                    sc.wait_ge(b_av.h, icnt - 1)
                                sc.activation(att[icnt % 2][:], ps_s[icnt % 2][:], AF.Exp,
                                              bias=ccol[:, 0:1], scale=L1_SCALE, _inc=b_ex)
                                icnt += 1
                            grp_l = qh * NQT + qt
                            sc.wait_ge(b_bc.h, grp_l + 1)
                            if grp_l >= 1:
                                sc.wait_ge(b_ao.h, grp_l)
                            sc.activation(rbc[:], pbc[:], AF.Copy, _inc=b_bcc)

            @blk.vector
            def _(v):
                v = Chain(v, ch["v"])
                grp = 0
                red = 0
                for kvh in range(NKV):
                    if kvh >= 1:
                        v.wait_ge(b_ncc.h, GPK * kvh)
                    for j in range(NRED_K):
                        v.wait_ge(b_nmm.h, red + 1)
                        dstc = 10 if j == NRED_K - 1 else (0 if j == 0 else 1)
                        if j == 0:
                            v.reduce_max(nrow[0:1, 0:1], pnrm[red % 2][:], axis=AX.X, _inc=b_nrd)
                        else:
                            v.reduce_max(nrow[0:1, 1:2], pnrm[red % 2][:], axis=AX.X, _inc=b_nrd)
                            v.tensor_max(nrow[0:1, 0:1], nrow[0:1, 0:1], nrow[0:1, 1:2])
                        red += 1
                    v.tensor_max(nrow[0:1, 10:11], nrow[0:1, 0:1], nrow[0:1, 0:1])
                    for qj in range(GPK):
                        qh = kvh * GPK + qj
                        for j in range(NRED_Q):
                            v.wait_ge(b_nmm.h, red + 1)
                            if j == 0:
                                v.reduce_max(nrow[0:1, 0:1], pnrm[red % 2][:], axis=AX.X, _inc=b_nrd)
                            else:
                                v.reduce_max(nrow[0:1, 1:2], pnrm[red % 2][:], axis=AX.X, _inc=b_nrd)
                                v.tensor_max(nrow[0:1, 0:1], nrow[0:1, 0:1], nrow[0:1, 1:2])
                            red += 1
                        v.tensor_max(nrow[0:1, 11:12], nrow[0:1, 0:1], nrow[0:1, 0:1],
                                     _inc=b_nrm)
                        for qt in range(NQT):
                            v.wait_ge(b_av.h, NC32 * (grp + 1))
                            v.tensor_copy(srow[0:1, :], prow[:])
                            v.reciprocal(srow[0:1, :], srow[0:1, :], _inc=b_rc)
                            v.wait_ge(b_bcc.h, grp + 1)
                            if grp >= 2:
                                v.wait_ge(b_sao.h, 16 * (grp - 1))
                            v.tensor_mul(aob[grp % 2][:], po[:], rbc[:], _inc=b_ao)
                            grp += 1

    with contextlib.ExitStack() as ces:
        wos = ces.enter_context(nc.sbuf_tensor(px + "wos", [128, 16 * 2048], BF16))
        aos = ces.enter_context(nc.sbuf_tensor(px + "aos", [128, 16 * TT], BF16))
        xt = [ces.enter_context(nc.sbuf_tensor(px + f"xt{j}", [128, TT], F32)) for j in range(2)]
        ot = [ces.enter_context(nc.sbuf_tensor(px + f"ot{j}", [128, TT], F32)) for j in range(2)]
        pc = [ces.enter_context(nc.psum_tensor(px + f"pc{j}", [128, TT], F32)) for j in range(2)]

        c_lw = Sem(es, nc, px + "c_lw")
        c_lao = Sem(es, nc, px + "c_lao")
        c_lx = Sem(es, nc, px + "c_lx")
        c_mm = Sem(es, nc, px + "c_mm")
        c_ev = Sem(es, nc, px + "c_ev")
        c_st = Sem(es, nc, px + "c_st")

        ao_r = ao_d.rearrange("(c p) t -> p c t", p=128)

        with nc.Block() as blk:
            @blk.gpsimd
            def _(g_):
                g_.wait_ge(b_sao.h, 16 * NH * NQT)
                g_.dma_start(out=wos[:].rearrange("p (c f) -> p c f", f=2048),
                             in_=dram["woT"].rearrange("(c p) f -> p c f", p=128)).then_inc(c_lw.h, 16)

            @blk.sync
            def _(sync):
                sync.wait_ge(b_sao.h, 16 * NH * NQT)
                idx = 0
                for qt in range(NQT):
                    sync.wait_ge(c_lao.h, 16 * qt)
                    if qt >= 1:
                        sync.wait_ge(c_mm.h, 16 * qt)
                    sync.dma_start(out=aos[:].rearrange("p (c t) -> p c t", t=TT),
                                   in_=ao_r[:, :, qt * TT:(qt + 1) * TT]).then_inc(c_lao.h, 16)
                    for dm in range(NCH):
                        sync.wait_ge(c_lx.h, 16 * idx)
                        if idx >= 2:
                            sync.wait_ge(c_ev.h, idx - 1)
                        sync.dma_start(out=xt[idx % 2][:],
                                       in_=x_in_r[:, dm, OFF + qt * TT:OFF + (qt + 1) * TT]).then_inc(c_lx.h, 16)
                        sync.wait_ge(c_ev.h, idx + 1)
                        sync.wait_ge(c_st.h, 16 * idx)
                        sync.dma_start(out=x_out_r[:, dm, qt * TT:(qt + 1) * TT],
                                       in_=ot[idx % 2][:]).then_inc(c_st.h, 16)
                        idx += 1
                sync.wait_ge(c_st.h, 16 * NQT * NCH)

            @blk.tensor
            def _(te):
                te.wait_ge(c_lw.h, 16)
                idx = 0
                for qt in range(NQT):
                    te.wait_ge(c_lao.h, 16 * (qt + 1))
                    for dm in range(NCH):
                        if idx >= 2:
                            te.wait_ge(c_ev.h, idx - 1)
                        for fc in range(NCH):
                            mm = te.matmul(pc[idx % 2][:],
                                           wos[:, fc * 2048 + dm * 128:fc * 2048 + (dm + 1) * 128],
                                           aos[:, fc * TT:(fc + 1) * TT],
                                           start=(fc == 0), stop=(fc == NCH - 1))
                        mm.then_inc(c_mm.h, 1)
                        idx += 1

            @blk.vector
            def _(v):
                v = Chain(v, ch["v"])
                idx = 0
                for qt in range(NQT):
                    for dm in range(NCH):
                        v.wait_ge(c_mm.h, idx + 1)
                        v.wait_ge(c_lx.h, 16 * (idx + 1))
                        if idx >= 2:
                            v.wait_ge(c_st.h, 16 * (idx - 1))
                        v.tensor_add(ot[idx % 2][:], xt[idx % 2][:], pc[idx % 2][:], _inc=c_ev)
                        idx += 1


def build_l1_attn(TF=4096, TO=2048, half=0):
    nc = bass.Bass("TRN2", target_bir_lowering=False)
    ins = {}
    ins["x_in"] = nc.dram_tensor("x_in", [D, TF], F32, kind="ExternalInput").ap()
    for nm, shp, dt in (("wqT", [2048, 2048], F32), ("wkT", [2048, 512], F32),
                        ("wvT", [2048, 512], F32), ("woT", [2048, 2048], F32),
                        ("ropeC", [128, TF], F32), ("ropeS", [128, TF], F32),
                        ("rot128", [128, 128], BF16),
                        ("onescol_bf", [128, 1], BF16), ("onesrow_f32", [1, 128], F32)):
        ins[nm] = nc.dram_tensor(nm, shp, dt, kind="ExternalInput").ap()
    x_out = nc.dram_tensor("x_out", [D, TO], F32, kind="ExternalOutput").ap()
    es = contextlib.ExitStack()
    with es:
        ch = {"v": ChainState(es, nc, "a_chv"), "s": ChainState(es, nc, "a_chs")}
        emit_l1_attn_body(nc, es, TF, TO, half, "a_", ins["x_in"], x_out,
                          {k: v for k, v in ins.items() if k != "x_in"}, ch)
    return nc



L0_SCALE = 192.0 ** -0.5


def prep_l0_attn_host(w, TF=4096):
    out = {
        "wdkvT": np.ascontiguousarray(w["W_down_kv"].T),
        "wdqT": np.ascontiguousarray(w["W_down_q"].T),
        "wupkT": np.ascontiguousarray(w["W_up_k"].T),
        "wupvT": np.ascontiguousarray(w["W_up_v"].T),
        "wupqT": np.ascontiguousarray(w["W_up_q"].T),
        "wqrT": np.ascontiguousarray(w["W_qr"].T),
        "wkrT": np.ascontiguousarray(w["W_kr"].T),
        "wfcT": np.ascontiguousarray(w["W_fc"].T),
    }

    def col(b):
        return np.ascontiguousarray(b.reshape(-1, 128).T.astype(np.float32))

    out["bdkv"] = col(w["b_down_kv"])
    out["bdq"] = col(w["b_down_q"])
    out["bupk"] = col(w["b_up_k"])
    out["bupv"] = col(w["b_up_v"])
    out["bupq"] = col(w["b_up_q"])
    out["bqr"] = col(w["b_qr"])
    out["bkr"] = np.ascontiguousarray(w["b_kr"].reshape(64, 1).astype(np.float32))
    out["bfc"] = col(w["b_fc"])
    theta = 1.0 / 10000.0 ** (np.arange(0, 64, 2) / 64.0)
    ang = np.outer(theta, np.arange(TF, dtype=np.float64))
    CP64 = np.repeat(np.sin(ang), 2, axis=0)
    SP64 = np.repeat(np.cos(ang), 2, axis=0)
    out["cp128"] = np.ascontiguousarray(np.tile(CP64, (2, 1)).astype(np.float32))
    out["sp128"] = np.ascontiguousarray(np.tile(SP64, (2, 1)).astype(np.float32))
    R64 = np.zeros((64, 64), dtype=np.float32)
    for i in range(32):
        R64[2 * i + 1, 2 * i] = -1.0
        R64[2 * i, 2 * i + 1] = 1.0
    R128 = np.zeros((128, 128), dtype=np.float32)
    R128[0:64, 0:64] = R64
    R128[64:128, 64:128] = R64
    out["rot128i"] = R128.astype(BF)
    return out


def emit_l0_attn_body(nc, es, TF, TO, half, px, x_in, x_out, dram, ch, bufes=None):
    TT = 512
    NTF = TF // TT
    NQT = TO // TT
    OFF = half * TO
    NH = 16
    NC32 = TF // 128

    if bufes is None:
        bufes = es
    onesc = bufes.enter_context(nc.sbuf_tensor(px + "onesc", [128, 1], BF16))
    onesr = bufes.enter_context(nc.sbuf_tensor(px + "onesr", [1, 128], F32))
    rot_s = bufes.enter_context(nc.sbuf_tensor(px + "rot", [128, 128], BF16))

    q_d = nc.dram_tensor(px + "q_d", [2048, TO], BF16).ap()
    k_d = nc.dram_tensor(px + "k_d", [2048, TF], BF16).ap()
    v_d = nc.dram_tensor(px + "v_d", [TF, 1024], BF16).ap()
    ao_d = nc.dram_tensor(px + "ao_d", [1024, TO], BF16).ap()

    x_in_r = x_in.rearrange("(c p) t -> p c t", p=128)
    x_out_r = x_out.rearrange("(c p) t -> p c t", p=128)
    q_hsd = q_d.rearrange("(h s d) t -> h s d t", s=2, d=64)
    k_hsd = k_d.rearrange("(h s d) t -> h s d t", s=2, d=64)

    ld0 = Sem(es, nc, px + "ld0")
    bias_s = {}
    bias_shapes = dict(bdkv=4, bdq=4, bupk=8, bupv=8, bupq=8, bqr=8, bfc=16)
    for nm, w_ in bias_shapes.items():
        bias_s[nm] = bufes.enter_context(nc.sbuf_tensor(px + nm, [128, w_], F32))
    bias_s["bkr"] = bufes.enter_context(nc.sbuf_tensor(px + "bkr", [64, 1], F32))

    with nc.Block() as blk0:
        @blk0.sync
        def _(sync):
            n = 0
            for nm, ap_ in (("onescol_bf", onesc), ("onesrow_f32", onesr), ("rot128i", rot_s)):
                sync.dma_start(out=ap_[:], in_=dram[nm]).then_inc(ld0.h, 16)
                n += 16
            for nm in list(bias_shapes) + ["bkr"]:
                sync.dma_start(out=bias_s[nm][:], in_=dram[nm]).then_inc(ld0.h, 16)
                n += 16
            ld0.n = n

        @blk0.tensor
        def _(e):
            e.wait_ge(ld0.h, 176)

        @blk0.vector
        def _(e):
            e.wait_ge(ld0.h, 176)

        @blk0.gpsimd
        def _(e):
            e.wait_ge(ld0.h, 176)

        @blk0.scalar
        def _(e):
            e.wait_ge(ld0.h, 176)

    own_tiles = sorted(set(range(OFF // TT, (OFF + TO) // TT)))
    sched = []
    g = jr = je = 0
    for i in range(NTF):
        own = i in own_tiles
        def add(kind, oc, **kw):
            nonlocal g, jr, je
            d = dict(i=i, kind=kind, oc=oc, g=g, **kw)
            if kind in ("QR", "KR"):
                d["jr"] = jr; jr += 1
            elif kind in ("KC", "VC", "QC"):
                d["je"] = je; je += 1
            sched.append(d); g += 1
        for oc in range(4):
            add("CKV", oc)
        for oc in range(8):
            add("KC", oc)
        for oc in range(8):
            add("VC", oc)
        add("KR", 0)
        if own:
            for oc in range(4):
                add("CQ", oc)
            for oc in range(8):
                add("QC", oc)
            for oc in range(8):
                add("QR", oc)
    NJR, NJE = jr, je
    rgroups = [s for s in sched if s["kind"] in ("QR", "KR")]

    with contextlib.ExitStack() as aes:
        xsl = aes.enter_context(nc.sbuf_tensor(px + "xsl", [128, NCH * TT], F32))
        xbf = aes.enter_context(nc.sbuf_tensor(px + "xbf", [128, NCH * TT], BF16))
        sq_b = aes.enter_context(nc.sbuf_tensor(px + "sq_b", [128, NCH * TT], BF16))
        xln = aes.enter_context(nc.sbuf_tensor(px + "xln", [128, NCH * TT], BF16))
        rows = aes.enter_context(nc.sbuf_tensor(px + "rows", [1, 6 * TT], F32))
        lntmp = aes.enter_context(nc.sbuf_tensor(px + "lntmp", [128, TT], F32))
        wdkv = aes.enter_context(nc.sbuf_tensor(px + "wdkv", [128, 16 * 512], BF16))
        wdq = aes.enter_context(nc.sbuf_tensor(px + "wdq", [128, 16 * 512], BF16))
        wupk = aes.enter_context(nc.sbuf_tensor(px + "wupk", [128, 4 * 1024], BF16))
        wupv = aes.enter_context(nc.sbuf_tensor(px + "wupv", [128, 4 * 1024], BF16))
        wupq = aes.enter_context(nc.sbuf_tensor(px + "wupq", [128, 4 * 1024], BF16))
        wqr = aes.enter_context(nc.sbuf_tensor(px + "wqr", [128, 4 * 1024], BF16))
        wkr = aes.enter_context(nc.sbuf_tensor(px + "wkr", [128, 16 * 64], BF16))
        ckv_sb = aes.enter_context(nc.sbuf_tensor(px + "ckv", [128, 4 * TT], BF16))
        cq_sb = aes.enter_context(nc.sbuf_tensor(px + "cq", [128, 4 * TT], BF16))
        cp_s = aes.enter_context(nc.sbuf_tensor(px + "cp", [128, TT], F32))
        sp_s = aes.enter_context(nc.sbuf_tensor(px + "sp", [128, TT], F32))
        qrbf = [aes.enter_context(nc.sbuf_tensor(px + f"qrbf{j}", [128, TT], BF16)) for j in range(2)]
        tmpA = aes.enter_context(nc.sbuf_tensor(px + "tmpA", [128, TT], F32))
        tmpB = aes.enter_context(nc.sbuf_tensor(px + "tmpB", [128, TT], F32))
        kout = [aes.enter_context(nc.sbuf_tensor(px + f"kout{j}", [128, TT], BF16)) for j in range(2)]
        eout = [aes.enter_context(nc.sbuf_tensor(px + f"eout{j}", [128, TT], BF16)) for j in range(2)]
        ps_ls = aes.enter_context(nc.psum_tensor(px + "ps_ls", [1, TT], F32))
        ps_lq = aes.enter_context(nc.psum_tensor(px + "ps_lq", [1, TT], F32))
        ps_br = aes.enter_context(nc.psum_tensor(px + "ps_br", [128, TT], F32))
        ps_bm = aes.enter_context(nc.psum_tensor(px + "ps_bm", [128, TT], F32))
        ppj = [aes.enter_context(nc.psum_tensor(px + f"ppj{j}", [128, TT], F32)) for j in range(2)]
        prot = aes.enter_context(nc.psum_tensor(px + "prot", [128, TT], F32))

        s_lx = Sem(es, nc, px + "s_lx")
        s_lb = Sem(es, nc, px + "s_lb")
        s_rp = Sem(es, nc, px + "s_rp")
        s_w = Sem(es, nc, px + "s_w")
        sqd = Sem(es, nc, px + "sqd")
        lmm = Sem(es, nc, px + "lmm")
        rwd = Sem(es, nc, px + "rwd")
        lact = Sem(es, nc, px + "lact")
        apd = Sem(es, nc, px + "apd")
        pjd = Sem(es, nc, px + "pjd")
        evd = Sem(es, nc, px + "evd")
        rotd = Sem(es, nc, px + "rotd")
        cmb = Sem(es, nc, px + "cmb")
        s_stkq = Sem(es, nc, px + "s_stkq")
        s_ste = Sem(es, nc, px + "s_ste")

        def dst_of(s):
            i, oc = s["i"], s["oc"]
            if s["kind"] == "KC":
                return k_hsd[2 * oc:2 * oc + 2, 0, :, i * TT:(i + 1) * TT]
            if s["kind"] == "QC":
                return q_hsd[2 * oc:2 * oc + 2, 0, :, i * TT - OFF:(i + 1) * TT - OFF]
            if s["kind"] == "VC":
                ts, fh = oc % 4, oc // 4
                r0 = i * TT + ts * 128
                return v_d[r0:r0 + 128, fh * 512:(fh + 1) * 512]
            raise AssertionError

        with nc.Block() as blk:
            @blk.sync
            def _(sync):
                def stores(i):
                    for s in [s for s in sched if s["i"] == i]:
                        if s["kind"] in ("KC", "VC", "QC"):
                            j = s["je"]
                            sync.wait_ge(s_ste.h, 16 * j)
                            sync.wait_ge(evd.h, s["g"] + 1)
                            sync.dma_start(out=dst_of(s), in_=eout[j % 2][:]).then_inc(s_ste.h, 16)
                        elif s["kind"] == "QR":
                            j = s["jr"]
                            sync.wait_ge(s_stkq.h, 16 * j)
                            sync.wait_ge(cmb.h, j + 1)
                            oc = s["oc"]
                            sync.dma_start(out=q_hsd[2 * oc:2 * oc + 2, 1, :, s["i"] * TT - OFF:(s["i"] + 1) * TT - OFF],
                                           in_=kout[j % 2][:]).then_inc(s_stkq.h, 16)
                        elif s["kind"] == "KR":
                            j = s["jr"]
                            sync.wait_ge(s_stkq.h, 16 * j)
                            sync.wait_ge(cmb.h, j + 1)
                            sync.dma_start(out=k_d.rearrange("(h s d) t -> d h s t", s=2, d=64)
                                           [:, :, 1, s["i"] * TT:(s["i"] + 1) * TT],
                                           in_=kout[j % 2][0:64, :].rearrange("d (o t) -> d o t", o=1)
                                           .broadcast_to([64, NH, TT])).then_inc(s_stkq.h, 16)

                for i in range(NTF):
                    sync.wait_ge(s_lx.h, 16 * i)
                    if i >= 1:
                        sync.wait_ge(apd.h, i)
                    sync.dma_start(out=xsl[:].rearrange("p (c t) -> p c t", t=TT),
                                   in_=x_in_r[:, :, i * TT:(i + 1) * TT]).then_inc(s_lx.h, 16)
                    if i >= 1:
                        stores(i - 1)
                    sync.wait_ge(s_rp.h, 32 * i)
                    if i >= 1:
                        last_jr = max(s["jr"] for s in sched if s["i"] == i - 1 and s["kind"] in ("QR", "KR"))
                        sync.wait_ge(cmb.h, last_jr + 1)
                    sync.dma_start(out=cp_s[:], in_=dram["cp128"][:, i * TT:(i + 1) * TT]).then_inc(s_rp.h, 16)
                    sync.dma_start(out=sp_s[:], in_=dram["sp128"][:, i * TT:(i + 1) * TT]).then_inc(s_rp.h, 16)
                stores(NTF - 1)
                sync.wait_ge(s_stkq.h, 16 * NJR)
                sync.wait_ge(s_ste.h, 16 * NJE)

            @blk.gpsimd
            def _(g_):
                for nm, ap_, f in (("wdkvT", wdkv, 512), ("wdqT", wdq, 512), ("wupkT", wupk, 1024),
                                   ("wupvT", wupv, 1024), ("wupqT", wupq, 1024), ("wqrT", wqr, 1024),
                                   ("wkrT", wkr, 64)):
                    g_.dma_start(out=ap_[:].rearrange("p (c f) -> p c f", f=f),
                                 in_=dram[nm].rearrange("(c p) f -> p c f", p=128)).then_inc(s_w.h, 16)
                for i in range(NTF):
                    g_.wait_ge(s_lb.h, 16 * i)
                    if i >= 1:
                        g_.wait_ge(lmm.h, 3 * i - 2)
                    g_.dma_start(out=xbf[:].rearrange("p (c t) -> p c t", t=TT),
                                 in_=x_in_r[:, :, i * TT:(i + 1) * TT]).then_inc(s_lb.h, 16)

            @blk.tensor
            def _(te):
                te.wait_ge(s_w.h, 16 * 7)
                for i in range(NTF):
                    te.wait_ge(sqd.h, i + 1)
                    if i >= 1:
                        te.wait_ge(rwd.h, 2 * i - 1)
                    for c in range(NCH):
                        te.matmul(ps_ls[:], onesc[:], xbf[:, c * TT:(c + 1) * TT],
                                  start=(c == 0), stop=(c == NCH - 1))
                    for c in range(NCH):
                        mm = te.matmul(ps_lq[:], onesc[:], sq_b[:, c * TT:(c + 1) * TT],
                                       start=(c == 0), stop=(c == NCH - 1))
                    mm.then_inc(lmm.h, 1)
                    te.wait_ge(rwd.h, 2 * i + 2)
                    if i >= 1:
                        te.wait_ge(apd.h, i)
                    te.matmul(ps_br[:], onesr[:], rows[0:1, 4 * TT:5 * TT],
                              start=True, stop=True).then_inc(lmm.h, 1)
                    te.matmul(ps_bm[:], onesr[:], rows[0:1, 5 * TT:6 * TT],
                              start=True, stop=True).then_inc(lmm.h, 1)
                    te.wait_ge(apd.h, i + 1)
                    tile_sched = [s for s in sched if s["i"] == i]
                    ckv_done = max(s["g"] for s in tile_sched if s["kind"] == "CKV") + 1
                    cq_done = max([s["g"] for s in tile_sched if s["kind"] == "CQ"], default=-1) + 1
                    for s in tile_sched:
                        gg, oc = s["g"], s["oc"]
                        if gg >= 2:
                            te.wait_ge(evd.h, gg - 1)
                        kind = s["kind"]
                        if kind in ("KC", "VC"):
                            te.wait_ge(evd.h, ckv_done)
                        elif kind in ("QC", "QR"):
                            te.wait_ge(evd.h, cq_done)
                        if kind in ("CKV", "CQ"):
                            w_ = wdkv if kind == "CKV" else wdq
                            for ic in range(NCH):
                                mm = te.matmul(ppj[gg % 2][:],
                                               w_[:, ic * 512 + oc * 128:ic * 512 + (oc + 1) * 128],
                                               xln[:, ic * TT:(ic + 1) * TT],
                                               start=(ic == 0), stop=(ic == NCH - 1))
                            mm.then_inc(pjd.h, 1)
                        elif kind in ("KC", "QC", "QR"):
                            w_ = {"KC": wupk, "QC": wupq, "QR": wqr}[kind]
                            src = ckv_sb if kind == "KC" else cq_sb
                            for ic in range(4):
                                mm = te.matmul(ppj[gg % 2][:],
                                               w_[:, ic * 1024 + oc * 128:ic * 1024 + (oc + 1) * 128],
                                               src[:, ic * TT:(ic + 1) * TT],
                                               start=(ic == 0), stop=(ic == 3))
                            mm.then_inc(pjd.h, 1)
                        elif kind == "VC":
                            ts, fh = oc % 4, oc // 4
                            for ic in range(4):
                                mm = te.matmul(ppj[gg % 2][:],
                                               ckv_sb[:, ic * TT + ts * 128:ic * TT + (ts + 1) * 128],
                                               wupv[:, ic * 1024 + fh * 512:ic * 1024 + (fh + 1) * 512],
                                               start=(ic == 0), stop=(ic == 3))
                            mm.then_inc(pjd.h, 1)
                        else:
                            for ic in range(NCH):
                                mm = te.matmul(ppj[gg % 2][0:64, :],
                                               wkr[:, ic * 64:(ic + 1) * 64],
                                               xln[:, ic * TT:(ic + 1) * TT],
                                               start=(ic == 0), stop=(ic == NCH - 1))
                            mm.then_inc(pjd.h, 1)
                        if kind in ("QR", "KR"):
                            j = s["jr"]
                            te.wait_ge(evd.h, gg + 1)
                            if j >= 1:
                                te.wait_ge(cmb.h, j)
                            if kind == "QR":
                                te.matmul(prot[:], rot_s[:], qrbf[j % 2][:],
                                          start=True, stop=True).then_inc(rotd.h, 1)
                            else:
                                te.matmul(prot[0:64, :], rot_s[0:64, 0:64], qrbf[j % 2][0:64, :],
                                          start=True, stop=True).then_inc(rotd.h, 1)

            @blk.scalar
            def _(sc):
                sc = Chain(sc, ch["s"])
                for i in range(NTF):
                    sc.wait_ge(s_lb.h, 16 * (i + 1))
                    if i >= 1:
                        sc.wait_ge(lmm.h, 3 * i - 2)
                    sc.activation(sq_b[:], xbf[:], AF.Square, _inc=sqd)
                    sc.wait_ge(rwd.h, 2 * i + 1)
                    sc.activation(rows[0:1, 4 * TT:5 * TT], rows[0:1, 3 * TT:4 * TT], AF.Ln)
                    sc.activation(rows[0:1, 4 * TT:5 * TT], rows[0:1, 4 * TT:5 * TT], AF.Exp,
                                  scale=-0.5, _inc=lact)
                    for s in [s for s in sched if s["i"] == i]:
                        gg, oc = s["g"], s["oc"]
                        kind = s["kind"]
                        sc.wait_ge(pjd.h, gg + 1)
                        if kind == "CKV":
                            if i >= 1:
                                prevg = max(x["g"] for x in sched if x["i"] == i - 1 and x["kind"] in ("KC", "VC"))
                                sc.wait_ge(pjd.h, prevg + 1)
                            sc.activation(ckv_sb[:, oc * TT:(oc + 1) * TT], ppj[gg % 2][:], AF.Identity,
                                          bias=bias_s["bdkv"][:, oc:oc + 1], _inc=evd)
                        elif kind == "CQ":
                            prev_own = [t for t in own_tiles if t < i]
                            if prev_own:
                                prevg = max(x["g"] for x in sched if x["i"] == prev_own[-1]
                                            and x["kind"] in ("QC", "QR"))
                                sc.wait_ge(pjd.h, prevg + 1)
                            sc.activation(cq_sb[:, oc * TT:(oc + 1) * TT], ppj[gg % 2][:], AF.Identity,
                                          bias=bias_s["bdq"][:, oc:oc + 1], _inc=evd)
                        elif kind in ("KC", "QC"):
                            j = s["je"]
                            if j >= 2:
                                sc.wait_ge(s_ste.h, 16 * (j - 1))
                            bnm = "bupk" if kind == "KC" else "bupq"
                            sc.activation(eout[j % 2][:], ppj[gg % 2][:], AF.Identity,
                                          bias=bias_s[bnm][:, oc:oc + 1], _inc=evd)
                        elif kind == "VC":
                            j = s["je"]
                            if j >= 2:
                                sc.wait_ge(s_ste.h, 16 * (j - 1))
                            sc.activation(eout[j % 2][:], ppj[gg % 2][:], AF.Copy, _inc=evd)
                        else:
                            j = s["jr"]
                            if j >= 2:
                                sc.wait_ge(rotd.h, j - 1)
                                sc.wait_ge(cmb.h, j - 1)
                            if kind == "QR":
                                sc.activation(qrbf[j % 2][:], ppj[gg % 2][:], AF.Identity,
                                              bias=bias_s["bqr"][:, oc:oc + 1], _inc=evd)
                            else:
                                sc.activation(qrbf[j % 2][0:64, :], ppj[gg % 2][0:64, :], AF.Identity,
                                              bias=bias_s["bkr"][:, 0:1], _inc=evd)

            @blk.vector
            def _(v):
                v = Chain(v, ch["v"])
                for i in range(NTF):
                    v.wait_ge(lmm.h, 3 * i + 1)
                    v.tensor_scalar(rows[0:1, 0:TT], ps_ls[:], 1.0 / D, None, op0=ALU.mult)
                    v.tensor_scalar(rows[0:1, TT:2 * TT], ps_lq[:], 1.0 / D, 1e-5,
                                    op0=ALU.mult, op1=ALU.add)
                    v.tensor_mul(rows[0:1, 2 * TT:3 * TT], rows[0:1, 0:TT], rows[0:1, 0:TT])
                    v.scalar_tensor_tensor(rows[0:1, 3 * TT:4 * TT], rows[0:1, 2 * TT:3 * TT], -1.0,
                                           rows[0:1, TT:2 * TT], op0=ALU.mult, op1=ALU.add, _inc=rwd)
                    v.wait_ge(lact.h, i + 1)
                    v.scalar_tensor_tensor(rows[0:1, 5 * TT:6 * TT], rows[0:1, 0:TT], -1.0,
                                           rows[0:1, 4 * TT:5 * TT], op0=ALU.mult, op1=ALU.mult, _inc=rwd)
                    v.wait_ge(lmm.h, 3 * i + 3)
                    v.wait_ge(s_lx.h, 16 * (i + 1))
                    for c in range(NCH):
                        v.tensor_mul(lntmp[:], xsl[:, c * TT:(c + 1) * TT], ps_br[:])
                        v.tensor_add(xln[:, c * TT:(c + 1) * TT], lntmp[:], ps_bm[:],
                                     _inc=apd if c == NCH - 1 else None)
                    v.wait_ge(s_rp.h, 32 * (i + 1))
                    for s in [s for s in sched if s["i"] == i and s["kind"] in ("QR", "KR")]:
                        j = s["jr"]
                        v.wait_ge(evd.h, s["g"] + 1)
                        if s["kind"] == "QR":
                            v.tensor_mul(tmpA[:], qrbf[j % 2][:], cp_s[:])
                            v.wait_ge(rotd.h, j + 1)
                            v.tensor_mul(tmpB[:], prot[:], sp_s[:])
                            if j >= 2:
                                v.wait_ge(s_stkq.h, 16 * (j - 1))
                            v.tensor_add(kout[j % 2][:], tmpA[:], tmpB[:], _inc=cmb)
                        else:
                            v.tensor_mul(tmpA[0:64, :], qrbf[j % 2][0:64, :], cp_s[0:64, :])
                            v.wait_ge(rotd.h, j + 1)
                            v.tensor_mul(tmpB[0:64, :], prot[0:64, :], sp_s[0:64, :])
                            if j >= 2:
                                v.wait_ge(s_stkq.h, 16 * (j - 1))
                            v.tensor_add(kout[j % 2][0:64, :], tmpA[0:64, :], tmpB[0:64, :], _inc=cmb)

    with contextlib.ExitStack() as bes:
        khs = bes.enter_context(nc.sbuf_tensor(px + "khs", [128, TF], BF16))
        vhs = bes.enter_context(nc.sbuf_tensor(px + "vhs", [128, NC32 * 65], BF16))
        qhs = bes.enter_context(nc.sbuf_tensor(px + "qhs", [128, TO], BF16))
        nsq = bes.enter_context(nc.sbuf_tensor(px + "nsq", [128, TF], BF16))
        nrow = bes.enter_context(nc.sbuf_tensor(px + "nrow", [1, 16], F32))
        srow = bes.enter_context(nc.sbuf_tensor(px + "srow", [1, TT], F32))
        ccol = bes.enter_context(nc.sbuf_tensor(px + "ccol", [128, 1], F32))
        att = [bes.enter_context(nc.sbuf_tensor(px + f"att{j}", [128, TT], BF16)) for j in range(2)]
        aof = bes.enter_context(nc.sbuf_tensor(px + "aof", [128, TT], F32))
        aob = [bes.enter_context(nc.sbuf_tensor(px + f"aob{j}", [128, TT], BF16)) for j in range(2)]
        rbc = bes.enter_context(nc.sbuf_tensor(px + "rbc", [128, TT], F32))
        ps_s = [bes.enter_context(nc.psum_tensor(px + f"ps_s{j}", [128, TT], F32)) for j in range(2)]
        po = bes.enter_context(nc.psum_tensor(px + "po", [128, TT], F32))
        pnrm = [bes.enter_context(nc.psum_tensor(px + f"pnrm{j}", [1, TT], F32)) for j in range(2)]
        pbc = bes.enter_context(nc.psum_tensor(px + "pbc", [128, TT], F32))

        b_kh = Sem(es, nc, px + "b_kh")
        b_vh = Sem(es, nc, px + "b_vh")
        b_qh = Sem(es, nc, px + "b_qh")
        b_ones = Sem(es, nc, px + "b_ones")
        b_sq = Sem(es, nc, px + "b_sq")
        b_nmm = Sem(es, nc, px + "b_nmm")
        b_nrd = Sem(es, nc, px + "b_nrd")
        b_nrm = Sem(es, nc, px + "b_nrm")
        b_ncc = Sem(es, nc, px + "b_ncc")
        b_cb = Sem(es, nc, px + "b_cb")
        b_cc = Sem(es, nc, px + "b_cc")
        b_sm = Sem(es, nc, px + "b_sm")
        b_ex = Sem(es, nc, px + "b_ex")
        b_av = Sem(es, nc, px + "b_av")
        b_rc = Sem(es, nc, px + "b_rc")
        b_bc = Sem(es, nc, px + "b_bc")
        b_bcc = Sem(es, nc, px + "b_bcc")
        b_aot = Sem(es, nc, px + "b_aot")
        b_ao = Sem(es, nc, px + "b_ao")
        b_sao = Sem(es, nc, px + "b_sao")

        NRED_K = TF // TT
        NRED_Q = TO // TT

        def red_base(qh, with_q=0):
            return qh * (NRED_K + NRED_Q) + with_q

        with nc.Block() as blk:
            @blk.vector
            def _(v):
                v = Chain(v, ch["v"])
                v.memset(vhs[:].rearrange("p (c w) -> p c w", w=65)[:, :, 64:65], 1.0, _inc=b_ones)
                grp = 0
                red = 0
                for qh in range(NH):
                    if qh >= 1:
                        v.wait_ge(b_ncc.h, qh)
                    for j in range(NRED_K):
                        v.wait_ge(b_nmm.h, red + 1)
                        if j == 0:
                            v.reduce_max(nrow[0:1, 0:1], pnrm[red % 2][:], axis=AX.X, _inc=b_nrd)
                        else:
                            v.reduce_max(nrow[0:1, 1:2], pnrm[red % 2][:], axis=AX.X, _inc=b_nrd)
                            v.tensor_max(nrow[0:1, 0:1], nrow[0:1, 0:1], nrow[0:1, 1:2])
                        red += 1
                    v.tensor_max(nrow[0:1, 10:11], nrow[0:1, 0:1], nrow[0:1, 0:1])
                    for j in range(NRED_Q):
                        v.wait_ge(b_nmm.h, red + 1)
                        if j == 0:
                            v.reduce_max(nrow[0:1, 0:1], pnrm[red % 2][:], axis=AX.X, _inc=b_nrd)
                        else:
                            v.reduce_max(nrow[0:1, 1:2], pnrm[red % 2][:], axis=AX.X, _inc=b_nrd)
                            v.tensor_max(nrow[0:1, 0:1], nrow[0:1, 0:1], nrow[0:1, 1:2])
                        red += 1
                    v.tensor_max(nrow[0:1, 11:12], nrow[0:1, 0:1], nrow[0:1, 0:1], _inc=b_nrm)
                    for qt in range(NQT):
                        v.wait_ge(b_av.h, NC32 * (grp + 1))
                        v.tensor_copy(srow[0:1, :], po[64:65, :])
                        v.reciprocal(srow[0:1, :], srow[0:1, :], _inc=b_rc)
                        v.wait_ge(b_bcc.h, grp + 1)
                        v.tensor_mul(aof[0:64, :], po[0:64, :], rbc[0:64, :], _inc=b_aot)
                        grp += 1

            @blk.sync
            def _(sync):
                grp = 0
                for qh in range(NH):
                    sync.wait_ge(b_kh.h, 16 * qh)
                    sync.wait_ge(b_vh.h, 16 * qh)
                    if qh == 0:
                        sync.wait_ge(b_ones.h, 1)
                    if qh >= 1:
                        sync.wait_ge(b_av.h, NC32 * NQT * qh)
                    sync.dma_start(out=khs[:], in_=k_d[qh * 128:(qh + 1) * 128, :]).then_inc(b_kh.h, 16)
                    sync.dma_start(out=vhs[:].rearrange("p (c w) -> p c w", w=65)[:, :, 0:64],
                                   in_=v_d.rearrange("(c p) f -> p c f", p=128)
                                   [:, :, qh * 64:(qh + 1) * 64]).then_inc(b_vh.h, 16)
                    sync.wait_ge(b_qh.h, 16 * qh)
                    sync.dma_start(out=qhs[:], in_=q_d[qh * 128:(qh + 1) * 128, :]).then_inc(b_qh.h, 16)
                    for qt in range(NQT):
                        sync.wait_ge(b_sao.h, 16 * grp)
                        sync.wait_ge(b_ao.h, grp + 1)
                        sync.dma_start(out=ao_d[qh * 64:(qh + 1) * 64, qt * TT:(qt + 1) * TT],
                                       in_=aob[grp % 2][0:64, :]).then_inc(b_sao.h, 16)
                        grp += 1

            @blk.tensor
            def _(te):
                grp = 0
                icnt = 0
                red = 0
                for qh in range(NH):
                    te.wait_ge(b_sq.h, 2 * qh + 1)
                    te.wait_ge(b_vh.h, 16 * (qh + 1))
                    for j in range(NRED_K):
                        if red >= 2:
                            te.wait_ge(b_nrd.h, red - 1)
                        te.matmul(pnrm[red % 2][:], onesc[:], nsq[:, j * TT:(j + 1) * TT],
                                  start=True, stop=True).then_inc(b_nmm.h, 1)
                        red += 1
                    te.wait_ge(b_sq.h, 2 * qh + 2)
                    for j in range(NRED_Q):
                        if red >= 2:
                            te.wait_ge(b_nrd.h, red - 1)
                        te.matmul(pnrm[red % 2][:], onesc[:], nsq[:, j * TT:(j + 1) * TT],
                                  start=True, stop=True).then_inc(b_nmm.h, 1)
                        red += 1
                    te.wait_ge(b_ncc.h, qh + 1)
                    if grp >= 1:
                        te.wait_ge(b_aot.h, grp)
                    te.matmul(pbc[:, 0:1], onesr[:], nrow[0:1, 15:16],
                              start=True, stop=True).then_inc(b_cb.h, 1)
                    for qt in range(NQT):
                        te.wait_ge(b_cc.h, qh + 1)
                        if grp >= 1:
                            te.wait_ge(b_aot.h, grp)
                        for c in range(NC32):
                            if icnt >= 2:
                                te.wait_ge(b_ex.h, icnt - 1)
                            te.matmul(ps_s[icnt % 2][:], khs[:, c * 128:(c + 1) * 128],
                                      qhs[:, qt * TT:(qt + 1) * TT],
                                      start=True, stop=True).then_inc(b_sm.h, 1)
                            te.wait_ge(b_ex.h, icnt + 1)
                            te.matmul(po[0:65, :], vhs[:, c * 65:(c + 1) * 65], att[icnt % 2][:],
                                      start=(c == 0), stop=(c == NC32 - 1)).then_inc(b_av.h, 1)
                            icnt += 1
                        te.wait_ge(b_rc.h, grp + 1)
                        te.matmul(pbc[:], onesr[:], srow[0:1, :],
                                  start=True, stop=True).then_inc(b_bc.h, 1)
                        grp += 1

            @blk.scalar
            def _(sc):
                sc = Chain(sc, ch["s"])
                icnt = 0
                for qh in range(NH):
                    sc.wait_ge(b_kh.h, 16 * (qh + 1))
                    if qh >= 1:
                        sc.wait_ge(b_nmm.h, red_base(qh))
                    sc.activation(nsq[:], khs[:], AF.Square, _inc=b_sq)
                    sc.wait_ge(b_qh.h, 16 * (qh + 1))
                    sc.wait_ge(b_nmm.h, red_base(qh, NRED_K))
                    sc.activation(nsq[:, 0:TO], qhs[:], AF.Square, _inc=b_sq)
                    sc.wait_ge(b_nrm.h, qh + 1)
                    sc.activation(nrow[0:1, 12:13], nrow[0:1, 11:12], AF.Copy,
                                  scale=nrow[0:1, 10:11])
                    sc.activation(nrow[0:1, 13:14], nrow[0:1, 12:13], AF.Sqrt)
                    sc.activation(nrow[0:1, 14:15], nrow[0:1, 13:14], AF.Copy,
                                  scale=L0_SCALE, bias=-EXP_SHIFT_MARGIN)
                    sc.activation(nrow[0:1, 12:13], nrow[0:1, 14:15], AF.Relu)
                    sc.activation(nrow[0:1, 15:16], nrow[0:1, 12:13], AF.Copy,
                                  scale=-1.0, _inc=b_ncc)
                    sc.wait_ge(b_cb.h, qh + 1)
                    sc.activation(ccol[:], pbc[:, 0:1], AF.Copy, _inc=b_cc)
                    for qt in range(NQT):
                        grp_l = qh * NQT + qt
                        for c in range(NC32):
                            sc.wait_ge(b_sm.h, icnt + 1)
                            if icnt >= 2:
                                sc.wait_ge(b_av.h, icnt - 1)
                            sc.activation(att[icnt % 2][:], ps_s[icnt % 2][:], AF.Exp,
                                          bias=ccol[:, 0:1], scale=L0_SCALE, _inc=b_ex)
                            icnt += 1
                        sc.wait_ge(b_bc.h, grp_l + 1)
                        if grp_l >= 1:
                            sc.wait_ge(b_aot.h, grp_l)
                        sc.activation(rbc[:], pbc[:], AF.Copy, _inc=b_bcc)
                        sc.wait_ge(b_aot.h, grp_l + 1)
                        if grp_l >= 2:
                            sc.wait_ge(b_sao.h, 16 * (grp_l - 1))
                        sc.activation(aob[grp_l % 2][0:64, :], aof[0:64, :], AF.Identity,
                                      bias=bias_s["bupv"][(qh % 2) * 64:(qh % 2) * 64 + 64, qh // 2:qh // 2 + 1],
                                      _inc=b_ao)

    with contextlib.ExitStack() as ces:
        wfc = ces.enter_context(nc.sbuf_tensor(px + "wfc", [128, 8 * 2048], BF16))
        aos = ces.enter_context(nc.sbuf_tensor(px + "aos", [128, 8 * TT], BF16))
        xt = [ces.enter_context(nc.sbuf_tensor(px + f"xt{j}", [128, TT], F32)) for j in range(2)]
        ot = [ces.enter_context(nc.sbuf_tensor(px + f"ot{j}", [128, TT], F32)) for j in range(2)]
        obf = ces.enter_context(nc.sbuf_tensor(px + "obf", [128, TT], F32))
        pc = [ces.enter_context(nc.psum_tensor(px + f"pc{j}", [128, TT], F32)) for j in range(2)]

        c_lw = Sem(es, nc, px + "c_lw")
        c_lao = Sem(es, nc, px + "c_lao")
        c_lx = Sem(es, nc, px + "c_lx")
        c_mm = Sem(es, nc, px + "c_mm")
        c_bi = Sem(es, nc, px + "c_bi")
        c_ev = Sem(es, nc, px + "c_ev")
        c_st = Sem(es, nc, px + "c_st")

        ao_r = ao_d.rearrange("(c p) t -> p c t", p=128)

        with nc.Block() as blk:
            @blk.gpsimd
            def _(g_):
                g_.wait_ge(b_sao.h, 16 * NH * NQT)
                g_.dma_start(out=wfc[:].rearrange("p (c f) -> p c f", f=2048),
                             in_=dram["wfcT"].rearrange("(c p) f -> p c f", p=128)).then_inc(c_lw.h, 16)

            @blk.sync
            def _(sync):
                sync.wait_ge(b_sao.h, 16 * NH * NQT)
                idx = 0
                for qt in range(NQT):
                    sync.wait_ge(c_lao.h, 16 * qt)
                    if qt >= 1:
                        sync.wait_ge(c_mm.h, 16 * qt)
                    sync.dma_start(out=aos[:].rearrange("p (c t) -> p c t", t=TT),
                                   in_=ao_r[:, :, qt * TT:(qt + 1) * TT]).then_inc(c_lao.h, 16)
                    for dm in range(NCH):
                        sync.wait_ge(c_lx.h, 16 * idx)
                        if idx >= 2:
                            sync.wait_ge(c_ev.h, idx - 1)
                        sync.dma_start(out=xt[idx % 2][:],
                                       in_=x_in_r[:, dm, OFF + qt * TT:OFF + (qt + 1) * TT]).then_inc(c_lx.h, 16)
                        sync.wait_ge(c_ev.h, idx + 1)
                        sync.wait_ge(c_st.h, 16 * idx)
                        sync.dma_start(out=x_out_r[:, dm, qt * TT:(qt + 1) * TT],
                                       in_=ot[idx % 2][:]).then_inc(c_st.h, 16)
                        idx += 1
                sync.wait_ge(c_st.h, 16 * NQT * NCH)

            @blk.tensor
            def _(te):
                te.wait_ge(c_lw.h, 16)
                idx = 0
                for qt in range(NQT):
                    te.wait_ge(c_lao.h, 16 * (qt + 1))
                    for dm in range(NCH):
                        if idx >= 2:
                            te.wait_ge(c_ev.h, idx - 1)
                        for fc in range(8):
                            mm = te.matmul(pc[idx % 2][:],
                                           wfc[:, fc * 2048 + dm * 128:fc * 2048 + (dm + 1) * 128],
                                           aos[:, fc * TT:(fc + 1) * TT],
                                           start=(fc == 0), stop=(fc == 7))
                        mm.then_inc(c_mm.h, 1)
                        idx += 1

            @blk.scalar
            def _(sc):
                sc = Chain(sc, ch["s"])
                idx = 0
                for qt in range(NQT):
                    for dm in range(NCH):
                        sc.wait_ge(c_mm.h, idx + 1)
                        if idx >= 1:
                            sc.wait_ge(c_ev.h, idx)
                        sc.activation(obf[:], pc[idx % 2][:], AF.Identity,
                                      bias=bias_s["bfc"][:, dm:dm + 1], _inc=c_bi)
                        idx += 1

            @blk.vector
            def _(v):
                v = Chain(v, ch["v"])
                idx = 0
                for qt in range(NQT):
                    for dm in range(NCH):
                        v.wait_ge(c_bi.h, idx + 1)
                        v.wait_ge(c_lx.h, 16 * (idx + 1))
                        if idx >= 2:
                            v.wait_ge(c_st.h, 16 * (idx - 1))
                        v.tensor_add(ot[idx % 2][:], xt[idx % 2][:], obf[:], _inc=c_ev)
                        idx += 1


def build_l0_attn(TF=4096, TO=2048, half=0):
    nc = bass.Bass("TRN2", target_bir_lowering=False)
    ins = {}
    ins["x_in"] = nc.dram_tensor("x_in", [D, TF], F32, kind="ExternalInput").ap()
    for nm, shp, dt in (("wdkvT", [2048, 512], F32), ("wdqT", [2048, 512], F32),
                        ("wupkT", [512, 1024], F32), ("wupvT", [512, 1024], F32),
                        ("wupqT", [512, 1024], F32), ("wqrT", [512, 1024], F32),
                        ("wkrT", [2048, 64], F32), ("wfcT", [1024, 2048], F32),
                        ("bdkv", [128, 4], F32), ("bdq", [128, 4], F32),
                        ("bupk", [128, 8], F32), ("bupv", [128, 8], F32),
                        ("bupq", [128, 8], F32), ("bqr", [128, 8], F32),
                        ("bkr", [64, 1], F32), ("bfc", [128, 16], F32),
                        ("cp128", [128, TF], F32), ("sp128", [128, TF], F32),
                        ("rot128i", [128, 128], BF16),
                        ("onescol_bf", [128, 1], BF16), ("onesrow_f32", [1, 128], F32)):
        ins[nm] = nc.dram_tensor(nm, shp, dt, kind="ExternalInput").ap()
    x_out = nc.dram_tensor("x_out", [D, TO], F32, kind="ExternalOutput").ap()
    es = contextlib.ExitStack()
    with es:
        ch = {"v": ChainState(es, nc, "m_chv"), "s": ChainState(es, nc, "m_chs")}
        emit_l0_attn_body(nc, es, TF, TO, half, "m_", ins["x_in"], x_out,
                          {k: v for k, v in ins.items() if k != "x_in"}, ch)
    return nc



def prep_l1_moe_host(gate_w, Wg, Wu, Wd, sWg, sWu, sWd, shared_gate_w):
    out = {
        "gwT": np.ascontiguousarray(gate_w.T),
        "sgw": np.ascontiguousarray(shared_gate_w.reshape(16, 128).T),
        "ident_f32": np.eye(128, dtype=np.float32),
    }
    out["eWg"] = np.ascontiguousarray(Wg)
    out["eWu"] = np.ascontiguousarray(Wu)
    out["eWd"] = np.ascontiguousarray(Wd)
    out["sWg"] = np.ascontiguousarray(sWg)
    out["sWu"] = np.ascontiguousarray(sWu)
    out["sWd"] = np.ascontiguousarray(sWd)
    return out


def emit_l1_moe_body(nc, es, T, px, x_in, x_out, dram, ch, bufes=None):
    TT = 512
    NTT = T // TT
    NE = 16
    NSUB = T // 128

    if bufes is None:
        bufes = es
    cst = {}
    cst["onescol_bf"] = bufes.enter_context(nc.sbuf_tensor(px + "onesc", [128, 1], BF16))
    cst["onesrow_f32"] = bufes.enter_context(nc.sbuf_tensor(px + "onesr", [1, 128], F32))
    ident = bufes.enter_context(nc.sbuf_tensor(px + "ident", [128, 128], F32))
    gws = bufes.enter_context(nc.sbuf_tensor(px + "gws", [128, 16 * 16], BF16))
    sgws = bufes.enter_context(nc.sbuf_tensor(px + "sgws", [128, 16], BF16))
    x_ln = bufes.enter_context(nc.sbuf_tensor(px + "x_ln", [128, NCH * T], BF16))
    combT_d = nc.dram_tensor(px + "combT_d", [17, T], F32).ap()

    ld0 = Sem(es, nc, px + "ld0")
    ld0g = Sem(es, nc, px + "ld0g")
    with nc.Block() as blk0:
        @blk0.sync
        def _(sync):
            sync.dma_start(out=cst["onescol_bf"][:], in_=dram["onescol_bf"]).then_inc(ld0.h, 16)
            sync.dma_start(out=cst["onesrow_f32"][:], in_=dram["onesrow_f32"]).then_inc(ld0.h, 16)
            sync.dma_start(out=ident[:], in_=dram["ident_f32"]).then_inc(ld0.h, 16)

        @blk0.gpsimd
        def _(g_):
            g_.dma_start(out=gws[:].rearrange("p (c f) -> p c f", f=16),
                         in_=dram["gwT"].rearrange("(c p) f -> p c f", p=128)).then_inc(ld0g.h, 16)
            g_.dma_start(out=sgws[:], in_=dram["sgw"]).then_inc(ld0g.h, 16)
            g_.wait_ge(ld0.h, 48)
            g_.wait_ge(ld0g.h, 32)

        @blk0.tensor
        def _(e):
            e.wait_ge(ld0.h, 48)
            e.wait_ge(ld0g.h, 32)

        @blk0.vector
        def _(e):
            e.wait_ge(ld0.h, 48)
            e.wait_ge(ld0g.h, 32)

        @blk0.scalar
        def _(e):
            e.wait_ge(ld0.h, 48)
            e.wait_ge(ld0g.h, 32)

    x_in_r = x_in.rearrange("(c p) t -> p c t", p=128)
    x_out_r = x_out.rearrange("(c p) t -> p c t", p=128)

    emit_layernorm(nc, es, T, px, x_in_r, x_ln, cst, ch)

    with contextlib.ExitStack() as res:
        lgs = res.enter_context(nc.sbuf_tensor(px + "lgs", [16, TT], F32))
        rsb = res.enter_context(nc.sbuf_tensor(px + "rsb", [128, 16], F32))
        sgr = res.enter_context(nc.sbuf_tensor(px + "sgr", [1, TT], F32))
        prb = res.enter_context(nc.sbuf_tensor(px + "prb", [128, 16], F32))
        wrk = res.enter_context(nc.sbuf_tensor(px + "wrk", [128, 16], F32))
        cmp_ = res.enter_context(nc.sbuf_tensor(px + "cmp", [128, 16], F32))
        ncol = res.enter_context(nc.sbuf_tensor(px + "ncol", [128, 8], F32))
        ctile = [res.enter_context(nc.sbuf_tensor(px + f"ctile{j}", [16, 128], F32)) for j in range(2)]
        plog = res.enter_context(nc.psum_tensor(px + "plog", [16, TT], F32))
        psg = res.enter_context(nc.psum_tensor(px + "psg", [1, TT], F32))
        ptr = res.enter_context(nc.psum_tensor(px + "ptr", [128, 16], F32))
        ptc = res.enter_context(nc.psum_tensor(px + "ptc", [16, 128], F32))

        r_mm = Sem(es, nc, px + "r_mm")
        r_ev = Sem(es, nc, px + "r_ev")
        r_tr = Sem(es, nc, px + "r_tr")
        r_te = Sem(es, nc, px + "r_te")
        r_cw = Sem(es, nc, px + "r_cw")
        r_tc = Sem(es, nc, px + "r_tc")
        r_ce = Sem(es, nc, px + "r_ce")
        r_sg = Sem(es, nc, px + "r_sg")
        r_ex = Sem(es, nc, px + "r_ex")
        r_sgm = Sem(es, nc, px + "r_sgm")
        r_cd = Sem(es, nc, px + "r_cd")

        with nc.Block() as blk:
            @blk.tensor
            def _(te):
                for tt in range(NTT):
                    if tt >= 1:
                        te.wait_ge(r_ev.h, tt)
                        te.wait_ge(r_sg.h, tt)
                    for c in range(NCH):
                        mm = te.matmul(plog[:], gws[:, c * 16:(c + 1) * 16],
                                       x_ln[:, c * T + tt * TT:c * T + (tt + 1) * TT],
                                       start=(c == 0), stop=(c == NCH - 1))
                    mm.then_inc(r_mm.h, 1)
                    for c in range(NCH):
                        mm = te.matmul(psg[:], sgws[:, c:c + 1],
                                       x_ln[:, c * T + tt * TT:c * T + (tt + 1) * TT],
                                       start=(c == 0), stop=(c == NCH - 1))
                    mm.then_inc(r_sgm.h, 1)
                    for sb in range(4):
                        s = tt * 4 + sb
                        te.wait_ge(r_ev.h, tt + 1)
                        if s >= 1:
                            te.wait_ge(r_te.h, s)
                        te.transpose(ptr[:], lgs[:, sb * 128:(sb + 1) * 128],
                                     ident[0:16, 0:16]).then_inc(r_tr.h, 1)
                        te.wait_ge(r_cw.h, s + 1)
                        if s >= 1:
                            te.wait_ge(r_ce.h, s)
                        te.transpose(ptc[:], prb[:], ident[:]).then_inc(r_tc.h, 1)

            @blk.scalar
            def _(sc):
                sc = Chain(sc, ch["s"])
                for tt in range(NTT):
                    sc.wait_ge(r_mm.h, tt + 1)
                    sc.activation(lgs[:], plog[:], AF.Copy, _inc=r_ev)
                    sc.wait_ge(r_sgm.h, tt + 1)
                    if tt >= 1:
                        sc.wait_ge(r_cd.h, 16 * (tt * 5))
                    sc.activation(sgr[0:1, :], psg[:], AF.Sigmoid, _inc=r_sg)
                    for sb in range(4):
                        s = tt * 4 + sb
                        sc.wait_ge(r_te.h, s + 1)
                        if s >= 1:
                            sc.wait_ge(r_cw.h, s)
                        sc.activation(wrk[:], rsb[:], AF.Exp, bias=ncol[:, 0:1], _inc=r_ex)

            @blk.vector
            def _(v):
                v = Chain(v, ch["v"])
                for tt in range(NTT):
                    for sb in range(4):
                        s = tt * 4 + sb
                        v.wait_ge(r_tr.h, s + 1)
                        if s >= 1:
                            v.wait_ge(r_ex.h, s)
                        v.tensor_copy(rsb[:], ptr[:])
                        v.reduce_max(ncol[:, 1:2], rsb[:], axis=AX.X)
                        v.tensor_scalar(ncol[:, 0:1], ncol[:, 1:2], -1.0, None,
                                        op0=ALU.mult, _inc=r_te)
                        v.wait_ge(r_ex.h, s + 1)
                        v.reduce_sum(ncol[:, 2:3], wrk[:], axis=AX.X)
                        v.reciprocal(ncol[:, 3:4], ncol[:, 2:3])
                        v.tensor_single_scalar(prb[:], wrk[:], ncol[:, 3:4], ALU.mult)
                        v.tensor_copy(wrk[:], prb[:])
                        for it in range(4):
                            v.reduce_max(ncol[:, 4:5], wrk[:], axis=AX.X)
                            if it < 3:
                                v.tensor_single_scalar(cmp_[:], wrk[:], ncol[:, 4:5], ALU.is_equal)
                                v.scalar_tensor_tensor(wrk[:], cmp_[:], -1e30, wrk[:],
                                                       op0=ALU.mult, op1=ALU.add)
                        v.tensor_single_scalar(cmp_[:], prb[:], ncol[:, 4:5], ALU.is_ge)
                        v.tensor_mul(prb[:], prb[:], cmp_[:])
                        v.reduce_sum(ncol[:, 5:6], prb[:], axis=AX.X)
                        v.reciprocal(ncol[:, 6:7], ncol[:, 5:6])
                        v.tensor_single_scalar(prb[:], prb[:], ncol[:, 6:7],
                                               ALU.mult, _inc=r_cw)
                        v.wait_ge(r_tc.h, s + 1)
                        if s >= 2:
                            v.wait_ge(r_cd.h, 16 * (s - 1 + (s - 1) // 4 + 1))
                        v.tensor_copy(ctile[s % 2][:], ptc[:], _inc=r_ce)

            @blk.sync
            def _(sync):
                nd = 0
                for tt in range(NTT):
                    for sb in range(4):
                        s = tt * 4 + sb
                        sync.wait_ge(r_cd.h, 16 * nd)
                        sync.wait_ge(r_ce.h, s + 1)
                        sync.dma_start(out=combT_d.rearrange("e (u t) -> e u t", t=128)
                                       [0:16, s, :],
                                       in_=ctile[s % 2][:]).then_inc(r_cd.h, 16)
                        nd += 1
                    sync.wait_ge(r_cd.h, 16 * nd)
                    sync.wait_ge(r_sg.h, tt + 1)
                    sync.dma_start(out=combT_d[16:17, tt * TT:(tt + 1) * TT],
                                   in_=sgr[0:1, :]).then_inc(r_cd.h, 16)
                    nd += 1

    with contextlib.ExitStack() as mes:
        wge = [mes.enter_context(nc.sbuf_tensor(px + f"wge{j}", [128, 16 * 512], BF16)) for j in range(2)]
        wue = [mes.enter_context(nc.sbuf_tensor(px + f"wue{j}", [128, 16 * 512], BF16)) for j in range(2)]
        wde = mes.enter_context(nc.sbuf_tensor(px + "wde", [128, 4 * 2048], BF16))
        silu_sb = mes.enter_context(nc.sbuf_tensor(px + "silu", [128, TT], F32))
        int0 = mes.enter_context(nc.sbuf_tensor(px + "int0", [128, TT], F32))
        inter = mes.enter_context(nc.sbuf_tensor(px + "inter", [128, 4 * TT], BF16))
        macc = mes.enter_context(nc.sbuf_tensor(px + "macc", [128, NCH * TT], F32))
        xres = [mes.enter_context(nc.sbuf_tensor(px + f"xres{j}", [128, TT], F32)) for j in range(2)]
        oout = [mes.enter_context(nc.sbuf_tensor(px + f"oout{j}", [128, TT], F32)) for j in range(2)]
        pg = [mes.enter_context(nc.psum_tensor(px + f"pg{j}", [128, TT], F32)) for j in range(2)]
        pu = [mes.enter_context(nc.psum_tensor(px + f"pu{j}", [128, TT], F32)) for j in range(2)]
        pd = [mes.enter_context(nc.psum_tensor(px + f"pd{j}", [128, TT], F32)) for j in range(2)]
        pcb = mes.enter_context(nc.psum_tensor(px + "pcb", [128, TT], F32))

        m_wg = Sem(es, nc, px + "m_wg")
        m_wd = Sem(es, nc, px + "m_wd")
        m_cb = Sem(es, nc, px + "m_cb")
        m_cbc = Sem(es, nc, px + "m_cbc")
        m_gu = Sem(es, nc, px + "m_gu")
        m_si = Sem(es, nc, px + "m_si")
        m_i0 = Sem(es, nc, px + "m_i0")
        m_it = Sem(es, nc, px + "m_it")
        m_dm = Sem(es, nc, px + "m_dm")
        m_ac = Sem(es, nc, px + "m_ac")
        m_xr = Sem(es, nc, px + "m_xr")
        m_st = Sem(es, nc, px + "m_st")
        m_ev = Sem(es, nc, px + "m_ev")

        cb_sb = mes.enter_context(nc.sbuf_tensor(px + "cb_sb", [128, TT], F32))
        crow = [mes.enter_context(nc.sbuf_tensor(px + f"crow{j}", [1, TT], F32)) for j in range(2)]
        m_cr = Sem(es, nc, px + "m_cr")

        NEX = NE + 1

        def wsrc(e, kind):
            if e < NE:
                t = {"g": "eWg", "u": "eWu", "d": "eWd"}[kind]
                return dram[t][e]
            return dram[{"g": "sWg", "u": "sWu", "d": "sWd"}[kind]]

        with nc.Block() as blk:
            @blk.gpsimd
            def _(g_):
                g_.wait_ge(r_cd.h, 16 * NTT * 5)
                for tt in range(NTT):
                    for e in range(NEX):
                        k = tt * NEX + e
                        b = k % 2
                        g_.wait_ge(m_wg.h, 32 * k)
                        if k >= 2:
                            g_.wait_ge(m_gu.h, 8 * (k - 1))
                        g_.dma_start(out=wge[b][:].rearrange("p (c f) -> p c f", f=512),
                                     in_=wsrc(e, "g").rearrange("(c p) f -> p c f", p=128)).then_inc(m_wg.h, 16)
                        g_.dma_start(out=wue[b][:].rearrange("p (c f) -> p c f", f=512),
                                     in_=wsrc(e, "u").rearrange("(c p) f -> p c f", p=128)).then_inc(m_wg.h, 16)
                        g_.wait_ge(m_wd.h, 16 * k)
                        if k >= 1:
                            g_.wait_ge(m_dm.h, 16 * k)
                        g_.dma_start(out=wde[:].rearrange("p (c f) -> p c f", f=2048),
                                     in_=wsrc(e, "d").rearrange("(c p) f -> p c f", p=128)).then_inc(m_wd.h, 16)

            @blk.sync
            def _(sync):
                sync.wait_ge(r_cd.h, 16 * NTT * 5)
                idx = 0

                def xflow(tt):
                    nonlocal idx
                    for dm in range(NCH):
                        sync.wait_ge(m_xr.h, 16 * idx)
                        if idx >= 2:
                            sync.wait_ge(m_ev.h, idx - 1)
                        sync.dma_start(out=xres[idx % 2][:],
                                       in_=x_in_r[:, dm, tt * TT:(tt + 1) * TT]).then_inc(m_xr.h, 16)
                        sync.wait_ge(m_ev.h, idx + 1)
                        sync.wait_ge(m_st.h, 16 * idx)
                        sync.dma_start(out=x_out_r[:, dm, tt * TT:(tt + 1) * TT],
                                       in_=oout[idx % 2][:]).then_inc(m_st.h, 16)
                        idx += 1

                for tt in range(NTT):
                    if tt >= 1:
                        xflow(tt - 1)
                    for e in range(NEX):
                        k = tt * NEX + e
                        sync.wait_ge(m_cr.h, 16 * k)
                        if k >= 2:
                            sync.wait_ge(m_cb.h, k - 1)
                        sync.dma_start(out=crow[k % 2][:],
                                       in_=combT_d[e:e + 1, tt * TT:(tt + 1) * TT]).then_inc(m_cr.h, 16)
                xflow(NTT - 1)
                sync.wait_ge(m_st.h, 16 * NTT * NCH)

            @blk.tensor
            def _(te):
                for tt in range(NTT):
                    for e in range(NEX):
                        k = tt * NEX + e
                        te.wait_ge(m_cr.h, 16 * (k + 1))
                        if k >= 1:
                            te.wait_ge(m_cbc.h, k)
                        te.matmul(pcb[:], cst["onesrow_f32"][:], crow[k % 2][:],
                                  start=True, stop=True).then_inc(m_cb.h, 1)
                        te.wait_ge(m_wg.h, 32 * (k + 1))
                        for oc in range(4):
                            j = 4 * k + oc
                            if j >= 2:
                                te.wait_ge(m_si.h, j - 1)
                            for ic in range(NCH):
                                mm = te.matmul(pg[j % 2][:],
                                               wge[k % 2][:, ic * 512 + oc * 128:ic * 512 + (oc + 1) * 128],
                                               x_ln[:, ic * T + tt * TT:ic * T + (tt + 1) * TT],
                                               start=(ic == 0), stop=(ic == NCH - 1))
                            mm.then_inc(m_gu.h, 1)
                            if j >= 2:
                                te.wait_ge(m_it.h, j - 1)
                            for ic in range(NCH):
                                mm = te.matmul(pu[j % 2][:],
                                               wue[k % 2][:, ic * 512 + oc * 128:ic * 512 + (oc + 1) * 128],
                                               x_ln[:, ic * T + tt * TT:ic * T + (tt + 1) * TT],
                                               start=(ic == 0), stop=(ic == NCH - 1))
                            mm.then_inc(m_gu.h, 1)
                        te.wait_ge(m_wd.h, 16 * (k + 1))
                        te.wait_ge(m_it.h, 4 * (k + 1))
                        for dm in range(NCH):
                            kd = k * NCH + dm
                            if kd >= 2:
                                te.wait_ge(m_ac.h, kd - 1)
                            for fc in range(4):
                                mm = te.matmul(pd[kd % 2][:],
                                               wde[:, fc * 2048 + dm * 128:fc * 2048 + (dm + 1) * 128],
                                               inter[:, fc * TT:(fc + 1) * TT],
                                               start=(fc == 0), stop=(fc == 3))
                            mm.then_inc(m_dm.h, 1)

            @blk.scalar
            def _(sc):
                sc = Chain(sc, ch["s"])
                for tt in range(NTT):
                    for e in range(NEX):
                        k = tt * NEX + e
                        sc.wait_ge(m_cb.h, k + 1)
                        if k >= 1:
                            sc.wait_ge(m_i0.h, 4 * k)
                        sc.activation(cb_sb[:], pcb[:], AF.Copy, _inc=m_cbc)
                        for oc in range(4):
                            j = 4 * k + oc
                            sc.wait_ge(m_gu.h, 2 * j + 1)
                            if j >= 1:
                                sc.wait_ge(m_i0.h, j)
                            sc.activation(silu_sb[:], pg[j % 2][:], AF.Silu, _inc=m_si)

            @blk.vector
            def _(v):
                v = Chain(v, ch["v"])
                idx = 0
                for tt in range(NTT):
                    for e in range(NEX):
                        k = tt * NEX + e
                        if k >= 1:
                            v.wait_ge(m_dm.h, 16 * k)
                        for oc in range(4):
                            j = 4 * k + oc
                            v.wait_ge(m_si.h, j + 1)
                            v.wait_ge(m_gu.h, 2 * j + 2)
                            v.tensor_mul(int0[:], silu_sb[:], pu[j % 2][:], _inc=m_i0)
                            v.wait_ge(m_cbc.h, k + 1)
                            v.tensor_mul(inter[:, oc * TT:(oc + 1) * TT], int0[:],
                                         cb_sb[:], _inc=m_it)
                        for dm in range(NCH):
                            kd = k * NCH + dm
                            v.wait_ge(m_dm.h, kd + 1)
                            if e == 0:
                                v.tensor_copy(macc[:, dm * TT:(dm + 1) * TT], pd[kd % 2][:],
                                              _inc=m_ac)
                            elif e < NEX - 1:
                                v.tensor_add(macc[:, dm * TT:(dm + 1) * TT],
                                             macc[:, dm * TT:(dm + 1) * TT], pd[kd % 2][:],
                                             _inc=m_ac)
                            else:
                                v.tensor_add(macc[:, dm * TT:(dm + 1) * TT],
                                             macc[:, dm * TT:(dm + 1) * TT], pd[kd % 2][:],
                                             _inc=m_ac)
                                v.wait_ge(m_xr.h, 16 * (idx + 1))
                                if idx >= 2:
                                    v.wait_ge(m_st.h, 16 * (idx - 1))
                                v.tensor_add(oout[idx % 2][:], macc[:, dm * TT:(dm + 1) * TT],
                                             xres[idx % 2][:], _inc=m_ev)
                                idx += 1


def build_l1_moe(T=2048):
    nc = bass.Bass("TRN2", target_bir_lowering=False)
    ins = {}
    ins["x_in"] = nc.dram_tensor("x_in", [D, T], F32, kind="ExternalInput").ap()
    for nm, shp in (("gwT", [2048, 16]), ("sgw", [128, 16]), ("ident_f32", [128, 128]),
                    ("eWg", [16, 2048, 512]), ("eWu", [16, 2048, 512]), ("eWd", [16, 512, 2048]),
                    ("sWg", [2048, 512]), ("sWu", [2048, 512]), ("sWd", [512, 2048])):
        ins[nm] = nc.dram_tensor(nm, shp, F32, kind="ExternalInput").ap()
    ins["onescol_bf"] = nc.dram_tensor("onescol_bf", [128, 1], BF16, kind="ExternalInput").ap()
    ins["onesrow_f32"] = nc.dram_tensor("onesrow_f32", [1, 128], F32, kind="ExternalInput").ap()
    x_out = nc.dram_tensor("x_out", [D, T], F32, kind="ExternalOutput").ap()
    es = contextlib.ExitStack()
    with es:
        ch = {"v": ChainState(es, nc, "e_chv"), "s": ChainState(es, nc, "e_chs")}
        emit_l1_moe_body(nc, es, T, "e_", ins["x_in"], x_out,
                         {k: v for k, v in ins.items() if k != "x_in"}, ch)
    return nc



def emit_boundary(nc, es, px, ch, free_sems=None):
    bar = Sem(es, nc, px + "bar")
    with nc.Block() as blk:
        @blk.tensor
        def _(te):
            te.nop().then_inc(bar.h, 1)
            te.wait_ge(bar.h, 5 if free_sems is None else 6)

        @blk.sync
        def _(s):
            s.nop().then_inc(bar.h, 1)
            s.wait_ge(bar.h, 5 if free_sems is None else 6)

        @blk.gpsimd
        def _(g):
            g.nop().then_inc(bar.h, 1)
            g.wait_ge(bar.h, 5)
            if free_sems is not None:
                nc.clear_and_free_semaphores(free_sems)
                g.nop().then_inc(bar.h, 1)
            else:
                pass

        @blk.vector
        def _(v):
            if ch["v"].last is not None:
                v.wait_ge(*ch["v"].last)
            v.nop().then_inc(bar.h, 1)
            v.wait_ge(bar.h, 5 if free_sems is None else 6)

        @blk.scalar
        def _(s):
            if ch["s"].last is not None:
                s.wait_ge(*ch["s"].last)
            s.nop().then_inc(bar.h, 1)
            s.wait_ge(bar.h, 5 if free_sems is None else 6)

    ch["v"].last = None
    ch["s"].last = None
    if "p" in ch:
        ch["p"].last = None


def emit_exchange(nc, es, px, x2_d, x2f_d, poff_in, TO=2048, groups=None):
    cc_in = nc.dram_tensor(px + "cc_in", [D, TO], F32).ap()
    cc_out_t = nc.dram_tensor(px + "cc_out", [2 * D, TO], F32)
    cc_out = cc_out_t.ap()
    poff_s = es.enter_context(nc.sbuf_tensor(px + "poff", [1, 1], I32))
    s1 = Sem(es, nc, px + "s1")
    s2 = Sem(es, nc, px + "s2")
    s3 = Sem(es, nc, px + "s3")

    with nc.Block() as blk:
        @blk.sync
        def _(sync):
            sync.dma_start(out=cc_in, in_=x2_d).then_inc(s1.h, 16)
            sync.dma_start(out=x2f_d[:, 0:TO], in_=x2_d).then_inc(s1.h, 16)
            sync.dma_start(out=poff_s[:], in_=poff_in).then_inc(s1.h, 16)
            sync.wait_ge(s3.h, 16)

        @blk.gpsimd
        def _(g):
            g.wait_ge(s1.h, 48)
            g.collective_compute(
                "AllGather", ALU.bypass,
                replica_groups=groups or [[0, 1], [2, 3], [4, 5], [6, 7]],
                ins=[cc_in.opt()], outs=[cc_out.opt()],
            ).then_inc(s2.h, 1)
            g.wait_ge(s2.h, 1)
            with g.register(px + "poffr") as reg:
                g.reg_load(reg, poff_s[0:1, 0:1])
                g.dma_start(out=x2f_d[:, TO:2 * TO],
                            in_=bass.AP(cc_out_t, reg, [[TO, D], [1, TO]])).then_inc(s3.h, 16)
            g.wait_ge(s3.h, 16)



TFULL = 4096
TOWN = 2048


def _perm_cols(a, h):
    if h == 0:
        return np.ascontiguousarray(a)
    return np.ascontiguousarray(np.concatenate([a[:, TOWN:], a[:, :TOWN]], axis=1))


def build_fused(TF=4096, TO=2048, groups=None):
    nc = bass.Bass("TRN2", target_bir_lowering=False)
    ins = {}

    def di(nm, shp, dt=F32):
        ins[nm] = nc.dram_tensor(nm, shp, dt, kind="ExternalInput").ap()
        return ins[nm]

    di("x0", [D, TF])
    di("poff", [1, 1], I32)
    for nm, shp, dt in (("wdkvT", [2048, 512], F32), ("wdqT", [2048, 512], F32),
                        ("wupkT", [512, 1024], F32), ("wupvT", [512, 1024], F32),
                        ("wupqT", [512, 1024], F32), ("wqrT", [512, 1024], F32),
                        ("wkrT", [2048, 64], F32), ("wfcT", [1024, 2048], F32),
                        ("bdkv", [128, 4], F32), ("bdq", [128, 4], F32),
                        ("bupk", [128, 8], F32), ("bupv", [128, 8], F32),
                        ("bupq", [128, 8], F32), ("bqr", [128, 8], F32),
                        ("bkr", [64, 1], F32), ("bfc", [128, 16], F32),
                        ("cp128", [128, TF], F32), ("sp128", [128, TF], F32),
                        ("rot128i", [128, 128], BF16)):
        di(nm, shp, dt)
    for nm, shp, dt in (("gmp", [D, NMLP], U8), ("gsp", [D, NMLP], U8),
                        ("ump", [D, NMLP], U8), ("usp", [D, NMLP], U8),
                        ("dmp", [NMLP, D], U8), ("dsp", [NMLP, D], U8),
                        ("g_scT", [64, NMLP], F32), ("u_scT", [64, NMLP], F32),
                        ("d_scT", [512, D], F32), ("sel16", [16 * 64, 128], BF16),
                        ("nibcols_f", [128, 3], F32)):
        di(nm, shp, dt)
    for nm, shp, dt in (("wqT", [2048, 2048], F32), ("wkT", [2048, 512], F32),
                        ("wvT", [2048, 512], F32), ("woT", [2048, 2048], F32),
                        ("ropeC", [128, TF], F32), ("ropeS", [128, TF], F32),
                        ("rot128", [128, 128], BF16)):
        di(nm, shp, dt)
    for nm, shp in (("gwT", [2048, 16]), ("sgw", [128, 16]), ("ident_f32", [128, 128]),
                    ("eWg", [16, 2048, 512]), ("eWu", [16, 2048, 512]), ("eWd", [16, 512, 2048]),
                    ("sWg", [2048, 512]), ("sWu", [2048, 512]), ("sWd", [512, 2048])):
        di(nm, shp, F32)
    di("onescol_bf", [128, 1], BF16)
    di("onesrow_f32", [1, 128], F32)

    x4 = nc.dram_tensor("x_out", [D, TO], F32, kind="ExternalOutput").ap()
    x1_d = nc.dram_tensor("x1_d", [D, TO], F32).ap()
    x2_d = nc.dram_tensor("x2_d", [D, TO], F32).ap()
    x2f_d = nc.dram_tensor("x2f_d", [D, TF], F32).ap()
    x3_d = nc.dram_tensor("x3_d", [D, TO], F32).ap()
    acc_d = nc.dram_tensor("facc", [D, TO], F32).ap()

    es = contextlib.ExitStack()
    with es:
        ch = {"v": ChainState(es, nc, "chv"), "s": ChainState(es, nc, "chs")}
        blocks_es = contextlib.ExitStack()

        snap = nc.snapshot_sems()
        with contextlib.ExitStack() as b1:
            emit_l0_attn_body(nc, es, TF, TO, 0, "m_", ins["x0"], x1_d, ins, ch, bufes=b1)
        emit_boundary(nc, es, "bd1", ch, nc.allocated_since(snap))

        snap = nc.snapshot_sems()
        with contextlib.ExitStack() as b2:
            emit_l0_ffn_body(nc, es, TO, "f_", x1_d, x2_d, acc_d,
                             {k: ins[k] for k in ("gmp", "gsp", "ump", "usp", "dmp", "dsp",
                                                  "g_scT", "u_scT", "d_scT", "sel16",
                                                  "nibcols_f", "onescol_bf", "onesrow_f32")},
                             ch=ch, bufes=b2)
        emit_boundary(nc, es, "bd2", ch, nc.allocated_since(snap))

        snap = nc.snapshot_sems()
        emit_exchange(nc, es, "xc_", x2_d, x2f_d, ins["poff"], TO, groups)
        emit_boundary(nc, es, "bd3", ch, nc.allocated_since(snap))

        snap = nc.snapshot_sems()
        with contextlib.ExitStack() as b3:
            emit_l1_attn_body(nc, es, TF, TO, 0, "a_", x2f_d, x3_d, ins, ch, bufes=b3)
        emit_boundary(nc, es, "bd4", ch, nc.allocated_since(snap))

        with contextlib.ExitStack() as b4:
            emit_l1_moe_body(nc, es, TO, "e_", x3_d, x4, ins, ch, bufes=b4)
    return nc


def kernel(**inputs):
    hc = host_consts()
    x0 = np.asarray(inputs["L0_attn_h"])
    l0w = {k[len("L0_attn_"):]: np.asarray(v) for k, v in inputs.items()
           if k.startswith("L0_attn_") and k != "L0_attn_h"}
    p0 = prep_l0_attn_host(l0w, TFULL)
    pf = prep_l0_ffn_host(np.asarray(inputs["L0_ffn_gate_W_packed"]), np.asarray(inputs["L0_ffn_gate_scales"]),
                          np.asarray(inputs["L0_ffn_up_W_packed"]), np.asarray(inputs["L0_ffn_up_scales"]),
                          np.asarray(inputs["L0_ffn_down_W_packed"]), np.asarray(inputs["L0_ffn_down_scales"]))
    p1 = prep_l1_attn_host(np.asarray(inputs["L1_attn_Wq"]), np.asarray(inputs["L1_attn_Wk"]),
                           np.asarray(inputs["L1_attn_Wv"]), np.asarray(inputs["L1_attn_Wo"]), TFULL)
    pm = prep_l1_moe_host(np.asarray(inputs["L1_ffn_gate_w"]), np.asarray(inputs["L1_ffn_Wg"]),
                          np.asarray(inputs["L1_ffn_Wu"]), np.asarray(inputs["L1_ffn_Wd"]),
                          np.asarray(inputs["L1_ffn_sWg"]), np.asarray(inputs["L1_ffn_sWu"]),
                          np.asarray(inputs["L1_ffn_sWd"]), np.asarray(inputs["L1_ffn_shared_gate_w"]))

    nc = build_fused(TFULL, TOWN)
    maps = []
    for c in range(8):
        h = c % 2
        m = {}
        m.update({k: v for k, v in p0.items() if k not in ("cp128", "sp128")})
        m["cp128"] = _perm_cols(p0["cp128"], h)
        m["sp128"] = _perm_cols(p0["sp128"], h)
        m.update(pf)
        m.update({k: v for k, v in p1.items() if k not in ("ropeC", "ropeS")})
        m["ropeC"] = _perm_cols(p1["ropeC"], h)
        m["ropeS"] = _perm_cols(p1["ropeS"], h)
        m.update(pm)
        m.update({k: hc[k] for k in ("sel16", "nibcols_f", "onescol_bf", "onesrow_f32")})
        m["x0"] = _perm_cols(np.ascontiguousarray(x0[c // 2].T), h)
        m["poff"] = np.array([[(1 - h) * D * TOWN]], dtype=np.int32)
        maps.append(m)
    res = run_bass_kernel_spmd(nc, maps, core_ids=list(range(8)))
    out = np.empty((4, 4096, 2048), dtype=np.float32)
    for c in range(8):
        s, h = c // 2, c % 2
        out[s, h * TOWN:(h + 1) * TOWN, :] = res.results[c]["x_out"].T
    return out


# version 05 (final)
# speedup vs baseline: 1.3130x; 1.0034x over previous
import contextlib
import numpy as np
import ml_dtypes
import concourse.bass as bass
import concourse.mybir as mybir
from concourse.bass_utils import run_bass_kernel_spmd

F32 = mybir.dt.float32
BF16 = mybir.dt.bfloat16
I32 = mybir.dt.int32
U8 = mybir.dt.uint8
AF = mybir.ActivationFunctionType
ALU = mybir.AluOpType
AX = mybir.AxisListType

D = 2048
NMLP = 16384
NCH = D // 128

GELU_C1 = 0.7978845608028654
GELU_C2 = GELU_C1 * 0.044715
BF = ml_dtypes.bfloat16



def _bytes_lo(packed_i32):
    a = np.ascontiguousarray(packed_i32)
    return np.ascontiguousarray(a.view(np.uint8).reshape(a.shape[0], a.shape[1], 4)[:, :, 0])


def _nib_planes(bT):
    m = np.empty((bT.shape[0] * 2, bT.shape[1]), np.uint8)
    s = np.empty_like(m)
    m[0::2] = bT & np.uint8(7)
    m[1::2] = (bT >> 4) & np.uint8(7)
    s[0::2] = (bT >> 3) & np.uint8(1)
    s[1::2] = (bT >> 7) & np.uint8(1)
    return np.ascontiguousarray(m), np.ascontiguousarray(s)


def prep_l0_ffn_host(gp, gs, up, us, dp, ds):
    out = {}
    for mn, sn, p in (("gmp", "gsp", gp), ("ump", "usp", up)):
        b = _bytes_lo(p)
        out[mn], out[sn] = _nib_planes(np.ascontiguousarray(b.T))
    db = _bytes_lo(dp)
    out["dmp"], out["dsp"] = _nib_planes(np.ascontiguousarray(db.T))
    out["g_scT"] = np.ascontiguousarray(gs.T)
    out["u_scT"] = np.ascontiguousarray(us.T)
    out["d_scT"] = np.ascontiguousarray(ds.T)
    return out


def host_consts():
    c = {}
    sel = np.zeros((16, 64, 128), dtype=np.float32)
    for j in range(16):
        for p in range(128):
            sel[j, 4 * j + p // 32, p] = 1.0
    c["sel16"] = np.ascontiguousarray(sel.astype(BF).reshape(16 * 64, 128))
    c["nibcols_f"] = np.concatenate([
        np.full((128, 1), -4.0),
        np.full((128, 1), -16.0),
        np.full((128, 1), 0.5),
    ], axis=1).astype(np.float32)
    c["onescol_bf"] = np.ones((128, 1), dtype=BF)
    c["onesrow_f32"] = np.ones((1, 128), dtype=np.float32)
    return c


class Sem:
    def __init__(self, es, nc, name):
        self.h = es.enter_context(nc.semaphore(name))
        self.n = 0


class ChainState:
    def __init__(self, es, nc, name):
        self.sem = Sem(es, nc, name)
        self.last = None


_CHAIN_PASSTHROUGH = {"wait_ge", "dma_start", "nop", "sem_inc", "memzero"}


class Chain:
    def __init__(self, eng, st):
        object.__setattr__(self, "eng", eng)
        object.__setattr__(self, "st", st)

    def __getattr__(self, name):
        target = getattr(self.eng, name)
        if name in _CHAIN_PASSTHROUGH or not callable(target):
            return target

        st = self.st
        eng = self.eng

        def wrapped(*a, _inc=None, **kw):
            if st.last is not None:
                eng.wait_ge(*st.last)
            ins = target(*a, **kw)
            if _inc is not None:
                _inc.n += 1
                ins.then_inc(_inc.h, 1)
                st.last = (_inc.h, _inc.n)
            else:
                st.sem.n += 1
                ins.then_inc(st.sem.h, 1)
                st.last = (st.sem.h, st.sem.n)
            return ins

        return wrapped



def emit_layernorm(nc, es, T, prefix, x_in_r, x_ln, cst, ch):
    TT = 512
    NTT = T // TT
    with contextlib.ExitStack() as ln_es:
        xs = [ln_es.enter_context(nc.sbuf_tensor(prefix + f"lnx{i}", [128, NCH * TT], F32)) for i in range(2)]
        xbf = [ln_es.enter_context(nc.sbuf_tensor(prefix + f"lnb{i}", [128, NCH * TT], BF16)) for i in range(2)]
        sq = ln_es.enter_context(nc.sbuf_tensor(prefix + "lnsq", [128, NCH * TT], BF16))
        rows = ln_es.enter_context(nc.sbuf_tensor(prefix + "lnrw", [1, 6 * TT], F32))
        tmp = ln_es.enter_context(nc.sbuf_tensor(prefix + "lntmp", [128, TT], F32))
        ps_sum = ln_es.enter_context(nc.psum_tensor(prefix + "ps_sum", [1, TT], F32))
        ps_sq = ln_es.enter_context(nc.psum_tensor(prefix + "ps_sq", [1, TT], F32))
        ps_br = ln_es.enter_context(nc.psum_tensor(prefix + "ps_br", [128, TT], F32))
        ps_bm = ln_es.enter_context(nc.psum_tensor(prefix + "ps_bm", [128, TT], F32))

        ldx = Sem(es, nc, prefix + "ln_ldx")
        ldb = Sem(es, nc, prefix + "ln_ldb")
        sqd = Sem(es, nc, prefix + "ln_sqd")
        mmd = Sem(es, nc, prefix + "ln_mmd")
        rwd = Sem(es, nc, prefix + "ln_rwd")
        act = Sem(es, nc, prefix + "ln_act")
        apd = Sem(es, nc, prefix + "ln_apd")

        with nc.Block() as blk:
            @blk.sync
            def _(sync):
                for i in range(NTT):
                    b = i % 2
                    sync.wait_ge(ldx.h, 16 * i)
                    if i >= 2:
                        sync.wait_ge(apd.h, i - 1)
                    sync.dma_start(out=xs[b][:].rearrange("p (c t) -> p c t", t=TT),
                                   in_=x_in_r[:, :, i * TT:(i + 1) * TT]).then_inc(ldx.h, 16)

            @blk.gpsimd
            def _(g):
                for i in range(NTT):
                    b = i % 2
                    g.wait_ge(ldb.h, 16 * i)
                    if i >= 2:
                        g.wait_ge(apd.h, i - 1)
                    g.dma_start(out=xbf[b][:].rearrange("p (c t) -> p c t", t=TT),
                                in_=x_in_r[:, :, i * TT:(i + 1) * TT]).then_inc(ldb.h, 16)

            @blk.scalar
            def _(sc):
                sc = Chain(sc, ch["s"])
                for i in range(NTT):
                    b = i % 2
                    sc.wait_ge(ldb.h, 16 * (i + 1))
                    if i >= 1:
                        sc.wait_ge(mmd.h, 3 * i - 2)
                    sc.activation(sq[:], xbf[b][:], AF.Square, _inc=sqd)
                    sc.wait_ge(rwd.h, 2 * i + 1)
                    sc.activation(rows[0:1, 4 * TT:5 * TT], rows[0:1, 3 * TT:4 * TT], AF.Ln)
                    sc.activation(rows[0:1, 4 * TT:5 * TT], rows[0:1, 4 * TT:5 * TT], AF.Exp,
                                  scale=-0.5, _inc=act)

            @blk.tensor
            def _(te):
                for i in range(NTT):
                    b = i % 2
                    te.wait_ge(ldx.h, 16 * (i + 1))
                    te.wait_ge(sqd.h, i + 1)
                    if i >= 1:
                        te.wait_ge(rwd.h, 2 * i - 1)
                    for c in range(NCH):
                        te.matmul(ps_sum[:], cst["onescol_bf"][:], xbf[b][:, c * TT:(c + 1) * TT],
                                  start=(c == 0), stop=(c == NCH - 1))
                    for c in range(NCH):
                        mm = te.matmul(ps_sq[:], cst["onescol_bf"][:], sq[:, c * TT:(c + 1) * TT],
                                       start=(c == 0), stop=(c == NCH - 1))
                    mm.then_inc(mmd.h, 1)
                    te.wait_ge(rwd.h, 2 * i + 2)
                    if i >= 1:
                        te.wait_ge(apd.h, i)
                    te.matmul(ps_br[:], cst["onesrow_f32"][:], rows[0:1, 4 * TT:5 * TT],
                              start=True, stop=True).then_inc(mmd.h, 1)
                    te.matmul(ps_bm[:], cst["onesrow_f32"][:], rows[0:1, 5 * TT:6 * TT],
                              start=True, stop=True).then_inc(mmd.h, 1)

            @blk.vector
            def _(v):
                v = Chain(v, ch["v"])
                for i in range(NTT):
                    b = i % 2
                    v.wait_ge(mmd.h, 3 * i + 1)
                    v.tensor_scalar(rows[0:1, 0:TT], ps_sum[:], 1.0 / D, None, op0=ALU.mult)
                    v.tensor_scalar(rows[0:1, TT:2 * TT], ps_sq[:], 1.0 / D, 1e-5,
                                    op0=ALU.mult, op1=ALU.add)
                    v.tensor_mul(rows[0:1, 2 * TT:3 * TT], rows[0:1, 0:TT], rows[0:1, 0:TT])
                    v.scalar_tensor_tensor(rows[0:1, 3 * TT:4 * TT], rows[0:1, 2 * TT:3 * TT], -1.0,
                                           rows[0:1, TT:2 * TT], op0=ALU.mult,
                                           op1=ALU.add, _inc=rwd)
                    v.wait_ge(act.h, i + 1)
                    v.scalar_tensor_tensor(rows[0:1, 5 * TT:6 * TT], rows[0:1, 0:TT], -1.0,
                                           rows[0:1, 4 * TT:5 * TT], op0=ALU.mult,
                                           op1=ALU.mult, _inc=rwd)
                    v.wait_ge(mmd.h, 3 * i + 3)
                    for c in range(NCH):
                        v.tensor_mul(tmp[:], xs[b][:, c * TT:(c + 1) * TT], ps_br[:])
                        v.tensor_add(x_ln[:, c * T + i * TT:c * T + (i + 1) * TT], tmp[:], ps_bm[:],
                                     _inc=apd if c == NCH - 1 else None)



def emit_l0_ffn_body(nc, es, T, px, x_in, x_out, acc_d, dram, ch=None, bufes=None):
    TT = 512
    NTT = T // TT
    SUP = 1024
    SCH = SUP // 128
    NSUP = NMLP // SUP
    NC = NMLP // 128

    if bufes is None:
        bufes = es
    cst = {}
    cst["nibf"] = bufes.enter_context(nc.sbuf_tensor(px + "nibf", [128, 3], F32))
    cst["onescol_bf"] = bufes.enter_context(nc.sbuf_tensor(px + "onesc", [128, 1], BF16))
    cst["onesrow_f32"] = bufes.enter_context(nc.sbuf_tensor(px + "onesr", [1, 128], F32))
    x_ln = bufes.enter_context(nc.sbuf_tensor(px + "x_ln", [128, NCH * T], BF16))

    ld0 = Sem(es, nc, px + "ld0")
    if ch is None:
        ch = {"v": ChainState(es, nc, px + "chv"), "s": ChainState(es, nc, px + "chs")}
    with nc.Block() as blk0:
        @blk0.sync
        def _(sync):
            sync.dma_start(out=cst["nibf"][:], in_=dram["nibcols_f"]).then_inc(ld0.h, 16)
            sync.dma_start(out=cst["onescol_bf"][:], in_=dram["onescol_bf"]).then_inc(ld0.h, 16)
            sync.dma_start(out=cst["onesrow_f32"][:], in_=dram["onesrow_f32"]).then_inc(ld0.h, 16)

        @blk0.tensor
        def _(te):
            te.wait_ge(ld0.h, 48)

        @blk0.vector
        def _(v):
            v.wait_ge(ld0.h, 48)

        @blk0.gpsimd
        def _(g):
            g.wait_ge(ld0.h, 48)

        @blk0.scalar
        def _(s):
            s.wait_ge(ld0.h, 48)

    x_in_r = x_in.rearrange("(c p) t -> p c t", p=128)
    x_out_r = x_out.rearrange("(c p) t -> p c t", p=128)
    acc_r = acc_d.rearrange("(c p) t -> p c t", p=128)

    emit_layernorm(nc, es, T, px, x_in_r, x_ln, cst, ch)

    with contextlib.ExitStack() as fes:
        wg = [fes.enter_context(nc.sbuf_tensor(px + f"wg{i}", [128, NCH * 128], BF16)) for i in range(2)]
        wu = [fes.enter_context(nc.sbuf_tensor(px + f"wu{i}", [128, NCH * 128], BF16)) for i in range(2)]
        gmb = [fes.enter_context(nc.sbuf_tensor(px + f"gmb{i}", [128, NCH * 128], U8)) for i in range(2)]
        gsb = [fes.enter_context(nc.sbuf_tensor(px + f"gsb{i}", [128, NCH * 128], U8)) for i in range(2)]
        umb = [fes.enter_context(nc.sbuf_tensor(px + f"umb{i}", [128, NCH * 128], U8)) for i in range(2)]
        usb = [fes.enter_context(nc.sbuf_tensor(px + f"usb{i}", [128, NCH * 128], U8)) for i in range(2)]
        dmb = fes.enter_context(nc.sbuf_tensor(px + "dmb", [128, D], U8))
        dsb = fes.enter_context(nc.sbuf_tensor(px + "dsb", [128, D], U8))
        gscb = fes.enter_context(nc.sbuf_tensor(px + "gscb", [128, NCH * 128], BF16))
        uscb = fes.enter_context(nc.sbuf_tensor(px + "uscb", [128, NCH * 128], BF16))
        dscb = fes.enter_context(nc.sbuf_tensor(px + "dscb", [128, D], BF16))
        wd = fes.enter_context(nc.sbuf_tensor(px + "wd", [128, SCH * D], BF16))
        inter = fes.enter_context(nc.sbuf_tensor(px + "inter", [128, SCH * NTT * TT], BF16))
        scC = [fes.enter_context(nc.sbuf_tensor(px + f"scC{i}", [128, 1024], BF16)) for i in range(2)]
        scD = [fes.enter_context(nc.sbuf_tensor(px + f"scD{i}", [128, 1024], BF16)) for i in range(2)]
        scF = [fes.enter_context(nc.sbuf_tensor(px + f"scF{i}", [128, 1024], BF16)) for i in range(2)]
        gel = [fes.enter_context(nc.sbuf_tensor(px + f"gel{i}", [128, TT], F32)) for i in range(2)]
        accs = [fes.enter_context(nc.sbuf_tensor(px + f"accs{i}", [128, TT], F32)) for i in range(4)]
        xres = [fes.enter_context(nc.sbuf_tensor(px + f"xres{i}", [128, TT], F32)) for i in range(2)]
        pg = [fes.enter_context(nc.psum_tensor(px + f"pg{i}", [128, TT], F32)) for i in range(2)]
        pu = [fes.enter_context(nc.psum_tensor(px + f"pu{i}", [128, TT], F32)) for i in range(2)]
        pd = [fes.enter_context(nc.psum_tensor(px + f"pd{i}", [128, 512], F32)) for i in range(4)]

        s_ad = Sem(es, nc, px + "s_ad")
        s_sb = Sem(es, nc, px + "s_sb")
        s_bmg = Sem(es, nc, px + "s_bmg")
        s_bmu = Sem(es, nc, px + "s_bmu")
        s_bmd = Sem(es, nc, px + "s_bmd")
        s_bsg = Sem(es, nc, px + "s_bsg")
        s_bsu = Sem(es, nc, px + "s_bsu")
        s_bsd = Sem(es, nc, px + "s_bsd")
        s_scg = Sem(es, nc, px + "s_scg")
        s_scu = Sem(es, nc, px + "s_scu")
        s_scd = Sem(es, nc, px + "s_scd")
        s_bgc = Sem(es, nc, px + "s_bgc")
        s_bdc = Sem(es, nc, px + "s_bdc")
        s_mm = Sem(es, nc, px + "s_mm")
        s_gact = Sem(es, nc, px + "s_gact")
        s_gtan = Sem(es, nc, px + "s_gtan")
        s_gdve = Sem(es, nc, px + "s_gdve")
        s_gz = Sem(es, nc, px + "s_gz")
        s_int = Sem(es, nc, px + "s_int")
        s_dmm = Sem(es, nc, px + "s_dmm")
        s_accl = Sem(es, nc, px + "s_accl")
        s_xr = Sem(es, nc, px + "s_xr")
        s_ev = Sem(es, nc, px + "s_ev")
        s_st = Sem(es, nc, px + "s_st")

        gm_r = dram["gmp"].rearrange("(c p) n -> p c n", p=128)
        gs_r = dram["gsp"].rearrange("(c p) n -> p c n", p=128)
        um_r = dram["ump"].rearrange("(c p) n -> p c n", p=128)
        us_r = dram["usp"].rearrange("(c p) n -> p c n", p=128)
        dm_r = dram["dmp"]
        ds_r = dram["dsp"]

        def kof(ncg, tt):
            return ncg * NTT + tt

        def mslab(ncg, slot):
            mat, h = slot // 2, slot % 2
            buf = dmb if mat == 2 else (gmb, umb)[mat][ncg % 2]
            return buf[:, h * 1024:(h + 1) * 1024]

        def sslab(ncg, slot):
            mat, h = slot // 2, slot % 2
            buf = dsb if mat == 2 else (gsb, usb)[mat][ncg % 2]
            return buf[:, h * 1024:(h + 1) * 1024]

        def ad_base(ncg, slot):
            return 18 * ncg + 3 * slot

        def decode_act(sc, ncg, slot):
            nf = cst["nibf"]
            m_ap = mslab(ncg, slot)
            h = slot % 2
            C, Dd, F = scC[h], scD[h], scF[h]
            if slot < 2:
                if ncg >= 1:
                    sc.wait_ge(s_bdc.h, 2 * (ncg - 1) + h + 1)
            elif slot < 4:
                sc.wait_ge(s_bgc.h, 4 * ncg + h + 1)
            else:
                sc.wait_ge(s_bgc.h, 4 * ncg + 3 + h)
            sc.activation(C[:], m_ap, AF.Identity, scale=2.0, bias=nf[:, 0:1], _inc=s_ad)
            sc.activation(Dd[:], m_ap, AF.Identity, scale=4.0, bias=nf[:, 1:2], _inc=s_ad)
            sc.activation(F[:], sslab(ncg, slot), AF.Identity, scale=-1.0,
                          bias=nf[:, 2:3], _inc=s_ad)

        def decode_dve(v, ncg, slot):
            mat, h = slot // 2, slot % 2
            b = ncg % 2
            C, Dd, F = scC[slot % 2], scD[slot % 2], scF[slot % 2]
            v.wait_ge(s_ad.h, ad_base(ncg, slot) + 2)
            v.tensor_tensor(C[:], C[:], Dd[:], ALU.max)
            v.scalar_tensor_tensor(C[:], mslab(ncg, slot), 1.0, C[:],
                                   op0=ALU.mult, op1=ALU.max, _inc=s_sb)
            v.wait_ge(s_ad.h, ad_base(ncg, slot) + 3)
            v.tensor_mul(F[:], F[:], C[:])
            if mat < 2:
                sem = s_scg if mat == 0 else s_scu
                v.wait_ge(sem.h, 64 * (ncg + 1))
                if slot == 0 and ncg >= 2:
                    v.wait_ge(s_mm.h, 2 * NTT * (ncg - 1))
                wdec = wg[b] if mat == 0 else wu[b]
                slab = gscb if mat == 0 else uscb
                v.tensor_mul(wdec[:, h * 1024:(h + 1) * 1024], F[:],
                             slab[:, h * 1024:(h + 1) * 1024], _inc=s_bgc)
            else:
                v.wait_ge(s_scd.h, 64 * (ncg + 1))
                cl = ncg % SCH
                v.tensor_mul(wd[:, cl * D + h * 1024:cl * D + (h + 1) * 1024], F[:],
                             dscb[:, h * 1024:(h + 1) * 1024], _inc=s_bdc)

        with nc.Block() as blk:
            @blk.sync
            def _(sync):
                def accflow(sup):
                    for tt in range(NTT):
                        for dm in range(NCH):
                            k = (sup * NTT + tt) * NCH + dm
                            sync.wait_ge(s_st.h, 16 * k)
                            if sup == NSUP - 1:
                                idx = tt * NCH + dm
                                sync.wait_ge(s_xr.h, 16 * idx)
                                if idx >= 2:
                                    sync.wait_ge(s_ev.h, k - 1)
                                sync.dma_start(out=xres[idx % 2][:],
                                               in_=x_in_r[:, dm, tt * TT:(tt + 1) * TT]).then_inc(s_xr.h, 16)
                            sync.wait_ge(s_ev.h, k + 1)
                            dst = x_out_r if sup == NSUP - 1 else acc_r
                            sync.dma_start(out=dst[:, dm, tt * TT:(tt + 1) * TT],
                                           in_=accs[k % 4][:]).then_inc(s_st.h, 16)

                def gu_bytes(ncg):
                    b = ncg % 2
                    sync.wait_ge(s_bmg.h, 16 * ncg)
                    if ncg >= 2:
                        sync.wait_ge(s_sb.h, 6 * (ncg - 2) + 2)
                    sync.dma_start(out=gmb[b][:].rearrange("p (c w) -> p c w", w=128),
                                   in_=gm_r[:, :, ncg * 128:(ncg + 1) * 128]).then_inc(s_bmg.h, 16)
                    sync.wait_ge(s_bmu.h, 16 * ncg)
                    if ncg >= 2:
                        sync.wait_ge(s_sb.h, 6 * (ncg - 2) + 4)
                    sync.dma_start(out=umb[b][:].rearrange("p (c w) -> p c w", w=128),
                                   in_=um_r[:, :, ncg * 128:(ncg + 1) * 128]).then_inc(s_bmu.h, 16)
                    sync.wait_ge(s_bsg.h, 16 * ncg)
                    if ncg >= 2:
                        sync.wait_ge(s_ad.h, 18 * (ncg - 2) + 6)
                    sync.dma_start(out=gsb[b][:].rearrange("p (c w) -> p c w", w=128),
                                   in_=gs_r[:, :, ncg * 128:(ncg + 1) * 128]).then_inc(s_bsg.h, 16)
                    sync.wait_ge(s_bsu.h, 16 * ncg)
                    if ncg >= 2:
                        sync.wait_ge(s_ad.h, 18 * (ncg - 2) + 12)
                    sync.dma_start(out=usb[b][:].rearrange("p (c w) -> p c w", w=128),
                                   in_=us_r[:, :, ncg * 128:(ncg + 1) * 128]).then_inc(s_bsu.h, 16)

                def d_bytes(ncg):
                    sync.wait_ge(s_bmd.h, 16 * ncg)
                    if ncg >= 1:
                        sync.wait_ge(s_sb.h, 6 * ncg)
                    sync.dma_start(out=dmb[:], in_=dm_r[ncg * 128:(ncg + 1) * 128, :]).then_inc(s_bmd.h, 16)
                    sync.wait_ge(s_bsd.h, 16 * ncg)
                    if ncg >= 1:
                        sync.wait_ge(s_ad.h, 18 * (ncg - 1) + 18)
                    sync.dma_start(out=dsb[:], in_=ds_r[ncg * 128:(ncg + 1) * 128, :]).then_inc(s_bsd.h, 16)

                for sup in range(NSUP):
                    gu_bytes(sup * SCH)
                    gu_bytes(sup * SCH + 1)
                    d_bytes(sup * SCH)
                    if sup >= 1:
                        accflow(sup - 1)
                    d_bytes(sup * SCH + 1)
                    for cl in range(2, SCH):
                        gu_bytes(sup * SCH + cl)
                        d_bytes(sup * SCH + cl)
                accflow(NSUP - 1)
                sync.wait_ge(s_st.h, 16 * NSUP * NTT * NCH)

            @blk.gpsimd
            def _(g):
                nacc = 0

                def acc_loads(sup):
                    nonlocal nacc
                    for tt in range(NTT):
                        for dm in range(NCH):
                            k = (sup * NTT + tt) * NCH + dm
                            g.wait_ge(s_accl.h, 16 * nacc)
                            g.wait_ge(s_st.h, 16 * (k - 3))
                            g.dma_start(out=accs[k % 4][:],
                                        in_=acc_r[:, dm, tt * TT:(tt + 1) * TT]).then_inc(s_accl.h, 16)
                            nacc += 1

                gsT = dram["g_scT"]
                usT = dram["u_scT"]
                dsT = dram["d_scT"]
                for ncg in range(NC):
                    b = ncg % 2
                    gsl = gsT[:, ncg * 128:(ncg + 1) * 128].rearrange("(c pa) w -> pa c w", pa=4)
                    usl = usT[:, ncg * 128:(ncg + 1) * 128].rearrange("(c pa) w -> pa c w", pa=4)
                    for pa in range(4):
                        g.wait_ge(s_scg.h, 16 * (4 * ncg + pa))
                        if pa == 0 and ncg >= 1:
                            g.wait_ge(s_bgc.h, 4 * (ncg - 1) + 2)
                        g.dma_start(out=gscb[32 * pa:32 * pa + 32, :].rearrange("p (c w) -> p c w", w=128),
                                    in_=gsl[pa:pa + 1].broadcast_to([32, NCH, 128])).then_inc(s_scg.h, 16)
                    for pa in range(4):
                        g.wait_ge(s_scu.h, 16 * (4 * ncg + pa))
                        if pa == 0 and ncg >= 1:
                            g.wait_ge(s_bgc.h, 4 * (ncg - 1) + 4)
                        g.dma_start(out=uscb[32 * pa:32 * pa + 32, :].rearrange("p (c w) -> p c w", w=128),
                                    in_=usl[pa:pa + 1].broadcast_to([32, NCH, 128])).then_inc(s_scu.h, 16)
                    for pa in range(4):
                        g.wait_ge(s_scd.h, 16 * (4 * ncg + pa))
                        if pa == 0 and ncg >= 1:
                            g.wait_ge(s_bdc.h, 2 * ncg)
                        g.dma_start(out=dscb[32 * pa:32 * pa + 32, :],
                                    in_=dsT[4 * ncg + pa:4 * ncg + pa + 1, :]
                                    .broadcast_to([32, D])).then_inc(s_scd.h, 16)
                    if ncg % SCH == 0 and ncg >= 2 * SCH:
                        acc_loads(ncg // SCH - 1)
                acc_loads(NSUP - 1)

            @blk.tensor
            def _(te):
                for sup in range(NSUP):
                    for cl in range(SCH):
                        ncg = sup * SCH + cl
                        b = ncg % 2
                        for tt in range(NTT):
                            k = kof(ncg, tt)
                            if k >= 2:
                                te.wait_ge(s_gz.h, k - 1)
                                te.wait_ge(s_int.h, k - 1)
                            for dc in range(NCH):
                                if tt == 0 and dc % 8 == 0:
                                    te.wait_ge(s_bgc.h, 4 * ncg + dc // 8 + 1)
                                mm = te.matmul(pg[k % 2][:], wg[b][:, dc * 128:(dc + 1) * 128],
                                               x_ln[:, dc * T + tt * TT:dc * T + (tt + 1) * TT],
                                               start=(dc == 0), stop=(dc == NCH - 1))
                            mm.then_inc(s_mm.h, 1)
                            for dc in range(NCH):
                                if tt == 0 and dc % 8 == 0:
                                    te.wait_ge(s_bgc.h, 4 * ncg + dc // 8 + 3)
                                mm = te.matmul(pu[k % 2][:], wu[b][:, dc * 128:(dc + 1) * 128],
                                               x_ln[:, dc * T + tt * TT:dc * T + (tt + 1) * TT],
                                               start=(dc == 0), stop=(dc == NCH - 1))
                            mm.then_inc(s_mm.h, 1)
                    te.wait_ge(s_bdc.h, 2 * SCH * (sup + 1))
                    for tt in range(NTT):
                        te.wait_ge(s_int.h, kof(sup * SCH + SCH - 1, tt) + 1)
                        for dm in range(NCH):
                            k = (sup * NTT + tt) * NCH + dm
                            if k >= 4:
                                te.wait_ge(s_ev.h, k - 3)
                            for i in range(SCH):
                                mm = te.matmul(pd[k % 4][:], wd[:, i * D + dm * 128:i * D + (dm + 1) * 128],
                                               inter[:, (i * NTT + tt) * TT:(i * NTT + tt + 1) * TT],
                                               start=(i == 0), stop=(i == SCH - 1))
                            mm.then_inc(s_dmm.h, 1)

            @blk.vector
            def _(v):
                v = Chain(v, ch["v"])
                prev_k = []

                def tail(pk):
                    v.wait_ge(s_gtan.h, pk + 1)
                    v.scalar_tensor_tensor(gel[pk % 2][:], gel[pk % 2][:], 1.0, pg[pk % 2][:],
                                           op0=ALU.add, op1=ALU.mult, _inc=s_gz)
                    v.wait_ge(s_mm.h, 2 * pk + 2)
                    pcl = (pk // NTT) % SCH
                    v.scalar_tensor_tensor(
                        inter[:, (pcl * NTT + (pk % NTT)) * TT:(pcl * NTT + (pk % NTT) + 1) * TT],
                        gel[pk % 2][:], 0.5, pu[pk % 2][:],
                        op0=ALU.mult, op1=ALU.mult, _inc=s_int)

                def gelu_dve(ncg):
                    for tt in range(NTT):
                        k = kof(ncg, tt)
                        v.wait_ge(s_gact.h, k + 1)
                        v.scalar_tensor_tensor(gel[k % 2][:], gel[k % 2][:], GELU_C1,
                                               pg[k % 2][:], op0=ALU.add, op1=ALU.mult,
                                               _inc=s_gdve)
                        if prev_k:
                            tail(prev_k.pop(0))
                        prev_k.append(k)

                def evacs(sup):
                    for tt in range(NTT):
                        for dm in range(NCH):
                            k = (sup * NTT + tt) * NCH + dm
                            bb = k % 4
                            v.wait_ge(s_dmm.h, k + 1)
                            if sup == 0 and k >= 4:
                                v.wait_ge(s_st.h, 16 * (k - 3))
                            last = sup != NSUP - 1
                            if sup == 0:
                                v.tensor_copy(accs[bb][:], pd[bb][:], _inc=s_ev if last else None)
                            else:
                                v.wait_ge(s_accl.h, 16 * (sup * NTT * NCH + tt * NCH + dm + 1 - NTT * NCH))
                                v.tensor_add(accs[bb][:], accs[bb][:], pd[bb][:],
                                             _inc=s_ev if last else None)
                            if sup == NSUP - 1:
                                v.wait_ge(s_xr.h, 16 * (tt * NCH + dm + 1))
                                v.tensor_add(accs[bb][:], accs[bb][:], xres[k % 2][:], _inc=s_ev)

                for ncg in range(NC):
                    decode_dve(v, ncg, 0)
                    decode_dve(v, ncg, 1)
                    if ncg >= 1:
                        gelu_dve(ncg - 1)
                    decode_dve(v, ncg, 2)
                    decode_dve(v, ncg, 3)
                    if ncg >= 1 and ncg % SCH == 0:
                        while prev_k:
                            tail(prev_k.pop(0))
                        evacs(ncg // SCH - 1)
                    if ncg // SCH > 0:
                        v.wait_ge(s_dmm.h, NTT * NCH * (ncg // SCH))
                    decode_dve(v, ncg, 4)
                    decode_dve(v, ncg, 5)
                gelu_dve(NC - 1)
                while prev_k:
                    tail(prev_k.pop(0))
                evacs(NSUP - 1)

            @blk.scalar
            def _(sc):
                sc = Chain(sc, ch["s"])

                def tanh_op(pk):
                    sc.wait_ge(s_gdve.h, pk + 1)
                    sc.activation(gel[pk % 2][:], gel[pk % 2][:], AF.Tanh, _inc=s_gtan)

                def gelu_act(ncg, skip_first):
                    for tt in range(NTT):
                        k = kof(ncg, tt)
                        if k >= 1 and not (skip_first and tt == 0):
                            tanh_op(k - 1)
                        sc.wait_ge(s_mm.h, 2 * k + 1)
                        if k >= 2:
                            sc.wait_ge(s_gz.h, k - 1)
                        sc.activation(gel[k % 2][:], pg[k % 2][:], AF.Square,
                                      scale=GELU_C2 ** 0.5, _inc=s_gact)

                for ncg in range(NC):
                    if ncg >= 2:
                        tanh_op(kof(ncg - 2, NTT - 1))
                    sc.wait_ge(s_bmg.h, 16 * (ncg + 1))
                    sc.wait_ge(s_bsg.h, 16 * (ncg + 1))
                    decode_act(sc, ncg, 0)
                    decode_act(sc, ncg, 1)
                    if ncg >= 1:
                        gelu_act(ncg - 1, skip_first=(ncg >= 2))
                    sc.wait_ge(s_bmu.h, 16 * (ncg + 1))
                    sc.wait_ge(s_bsu.h, 16 * (ncg + 1))
                    decode_act(sc, ncg, 2)
                    decode_act(sc, ncg, 3)
                    sc.wait_ge(s_bmd.h, 16 * (ncg + 1))
                    sc.wait_ge(s_bsd.h, 16 * (ncg + 1))
                    decode_act(sc, ncg, 4)
                    decode_act(sc, ncg, 5)
                tanh_op(kof(NC - 2, NTT - 1))
                gelu_act(NC - 1, skip_first=True)
                tanh_op(kof(NC - 1, NTT - 1))

    return



def build_l0_ffn(T=2048):
    nc = bass.Bass("TRN2", target_bir_lowering=False)
    ins = {}
    ins["x_in"] = nc.dram_tensor("x_in", [D, T], F32, kind="ExternalInput").ap()
    for nm, shp in (("gmp", [D, NMLP]), ("gsp", [D, NMLP]), ("ump", [D, NMLP]),
                    ("usp", [D, NMLP]), ("dmp", [NMLP, D]), ("dsp", [NMLP, D])):
        ins[nm] = nc.dram_tensor(nm, shp, U8, kind="ExternalInput").ap()
    ins["g_scT"] = nc.dram_tensor("g_scT", [64, NMLP], F32, kind="ExternalInput").ap()
    ins["u_scT"] = nc.dram_tensor("u_scT", [64, NMLP], F32, kind="ExternalInput").ap()
    ins["d_scT"] = nc.dram_tensor("d_scT", [512, D], F32, kind="ExternalInput").ap()
    ins["sel16"] = nc.dram_tensor("sel16", [16 * 64, 128], BF16, kind="ExternalInput").ap()
    ins["nibcols_f"] = nc.dram_tensor("nibcols_f", [128, 3], F32, kind="ExternalInput").ap()
    ins["onescol_bf"] = nc.dram_tensor("onescol_bf", [128, 1], BF16, kind="ExternalInput").ap()
    ins["onesrow_f32"] = nc.dram_tensor("onesrow_f32", [1, 128], F32, kind="ExternalInput").ap()
    x_out = nc.dram_tensor("x_out", [D, T], F32, kind="ExternalOutput").ap()
    acc_d = nc.dram_tensor("acc", [D, T], F32).ap()
    es = contextlib.ExitStack()
    with es:
        emit_l0_ffn_body(nc, es, T, "f_", ins["x_in"], x_out, acc_d,
                         {k: v for k, v in ins.items() if k != "x_in"})
    return nc




L1_SCALE = 128.0 ** -0.5
EXP_SHIFT_MARGIN = 40.0


def prep_l1_attn_host(Wq, Wk, Wv, Wo, TF=4096):
    out = {
        "wqT": np.ascontiguousarray(Wq.T),
        "wkT": np.ascontiguousarray(Wk.T),
        "wvT": np.ascontiguousarray(Wv.T),
        "woT": np.ascontiguousarray(Wo.T),
    }
    inv = 1.0 / 10000.0 ** (np.arange(0, 128, 2) / 128.0)
    t = np.arange(TF, dtype=np.float64)
    emb = np.concatenate([np.outer(inv, t), np.outer(inv, t)], axis=0)
    out["ropeC"] = np.cos(emb).astype(np.float32)
    out["ropeS"] = np.sin(emb).astype(np.float32)
    Rt = np.zeros((128, 128), dtype=np.float32)
    for i in range(64):
        Rt[i + 64, i] = -1.0
        Rt[i, i + 64] = 1.0
    out["rot128"] = Rt.astype(BF)
    return out


def emit_l1_attn_body(nc, es, TF, TO, half, px, x_in, x_out, dram, ch, bufes=None):
    TT = 512
    NTF = TF // TT
    NQT = TO // TT
    OFF = half * TO
    NH, NKV = 16, 4
    GPK = NH // NKV
    NC32 = TF // 128

    if bufes is None:
        bufes = es
    onesc = bufes.enter_context(nc.sbuf_tensor(px + "onesc", [128, 1], BF16))
    onesr = bufes.enter_context(nc.sbuf_tensor(px + "onesr", [1, 128], F32))
    rot_s = bufes.enter_context(nc.sbuf_tensor(px + "rot", [128, 128], BF16))

    q_d = nc.dram_tensor(px + "q_d", [2048, TO], BF16).ap()
    k_d = nc.dram_tensor(px + "k_d", [512, TF], BF16).ap()
    v_d = nc.dram_tensor(px + "v_d", [TF, 512], BF16).ap()
    ao_d = nc.dram_tensor(px + "ao_d", [2048, TO], BF16).ap()

    x_in_r = x_in.rearrange("(c p) t -> p c t", p=128)
    x_out_r = x_out.rearrange("(c p) t -> p c t", p=128)

    ld0 = Sem(es, nc, px + "ld0")
    with nc.Block() as blk0:
        @blk0.sync
        def _(sync):
            sync.dma_start(out=onesc[:], in_=dram["onescol_bf"]).then_inc(ld0.h, 16)
            sync.dma_start(out=onesr[:], in_=dram["onesrow_f32"]).then_inc(ld0.h, 16)
            sync.dma_start(out=rot_s[:], in_=dram["rot128"]).then_inc(ld0.h, 16)

        @blk0.tensor
        def _(e):
            e.wait_ge(ld0.h, 48)

        @blk0.vector
        def _(e):
            e.wait_ge(ld0.h, 48)

        @blk0.gpsimd
        def _(e):
            e.wait_ge(ld0.h, 48)

        @blk0.scalar
        def _(e):
            e.wait_ge(ld0.h, 48)

    own_tiles = sorted(set(range(OFF // TT, (OFF + TO) // TT)))
    sched = []
    g = kq = vv = qn = 0
    for i in range(NTF):
        for oc in range(NKV):
            sched.append(dict(i=i, kind="K", oc=oc, g=g, kq=kq)); g += 1; kq += 1
        for ts in range(4):
            sched.append(dict(i=i, kind="V", oc=ts, g=g, v=vv)); g += 1; vv += 1
        if i in own_tiles:
            for oc in range(NH):
                sched.append(dict(i=i, kind="Q", oc=oc, g=g, kq=kq, qn=qn))
                g += 1; kq += 1; qn += 1
    qgroups = [s for s in sched if s["kind"] == "Q"]

    with contextlib.ExitStack() as aes:
        xsl = aes.enter_context(nc.sbuf_tensor(px + "xsl", [128, NCH * TT], F32))
        xbf = aes.enter_context(nc.sbuf_tensor(px + "xbf", [128, NCH * TT], BF16))
        sq_b = aes.enter_context(nc.sbuf_tensor(px + "sq_b", [128, NCH * TT], BF16))
        xln = aes.enter_context(nc.sbuf_tensor(px + "xln", [128, NCH * TT], BF16))
        rows = aes.enter_context(nc.sbuf_tensor(px + "rows", [1, 6 * TT], F32))
        lntmp = aes.enter_context(nc.sbuf_tensor(px + "lntmp", [128, TT], F32))
        wks = aes.enter_context(nc.sbuf_tensor(px + "wks", [128, 16 * 512], BF16))
        wvs = aes.enter_context(nc.sbuf_tensor(px + "wvs", [128, 16 * 512], BF16))
        wqt = [aes.enter_context(nc.sbuf_tensor(px + f"wqt{j}", [128, 16 * 128], BF16)) for j in range(2)]
        ropc = aes.enter_context(nc.sbuf_tensor(px + "ropc", [128, TT], F32))
        rops = aes.enter_context(nc.sbuf_tensor(px + "rops", [128, TT], F32))
        kbf = [aes.enter_context(nc.sbuf_tensor(px + f"kbf{j}", [128, TT], BF16)) for j in range(2)]
        tmpA = aes.enter_context(nc.sbuf_tensor(px + "tmpA", [128, TT], F32))
        tmpB = aes.enter_context(nc.sbuf_tensor(px + "tmpB", [128, TT], F32))
        kout = [aes.enter_context(nc.sbuf_tensor(px + f"kout{j}", [128, TT], BF16)) for j in range(2)]
        vout = [aes.enter_context(nc.sbuf_tensor(px + f"vout{j}", [128, TT], BF16)) for j in range(2)]
        ps_ls = aes.enter_context(nc.psum_tensor(px + "ps_ls", [1, TT], F32))
        ps_lq = aes.enter_context(nc.psum_tensor(px + "ps_lq", [1, TT], F32))
        ps_br = aes.enter_context(nc.psum_tensor(px + "ps_br", [128, TT], F32))
        ps_bm = aes.enter_context(nc.psum_tensor(px + "ps_bm", [128, TT], F32))
        ppj = [aes.enter_context(nc.psum_tensor(px + f"ppj{j}", [128, TT], F32)) for j in range(2)]
        prot = aes.enter_context(nc.psum_tensor(px + "prot", [128, TT], F32))

        s_lx = Sem(es, nc, px + "s_lx")
        s_lb = Sem(es, nc, px + "s_lb")
        s_rp = Sem(es, nc, px + "s_rp")
        s_wkv = Sem(es, nc, px + "s_wkv")
        s_wq = Sem(es, nc, px + "s_wq")
        sqd = Sem(es, nc, px + "sqd")
        lmm = Sem(es, nc, px + "lmm")
        rwd = Sem(es, nc, px + "rwd")
        lact = Sem(es, nc, px + "lact")
        apd = Sem(es, nc, px + "apd")
        pjd = Sem(es, nc, px + "pjd")
        evd = Sem(es, nc, px + "evd")
        rotd = Sem(es, nc, px + "rotd")
        cmb = Sem(es, nc, px + "cmb")
        s_stkq = Sem(es, nc, px + "s_stkq")
        s_stv = Sem(es, nc, px + "s_stv")

        with nc.Block() as blk:
            @blk.sync
            def _(sync):
                def stores(i):
                    for s in [s for s in sched if s["i"] == i]:
                        if s["kind"] in ("K", "Q"):
                            j = s["kq"]
                            sync.wait_ge(s_stkq.h, 16 * j)
                            sync.wait_ge(cmb.h, j + 1)
                            if s["kind"] == "K":
                                dst = k_d[s["oc"] * 128:(s["oc"] + 1) * 128, s["i"] * TT:(s["i"] + 1) * TT]
                            else:
                                dst = q_d[s["oc"] * 128:(s["oc"] + 1) * 128,
                                          s["i"] * TT - OFF:(s["i"] + 1) * TT - OFF]
                            sync.dma_start(out=dst, in_=kout[j % 2][:]).then_inc(s_stkq.h, 16)
                        else:
                            j = s["v"]
                            sync.wait_ge(s_stv.h, 16 * j)
                            sync.wait_ge(evd.h, s["g"] + 1)
                            dst = v_d[s["i"] * TT + s["oc"] * 128:s["i"] * TT + (s["oc"] + 1) * 128, :]
                            sync.dma_start(out=dst, in_=vout[j % 2][:]).then_inc(s_stv.h, 16)

                for i in range(NTF):
                    sync.wait_ge(s_lx.h, 16 * i)
                    if i >= 1:
                        sync.wait_ge(apd.h, i)
                    sync.dma_start(out=xsl[:].rearrange("p (c t) -> p c t", t=TT),
                                   in_=x_in_r[:, :, i * TT:(i + 1) * TT]).then_inc(s_lx.h, 16)
                    if i >= 1:
                        stores(i - 1)
                    sync.wait_ge(s_rp.h, 32 * i)
                    if i >= 1:
                        last_kq = max(s["kq"] for s in sched if s["i"] == i - 1 and s["kind"] != "V")
                        sync.wait_ge(cmb.h, last_kq + 1)
                    sync.dma_start(out=ropc[:], in_=dram["ropeC"][:, i * TT:(i + 1) * TT]).then_inc(s_rp.h, 16)
                    sync.dma_start(out=rops[:], in_=dram["ropeS"][:, i * TT:(i + 1) * TT]).then_inc(s_rp.h, 16)
                stores(NTF - 1)
                nkq_all = sum(1 for s in sched if s["kind"] != "V")
                nv_all = sum(1 for s in sched if s["kind"] == "V")
                sync.wait_ge(s_stkq.h, 16 * nkq_all)
                sync.wait_ge(s_stv.h, 16 * nv_all)

            @blk.gpsimd
            def _(g_):
                g_.dma_start(out=wks[:].rearrange("p (c f) -> p c f", f=512),
                             in_=dram["wkT"].rearrange("(c p) f -> p c f", p=128)).then_inc(s_wkv.h, 16)
                g_.dma_start(out=wvs[:].rearrange("p (c f) -> p c f", f=512),
                             in_=dram["wvT"].rearrange("(c p) f -> p c f", p=128)).then_inc(s_wkv.h, 16)
                for i in range(NTF):
                    g_.wait_ge(s_lb.h, 16 * i)
                    if i >= 1:
                        g_.wait_ge(lmm.h, 3 * i - 2)
                    g_.dma_start(out=xbf[:].rearrange("p (c t) -> p c t", t=TT),
                                 in_=x_in_r[:, :, i * TT:(i + 1) * TT]).then_inc(s_lb.h, 16)
                    for s in [s for s in sched if s["i"] == i and s["kind"] == "Q"]:
                        nq = s["qn"]
                        g_.wait_ge(s_wq.h, 16 * nq)
                        if nq >= 2:
                            g_.wait_ge(pjd.h, qgroups[nq - 2]["g"] + 1)
                        g_.dma_start(out=wqt[nq % 2][:].rearrange("p (c f) -> p c f", f=128),
                                     in_=dram["wqT"].rearrange("(c p) f -> p c f", p=128)
                                     [:, :, s["oc"] * 128:(s["oc"] + 1) * 128]).then_inc(s_wq.h, 16)

            @blk.tensor
            def _(te):
                for i in range(NTF):
                    te.wait_ge(sqd.h, i + 1)
                    if i >= 1:
                        te.wait_ge(rwd.h, 2 * i - 1)
                    for c in range(NCH):
                        te.matmul(ps_ls[:], onesc[:], xbf[:, c * TT:(c + 1) * TT],
                                  start=(c == 0), stop=(c == NCH - 1))
                    for c in range(NCH):
                        mm = te.matmul(ps_lq[:], onesc[:], sq_b[:, c * TT:(c + 1) * TT],
                                       start=(c == 0), stop=(c == NCH - 1))
                    mm.then_inc(lmm.h, 1)
                    te.wait_ge(rwd.h, 2 * i + 2)
                    if i >= 1:
                        te.wait_ge(apd.h, i)
                    te.matmul(ps_br[:], onesr[:], rows[0:1, 4 * TT:5 * TT],
                              start=True, stop=True).then_inc(lmm.h, 1)
                    te.matmul(ps_bm[:], onesr[:], rows[0:1, 5 * TT:6 * TT],
                              start=True, stop=True).then_inc(lmm.h, 1)
                    te.wait_ge(apd.h, i + 1)
                    for s in [s for s in sched if s["i"] == i]:
                        gg = s["g"]
                        if gg >= 2:
                            te.wait_ge(evd.h, gg - 1)
                        if s["kind"] == "K":
                            te.wait_ge(s_wkv.h, 32)
                            for ic in range(NCH):
                                mm = te.matmul(ppj[gg % 2][:],
                                               wks[:, ic * 512 + s["oc"] * 128:ic * 512 + (s["oc"] + 1) * 128],
                                               xln[:, ic * TT:(ic + 1) * TT],
                                               start=(ic == 0), stop=(ic == NCH - 1))
                            mm.then_inc(pjd.h, 1)
                        elif s["kind"] == "Q":
                            nq = s["qn"]
                            te.wait_ge(s_wq.h, 16 * (nq + 1))
                            for ic in range(NCH):
                                mm = te.matmul(ppj[gg % 2][:],
                                               wqt[nq % 2][:, ic * 128:(ic + 1) * 128],
                                               xln[:, ic * TT:(ic + 1) * TT],
                                               start=(ic == 0), stop=(ic == NCH - 1))
                            mm.then_inc(pjd.h, 1)
                        else:
                            te.wait_ge(s_wkv.h, 32)
                            ts = s["oc"]
                            for ic in range(NCH):
                                mm = te.matmul(ppj[gg % 2][:],
                                               xln[:, ic * TT + ts * 128:ic * TT + (ts + 1) * 128],
                                               wvs[:, ic * 512:(ic + 1) * 512],
                                               start=(ic == 0), stop=(ic == NCH - 1))
                            mm.then_inc(pjd.h, 1)
                        if s["kind"] in ("K", "Q"):
                            j = s["kq"]
                            te.wait_ge(evd.h, gg + 1)
                            if j >= 1:
                                te.wait_ge(cmb.h, j)
                            te.matmul(prot[:], rot_s[:], kbf[j % 2][:],
                                      start=True, stop=True).then_inc(rotd.h, 1)

            @blk.scalar
            def _(sc):
                sc = Chain(sc, ch["s"])
                for i in range(NTF):
                    sc.wait_ge(s_lb.h, 16 * (i + 1))
                    if i >= 1:
                        sc.wait_ge(lmm.h, 3 * i - 2)
                    sc.activation(sq_b[:], xbf[:], AF.Square, _inc=sqd)
                    sc.wait_ge(rwd.h, 2 * i + 1)
                    sc.activation(rows[0:1, 4 * TT:5 * TT], rows[0:1, 3 * TT:4 * TT], AF.Ln)
                    sc.activation(rows[0:1, 4 * TT:5 * TT], rows[0:1, 4 * TT:5 * TT], AF.Exp,
                                  scale=-0.5, _inc=lact)
                    for s in [s for s in sched if s["i"] == i]:
                        gg = s["g"]
                        sc.wait_ge(pjd.h, gg + 1)
                        if s["kind"] in ("K", "Q"):
                            j = s["kq"]
                            if j >= 2:
                                sc.wait_ge(rotd.h, j - 1)
                                sc.wait_ge(cmb.h, j - 1)
                            sc.activation(kbf[j % 2][:], ppj[gg % 2][:], AF.Copy, _inc=evd)
                        else:
                            jv = s["v"]
                            if jv >= 2:
                                sc.wait_ge(s_stv.h, 16 * (jv - 1))
                            sc.activation(vout[jv % 2][:], ppj[gg % 2][:], AF.Copy, _inc=evd)

            @blk.vector
            def _(v):
                v = Chain(v, ch["v"])
                for i in range(NTF):
                    v.wait_ge(lmm.h, 3 * i + 1)
                    v.tensor_scalar(rows[0:1, 0:TT], ps_ls[:], 1.0 / D, None, op0=ALU.mult)
                    v.tensor_scalar(rows[0:1, TT:2 * TT], ps_lq[:], 1.0 / D, 1e-5,
                                    op0=ALU.mult, op1=ALU.add)
                    v.tensor_mul(rows[0:1, 2 * TT:3 * TT], rows[0:1, 0:TT], rows[0:1, 0:TT])
                    v.scalar_tensor_tensor(rows[0:1, 3 * TT:4 * TT], rows[0:1, 2 * TT:3 * TT], -1.0,
                                           rows[0:1, TT:2 * TT], op0=ALU.mult, op1=ALU.add, _inc=rwd)
                    v.wait_ge(lact.h, i + 1)
                    v.scalar_tensor_tensor(rows[0:1, 5 * TT:6 * TT], rows[0:1, 0:TT], -1.0,
                                           rows[0:1, 4 * TT:5 * TT], op0=ALU.mult, op1=ALU.mult, _inc=rwd)
                    v.wait_ge(lmm.h, 3 * i + 3)
                    v.wait_ge(s_lx.h, 16 * (i + 1))
                    for c in range(NCH):
                        v.tensor_mul(lntmp[:], xsl[:, c * TT:(c + 1) * TT], ps_br[:])
                        v.tensor_add(xln[:, c * TT:(c + 1) * TT], lntmp[:], ps_bm[:],
                                     _inc=apd if c == NCH - 1 else None)
                    v.wait_ge(s_rp.h, 32 * (i + 1))
                    for s in [s for s in sched if s["i"] == i and s["kind"] != "V"]:
                        j = s["kq"]
                        v.wait_ge(evd.h, s["g"] + 1)
                        v.tensor_mul(tmpA[:], kbf[j % 2][:], ropc[:])
                        v.wait_ge(rotd.h, j + 1)
                        v.tensor_mul(tmpB[:], prot[:], rops[:])
                        if j >= 2:
                            v.wait_ge(s_stkq.h, 16 * (j - 1))
                        v.tensor_add(kout[j % 2][:], tmpA[:], tmpB[:], _inc=cmb)

    NRED_K = TF // TT
    NRED_Q = TO // TT

    def red_base(kvh, qj=None):
        b = kvh * (NRED_K + GPK * NRED_Q)
        if qj is None:
            return b
        return b + NRED_K + qj * NRED_Q

    with contextlib.ExitStack() as bes:
        khs = bes.enter_context(nc.sbuf_tensor(px + "khs", [128, TF], BF16))
        vhs = bes.enter_context(nc.sbuf_tensor(px + "vhs", [128, NC32 * 128], BF16))
        qhs = bes.enter_context(nc.sbuf_tensor(px + "qhs", [128, TO], BF16))
        nsq = bes.enter_context(nc.sbuf_tensor(px + "nsq", [128, TF], BF16))
        nrow = bes.enter_context(nc.sbuf_tensor(px + "nrow", [1, 16], F32))
        srow = bes.enter_context(nc.sbuf_tensor(px + "srow", [1, TT], F32))
        ccol = bes.enter_context(nc.sbuf_tensor(px + "ccol", [128, 1], F32))
        att = [bes.enter_context(nc.sbuf_tensor(px + f"att{j}", [128, TT], BF16)) for j in range(2)]
        aob = [bes.enter_context(nc.sbuf_tensor(px + f"aob{j}", [128, TT], BF16)) for j in range(2)]
        rbc = bes.enter_context(nc.sbuf_tensor(px + "rbc", [128, TT], F32))
        ps_s = [bes.enter_context(nc.psum_tensor(px + f"ps_s{j}", [128, TT], F32)) for j in range(2)]
        po = bes.enter_context(nc.psum_tensor(px + "po", [128, TT], F32))
        prow = bes.enter_context(nc.psum_tensor(px + "prow", [1, TT], F32))
        pnrm = [bes.enter_context(nc.psum_tensor(px + f"pnrm{j}", [1, TT], F32)) for j in range(2)]
        pbc = bes.enter_context(nc.psum_tensor(px + "pbc", [128, TT], F32))

        b_kh = Sem(es, nc, px + "b_kh")
        b_vh = Sem(es, nc, px + "b_vh")
        b_qh = Sem(es, nc, px + "b_qh")
        b_sq = Sem(es, nc, px + "b_sq")
        b_nmm = Sem(es, nc, px + "b_nmm")
        b_nrd = Sem(es, nc, px + "b_nrd")
        b_nrm = Sem(es, nc, px + "b_nrm")
        b_ncc = Sem(es, nc, px + "b_ncc")
        b_cb = Sem(es, nc, px + "b_cb")
        b_cc = Sem(es, nc, px + "b_cc")
        b_sm = Sem(es, nc, px + "b_sm")
        b_ex = Sem(es, nc, px + "b_ex")
        b_av = Sem(es, nc, px + "b_av")
        b_rc = Sem(es, nc, px + "b_rc")
        b_bc = Sem(es, nc, px + "b_bc")
        b_bcc = Sem(es, nc, px + "b_bcc")
        b_ao = Sem(es, nc, px + "b_ao")
        b_sao = Sem(es, nc, px + "b_sao")

        with nc.Block() as blk:
            @blk.sync
            def _(sync):
                grp = 0
                for kvh in range(NKV):
                    sync.wait_ge(b_kh.h, 16 * kvh)
                    sync.wait_ge(b_vh.h, 16 * kvh)
                    if kvh >= 1:
                        sync.wait_ge(b_av.h, NC32 * NQT * GPK * kvh)
                    sync.dma_start(out=khs[:], in_=k_d[kvh * 128:(kvh + 1) * 128, :]).then_inc(b_kh.h, 16)
                    sync.dma_start(out=vhs[:].rearrange("p (c f) -> p c f", f=128),
                                   in_=v_d.rearrange("(c p) f -> p c f", p=128)
                                   [:, :, kvh * 128:(kvh + 1) * 128]).then_inc(b_vh.h, 16)
                    for qj in range(GPK):
                        qh = kvh * GPK + qj
                        sync.wait_ge(b_qh.h, 16 * qh)
                        if qh >= 1:
                            sync.wait_ge(b_av.h, NC32 * NQT * qh)
                        sync.dma_start(out=qhs[:], in_=q_d[qh * 128:(qh + 1) * 128, :]).then_inc(b_qh.h, 16)
                        for qt in range(NQT):
                            sync.wait_ge(b_sao.h, 16 * grp)
                            sync.wait_ge(b_ao.h, grp + 1)
                            sync.dma_start(out=ao_d[qh * 128:(qh + 1) * 128, qt * TT:(qt + 1) * TT],
                                           in_=aob[grp % 2][:]).then_inc(b_sao.h, 16)
                            grp += 1

            @blk.tensor
            def _(te):
                grp = 0
                icnt = 0
                red = 0
                for kvh in range(NKV):
                    te.wait_ge(b_sq.h, (GPK + 1) * kvh + 1)
                    te.wait_ge(b_vh.h, 16 * (kvh + 1))
                    for j in range(NRED_K):
                        if red >= 2:
                            te.wait_ge(b_nrd.h, red - 1)
                        te.matmul(pnrm[red % 2][:], onesc[:], nsq[:, j * TT:(j + 1) * TT],
                                  start=True, stop=True).then_inc(b_nmm.h, 1)
                        red += 1
                    for qj in range(GPK):
                        qh = kvh * GPK + qj
                        te.wait_ge(b_sq.h, (GPK + 1) * kvh + qj + 2)
                        for j in range(NRED_Q):
                            if red >= 2:
                                te.wait_ge(b_nrd.h, red - 1)
                            te.matmul(pnrm[red % 2][:], onesc[:], nsq[:, j * TT:(j + 1) * TT],
                                      start=True, stop=True).then_inc(b_nmm.h, 1)
                            red += 1
                        te.wait_ge(b_ncc.h, qh + 1)
                        if grp >= 1:
                            te.wait_ge(b_ao.h, grp)
                        te.matmul(pbc[:, 0:1], onesr[:], nrow[0:1, 15:16],
                                  start=True, stop=True).then_inc(b_cb.h, 1)
                        for qt in range(NQT):
                            te.wait_ge(b_cc.h, qh + 1)
                            if grp >= 1:
                                te.wait_ge(b_ao.h, grp)
                            for c in range(NC32):
                                if icnt >= 2:
                                    te.wait_ge(b_ex.h, icnt - 1)
                                te.matmul(ps_s[icnt % 2][:], khs[:, c * 128:(c + 1) * 128],
                                          qhs[:, qt * TT:(qt + 1) * TT],
                                          start=True, stop=True).then_inc(b_sm.h, 1)
                                te.wait_ge(b_ex.h, icnt + 1)
                                te.matmul(po[:], vhs[:, c * 128:(c + 1) * 128], att[icnt % 2][:],
                                          start=(c == 0), stop=(c == NC32 - 1))
                                te.matmul(prow[:], onesc[:], att[icnt % 2][:],
                                          start=(c == 0), stop=(c == NC32 - 1)).then_inc(b_av.h, 1)
                                icnt += 1
                            te.wait_ge(b_rc.h, grp + 1)
                            te.matmul(pbc[:], onesr[:], srow[0:1, :],
                                      start=True, stop=True).then_inc(b_bc.h, 1)
                            grp += 1

            @blk.scalar
            def _(sc):
                sc = Chain(sc, ch["s"])
                icnt = 0
                for kvh in range(NKV):
                    sc.wait_ge(b_kh.h, 16 * (kvh + 1))
                    if kvh >= 1:
                        sc.wait_ge(b_nmm.h, red_base(kvh))
                    sc.activation(nsq[:], khs[:], AF.Square, _inc=b_sq)
                    for qj in range(GPK):
                        qh = kvh * GPK + qj
                        sc.wait_ge(b_qh.h, 16 * (qh + 1))
                        sc.wait_ge(b_nmm.h, red_base(kvh, qj))
                        sc.activation(nsq[:, 0:TO], qhs[:], AF.Square, _inc=b_sq)
                        sc.wait_ge(b_nrm.h, qh + 1)
                        sc.activation(nrow[0:1, 12:13], nrow[0:1, 11:12], AF.Copy,
                                      scale=nrow[0:1, 10:11])
                        sc.activation(nrow[0:1, 13:14], nrow[0:1, 12:13], AF.Sqrt)
                        sc.activation(nrow[0:1, 14:15], nrow[0:1, 13:14], AF.Copy,
                                      scale=L1_SCALE, bias=-EXP_SHIFT_MARGIN)
                        sc.activation(nrow[0:1, 12:13], nrow[0:1, 14:15], AF.Relu)
                        sc.activation(nrow[0:1, 15:16], nrow[0:1, 12:13], AF.Copy,
                                      scale=-1.0, _inc=b_ncc)
                        sc.wait_ge(b_cb.h, qh + 1)
                        sc.activation(ccol[:], pbc[:, 0:1], AF.Copy, _inc=b_cc)
                        for qt in range(NQT):
                            for c in range(NC32):
                                sc.wait_ge(b_sm.h, icnt + 1)
                                if icnt >= 2:
                                    sc.wait_ge(b_av.h, icnt - 1)
                                sc.activation(att[icnt % 2][:], ps_s[icnt % 2][:], AF.Exp,
                                              bias=ccol[:, 0:1], scale=L1_SCALE, _inc=b_ex)
                                icnt += 1
                            grp_l = qh * NQT + qt
                            sc.wait_ge(b_bc.h, grp_l + 1)
                            if grp_l >= 1:
                                sc.wait_ge(b_ao.h, grp_l)
                            sc.activation(rbc[:], pbc[:], AF.Copy, _inc=b_bcc)

            @blk.vector
            def _(v):
                v = Chain(v, ch["v"])
                grp = 0
                red = 0
                for kvh in range(NKV):
                    if kvh >= 1:
                        v.wait_ge(b_ncc.h, GPK * kvh)
                    for j in range(NRED_K):
                        v.wait_ge(b_nmm.h, red + 1)
                        dstc = 10 if j == NRED_K - 1 else (0 if j == 0 else 1)
                        if j == 0:
                            v.reduce_max(nrow[0:1, 0:1], pnrm[red % 2][:], axis=AX.X, _inc=b_nrd)
                        else:
                            v.reduce_max(nrow[0:1, 1:2], pnrm[red % 2][:], axis=AX.X, _inc=b_nrd)
                            v.tensor_max(nrow[0:1, 0:1], nrow[0:1, 0:1], nrow[0:1, 1:2])
                        red += 1
                    v.tensor_max(nrow[0:1, 10:11], nrow[0:1, 0:1], nrow[0:1, 0:1])
                    for qj in range(GPK):
                        qh = kvh * GPK + qj
                        for j in range(NRED_Q):
                            v.wait_ge(b_nmm.h, red + 1)
                            if j == 0:
                                v.reduce_max(nrow[0:1, 0:1], pnrm[red % 2][:], axis=AX.X, _inc=b_nrd)
                            else:
                                v.reduce_max(nrow[0:1, 1:2], pnrm[red % 2][:], axis=AX.X, _inc=b_nrd)
                                v.tensor_max(nrow[0:1, 0:1], nrow[0:1, 0:1], nrow[0:1, 1:2])
                            red += 1
                        v.tensor_max(nrow[0:1, 11:12], nrow[0:1, 0:1], nrow[0:1, 0:1],
                                     _inc=b_nrm)
                        for qt in range(NQT):
                            v.wait_ge(b_av.h, NC32 * (grp + 1))
                            v.tensor_copy(srow[0:1, :], prow[:])
                            v.reciprocal(srow[0:1, :], srow[0:1, :], _inc=b_rc)
                            v.wait_ge(b_bcc.h, grp + 1)
                            if grp >= 2:
                                v.wait_ge(b_sao.h, 16 * (grp - 1))
                            v.tensor_mul(aob[grp % 2][:], po[:], rbc[:], _inc=b_ao)
                            grp += 1

    with contextlib.ExitStack() as ces:
        wos = ces.enter_context(nc.sbuf_tensor(px + "wos", [128, 16 * 2048], BF16))
        aos = ces.enter_context(nc.sbuf_tensor(px + "aos", [128, 16 * TT], BF16))
        xt = [ces.enter_context(nc.sbuf_tensor(px + f"xt{j}", [128, TT], F32)) for j in range(2)]
        ot = [ces.enter_context(nc.sbuf_tensor(px + f"ot{j}", [128, TT], F32)) for j in range(2)]
        pc = [ces.enter_context(nc.psum_tensor(px + f"pc{j}", [128, TT], F32)) for j in range(2)]

        c_lw = Sem(es, nc, px + "c_lw")
        c_lao = Sem(es, nc, px + "c_lao")
        c_lx = Sem(es, nc, px + "c_lx")
        c_mm = Sem(es, nc, px + "c_mm")
        c_ev = Sem(es, nc, px + "c_ev")
        c_st = Sem(es, nc, px + "c_st")

        ao_r = ao_d.rearrange("(c p) t -> p c t", p=128)

        with nc.Block() as blk:
            @blk.gpsimd
            def _(g_):
                g_.wait_ge(b_sao.h, 16 * NH * NQT)
                g_.dma_start(out=wos[:].rearrange("p (c f) -> p c f", f=2048),
                             in_=dram["woT"].rearrange("(c p) f -> p c f", p=128)).then_inc(c_lw.h, 16)

            @blk.sync
            def _(sync):
                sync.wait_ge(b_sao.h, 16 * NH * NQT)
                idx = 0
                for qt in range(NQT):
                    sync.wait_ge(c_lao.h, 16 * qt)
                    if qt >= 1:
                        sync.wait_ge(c_mm.h, 16 * qt)
                    sync.dma_start(out=aos[:].rearrange("p (c t) -> p c t", t=TT),
                                   in_=ao_r[:, :, qt * TT:(qt + 1) * TT]).then_inc(c_lao.h, 16)
                    for dm in range(NCH):
                        sync.wait_ge(c_lx.h, 16 * idx)
                        if idx >= 2:
                            sync.wait_ge(c_ev.h, idx - 1)
                        sync.dma_start(out=xt[idx % 2][:],
                                       in_=x_in_r[:, dm, OFF + qt * TT:OFF + (qt + 1) * TT]).then_inc(c_lx.h, 16)
                        sync.wait_ge(c_ev.h, idx + 1)
                        sync.wait_ge(c_st.h, 16 * idx)
                        sync.dma_start(out=x_out_r[:, dm, qt * TT:(qt + 1) * TT],
                                       in_=ot[idx % 2][:]).then_inc(c_st.h, 16)
                        idx += 1
                sync.wait_ge(c_st.h, 16 * NQT * NCH)

            @blk.tensor
            def _(te):
                te.wait_ge(c_lw.h, 16)
                idx = 0
                for qt in range(NQT):
                    te.wait_ge(c_lao.h, 16 * (qt + 1))
                    for dm in range(NCH):
                        if idx >= 2:
                            te.wait_ge(c_ev.h, idx - 1)
                        for fc in range(NCH):
                            mm = te.matmul(pc[idx % 2][:],
                                           wos[:, fc * 2048 + dm * 128:fc * 2048 + (dm + 1) * 128],
                                           aos[:, fc * TT:(fc + 1) * TT],
                                           start=(fc == 0), stop=(fc == NCH - 1))
                        mm.then_inc(c_mm.h, 1)
                        idx += 1

            @blk.vector
            def _(v):
                v = Chain(v, ch["v"])
                idx = 0
                for qt in range(NQT):
                    for dm in range(NCH):
                        v.wait_ge(c_mm.h, idx + 1)
                        v.wait_ge(c_lx.h, 16 * (idx + 1))
                        if idx >= 2:
                            v.wait_ge(c_st.h, 16 * (idx - 1))
                        v.tensor_add(ot[idx % 2][:], xt[idx % 2][:], pc[idx % 2][:], _inc=c_ev)
                        idx += 1


def build_l1_attn(TF=4096, TO=2048, half=0):
    nc = bass.Bass("TRN2", target_bir_lowering=False)
    ins = {}
    ins["x_in"] = nc.dram_tensor("x_in", [D, TF], F32, kind="ExternalInput").ap()
    for nm, shp, dt in (("wqT", [2048, 2048], F32), ("wkT", [2048, 512], F32),
                        ("wvT", [2048, 512], F32), ("woT", [2048, 2048], F32),
                        ("ropeC", [128, TF], F32), ("ropeS", [128, TF], F32),
                        ("rot128", [128, 128], BF16),
                        ("onescol_bf", [128, 1], BF16), ("onesrow_f32", [1, 128], F32)):
        ins[nm] = nc.dram_tensor(nm, shp, dt, kind="ExternalInput").ap()
    x_out = nc.dram_tensor("x_out", [D, TO], F32, kind="ExternalOutput").ap()
    es = contextlib.ExitStack()
    with es:
        ch = {"v": ChainState(es, nc, "a_chv"), "s": ChainState(es, nc, "a_chs")}
        emit_l1_attn_body(nc, es, TF, TO, half, "a_", ins["x_in"], x_out,
                          {k: v for k, v in ins.items() if k != "x_in"}, ch)
    return nc



L0_SCALE = 192.0 ** -0.5


def prep_l0_attn_host(w, TF=4096):
    out = {
        "wdkvT": np.ascontiguousarray(w["W_down_kv"].T),
        "wdqT": np.ascontiguousarray(w["W_down_q"].T),
        "wupkT": np.ascontiguousarray(w["W_up_k"].T),
        "wupvT": np.ascontiguousarray(w["W_up_v"].T),
        "wupqT": np.ascontiguousarray(w["W_up_q"].T),
        "wqrT": np.ascontiguousarray(w["W_qr"].T),
        "wkrT": np.ascontiguousarray(w["W_kr"].T),
        "wfcT": np.ascontiguousarray(w["W_fc"].T),
    }

    def col(b):
        return np.ascontiguousarray(b.reshape(-1, 128).T.astype(np.float32))

    out["bdkv"] = col(w["b_down_kv"])
    out["bdq"] = col(w["b_down_q"])
    out["bupk"] = col(w["b_up_k"])
    out["bupv"] = col(w["b_up_v"])
    out["bupq"] = col(w["b_up_q"])
    out["bqr"] = col(w["b_qr"])
    out["bkr"] = np.ascontiguousarray(w["b_kr"].reshape(64, 1).astype(np.float32))
    out["bfc"] = col(w["b_fc"])
    theta = 1.0 / 10000.0 ** (np.arange(0, 64, 2) / 64.0)
    ang = np.outer(theta, np.arange(TF, dtype=np.float64))
    CP64 = np.repeat(np.sin(ang), 2, axis=0)
    SP64 = np.repeat(np.cos(ang), 2, axis=0)
    out["cp128"] = np.ascontiguousarray(np.tile(CP64, (2, 1)).astype(np.float32))
    out["sp128"] = np.ascontiguousarray(np.tile(SP64, (2, 1)).astype(np.float32))
    R64 = np.zeros((64, 64), dtype=np.float32)
    for i in range(32):
        R64[2 * i + 1, 2 * i] = -1.0
        R64[2 * i, 2 * i + 1] = 1.0
    R128 = np.zeros((128, 128), dtype=np.float32)
    R128[0:64, 0:64] = R64
    R128[64:128, 64:128] = R64
    out["rot128i"] = R128.astype(BF)
    return out


def emit_l0_attn_body(nc, es, TF, TO, half, px, x_in, x_out, dram, ch, bufes=None):
    TT = 512
    NTF = TF // TT
    NQT = TO // TT
    OFF = half * TO
    NH = 16
    NC32 = TF // 128

    if bufes is None:
        bufes = es
    onesc = bufes.enter_context(nc.sbuf_tensor(px + "onesc", [128, 1], BF16))
    onesr = bufes.enter_context(nc.sbuf_tensor(px + "onesr", [1, 128], F32))
    rot_s = bufes.enter_context(nc.sbuf_tensor(px + "rot", [128, 128], BF16))

    q_d = nc.dram_tensor(px + "q_d", [2048, TO], BF16).ap()
    k_d = nc.dram_tensor(px + "k_d", [2048, TF], BF16).ap()
    v_d = nc.dram_tensor(px + "v_d", [TF, 1024], BF16).ap()
    ao_d = nc.dram_tensor(px + "ao_d", [1024, TO], BF16).ap()

    x_in_r = x_in.rearrange("(c p) t -> p c t", p=128)
    x_out_r = x_out.rearrange("(c p) t -> p c t", p=128)
    q_hsd = q_d.rearrange("(h s d) t -> h s d t", s=2, d=64)
    k_hsd = k_d.rearrange("(h s d) t -> h s d t", s=2, d=64)

    ld0 = Sem(es, nc, px + "ld0")
    bias_s = {}
    bias_shapes = dict(bdkv=4, bdq=4, bupk=8, bupv=8, bupq=8, bqr=8, bfc=16)
    for nm, w_ in bias_shapes.items():
        bias_s[nm] = bufes.enter_context(nc.sbuf_tensor(px + nm, [128, w_], F32))
    bias_s["bkr"] = bufes.enter_context(nc.sbuf_tensor(px + "bkr", [64, 1], F32))

    with nc.Block() as blk0:
        @blk0.sync
        def _(sync):
            n = 0
            for nm, ap_ in (("onescol_bf", onesc), ("onesrow_f32", onesr), ("rot128i", rot_s)):
                sync.dma_start(out=ap_[:], in_=dram[nm]).then_inc(ld0.h, 16)
                n += 16
            for nm in list(bias_shapes) + ["bkr"]:
                sync.dma_start(out=bias_s[nm][:], in_=dram[nm]).then_inc(ld0.h, 16)
                n += 16
            ld0.n = n

        @blk0.tensor
        def _(e):
            e.wait_ge(ld0.h, 176)

        @blk0.vector
        def _(e):
            e.wait_ge(ld0.h, 176)

        @blk0.gpsimd
        def _(e):
            e.wait_ge(ld0.h, 176)

        @blk0.scalar
        def _(e):
            e.wait_ge(ld0.h, 176)

    own_tiles = sorted(set(range(OFF // TT, (OFF + TO) // TT)))
    sched = []
    g = jr = je = 0
    for i in range(NTF):
        own = i in own_tiles
        def add(kind, oc, **kw):
            nonlocal g, jr, je
            d = dict(i=i, kind=kind, oc=oc, g=g, **kw)
            if kind in ("QR", "KR"):
                d["jr"] = jr; jr += 1
            elif kind in ("KC", "VC", "QC"):
                d["je"] = je; je += 1
            sched.append(d); g += 1
        for oc in range(4):
            add("CKV", oc)
        for oc in range(8):
            add("KC", oc)
        for oc in range(8):
            add("VC", oc)
        add("KR", 0)
        if own:
            for oc in range(4):
                add("CQ", oc)
            for oc in range(8):
                add("QC", oc)
            for oc in range(8):
                add("QR", oc)
    NJR, NJE = jr, je
    rgroups = [s for s in sched if s["kind"] in ("QR", "KR")]

    with contextlib.ExitStack() as aes:
        xsl = aes.enter_context(nc.sbuf_tensor(px + "xsl", [128, NCH * TT], F32))
        xbf = aes.enter_context(nc.sbuf_tensor(px + "xbf", [128, NCH * TT], BF16))
        sq_b = aes.enter_context(nc.sbuf_tensor(px + "sq_b", [128, NCH * TT], BF16))
        xln = aes.enter_context(nc.sbuf_tensor(px + "xln", [128, NCH * TT], BF16))
        rows = aes.enter_context(nc.sbuf_tensor(px + "rows", [1, 6 * TT], F32))
        lntmp = aes.enter_context(nc.sbuf_tensor(px + "lntmp", [128, TT], F32))
        wdkv = aes.enter_context(nc.sbuf_tensor(px + "wdkv", [128, 16 * 512], BF16))
        wdq = aes.enter_context(nc.sbuf_tensor(px + "wdq", [128, 16 * 512], BF16))
        wupk = aes.enter_context(nc.sbuf_tensor(px + "wupk", [128, 4 * 1024], BF16))
        wupv = aes.enter_context(nc.sbuf_tensor(px + "wupv", [128, 4 * 1024], BF16))
        wupq = aes.enter_context(nc.sbuf_tensor(px + "wupq", [128, 4 * 1024], BF16))
        wqr = aes.enter_context(nc.sbuf_tensor(px + "wqr", [128, 4 * 1024], BF16))
        wkr = aes.enter_context(nc.sbuf_tensor(px + "wkr", [128, 16 * 64], BF16))
        ckv_sb = aes.enter_context(nc.sbuf_tensor(px + "ckv", [128, 4 * TT], BF16))
        cq_sb = aes.enter_context(nc.sbuf_tensor(px + "cq", [128, 4 * TT], BF16))
        cp_s = aes.enter_context(nc.sbuf_tensor(px + "cp", [128, TT], F32))
        sp_s = aes.enter_context(nc.sbuf_tensor(px + "sp", [128, TT], F32))
        qrbf = [aes.enter_context(nc.sbuf_tensor(px + f"qrbf{j}", [128, TT], BF16)) for j in range(2)]
        tmpA = aes.enter_context(nc.sbuf_tensor(px + "tmpA", [128, TT], F32))
        tmpB = aes.enter_context(nc.sbuf_tensor(px + "tmpB", [128, TT], F32))
        kout = [aes.enter_context(nc.sbuf_tensor(px + f"kout{j}", [128, TT], BF16)) for j in range(2)]
        eout = [aes.enter_context(nc.sbuf_tensor(px + f"eout{j}", [128, TT], BF16)) for j in range(2)]
        ps_ls = aes.enter_context(nc.psum_tensor(px + "ps_ls", [1, TT], F32))
        ps_lq = aes.enter_context(nc.psum_tensor(px + "ps_lq", [1, TT], F32))
        ps_br = aes.enter_context(nc.psum_tensor(px + "ps_br", [128, TT], F32))
        ps_bm = aes.enter_context(nc.psum_tensor(px + "ps_bm", [128, TT], F32))
        ppj = [aes.enter_context(nc.psum_tensor(px + f"ppj{j}", [128, TT], F32)) for j in range(2)]
        prot = aes.enter_context(nc.psum_tensor(px + "prot", [128, TT], F32))

        s_lx = Sem(es, nc, px + "s_lx")
        s_lb = Sem(es, nc, px + "s_lb")
        s_rp = Sem(es, nc, px + "s_rp")
        s_w = Sem(es, nc, px + "s_w")
        sqd = Sem(es, nc, px + "sqd")
        lmm = Sem(es, nc, px + "lmm")
        rwd = Sem(es, nc, px + "rwd")
        lact = Sem(es, nc, px + "lact")
        apd = Sem(es, nc, px + "apd")
        pjd = Sem(es, nc, px + "pjd")
        evd = Sem(es, nc, px + "evd")
        rotd = Sem(es, nc, px + "rotd")
        cmb = Sem(es, nc, px + "cmb")
        s_stkq = Sem(es, nc, px + "s_stkq")
        s_ste = Sem(es, nc, px + "s_ste")

        def dst_of(s):
            i, oc = s["i"], s["oc"]
            if s["kind"] == "KC":
                return k_hsd[2 * oc:2 * oc + 2, 0, :, i * TT:(i + 1) * TT]
            if s["kind"] == "QC":
                return q_hsd[2 * oc:2 * oc + 2, 0, :, i * TT - OFF:(i + 1) * TT - OFF]
            if s["kind"] == "VC":
                ts, fh = oc % 4, oc // 4
                r0 = i * TT + ts * 128
                return v_d[r0:r0 + 128, fh * 512:(fh + 1) * 512]
            raise AssertionError

        with nc.Block() as blk:
            @blk.sync
            def _(sync):
                def stores(i):
                    for s in [s for s in sched if s["i"] == i]:
                        if s["kind"] in ("KC", "VC", "QC"):
                            j = s["je"]
                            sync.wait_ge(s_ste.h, 16 * j)
                            sync.wait_ge(evd.h, s["g"] + 1)
                            sync.dma_start(out=dst_of(s), in_=eout[j % 2][:]).then_inc(s_ste.h, 16)
                        elif s["kind"] == "QR":
                            j = s["jr"]
                            sync.wait_ge(s_stkq.h, 16 * j)
                            sync.wait_ge(cmb.h, j + 1)
                            oc = s["oc"]
                            sync.dma_start(out=q_hsd[2 * oc:2 * oc + 2, 1, :, s["i"] * TT - OFF:(s["i"] + 1) * TT - OFF],
                                           in_=kout[j % 2][:]).then_inc(s_stkq.h, 16)
                        elif s["kind"] == "KR":
                            j = s["jr"]
                            sync.wait_ge(s_stkq.h, 16 * j)
                            sync.wait_ge(cmb.h, j + 1)
                            sync.dma_start(out=k_d.rearrange("(h s d) t -> d h s t", s=2, d=64)
                                           [:, :, 1, s["i"] * TT:(s["i"] + 1) * TT],
                                           in_=kout[j % 2][0:64, :].rearrange("d (o t) -> d o t", o=1)
                                           .broadcast_to([64, NH, TT])).then_inc(s_stkq.h, 16)

                for i in range(NTF):
                    sync.wait_ge(s_lx.h, 16 * i)
                    if i >= 1:
                        sync.wait_ge(apd.h, i)
                    sync.dma_start(out=xsl[:].rearrange("p (c t) -> p c t", t=TT),
                                   in_=x_in_r[:, :, i * TT:(i + 1) * TT]).then_inc(s_lx.h, 16)
                    if i >= 1:
                        stores(i - 1)
                    sync.wait_ge(s_rp.h, 32 * i)
                    if i >= 1:
                        last_jr = max(s["jr"] for s in sched if s["i"] == i - 1 and s["kind"] in ("QR", "KR"))
                        sync.wait_ge(cmb.h, last_jr + 1)
                    sync.dma_start(out=cp_s[:], in_=dram["cp128"][:, i * TT:(i + 1) * TT]).then_inc(s_rp.h, 16)
                    sync.dma_start(out=sp_s[:], in_=dram["sp128"][:, i * TT:(i + 1) * TT]).then_inc(s_rp.h, 16)
                stores(NTF - 1)
                sync.wait_ge(s_stkq.h, 16 * NJR)
                sync.wait_ge(s_ste.h, 16 * NJE)

            @blk.gpsimd
            def _(g_):
                for nm, ap_, f in (("wdkvT", wdkv, 512), ("wdqT", wdq, 512), ("wupkT", wupk, 1024),
                                   ("wupvT", wupv, 1024), ("wupqT", wupq, 1024), ("wqrT", wqr, 1024),
                                   ("wkrT", wkr, 64)):
                    g_.dma_start(out=ap_[:].rearrange("p (c f) -> p c f", f=f),
                                 in_=dram[nm].rearrange("(c p) f -> p c f", p=128)).then_inc(s_w.h, 16)
                for i in range(NTF):
                    g_.wait_ge(s_lb.h, 16 * i)
                    if i >= 1:
                        g_.wait_ge(lmm.h, 3 * i - 2)
                    g_.dma_start(out=xbf[:].rearrange("p (c t) -> p c t", t=TT),
                                 in_=x_in_r[:, :, i * TT:(i + 1) * TT]).then_inc(s_lb.h, 16)

            @blk.tensor
            def _(te):
                te.wait_ge(s_w.h, 16 * 7)
                for i in range(NTF):
                    te.wait_ge(sqd.h, i + 1)
                    if i >= 1:
                        te.wait_ge(rwd.h, 2 * i - 1)
                    for c in range(NCH):
                        te.matmul(ps_ls[:], onesc[:], xbf[:, c * TT:(c + 1) * TT],
                                  start=(c == 0), stop=(c == NCH - 1))
                    for c in range(NCH):
                        mm = te.matmul(ps_lq[:], onesc[:], sq_b[:, c * TT:(c + 1) * TT],
                                       start=(c == 0), stop=(c == NCH - 1))
                    mm.then_inc(lmm.h, 1)
                    te.wait_ge(rwd.h, 2 * i + 2)
                    if i >= 1:
                        te.wait_ge(apd.h, i)
                    te.matmul(ps_br[:], onesr[:], rows[0:1, 4 * TT:5 * TT],
                              start=True, stop=True).then_inc(lmm.h, 1)
                    te.matmul(ps_bm[:], onesr[:], rows[0:1, 5 * TT:6 * TT],
                              start=True, stop=True).then_inc(lmm.h, 1)
                    te.wait_ge(apd.h, i + 1)
                    tile_sched = [s for s in sched if s["i"] == i]
                    ckv_done = max(s["g"] for s in tile_sched if s["kind"] == "CKV") + 1
                    cq_done = max([s["g"] for s in tile_sched if s["kind"] == "CQ"], default=-1) + 1
                    for s in tile_sched:
                        gg, oc = s["g"], s["oc"]
                        if gg >= 2:
                            te.wait_ge(evd.h, gg - 1)
                        kind = s["kind"]
                        if kind in ("KC", "VC"):
                            te.wait_ge(evd.h, ckv_done)
                        elif kind in ("QC", "QR"):
                            te.wait_ge(evd.h, cq_done)
                        if kind in ("CKV", "CQ"):
                            w_ = wdkv if kind == "CKV" else wdq
                            for ic in range(NCH):
                                mm = te.matmul(ppj[gg % 2][:],
                                               w_[:, ic * 512 + oc * 128:ic * 512 + (oc + 1) * 128],
                                               xln[:, ic * TT:(ic + 1) * TT],
                                               start=(ic == 0), stop=(ic == NCH - 1))
                            mm.then_inc(pjd.h, 1)
                        elif kind in ("KC", "QC", "QR"):
                            w_ = {"KC": wupk, "QC": wupq, "QR": wqr}[kind]
                            src = ckv_sb if kind == "KC" else cq_sb
                            for ic in range(4):
                                mm = te.matmul(ppj[gg % 2][:],
                                               w_[:, ic * 1024 + oc * 128:ic * 1024 + (oc + 1) * 128],
                                               src[:, ic * TT:(ic + 1) * TT],
                                               start=(ic == 0), stop=(ic == 3))
                            mm.then_inc(pjd.h, 1)
                        elif kind == "VC":
                            ts, fh = oc % 4, oc // 4
                            for ic in range(4):
                                mm = te.matmul(ppj[gg % 2][:],
                                               ckv_sb[:, ic * TT + ts * 128:ic * TT + (ts + 1) * 128],
                                               wupv[:, ic * 1024 + fh * 512:ic * 1024 + (fh + 1) * 512],
                                               start=(ic == 0), stop=(ic == 3))
                            mm.then_inc(pjd.h, 1)
                        else:
                            for ic in range(NCH):
                                mm = te.matmul(ppj[gg % 2][0:64, :],
                                               wkr[:, ic * 64:(ic + 1) * 64],
                                               xln[:, ic * TT:(ic + 1) * TT],
                                               start=(ic == 0), stop=(ic == NCH - 1))
                            mm.then_inc(pjd.h, 1)
                        if kind in ("QR", "KR"):
                            j = s["jr"]
                            te.wait_ge(evd.h, gg + 1)
                            if j >= 1:
                                te.wait_ge(cmb.h, j)
                            if kind == "QR":
                                te.matmul(prot[:], rot_s[:], qrbf[j % 2][:],
                                          start=True, stop=True).then_inc(rotd.h, 1)
                            else:
                                te.matmul(prot[0:64, :], rot_s[0:64, 0:64], qrbf[j % 2][0:64, :],
                                          start=True, stop=True).then_inc(rotd.h, 1)

            @blk.scalar
            def _(sc):
                sc = Chain(sc, ch["s"])
                for i in range(NTF):
                    sc.wait_ge(s_lb.h, 16 * (i + 1))
                    if i >= 1:
                        sc.wait_ge(lmm.h, 3 * i - 2)
                    sc.activation(sq_b[:], xbf[:], AF.Square, _inc=sqd)
                    sc.wait_ge(rwd.h, 2 * i + 1)
                    sc.activation(rows[0:1, 4 * TT:5 * TT], rows[0:1, 3 * TT:4 * TT], AF.Ln)
                    sc.activation(rows[0:1, 4 * TT:5 * TT], rows[0:1, 4 * TT:5 * TT], AF.Exp,
                                  scale=-0.5, _inc=lact)
                    for s in [s for s in sched if s["i"] == i]:
                        gg, oc = s["g"], s["oc"]
                        kind = s["kind"]
                        sc.wait_ge(pjd.h, gg + 1)
                        if kind == "CKV":
                            if i >= 1:
                                prevg = max(x["g"] for x in sched if x["i"] == i - 1 and x["kind"] in ("KC", "VC"))
                                sc.wait_ge(pjd.h, prevg + 1)
                            sc.activation(ckv_sb[:, oc * TT:(oc + 1) * TT], ppj[gg % 2][:], AF.Identity,
                                          bias=bias_s["bdkv"][:, oc:oc + 1], _inc=evd)
                        elif kind == "CQ":
                            prev_own = [t for t in own_tiles if t < i]
                            if prev_own:
                                prevg = max(x["g"] for x in sched if x["i"] == prev_own[-1]
                                            and x["kind"] in ("QC", "QR"))
                                sc.wait_ge(pjd.h, prevg + 1)
                            sc.activation(cq_sb[:, oc * TT:(oc + 1) * TT], ppj[gg % 2][:], AF.Identity,
                                          bias=bias_s["bdq"][:, oc:oc + 1], _inc=evd)
                        elif kind in ("KC", "QC"):
                            j = s["je"]
                            if j >= 2:
                                sc.wait_ge(s_ste.h, 16 * (j - 1))
                            bnm = "bupk" if kind == "KC" else "bupq"
                            sc.activation(eout[j % 2][:], ppj[gg % 2][:], AF.Identity,
                                          bias=bias_s[bnm][:, oc:oc + 1], _inc=evd)
                        elif kind == "VC":
                            j = s["je"]
                            if j >= 2:
                                sc.wait_ge(s_ste.h, 16 * (j - 1))
                            sc.activation(eout[j % 2][:], ppj[gg % 2][:], AF.Copy, _inc=evd)
                        else:
                            j = s["jr"]
                            if j >= 2:
                                sc.wait_ge(rotd.h, j - 1)
                                sc.wait_ge(cmb.h, j - 1)
                            if kind == "QR":
                                sc.activation(qrbf[j % 2][:], ppj[gg % 2][:], AF.Identity,
                                              bias=bias_s["bqr"][:, oc:oc + 1], _inc=evd)
                            else:
                                sc.activation(qrbf[j % 2][0:64, :], ppj[gg % 2][0:64, :], AF.Identity,
                                              bias=bias_s["bkr"][:, 0:1], _inc=evd)

            @blk.vector
            def _(v):
                v = Chain(v, ch["v"])
                for i in range(NTF):
                    v.wait_ge(lmm.h, 3 * i + 1)
                    v.tensor_scalar(rows[0:1, 0:TT], ps_ls[:], 1.0 / D, None, op0=ALU.mult)
                    v.tensor_scalar(rows[0:1, TT:2 * TT], ps_lq[:], 1.0 / D, 1e-5,
                                    op0=ALU.mult, op1=ALU.add)
                    v.tensor_mul(rows[0:1, 2 * TT:3 * TT], rows[0:1, 0:TT], rows[0:1, 0:TT])
                    v.scalar_tensor_tensor(rows[0:1, 3 * TT:4 * TT], rows[0:1, 2 * TT:3 * TT], -1.0,
                                           rows[0:1, TT:2 * TT], op0=ALU.mult, op1=ALU.add, _inc=rwd)
                    v.wait_ge(lact.h, i + 1)
                    v.scalar_tensor_tensor(rows[0:1, 5 * TT:6 * TT], rows[0:1, 0:TT], -1.0,
                                           rows[0:1, 4 * TT:5 * TT], op0=ALU.mult, op1=ALU.mult, _inc=rwd)
                    v.wait_ge(lmm.h, 3 * i + 3)
                    v.wait_ge(s_lx.h, 16 * (i + 1))
                    for c in range(NCH):
                        v.tensor_mul(lntmp[:], xsl[:, c * TT:(c + 1) * TT], ps_br[:])
                        v.tensor_add(xln[:, c * TT:(c + 1) * TT], lntmp[:], ps_bm[:],
                                     _inc=apd if c == NCH - 1 else None)
                    v.wait_ge(s_rp.h, 32 * (i + 1))
                    for s in [s for s in sched if s["i"] == i and s["kind"] in ("QR", "KR")]:
                        j = s["jr"]
                        v.wait_ge(evd.h, s["g"] + 1)
                        if s["kind"] == "QR":
                            v.tensor_mul(tmpA[:], qrbf[j % 2][:], cp_s[:])
                            v.wait_ge(rotd.h, j + 1)
                            v.tensor_mul(tmpB[:], prot[:], sp_s[:])
                            if j >= 2:
                                v.wait_ge(s_stkq.h, 16 * (j - 1))
                            v.tensor_add(kout[j % 2][:], tmpA[:], tmpB[:], _inc=cmb)
                        else:
                            v.tensor_mul(tmpA[0:64, :], qrbf[j % 2][0:64, :], cp_s[0:64, :])
                            v.wait_ge(rotd.h, j + 1)
                            v.tensor_mul(tmpB[0:64, :], prot[0:64, :], sp_s[0:64, :])
                            if j >= 2:
                                v.wait_ge(s_stkq.h, 16 * (j - 1))
                            v.tensor_add(kout[j % 2][0:64, :], tmpA[0:64, :], tmpB[0:64, :], _inc=cmb)

    with contextlib.ExitStack() as bes:
        khs = bes.enter_context(nc.sbuf_tensor(px + "khs", [128, TF], BF16))
        vhs = bes.enter_context(nc.sbuf_tensor(px + "vhs", [128, NC32 * 65], BF16))
        qhs = bes.enter_context(nc.sbuf_tensor(px + "qhs", [128, TO], BF16))
        nsq = bes.enter_context(nc.sbuf_tensor(px + "nsq", [128, TF], BF16))
        nrow = bes.enter_context(nc.sbuf_tensor(px + "nrow", [1, 16], F32))
        srow = bes.enter_context(nc.sbuf_tensor(px + "srow", [1, TT], F32))
        ccol = bes.enter_context(nc.sbuf_tensor(px + "ccol", [128, 1], F32))
        att = [bes.enter_context(nc.sbuf_tensor(px + f"att{j}", [128, TT], BF16)) for j in range(2)]
        aof = bes.enter_context(nc.sbuf_tensor(px + "aof", [128, TT], F32))
        aob = [bes.enter_context(nc.sbuf_tensor(px + f"aob{j}", [128, TT], BF16)) for j in range(2)]
        rbc = bes.enter_context(nc.sbuf_tensor(px + "rbc", [128, TT], F32))
        ps_s = [bes.enter_context(nc.psum_tensor(px + f"ps_s{j}", [128, TT], F32)) for j in range(2)]
        po = bes.enter_context(nc.psum_tensor(px + "po", [128, TT], F32))
        pnrm = [bes.enter_context(nc.psum_tensor(px + f"pnrm{j}", [1, TT], F32)) for j in range(2)]
        pbc = bes.enter_context(nc.psum_tensor(px + "pbc", [128, TT], F32))

        b_kh = Sem(es, nc, px + "b_kh")
        b_vh = Sem(es, nc, px + "b_vh")
        b_qh = Sem(es, nc, px + "b_qh")
        b_ones = Sem(es, nc, px + "b_ones")
        b_sq = Sem(es, nc, px + "b_sq")
        b_nmm = Sem(es, nc, px + "b_nmm")
        b_nrd = Sem(es, nc, px + "b_nrd")
        b_nrm = Sem(es, nc, px + "b_nrm")
        b_ncc = Sem(es, nc, px + "b_ncc")
        b_cb = Sem(es, nc, px + "b_cb")
        b_cc = Sem(es, nc, px + "b_cc")
        b_sm = Sem(es, nc, px + "b_sm")
        b_ex = Sem(es, nc, px + "b_ex")
        b_av = Sem(es, nc, px + "b_av")
        b_rc = Sem(es, nc, px + "b_rc")
        b_bc = Sem(es, nc, px + "b_bc")
        b_bcc = Sem(es, nc, px + "b_bcc")
        b_aot = Sem(es, nc, px + "b_aot")
        b_ao = Sem(es, nc, px + "b_ao")
        b_sao = Sem(es, nc, px + "b_sao")

        NRED_K = TF // TT
        NRED_Q = TO // TT

        def red_base(qh, with_q=0):
            return qh * (NRED_K + NRED_Q) + with_q

        with nc.Block() as blk:
            @blk.vector
            def _(v):
                v = Chain(v, ch["v"])
                v.memset(vhs[:].rearrange("p (c w) -> p c w", w=65)[:, :, 64:65], 1.0, _inc=b_ones)
                grp = 0
                red = 0
                for qh in range(NH):
                    if qh >= 1:
                        v.wait_ge(b_ncc.h, qh)
                    for j in range(NRED_K):
                        v.wait_ge(b_nmm.h, red + 1)
                        if j == 0:
                            v.reduce_max(nrow[0:1, 0:1], pnrm[red % 2][:], axis=AX.X, _inc=b_nrd)
                        else:
                            v.reduce_max(nrow[0:1, 1:2], pnrm[red % 2][:], axis=AX.X, _inc=b_nrd)
                            v.tensor_max(nrow[0:1, 0:1], nrow[0:1, 0:1], nrow[0:1, 1:2])
                        red += 1
                    v.tensor_max(nrow[0:1, 10:11], nrow[0:1, 0:1], nrow[0:1, 0:1])
                    for j in range(NRED_Q):
                        v.wait_ge(b_nmm.h, red + 1)
                        if j == 0:
                            v.reduce_max(nrow[0:1, 0:1], pnrm[red % 2][:], axis=AX.X, _inc=b_nrd)
                        else:
                            v.reduce_max(nrow[0:1, 1:2], pnrm[red % 2][:], axis=AX.X, _inc=b_nrd)
                            v.tensor_max(nrow[0:1, 0:1], nrow[0:1, 0:1], nrow[0:1, 1:2])
                        red += 1
                    v.tensor_max(nrow[0:1, 11:12], nrow[0:1, 0:1], nrow[0:1, 0:1], _inc=b_nrm)
                    for qt in range(NQT):
                        v.wait_ge(b_av.h, NC32 * (grp + 1))
                        v.tensor_copy(srow[0:1, :], po[64:65, :])
                        v.reciprocal(srow[0:1, :], srow[0:1, :], _inc=b_rc)
                        v.wait_ge(b_bcc.h, grp + 1)
                        v.tensor_mul(aof[0:64, :], po[0:64, :], rbc[0:64, :], _inc=b_aot)
                        grp += 1

            @blk.sync
            def _(sync):
                grp = 0
                for qh in range(NH):
                    sync.wait_ge(b_kh.h, 16 * qh)
                    sync.wait_ge(b_vh.h, 16 * qh)
                    if qh == 0:
                        sync.wait_ge(b_ones.h, 1)
                    if qh >= 1:
                        sync.wait_ge(b_av.h, NC32 * NQT * qh)
                    sync.dma_start(out=khs[:], in_=k_d[qh * 128:(qh + 1) * 128, :]).then_inc(b_kh.h, 16)
                    sync.dma_start(out=vhs[:].rearrange("p (c w) -> p c w", w=65)[:, :, 0:64],
                                   in_=v_d.rearrange("(c p) f -> p c f", p=128)
                                   [:, :, qh * 64:(qh + 1) * 64]).then_inc(b_vh.h, 16)
                    sync.wait_ge(b_qh.h, 16 * qh)
                    sync.dma_start(out=qhs[:], in_=q_d[qh * 128:(qh + 1) * 128, :]).then_inc(b_qh.h, 16)
                    for qt in range(NQT):
                        sync.wait_ge(b_sao.h, 16 * grp)
                        sync.wait_ge(b_ao.h, grp + 1)
                        sync.dma_start(out=ao_d[qh * 64:(qh + 1) * 64, qt * TT:(qt + 1) * TT],
                                       in_=aob[grp % 2][0:64, :]).then_inc(b_sao.h, 16)
                        grp += 1

            @blk.tensor
            def _(te):
                grp = 0
                icnt = 0
                red = 0
                for qh in range(NH):
                    te.wait_ge(b_sq.h, 2 * qh + 1)
                    te.wait_ge(b_vh.h, 16 * (qh + 1))
                    for j in range(NRED_K):
                        if red >= 2:
                            te.wait_ge(b_nrd.h, red - 1)
                        te.matmul(pnrm[red % 2][:], onesc[:], nsq[:, j * TT:(j + 1) * TT],
                                  start=True, stop=True).then_inc(b_nmm.h, 1)
                        red += 1
                    te.wait_ge(b_sq.h, 2 * qh + 2)
                    for j in range(NRED_Q):
                        if red >= 2:
                            te.wait_ge(b_nrd.h, red - 1)
                        te.matmul(pnrm[red % 2][:], onesc[:], nsq[:, j * TT:(j + 1) * TT],
                                  start=True, stop=True).then_inc(b_nmm.h, 1)
                        red += 1
                    te.wait_ge(b_ncc.h, qh + 1)
                    if grp >= 1:
                        te.wait_ge(b_aot.h, grp)
                    te.matmul(pbc[:, 0:1], onesr[:], nrow[0:1, 15:16],
                              start=True, stop=True).then_inc(b_cb.h, 1)
                    for qt in range(NQT):
                        te.wait_ge(b_cc.h, qh + 1)
                        if grp >= 1:
                            te.wait_ge(b_aot.h, grp)
                        for c in range(NC32):
                            if icnt >= 2:
                                te.wait_ge(b_ex.h, icnt - 1)
                            te.matmul(ps_s[icnt % 2][:], khs[:, c * 128:(c + 1) * 128],
                                      qhs[:, qt * TT:(qt + 1) * TT],
                                      start=True, stop=True).then_inc(b_sm.h, 1)
                            te.wait_ge(b_ex.h, icnt + 1)
                            te.matmul(po[0:65, :], vhs[:, c * 65:(c + 1) * 65], att[icnt % 2][:],
                                      start=(c == 0), stop=(c == NC32 - 1)).then_inc(b_av.h, 1)
                            icnt += 1
                        te.wait_ge(b_rc.h, grp + 1)
                        te.matmul(pbc[:], onesr[:], srow[0:1, :],
                                  start=True, stop=True).then_inc(b_bc.h, 1)
                        grp += 1

            @blk.scalar
            def _(sc):
                sc = Chain(sc, ch["s"])
                icnt = 0
                for qh in range(NH):
                    sc.wait_ge(b_kh.h, 16 * (qh + 1))
                    if qh >= 1:
                        sc.wait_ge(b_nmm.h, red_base(qh))
                    sc.activation(nsq[:], khs[:], AF.Square, _inc=b_sq)
                    sc.wait_ge(b_qh.h, 16 * (qh + 1))
                    sc.wait_ge(b_nmm.h, red_base(qh, NRED_K))
                    sc.activation(nsq[:, 0:TO], qhs[:], AF.Square, _inc=b_sq)
                    sc.wait_ge(b_nrm.h, qh + 1)
                    sc.activation(nrow[0:1, 12:13], nrow[0:1, 11:12], AF.Copy,
                                  scale=nrow[0:1, 10:11])
                    sc.activation(nrow[0:1, 13:14], nrow[0:1, 12:13], AF.Sqrt)
                    sc.activation(nrow[0:1, 14:15], nrow[0:1, 13:14], AF.Copy,
                                  scale=L0_SCALE, bias=-EXP_SHIFT_MARGIN)
                    sc.activation(nrow[0:1, 12:13], nrow[0:1, 14:15], AF.Relu)
                    sc.activation(nrow[0:1, 15:16], nrow[0:1, 12:13], AF.Copy,
                                  scale=-1.0, _inc=b_ncc)
                    sc.wait_ge(b_cb.h, qh + 1)
                    sc.activation(ccol[:], pbc[:, 0:1], AF.Copy, _inc=b_cc)
                    for qt in range(NQT):
                        grp_l = qh * NQT + qt
                        for c in range(NC32):
                            sc.wait_ge(b_sm.h, icnt + 1)
                            if icnt >= 2:
                                sc.wait_ge(b_av.h, icnt - 1)
                            sc.activation(att[icnt % 2][:], ps_s[icnt % 2][:], AF.Exp,
                                          bias=ccol[:, 0:1], scale=L0_SCALE, _inc=b_ex)
                            icnt += 1
                        sc.wait_ge(b_bc.h, grp_l + 1)
                        if grp_l >= 1:
                            sc.wait_ge(b_aot.h, grp_l)
                        sc.activation(rbc[:], pbc[:], AF.Copy, _inc=b_bcc)
                        sc.wait_ge(b_aot.h, grp_l + 1)
                        if grp_l >= 2:
                            sc.wait_ge(b_sao.h, 16 * (grp_l - 1))
                        sc.activation(aob[grp_l % 2][0:64, :], aof[0:64, :], AF.Identity,
                                      bias=bias_s["bupv"][(qh % 2) * 64:(qh % 2) * 64 + 64, qh // 2:qh // 2 + 1],
                                      _inc=b_ao)

    with contextlib.ExitStack() as ces:
        wfc = ces.enter_context(nc.sbuf_tensor(px + "wfc", [128, 8 * 2048], BF16))
        aos = ces.enter_context(nc.sbuf_tensor(px + "aos", [128, 8 * TT], BF16))
        xt = [ces.enter_context(nc.sbuf_tensor(px + f"xt{j}", [128, TT], F32)) for j in range(2)]
        ot = [ces.enter_context(nc.sbuf_tensor(px + f"ot{j}", [128, TT], F32)) for j in range(2)]
        obf = ces.enter_context(nc.sbuf_tensor(px + "obf", [128, TT], F32))
        pc = [ces.enter_context(nc.psum_tensor(px + f"pc{j}", [128, TT], F32)) for j in range(2)]

        c_lw = Sem(es, nc, px + "c_lw")
        c_lao = Sem(es, nc, px + "c_lao")
        c_lx = Sem(es, nc, px + "c_lx")
        c_mm = Sem(es, nc, px + "c_mm")
        c_bi = Sem(es, nc, px + "c_bi")
        c_ev = Sem(es, nc, px + "c_ev")
        c_st = Sem(es, nc, px + "c_st")

        ao_r = ao_d.rearrange("(c p) t -> p c t", p=128)

        with nc.Block() as blk:
            @blk.gpsimd
            def _(g_):
                g_.wait_ge(b_sao.h, 16 * NH * NQT)
                g_.dma_start(out=wfc[:].rearrange("p (c f) -> p c f", f=2048),
                             in_=dram["wfcT"].rearrange("(c p) f -> p c f", p=128)).then_inc(c_lw.h, 16)

            @blk.sync
            def _(sync):
                sync.wait_ge(b_sao.h, 16 * NH * NQT)
                idx = 0
                for qt in range(NQT):
                    sync.wait_ge(c_lao.h, 16 * qt)
                    if qt >= 1:
                        sync.wait_ge(c_mm.h, 16 * qt)
                    sync.dma_start(out=aos[:].rearrange("p (c t) -> p c t", t=TT),
                                   in_=ao_r[:, :, qt * TT:(qt + 1) * TT]).then_inc(c_lao.h, 16)
                    for dm in range(NCH):
                        sync.wait_ge(c_lx.h, 16 * idx)
                        if idx >= 2:
                            sync.wait_ge(c_ev.h, idx - 1)
                        sync.dma_start(out=xt[idx % 2][:],
                                       in_=x_in_r[:, dm, OFF + qt * TT:OFF + (qt + 1) * TT]).then_inc(c_lx.h, 16)
                        sync.wait_ge(c_ev.h, idx + 1)
                        sync.wait_ge(c_st.h, 16 * idx)
                        sync.dma_start(out=x_out_r[:, dm, qt * TT:(qt + 1) * TT],
                                       in_=ot[idx % 2][:]).then_inc(c_st.h, 16)
                        idx += 1
                sync.wait_ge(c_st.h, 16 * NQT * NCH)

            @blk.tensor
            def _(te):
                te.wait_ge(c_lw.h, 16)
                idx = 0
                for qt in range(NQT):
                    te.wait_ge(c_lao.h, 16 * (qt + 1))
                    for dm in range(NCH):
                        if idx >= 2:
                            te.wait_ge(c_ev.h, idx - 1)
                        for fc in range(8):
                            mm = te.matmul(pc[idx % 2][:],
                                           wfc[:, fc * 2048 + dm * 128:fc * 2048 + (dm + 1) * 128],
                                           aos[:, fc * TT:(fc + 1) * TT],
                                           start=(fc == 0), stop=(fc == 7))
                        mm.then_inc(c_mm.h, 1)
                        idx += 1

            @blk.scalar
            def _(sc):
                sc = Chain(sc, ch["s"])
                idx = 0
                for qt in range(NQT):
                    for dm in range(NCH):
                        sc.wait_ge(c_mm.h, idx + 1)
                        if idx >= 1:
                            sc.wait_ge(c_ev.h, idx)
                        sc.activation(obf[:], pc[idx % 2][:], AF.Identity,
                                      bias=bias_s["bfc"][:, dm:dm + 1], _inc=c_bi)
                        idx += 1

            @blk.vector
            def _(v):
                v = Chain(v, ch["v"])
                idx = 0
                for qt in range(NQT):
                    for dm in range(NCH):
                        v.wait_ge(c_bi.h, idx + 1)
                        v.wait_ge(c_lx.h, 16 * (idx + 1))
                        if idx >= 2:
                            v.wait_ge(c_st.h, 16 * (idx - 1))
                        v.tensor_add(ot[idx % 2][:], xt[idx % 2][:], obf[:], _inc=c_ev)
                        idx += 1


def build_l0_attn(TF=4096, TO=2048, half=0):
    nc = bass.Bass("TRN2", target_bir_lowering=False)
    ins = {}
    ins["x_in"] = nc.dram_tensor("x_in", [D, TF], F32, kind="ExternalInput").ap()
    for nm, shp, dt in (("wdkvT", [2048, 512], F32), ("wdqT", [2048, 512], F32),
                        ("wupkT", [512, 1024], F32), ("wupvT", [512, 1024], F32),
                        ("wupqT", [512, 1024], F32), ("wqrT", [512, 1024], F32),
                        ("wkrT", [2048, 64], F32), ("wfcT", [1024, 2048], F32),
                        ("bdkv", [128, 4], F32), ("bdq", [128, 4], F32),
                        ("bupk", [128, 8], F32), ("bupv", [128, 8], F32),
                        ("bupq", [128, 8], F32), ("bqr", [128, 8], F32),
                        ("bkr", [64, 1], F32), ("bfc", [128, 16], F32),
                        ("cp128", [128, TF], F32), ("sp128", [128, TF], F32),
                        ("rot128i", [128, 128], BF16),
                        ("onescol_bf", [128, 1], BF16), ("onesrow_f32", [1, 128], F32)):
        ins[nm] = nc.dram_tensor(nm, shp, dt, kind="ExternalInput").ap()
    x_out = nc.dram_tensor("x_out", [D, TO], F32, kind="ExternalOutput").ap()
    es = contextlib.ExitStack()
    with es:
        ch = {"v": ChainState(es, nc, "m_chv"), "s": ChainState(es, nc, "m_chs")}
        emit_l0_attn_body(nc, es, TF, TO, half, "m_", ins["x_in"], x_out,
                          {k: v for k, v in ins.items() if k != "x_in"}, ch)
    return nc



def prep_l1_moe_host(gate_w, Wg, Wu, Wd, sWg, sWu, sWd, shared_gate_w):
    out = {
        "gwT": np.ascontiguousarray(gate_w.T),
        "sgw": np.ascontiguousarray(shared_gate_w.reshape(16, 128).T),
        "ident_f32": np.eye(128, dtype=np.float32),
    }
    out["eWg"] = np.ascontiguousarray(Wg)
    out["eWu"] = np.ascontiguousarray(Wu)
    out["eWd"] = np.ascontiguousarray(Wd)
    out["sWg"] = np.ascontiguousarray(sWg)
    out["sWu"] = np.ascontiguousarray(sWu)
    out["sWd"] = np.ascontiguousarray(sWd)
    return out


def emit_l1_moe_body(nc, es, T, px, x_in, x_out, dram, ch, bufes=None):
    TT = 512
    NTT = T // TT
    NE = 16
    NSUB = T // 128

    if bufes is None:
        bufes = es
    cst = {}
    cst["onescol_bf"] = bufes.enter_context(nc.sbuf_tensor(px + "onesc", [128, 1], BF16))
    cst["onesrow_f32"] = bufes.enter_context(nc.sbuf_tensor(px + "onesr", [1, 128], F32))
    ident = bufes.enter_context(nc.sbuf_tensor(px + "ident", [128, 128], F32))
    gws = bufes.enter_context(nc.sbuf_tensor(px + "gws", [128, 16 * 16], BF16))
    sgws = bufes.enter_context(nc.sbuf_tensor(px + "sgws", [128, 16], BF16))
    x_ln = bufes.enter_context(nc.sbuf_tensor(px + "x_ln", [128, NCH * T], BF16))
    combT_d = nc.dram_tensor(px + "combT_d", [17, T], F32).ap()

    ld0 = Sem(es, nc, px + "ld0")
    ld0g = Sem(es, nc, px + "ld0g")
    with nc.Block() as blk0:
        @blk0.sync
        def _(sync):
            sync.dma_start(out=cst["onescol_bf"][:], in_=dram["onescol_bf"]).then_inc(ld0.h, 16)
            sync.dma_start(out=cst["onesrow_f32"][:], in_=dram["onesrow_f32"]).then_inc(ld0.h, 16)
            sync.dma_start(out=ident[:], in_=dram["ident_f32"]).then_inc(ld0.h, 16)

        @blk0.gpsimd
        def _(g_):
            g_.dma_start(out=gws[:].rearrange("p (c f) -> p c f", f=16),
                         in_=dram["gwT"].rearrange("(c p) f -> p c f", p=128)).then_inc(ld0g.h, 16)
            g_.dma_start(out=sgws[:], in_=dram["sgw"]).then_inc(ld0g.h, 16)
            g_.wait_ge(ld0.h, 48)
            g_.wait_ge(ld0g.h, 32)

        @blk0.tensor
        def _(e):
            e.wait_ge(ld0.h, 48)
            e.wait_ge(ld0g.h, 32)

        @blk0.vector
        def _(e):
            e.wait_ge(ld0.h, 48)
            e.wait_ge(ld0g.h, 32)

        @blk0.scalar
        def _(e):
            e.wait_ge(ld0.h, 48)
            e.wait_ge(ld0g.h, 32)

    x_in_r = x_in.rearrange("(c p) t -> p c t", p=128)
    x_out_r = x_out.rearrange("(c p) t -> p c t", p=128)

    emit_layernorm(nc, es, T, px, x_in_r, x_ln, cst, ch)

    with contextlib.ExitStack() as res:
        lgs = res.enter_context(nc.sbuf_tensor(px + "lgs", [16, TT], F32))
        rsb = res.enter_context(nc.sbuf_tensor(px + "rsb", [128, 16], F32))
        sgr = res.enter_context(nc.sbuf_tensor(px + "sgr", [1, TT], F32))
        prb = res.enter_context(nc.sbuf_tensor(px + "prb", [128, 16], F32))
        wrk = res.enter_context(nc.sbuf_tensor(px + "wrk", [128, 16], F32))
        cmp_ = res.enter_context(nc.sbuf_tensor(px + "cmp", [128, 16], F32))
        ncol = res.enter_context(nc.sbuf_tensor(px + "ncol", [128, 8], F32))
        ctile = [res.enter_context(nc.sbuf_tensor(px + f"ctile{j}", [16, 128], F32)) for j in range(2)]
        plog = res.enter_context(nc.psum_tensor(px + "plog", [16, TT], F32))
        psg = res.enter_context(nc.psum_tensor(px + "psg", [1, TT], F32))
        ptr = res.enter_context(nc.psum_tensor(px + "ptr", [128, 16], F32))
        ptc = res.enter_context(nc.psum_tensor(px + "ptc", [16, 128], F32))

        r_mm = Sem(es, nc, px + "r_mm")
        r_ev = Sem(es, nc, px + "r_ev")
        r_tr = Sem(es, nc, px + "r_tr")
        r_te = Sem(es, nc, px + "r_te")
        r_cw = Sem(es, nc, px + "r_cw")
        r_tc = Sem(es, nc, px + "r_tc")
        r_ce = Sem(es, nc, px + "r_ce")
        r_sg = Sem(es, nc, px + "r_sg")
        r_ex = Sem(es, nc, px + "r_ex")
        r_sgm = Sem(es, nc, px + "r_sgm")
        r_cd = Sem(es, nc, px + "r_cd")

        with nc.Block() as blk:
            @blk.tensor
            def _(te):
                for tt in range(NTT):
                    if tt >= 1:
                        te.wait_ge(r_ev.h, tt)
                        te.wait_ge(r_sg.h, tt)
                    for c in range(NCH):
                        mm = te.matmul(plog[:], gws[:, c * 16:(c + 1) * 16],
                                       x_ln[:, c * T + tt * TT:c * T + (tt + 1) * TT],
                                       start=(c == 0), stop=(c == NCH - 1))
                    mm.then_inc(r_mm.h, 1)
                    for c in range(NCH):
                        mm = te.matmul(psg[:], sgws[:, c:c + 1],
                                       x_ln[:, c * T + tt * TT:c * T + (tt + 1) * TT],
                                       start=(c == 0), stop=(c == NCH - 1))
                    mm.then_inc(r_sgm.h, 1)
                    for sb in range(4):
                        s = tt * 4 + sb
                        te.wait_ge(r_ev.h, tt + 1)
                        if s >= 1:
                            te.wait_ge(r_te.h, s)
                        te.transpose(ptr[:], lgs[:, sb * 128:(sb + 1) * 128],
                                     ident[0:16, 0:16]).then_inc(r_tr.h, 1)
                        te.wait_ge(r_cw.h, s + 1)
                        if s >= 1:
                            te.wait_ge(r_ce.h, s)
                        te.transpose(ptc[:], prb[:], ident[:]).then_inc(r_tc.h, 1)

            @blk.scalar
            def _(sc):
                sc = Chain(sc, ch["s"])
                for tt in range(NTT):
                    sc.wait_ge(r_mm.h, tt + 1)
                    sc.activation(lgs[:], plog[:], AF.Copy, _inc=r_ev)
                    sc.wait_ge(r_sgm.h, tt + 1)
                    if tt >= 1:
                        sc.wait_ge(r_cd.h, 16 * (tt * 5))
                    sc.activation(sgr[0:1, :], psg[:], AF.Sigmoid, _inc=r_sg)
                    for sb in range(4):
                        s = tt * 4 + sb
                        sc.wait_ge(r_te.h, s + 1)
                        if s >= 1:
                            sc.wait_ge(r_cw.h, s)
                        sc.activation(wrk[:], rsb[:], AF.Exp, bias=ncol[:, 0:1], _inc=r_ex)

            @blk.vector
            def _(v):
                v = Chain(v, ch["v"])
                for tt in range(NTT):
                    for sb in range(4):
                        s = tt * 4 + sb
                        v.wait_ge(r_tr.h, s + 1)
                        if s >= 1:
                            v.wait_ge(r_ex.h, s)
                        v.tensor_copy(rsb[:], ptr[:])
                        v.reduce_max(ncol[:, 1:2], rsb[:], axis=AX.X)
                        v.tensor_scalar(ncol[:, 0:1], ncol[:, 1:2], -1.0, None,
                                        op0=ALU.mult, _inc=r_te)
                        v.wait_ge(r_ex.h, s + 1)
                        v.reduce_sum(ncol[:, 2:3], wrk[:], axis=AX.X)
                        v.reciprocal(ncol[:, 3:4], ncol[:, 2:3])
                        v.tensor_single_scalar(prb[:], wrk[:], ncol[:, 3:4], ALU.mult)
                        v.tensor_copy(wrk[:], prb[:])
                        for it in range(4):
                            v.reduce_max(ncol[:, 4:5], wrk[:], axis=AX.X)
                            if it < 3:
                                v.tensor_single_scalar(cmp_[:], wrk[:], ncol[:, 4:5], ALU.is_equal)
                                v.scalar_tensor_tensor(wrk[:], cmp_[:], -1e30, wrk[:],
                                                       op0=ALU.mult, op1=ALU.add)
                        v.tensor_single_scalar(cmp_[:], prb[:], ncol[:, 4:5], ALU.is_ge)
                        v.tensor_mul(prb[:], prb[:], cmp_[:])
                        v.reduce_sum(ncol[:, 5:6], prb[:], axis=AX.X)
                        v.reciprocal(ncol[:, 6:7], ncol[:, 5:6])
                        v.tensor_single_scalar(prb[:], prb[:], ncol[:, 6:7],
                                               ALU.mult, _inc=r_cw)
                        v.wait_ge(r_tc.h, s + 1)
                        if s >= 2:
                            v.wait_ge(r_cd.h, 16 * (s - 1 + (s - 1) // 4 + 1))
                        v.tensor_copy(ctile[s % 2][:], ptc[:], _inc=r_ce)

            @blk.sync
            def _(sync):
                nd = 0
                for tt in range(NTT):
                    for sb in range(4):
                        s = tt * 4 + sb
                        sync.wait_ge(r_cd.h, 16 * nd)
                        sync.wait_ge(r_ce.h, s + 1)
                        sync.dma_start(out=combT_d.rearrange("e (u t) -> e u t", t=128)
                                       [0:16, s, :],
                                       in_=ctile[s % 2][:]).then_inc(r_cd.h, 16)
                        nd += 1
                    sync.wait_ge(r_cd.h, 16 * nd)
                    sync.wait_ge(r_sg.h, tt + 1)
                    sync.dma_start(out=combT_d[16:17, tt * TT:(tt + 1) * TT],
                                   in_=sgr[0:1, :]).then_inc(r_cd.h, 16)
                    nd += 1

    with contextlib.ExitStack() as mes:
        wge = [mes.enter_context(nc.sbuf_tensor(px + f"wge{j}", [128, 16 * 512], BF16)) for j in range(2)]
        wue = [mes.enter_context(nc.sbuf_tensor(px + f"wue{j}", [128, 16 * 512], BF16)) for j in range(2)]
        wde = mes.enter_context(nc.sbuf_tensor(px + "wde", [128, 4 * 2048], BF16))
        silu_sb = mes.enter_context(nc.sbuf_tensor(px + "silu", [128, TT], F32))
        int0 = mes.enter_context(nc.sbuf_tensor(px + "int0", [128, TT], F32))
        inter = mes.enter_context(nc.sbuf_tensor(px + "inter", [128, 4 * TT], BF16))
        macc = mes.enter_context(nc.sbuf_tensor(px + "macc", [128, NCH * TT], F32))
        xres = [mes.enter_context(nc.sbuf_tensor(px + f"xres{j}", [128, TT], F32)) for j in range(2)]
        oout = [mes.enter_context(nc.sbuf_tensor(px + f"oout{j}", [128, TT], F32)) for j in range(2)]
        pg = [mes.enter_context(nc.psum_tensor(px + f"pg{j}", [128, TT], F32)) for j in range(2)]
        pu = [mes.enter_context(nc.psum_tensor(px + f"pu{j}", [128, TT], F32)) for j in range(2)]
        pd = [mes.enter_context(nc.psum_tensor(px + f"pd{j}", [128, TT], F32)) for j in range(2)]
        pcb = mes.enter_context(nc.psum_tensor(px + "pcb", [128, TT], F32))

        m_wg = Sem(es, nc, px + "m_wg")
        m_wd = Sem(es, nc, px + "m_wd")
        m_cb = Sem(es, nc, px + "m_cb")
        m_cbc = Sem(es, nc, px + "m_cbc")
        m_gu = Sem(es, nc, px + "m_gu")
        m_si = Sem(es, nc, px + "m_si")
        m_i0 = Sem(es, nc, px + "m_i0")
        m_it = Sem(es, nc, px + "m_it")
        m_dm = Sem(es, nc, px + "m_dm")
        m_ac = Sem(es, nc, px + "m_ac")
        m_xr = Sem(es, nc, px + "m_xr")
        m_st = Sem(es, nc, px + "m_st")
        m_ev = Sem(es, nc, px + "m_ev")

        cb_sb = mes.enter_context(nc.sbuf_tensor(px + "cb_sb", [128, TT], F32))
        crow = [mes.enter_context(nc.sbuf_tensor(px + f"crow{j}", [1, TT], F32)) for j in range(2)]
        m_cr = Sem(es, nc, px + "m_cr")

        NEX = NE + 1

        def wsrc(e, kind):
            if e < NE:
                t = {"g": "eWg", "u": "eWu", "d": "eWd"}[kind]
                return dram[t][e]
            return dram[{"g": "sWg", "u": "sWu", "d": "sWd"}[kind]]

        with nc.Block() as blk:
            @blk.gpsimd
            def _(g_):
                g_.wait_ge(r_cd.h, 16 * NTT * 5)
                for tt in range(NTT):
                    for e in range(NEX):
                        k = tt * NEX + e
                        b = k % 2
                        g_.wait_ge(m_wg.h, 32 * k)
                        if k >= 2:
                            g_.wait_ge(m_gu.h, 8 * (k - 1))
                        g_.dma_start(out=wge[b][:].rearrange("p (c f) -> p c f", f=512),
                                     in_=wsrc(e, "g").rearrange("(c p) f -> p c f", p=128)).then_inc(m_wg.h, 16)
                        g_.dma_start(out=wue[b][:].rearrange("p (c f) -> p c f", f=512),
                                     in_=wsrc(e, "u").rearrange("(c p) f -> p c f", p=128)).then_inc(m_wg.h, 16)
                        g_.wait_ge(m_wd.h, 16 * k)
                        if k >= 1:
                            g_.wait_ge(m_dm.h, 16 * k)
                        g_.dma_start(out=wde[:].rearrange("p (c f) -> p c f", f=2048),
                                     in_=wsrc(e, "d").rearrange("(c p) f -> p c f", p=128)).then_inc(m_wd.h, 16)

            @blk.sync
            def _(sync):
                sync.wait_ge(r_cd.h, 16 * NTT * 5)
                idx = 0

                def xflow(tt):
                    nonlocal idx
                    for dm in range(NCH):
                        sync.wait_ge(m_xr.h, 16 * idx)
                        if idx >= 2:
                            sync.wait_ge(m_ev.h, idx - 1)
                        sync.dma_start(out=xres[idx % 2][:],
                                       in_=x_in_r[:, dm, tt * TT:(tt + 1) * TT]).then_inc(m_xr.h, 16)
                        sync.wait_ge(m_ev.h, idx + 1)
                        sync.wait_ge(m_st.h, 16 * idx)
                        sync.dma_start(out=x_out_r[:, dm, tt * TT:(tt + 1) * TT],
                                       in_=oout[idx % 2][:]).then_inc(m_st.h, 16)
                        idx += 1

                for tt in range(NTT):
                    if tt >= 1:
                        xflow(tt - 1)
                    for e in range(NEX):
                        k = tt * NEX + e
                        sync.wait_ge(m_cr.h, 16 * k)
                        if k >= 2:
                            sync.wait_ge(m_cb.h, k - 1)
                        sync.dma_start(out=crow[k % 2][:],
                                       in_=combT_d[e:e + 1, tt * TT:(tt + 1) * TT]).then_inc(m_cr.h, 16)
                xflow(NTT - 1)
                sync.wait_ge(m_st.h, 16 * NTT * NCH)

            @blk.tensor
            def _(te):
                for tt in range(NTT):
                    for e in range(NEX):
                        k = tt * NEX + e
                        te.wait_ge(m_cr.h, 16 * (k + 1))
                        if k >= 1:
                            te.wait_ge(m_cbc.h, k)
                        te.matmul(pcb[:], cst["onesrow_f32"][:], crow[k % 2][:],
                                  start=True, stop=True).then_inc(m_cb.h, 1)
                        te.wait_ge(m_wg.h, 32 * (k + 1))
                        for oc in range(4):
                            j = 4 * k + oc
                            if j >= 2:
                                te.wait_ge(m_si.h, j - 1)
                            for ic in range(NCH):
                                mm = te.matmul(pg[j % 2][:],
                                               wge[k % 2][:, ic * 512 + oc * 128:ic * 512 + (oc + 1) * 128],
                                               x_ln[:, ic * T + tt * TT:ic * T + (tt + 1) * TT],
                                               start=(ic == 0), stop=(ic == NCH - 1))
                            mm.then_inc(m_gu.h, 1)
                            if j >= 2:
                                te.wait_ge(m_it.h, j - 1)
                            for ic in range(NCH):
                                mm = te.matmul(pu[j % 2][:],
                                               wue[k % 2][:, ic * 512 + oc * 128:ic * 512 + (oc + 1) * 128],
                                               x_ln[:, ic * T + tt * TT:ic * T + (tt + 1) * TT],
                                               start=(ic == 0), stop=(ic == NCH - 1))
                            mm.then_inc(m_gu.h, 1)
                        te.wait_ge(m_wd.h, 16 * (k + 1))
                        te.wait_ge(m_it.h, 4 * (k + 1))
                        for dm in range(NCH):
                            kd = k * NCH + dm
                            if kd >= 2:
                                te.wait_ge(m_ac.h, kd - 1)
                            for fc in range(4):
                                mm = te.matmul(pd[kd % 2][:],
                                               wde[:, fc * 2048 + dm * 128:fc * 2048 + (dm + 1) * 128],
                                               inter[:, fc * TT:(fc + 1) * TT],
                                               start=(fc == 0), stop=(fc == 3))
                            mm.then_inc(m_dm.h, 1)

            @blk.scalar
            def _(sc):
                sc = Chain(sc, ch["s"])
                for tt in range(NTT):
                    for e in range(NEX):
                        k = tt * NEX + e
                        sc.wait_ge(m_cb.h, k + 1)
                        if k >= 1:
                            sc.wait_ge(m_i0.h, 4 * k)
                        sc.activation(cb_sb[:], pcb[:], AF.Copy, _inc=m_cbc)
                        for oc in range(4):
                            j = 4 * k + oc
                            sc.wait_ge(m_gu.h, 2 * j + 1)
                            if j >= 1:
                                sc.wait_ge(m_i0.h, j)
                            sc.activation(silu_sb[:], pg[j % 2][:], AF.Silu, _inc=m_si)

            @blk.vector
            def _(v):
                v = Chain(v, ch["v"])
                idx = 0
                for tt in range(NTT):
                    for e in range(NEX):
                        k = tt * NEX + e
                        if k >= 1:
                            v.wait_ge(m_dm.h, 16 * k)
                        for oc in range(4):
                            j = 4 * k + oc
                            v.wait_ge(m_si.h, j + 1)
                            v.wait_ge(m_gu.h, 2 * j + 2)
                            v.tensor_mul(int0[:], silu_sb[:], pu[j % 2][:], _inc=m_i0)
                            v.wait_ge(m_cbc.h, k + 1)
                            v.tensor_mul(inter[:, oc * TT:(oc + 1) * TT], int0[:],
                                         cb_sb[:], _inc=m_it)
                        for dm in range(NCH):
                            kd = k * NCH + dm
                            v.wait_ge(m_dm.h, kd + 1)
                            if e == 0:
                                v.tensor_copy(macc[:, dm * TT:(dm + 1) * TT], pd[kd % 2][:],
                                              _inc=m_ac)
                            elif e < NEX - 1:
                                v.tensor_add(macc[:, dm * TT:(dm + 1) * TT],
                                             macc[:, dm * TT:(dm + 1) * TT], pd[kd % 2][:],
                                             _inc=m_ac)
                            else:
                                v.tensor_add(macc[:, dm * TT:(dm + 1) * TT],
                                             macc[:, dm * TT:(dm + 1) * TT], pd[kd % 2][:],
                                             _inc=m_ac)
                                v.wait_ge(m_xr.h, 16 * (idx + 1))
                                if idx >= 2:
                                    v.wait_ge(m_st.h, 16 * (idx - 1))
                                v.tensor_add(oout[idx % 2][:], macc[:, dm * TT:(dm + 1) * TT],
                                             xres[idx % 2][:], _inc=m_ev)
                                idx += 1


def build_l1_moe(T=2048):
    nc = bass.Bass("TRN2", target_bir_lowering=False)
    ins = {}
    ins["x_in"] = nc.dram_tensor("x_in", [D, T], F32, kind="ExternalInput").ap()
    for nm, shp in (("gwT", [2048, 16]), ("sgw", [128, 16]), ("ident_f32", [128, 128]),
                    ("eWg", [16, 2048, 512]), ("eWu", [16, 2048, 512]), ("eWd", [16, 512, 2048]),
                    ("sWg", [2048, 512]), ("sWu", [2048, 512]), ("sWd", [512, 2048])):
        ins[nm] = nc.dram_tensor(nm, shp, F32, kind="ExternalInput").ap()
    ins["onescol_bf"] = nc.dram_tensor("onescol_bf", [128, 1], BF16, kind="ExternalInput").ap()
    ins["onesrow_f32"] = nc.dram_tensor("onesrow_f32", [1, 128], F32, kind="ExternalInput").ap()
    x_out = nc.dram_tensor("x_out", [D, T], F32, kind="ExternalOutput").ap()
    es = contextlib.ExitStack()
    with es:
        ch = {"v": ChainState(es, nc, "e_chv"), "s": ChainState(es, nc, "e_chs")}
        emit_l1_moe_body(nc, es, T, "e_", ins["x_in"], x_out,
                         {k: v for k, v in ins.items() if k != "x_in"}, ch)
    return nc



def emit_boundary(nc, es, px, ch, free_sems=None):
    bar = Sem(es, nc, px + "bar")
    with nc.Block() as blk:
        @blk.tensor
        def _(te):
            te.nop().then_inc(bar.h, 1)
            te.wait_ge(bar.h, 5 if free_sems is None else 6)

        @blk.sync
        def _(s):
            s.nop().then_inc(bar.h, 1)
            s.wait_ge(bar.h, 5 if free_sems is None else 6)

        @blk.gpsimd
        def _(g):
            g.nop().then_inc(bar.h, 1)
            g.wait_ge(bar.h, 5)
            if free_sems is not None:
                nc.clear_and_free_semaphores(free_sems)
                g.nop().then_inc(bar.h, 1)
            else:
                pass

        @blk.vector
        def _(v):
            if ch["v"].last is not None:
                v.wait_ge(*ch["v"].last)
            v.nop().then_inc(bar.h, 1)
            v.wait_ge(bar.h, 5 if free_sems is None else 6)

        @blk.scalar
        def _(s):
            if ch["s"].last is not None:
                s.wait_ge(*ch["s"].last)
            s.nop().then_inc(bar.h, 1)
            s.wait_ge(bar.h, 5 if free_sems is None else 6)

    ch["v"].last = None
    ch["s"].last = None
    if "p" in ch:
        ch["p"].last = None


def emit_exchange(nc, es, px, x2_d, x2f_d, poff_in, TO=2048, groups=None):
    cc_in = nc.dram_tensor(px + "cc_in", [D, TO], F32).ap()
    cc_out_t = nc.dram_tensor(px + "cc_out", [2 * D, TO], F32)
    cc_out = cc_out_t.ap()
    poff_s = es.enter_context(nc.sbuf_tensor(px + "poff", [1, 1], I32))
    s1 = Sem(es, nc, px + "s1")
    s2 = Sem(es, nc, px + "s2")
    s3 = Sem(es, nc, px + "s3")

    with nc.Block() as blk:
        @blk.sync
        def _(sync):
            sync.dma_start(out=cc_in, in_=x2_d).then_inc(s1.h, 16)
            sync.dma_start(out=x2f_d[:, 0:TO], in_=x2_d).then_inc(s1.h, 16)
            sync.dma_start(out=poff_s[:], in_=poff_in).then_inc(s1.h, 16)
            sync.wait_ge(s3.h, 16)

        @blk.gpsimd
        def _(g):
            g.wait_ge(s1.h, 48)
            g.collective_compute(
                "AllGather", ALU.bypass,
                replica_groups=groups or [[0, 1], [2, 3], [4, 5], [6, 7]],
                ins=[cc_in.opt()], outs=[cc_out.opt()],
            ).then_inc(s2.h, 1)
            g.wait_ge(s2.h, 1)
            with g.register(px + "poffr") as reg:
                g.reg_load(reg, poff_s[0:1, 0:1])
                g.dma_start(out=x2f_d[:, TO:2 * TO],
                            in_=bass.AP(cc_out_t, reg, [[TO, D], [1, TO]])).then_inc(s3.h, 16)
            g.wait_ge(s3.h, 16)



TFULL = 4096
TOWN = 2048


def _perm_cols(a, h):
    if h == 0:
        return np.ascontiguousarray(a)
    return np.ascontiguousarray(np.concatenate([a[:, TOWN:], a[:, :TOWN]], axis=1))


def build_fused(TF=4096, TO=2048, groups=None):
    nc = bass.Bass("TRN2", target_bir_lowering=False)
    ins = {}

    def di(nm, shp, dt=F32):
        ins[nm] = nc.dram_tensor(nm, shp, dt, kind="ExternalInput").ap()
        return ins[nm]

    di("x0", [D, TF])
    di("poff", [1, 1], I32)
    for nm, shp, dt in (("wdkvT", [2048, 512], F32), ("wdqT", [2048, 512], F32),
                        ("wupkT", [512, 1024], F32), ("wupvT", [512, 1024], F32),
                        ("wupqT", [512, 1024], F32), ("wqrT", [512, 1024], F32),
                        ("wkrT", [2048, 64], F32), ("wfcT", [1024, 2048], F32),
                        ("bdkv", [128, 4], F32), ("bdq", [128, 4], F32),
                        ("bupk", [128, 8], F32), ("bupv", [128, 8], F32),
                        ("bupq", [128, 8], F32), ("bqr", [128, 8], F32),
                        ("bkr", [64, 1], F32), ("bfc", [128, 16], F32),
                        ("cp128", [128, TF], F32), ("sp128", [128, TF], F32),
                        ("rot128i", [128, 128], BF16)):
        di(nm, shp, dt)
    for nm, shp, dt in (("gmp", [D, NMLP], U8), ("gsp", [D, NMLP], U8),
                        ("ump", [D, NMLP], U8), ("usp", [D, NMLP], U8),
                        ("dmp", [NMLP, D], U8), ("dsp", [NMLP, D], U8),
                        ("g_scT", [64, NMLP], F32), ("u_scT", [64, NMLP], F32),
                        ("d_scT", [512, D], F32), ("sel16", [16 * 64, 128], BF16),
                        ("nibcols_f", [128, 3], F32)):
        di(nm, shp, dt)
    for nm, shp, dt in (("wqT", [2048, 2048], F32), ("wkT", [2048, 512], F32),
                        ("wvT", [2048, 512], F32), ("woT", [2048, 2048], F32),
                        ("ropeC", [128, TF], F32), ("ropeS", [128, TF], F32),
                        ("rot128", [128, 128], BF16)):
        di(nm, shp, dt)
    for nm, shp in (("gwT", [2048, 16]), ("sgw", [128, 16]), ("ident_f32", [128, 128]),
                    ("eWg", [16, 2048, 512]), ("eWu", [16, 2048, 512]), ("eWd", [16, 512, 2048]),
                    ("sWg", [2048, 512]), ("sWu", [2048, 512]), ("sWd", [512, 2048])):
        di(nm, shp, F32)
    di("onescol_bf", [128, 1], BF16)
    di("onesrow_f32", [1, 128], F32)

    x4 = nc.dram_tensor("x_out", [D, TO], F32, kind="ExternalOutput").ap()
    x1_d = nc.dram_tensor("x1_d", [D, TO], F32).ap()
    x2_d = nc.dram_tensor("x2_d", [D, TO], F32).ap()
    x2f_d = nc.dram_tensor("x2f_d", [D, TF], F32).ap()
    x3_d = nc.dram_tensor("x3_d", [D, TO], F32).ap()
    acc_d = nc.dram_tensor("facc", [D, TO], F32).ap()

    es = contextlib.ExitStack()
    with es:
        ch = {"v": ChainState(es, nc, "chv"), "s": ChainState(es, nc, "chs")}
        blocks_es = contextlib.ExitStack()

        snap = nc.snapshot_sems()
        with contextlib.ExitStack() as b1:
            emit_l0_attn_body(nc, es, TF, TO, 0, "m_", ins["x0"], x1_d, ins, ch, bufes=b1)
        emit_boundary(nc, es, "bd1", ch, nc.allocated_since(snap))

        snap = nc.snapshot_sems()
        with contextlib.ExitStack() as b2:
            emit_l0_ffn_body(nc, es, TO, "f_", x1_d, x2_d, acc_d,
                             {k: ins[k] for k in ("gmp", "gsp", "ump", "usp", "dmp", "dsp",
                                                  "g_scT", "u_scT", "d_scT", "sel16",
                                                  "nibcols_f", "onescol_bf", "onesrow_f32")},
                             ch=ch, bufes=b2)
        emit_boundary(nc, es, "bd2", ch, nc.allocated_since(snap))

        snap = nc.snapshot_sems()
        emit_exchange(nc, es, "xc_", x2_d, x2f_d, ins["poff"], TO, groups)
        emit_boundary(nc, es, "bd3", ch, nc.allocated_since(snap))

        snap = nc.snapshot_sems()
        with contextlib.ExitStack() as b3:
            emit_l1_attn_body(nc, es, TF, TO, 0, "a_", x2f_d, x3_d, ins, ch, bufes=b3)
        emit_boundary(nc, es, "bd4", ch, nc.allocated_since(snap))

        with contextlib.ExitStack() as b4:
            emit_l1_moe_body(nc, es, TO, "e_", x3_d, x4, ins, ch, bufes=b4)
    return nc


def kernel(**inputs):
    hc = host_consts()
    x0 = np.asarray(inputs["L0_attn_h"])
    l0w = {k[len("L0_attn_"):]: np.asarray(v) for k, v in inputs.items()
           if k.startswith("L0_attn_") and k != "L0_attn_h"}
    p0 = prep_l0_attn_host(l0w, TFULL)
    pf = prep_l0_ffn_host(np.asarray(inputs["L0_ffn_gate_W_packed"]), np.asarray(inputs["L0_ffn_gate_scales"]),
                          np.asarray(inputs["L0_ffn_up_W_packed"]), np.asarray(inputs["L0_ffn_up_scales"]),
                          np.asarray(inputs["L0_ffn_down_W_packed"]), np.asarray(inputs["L0_ffn_down_scales"]))
    p1 = prep_l1_attn_host(np.asarray(inputs["L1_attn_Wq"]), np.asarray(inputs["L1_attn_Wk"]),
                           np.asarray(inputs["L1_attn_Wv"]), np.asarray(inputs["L1_attn_Wo"]), TFULL)
    pm = prep_l1_moe_host(np.asarray(inputs["L1_ffn_gate_w"]), np.asarray(inputs["L1_ffn_Wg"]),
                          np.asarray(inputs["L1_ffn_Wu"]), np.asarray(inputs["L1_ffn_Wd"]),
                          np.asarray(inputs["L1_ffn_sWg"]), np.asarray(inputs["L1_ffn_sWu"]),
                          np.asarray(inputs["L1_ffn_sWd"]), np.asarray(inputs["L1_ffn_shared_gate_w"]))

    nc = build_fused(TFULL, TOWN)
    maps = []
    for c in range(8):
        h = c % 2
        m = {}
        m.update({k: v for k, v in p0.items() if k not in ("cp128", "sp128")})
        m["cp128"] = _perm_cols(p0["cp128"], h)
        m["sp128"] = _perm_cols(p0["sp128"], h)
        m.update(pf)
        m.update({k: v for k, v in p1.items() if k not in ("ropeC", "ropeS")})
        m["ropeC"] = _perm_cols(p1["ropeC"], h)
        m["ropeS"] = _perm_cols(p1["ropeS"], h)
        m.update(pm)
        m.update({k: hc[k] for k in ("sel16", "nibcols_f", "onescol_bf", "onesrow_f32")})
        m["x0"] = _perm_cols(np.ascontiguousarray(x0[c // 2].T), h)
        m["poff"] = np.array([[(1 - h) * D * TOWN]], dtype=np.int32)
        maps.append(m)
    res = run_bass_kernel_spmd(nc, maps, core_ids=list(range(8)))
    out = np.empty((4, 4096, 2048), dtype=np.float32)
    for c in range(8):
        s, h = c // 2, c % 2
        out[s, h * TOWN:(h + 1) * TOWN, :] = res.results[c]["x_out"].T
    return out
